# version 08
# baseline (speedup 1.0000x reference)
import math
import numpy as np
import ml_dtypes
from contextlib import ExitStack

import concourse.bass as bass
import concourse.mybir as mybir
from concourse.bass_utils import run_bass_kernel_spmd

F32 = mybir.dt.float32
BF16 = mybir.dt.bfloat16
AF = mybir.ActivationFunctionType
ALU = mybir.AluOpType
AX = mybir.AxisListType
bfnp = ml_dtypes.bfloat16

D = 1024
B = 4
S = 8192
SH = S // 2
NCH = D // 128
TT = 512

ENGS = ("sync", "tensor", "scalar", "vector", "gpsimd")
SAME_ENG_FIFO = False


def core_info(c):
    return {"b": c // 2, "h": c % 2, "has_prev": (c % 2 == 1)}


def _access(ap):
    t = ap.tensor
    name = t.name
    shape = list(t.shape)
    dims = ap.ap.to_list() if hasattr(ap.ap, "to_list") else list(ap.ap)
    off = ap.offset
    is_dram = "dram" in type(t).__name__.lower()
    if is_dram or len(shape) != 2:
        span = 1
        for st_, cnt in dims:
            span += abs(st_) * (cnt - 1)
        return (name, "flat", (off, off + span))
    F = shape[1]
    p0, f0 = off // F, off % F
    if dims and (dims[0][0] == F or dims[0][1] == 1):
        pcnt = dims[0][1]
        rest = dims[1:]
    else:
        pcnt = 1
        rest = dims
    fspan = 1
    for st_, cnt in rest:
        fspan += abs(st_) * (cnt - 1)
    return (name, "sb", (p0, p0 + pcnt, f0, f0 + fspan))


def _overlap(a, b):
    ka, ra = a
    kb, rb = b
    if ka != kb:
        return True
    if ka == "flat":
        return ra[0] < rb[1] and rb[0] < ra[1]
    return (ra[0] < rb[1] and rb[0] < ra[1] and
            ra[2] < rb[3] and rb[2] < ra[3])


def _covers(a, b):
    ka, ra = a
    kb, rb = b
    if ka != kb:
        return False
    if ka == "flat":
        return ra[0] <= rb[0] and ra[1] >= rb[1]
    return (ra[0] <= rb[0] and ra[1] >= rb[1] and
            ra[2] <= rb[2] and ra[3] >= rb[3])


class Ctx:
    def __init__(self, nc, es):
        self.nc = nc
        self.es = es
        self.streams = {e: [] for e in ENGS}
        self.sem = {e: es.enter_context(nc.semaphore(f"sem_{e}")) for e in ENGS}
        self.cnt = {e: 0 for e in ENGS}
        self.NSLOT = 8
        self.qsem = {q: [es.enter_context(nc.semaphore(f"qsem_{q}{i}"))
                         for i in range(self.NSLOT)]
                     for q in ("sync", "scalar", "gpsimd")}
        self.qcnt = {q: [0] * self.NSLOT for q in self.qsem}
        self.qidx = {q: 0 for q in self.qsem}
        self.water = {}
        self.acc = {}

    def _wait(self, eng, token):
        if token is None:
            return
        (kind, name), val = token
        if kind == "e":
            if name == eng and (name == "tensor" or SAME_ENG_FIFO):
                return
            sem = self.sem[name]
        else:
            q, slot = name
            sem = self.qsem[q][slot]
        key = (eng, kind, name)
        if self.water.get(key, 0) >= val:
            return
        self.water[key] = val

        def thunk(e, sem=sem, val=val):
            e.wait_ge(sem, val)
        self.streams[eng].append(thunk)

    def _deps(self, eng, reads, writes):
        toks = []
        for ap in reads:
            nm, kind, rect = _access(ap)
            for rec in self.acc.get(nm, ()):
                if rec["write"] and _overlap((kind, rect), rec["rect"]):
                    toks.append(rec["tok"])
        for ap in writes:
            nm, kind, rect = _access(ap)
            for rec in self.acc.get(nm, ()):
                if _overlap((kind, rect), rec["rect"]):
                    toks.append(rec["tok"])
        for t in toks:
            self._wait(eng, t)

    def _record(self, token, reads, writes):
        for ap in writes:
            nm, kind, rect = _access(ap)
            lst = self.acc.setdefault(nm, [])
            lst[:] = [r for r in lst if not _covers((kind, rect), r["rect"])]
            lst.append({"rect": (kind, rect), "tok": token, "write": True})
        for ap in reads:
            nm, kind, rect = _access(ap)
            lst = self.acc.setdefault(nm, [])
            lst.append({"rect": (kind, rect), "tok": token, "write": False})
            if len(lst) > 96:
                ws = [r for r in lst if r["write"]]
                rs = {}
                for r in lst:
                    if not r["write"]:
                        k = r["tok"][0]
                        if k not in rs or r["tok"][1] > rs[k]["tok"][1]:
                            rs[k] = r
                full = ("flat", (0, 1 << 60)) if kind == "flat" else ("sb", (0, 1 << 30, 0, 1 << 30))
                for r in rs.values():
                    r["rect"] = full
                lst[:] = ws + list(rs.values())

    def op(self, eng, fn, reads=(), writes=()):
        self._deps(eng, reads, writes)
        self.cnt[eng] += 1
        val = self.cnt[eng]
        sem = self.sem[eng]

        def thunk(e, fn=fn, sem=sem):
            fn(e).then_inc(sem, 1)
        self.streams[eng].append(thunk)
        tok = (("e", eng), val)
        self._record(tok, reads, writes)
        return tok

    def mm(self, out, lhsT, rhs, start, stop, extra_reads=()):
        reads = [lhsT, rhs] + list(extra_reads)
        return self.op("tensor",
                       lambda e: e.matmul(out, lhsT, rhs, start=start, stop=stop),
                       reads=reads, writes=[out])

    def dma(self, q, out, in_, **kw):
        self._deps(q, [in_], [out])
        slot = self.qidx[q]
        self.qidx[q] = (slot + 1) % self.NSLOT
        if self.qcnt[q][slot] > 0:
            self._wait(q, (("q", (q, slot)), self.qcnt[q][slot]))
        self.qcnt[q][slot] += 16
        qsem = self.qsem[q][slot]

        def thunk(e, out=out, in_=in_, qsem=qsem, kw=kw):
            e.dma_start(out=out, in_=in_, **kw).then_inc(qsem, 16)
        self.streams[q].append(thunk)
        tok = (("q", (q, slot)), self.qcnt[q][slot])
        self._record(tok, [in_], [out])
        return tok

    def collective(self, kind, op_, rg_groups, ins, outs):
        self._deps("gpsimd", ins, outs)
        self.cnt["gpsimd"] += 1
        val = self.cnt["gpsimd"]
        sem = self.sem["gpsimd"]

        def thunk(e, ins=ins, outs=outs):
            e.collective_compute(kind, op_, replica_groups=rg_groups,
                                 ins=[a.opt() for a in ins],
                                 outs=[a.opt() for a in outs]).then_inc(sem, 1)
        self.streams["gpsimd"].append(thunk)
        tok = (("e", "gpsimd"), val)
        self._record(tok, ins, outs)
        return tok

    def barrier(self):
        snap_e = dict(self.cnt)
        for eng in ENGS:
            for p in ENGS:
                if snap_e[p] > 0 and not (p == eng == "tensor"):
                    self._wait(eng, (("e", p), snap_e[p]))
            for q in self.qcnt:
                for slot in range(self.NSLOT):
                    v = self.qcnt[q][slot]
                    if v > 0:
                        self._wait(eng, (("q", (q, slot)), v))

    def finalize(self):
        with self.nc.Block() as block:
            for eng in ENGS:
                stream = self.streams[eng]

                def run(e, stream=stream):
                    for f in stream:
                        f(e)
                getattr(block, eng)(run)


class Regs:
    def __init__(self, nc, es):
        self.nc, self.es = nc, es
        self.map = {}
        self.scopes = []
        self.uid = 0
        self.banks = [es.enter_context(nc.psum_tensor(f"psb{i}", [128, 512], F32))
                      for i in range(8)]

    def push(self):
        self.uid += 1
        self.scopes.append((ExitStack(), [], self.uid))

    def pop(self):
        es2, names, _ = self.scopes.pop()
        es2.close()
        for n in names:
            del self.map[n]

    def sb(self, name, shape, dtype=F32):
        if name not in self.map:
            if self.scopes:
                es2, names, uid = self.scopes[-1]
                self.map[name] = es2.enter_context(
                    self.nc.sbuf_tensor(f"{name}__{uid}", list(shape), dtype))
                names.append(name)
            else:
                self.map[name] = self.es.enter_context(
                    self.nc.sbuf_tensor(name, list(shape), dtype))
        return self.map[name]


def emit_setup(ctx, rg):
    for nm, shape, dt_, v in [("ones_col", [128, 1], F32, 1.0),
                              ("ones_row", [1, 128], F32, 1.0),
                              ("ones_col_b", [128, 1], BF16, 1.0),
                              ("eps5", [1, 1], F32, 1e-5)]:
        t = rg.sb(nm, shape, dt_)
        ctx.op("vector", lambda e, t=t, v=v: e.memset(t[:, :], v), writes=[t[:, :]])


def emit_ln_tile(ctx, rg, st, T, x_aps, out_f_aps, out_b_aps,
                 gamma_cols=None, beta_cols=None):
    sq, tmp, row = st["sq"], st["tmp"], st["row"]
    ps_sum, ps_sq, ps_rb, ps_zb = st["ps_sum"], st["ps_sq"], st["ps_rb"], st["ps_zb"]
    ones_col = rg.sb("ones_col", [128, 1])
    ones_row = rg.sb("ones_row", [1, 128])
    eps5 = rg.sb("eps5", [1, 1])
    MU, SA, SB = 0, T, 2 * T

    def R(a, b_):
        return row[0:1, a:b_]

    for c in range(NCH):
        ctx.op("scalar", lambda e, c=c: e.activation(sq[:, :], x_aps[c], AF.Square),
               reads=[x_aps[c]], writes=[sq[:, :]])
        ctx.mm(ps_sum, ones_col[:, :], x_aps[c], c == 0, c == NCH - 1)
        ctx.mm(ps_sq, ones_col[:, :], sq[:, :], c == 0, c == NCH - 1)
    ctx.op("vector", lambda e: e.tensor_scalar_mul(R(MU, MU + T), ps_sum, 1.0 / D),
           reads=[ps_sum], writes=[R(MU, MU + T)])
    ctx.op("vector", lambda e: e.tensor_scalar_mul(R(SA, SA + T), ps_sq, 1.0 / D),
           reads=[ps_sq], writes=[R(SA, SA + T)])
    ctx.op("vector", lambda e: e.tensor_mul(R(SB, SB + T), R(MU, MU + T), R(MU, MU + T)),
           reads=[R(MU, MU + T)], writes=[R(SB, SB + T)])
    ctx.op("vector", lambda e: e.tensor_sub(R(SA, SA + T), R(SA, SA + T), R(SB, SB + T)),
           reads=[R(SA, SA + T), R(SB, SB + T)], writes=[R(SA, SA + T)])
    ctx.op("scalar", lambda e: e.activation(R(SB, SB + T), R(SA, SA + T), AF.Sqrt, bias=eps5[0:1, 0:1]),
           reads=[R(SA, SA + T), eps5[0:1, 0:1]], writes=[R(SB, SB + T)])
    ctx.op("vector", lambda e: e.reciprocal(R(SA, SA + T), R(SB, SB + T)),
           reads=[R(SB, SB + T)], writes=[R(SA, SA + T)])
    ctx.op("vector", lambda e: e.tensor_mul(R(SB, SB + T), R(MU, MU + T), R(SA, SA + T)),
           reads=[R(MU, MU + T), R(SA, SA + T)], writes=[R(SB, SB + T)])
    ctx.mm(ps_rb, ones_row[0:1, :], R(SA, SA + T), True, True)
    ctx.mm(ps_zb, ones_row[0:1, :], R(SB, SB + T), True, True)

    toks_out = []
    for c in range(NCH):
        ctx.op("vector", lambda e, c=c: e.tensor_mul(tmp[:, :], x_aps[c], ps_rb),
               reads=[x_aps[c], ps_rb], writes=[tmp[:, :]])
        dst = out_f_aps[c] if out_f_aps is not None else out_b_aps[c]
        if gamma_cols is not None:
            ctx.op("vector", lambda e: e.tensor_sub(tmp[:, :], tmp[:, :], ps_zb),
                   reads=[tmp[:, :], ps_zb], writes=[tmp[:, :]])
            tcx = ctx.op("vector", lambda e, c=c, dst=dst: e.tensor_scalar(
                dst, tmp[:, :], gamma_cols[c], beta_cols[c], ALU.mult, ALU.add),
                reads=[tmp[:, :], gamma_cols[c], beta_cols[c]], writes=[dst])
        else:
            tcx = ctx.op("vector", lambda e, c=c, dst=dst: e.tensor_sub(dst, tmp[:, :], ps_zb),
                         reads=[tmp[:, :], ps_zb], writes=[dst])
        if out_f_aps is not None and out_b_aps is not None:
            tcx = ctx.op("gpsimd", lambda e, c=c: e.tensor_copy(out_b_aps[c], out_f_aps[c]),
                         reads=[out_f_aps[c]], writes=[out_b_aps[c]])
        toks_out.append(tcx)
    return toks_out


def make_ln_state(rg, tag, T, bank_ids):
    return {
        "sq": rg.sb(f"{tag}_sq", [128, T]),
        "tmp": rg.sb(f"{tag}_tmp", [128, T]),
        "row": rg.sb(f"{tag}_row", [1, 3 * T]),
        "ps_sum": rg.banks[bank_ids[0]][0:1, 0:T],
        "ps_sq": rg.banks[bank_ids[1]][0:1, 0:T],
        "ps_rb": rg.banks[bank_ids[2]][:, 0:T],
        "ps_zb": rg.banks[bank_ids[3]][:, 0:T],
    }


def build_b0_lnattn():
    nc = bass.Bass("TRN2", target_bir_lowering=False)
    es = ExitStack()
    ctx = Ctx(nc, es)
    rg = Regs(nc, es)
    io = {}
    for nm, shp, dt_, kind in [("x0T", [D, S + SH], F32, "ExternalInput"),
                               ("pk", [D, 128], F32, "ExternalInput"),
                               ("pv", [D, 128], F32, "ExternalInput"),
                               ("pq", [8, D, 128], F32, "ExternalInput"),
                               ("poT", [8, 128, D], F32, "ExternalInput"),
                               ("xn0f", [D, S + SH], F32, "ExternalOutput"),
                               ("xn0b", [D, S + SH], BF16, "ExternalOutput"),
                               ("s0", [D, SH], F32, "ExternalOutput")]:
        io[nm] = nc.dram_tensor(nm, shp, dt_, kind=kind).ap()
    emit_setup(ctx, rg)
    emit_b0_lnattn(ctx, rg, io)
    ctx.finalize()
    return nc, es


def emit_b0_lnattn(ctx, rg, io):
    NT_OWN = SH // TT
    NT_FULL = S // TT
    x0T, pk, pv, pq, poT = io["x0T"], io["pk"], io["pv"], io["pq"], io["poT"]
    xn0f, xn0b, s0 = io["xn0f"], io["xn0b"], io["s0"]

    st = make_ln_state(rg, "p1", TT, [0, 1, 2, 3])
    xs = rg.sb("p1_x", [128, NCH * TT])
    of = rg.sb("p1_of", [128, NCH * TT])
    ob = rg.sb("p1_ob", [128, NCH * TT], BF16)
    for it in range(NT_FULL + NT_OWN):
        t0 = it * TT
        x_aps = []
        for c in range(NCH):
            ap = xs[:, c * TT:(c + 1) * TT]
            ctx.dma("sync", ap, x0T[c * 128:(c + 1) * 128, t0:t0 + TT])
            x_aps.append(ap)
        of_aps = [of[:, c * TT:(c + 1) * TT] for c in range(NCH)]
        ob_aps = [ob[:, c * TT:(c + 1) * TT] for c in range(NCH)]
        emit_ln_tile(ctx, rg, st, TT, x_aps, of_aps, ob_aps)
        for c in range(NCH):
            ctx.dma("scalar", xn0f[c * 128:(c + 1) * 128, t0:t0 + TT], of_aps[c])
            ctx.dma("gpsimd", xn0b[c * 128:(c + 1) * 128, t0:t0 + TT], ob_aps[c])
    ctx.barrier()

    pk_sb = rg.sb("pk_sb", [128, NCH * 128], BF16)
    pv_sb = rg.sb("pv_sb", [128, NCH * 128], BF16)
    pq_sb = rg.sb("pq_sb", [128, 8 * NCH * 128], BF16)
    po_sb = rg.sb("po_sb", [128, 8 * D], BF16)
    K_sb = rg.sb("K_sb", [128, S], BF16)
    V_tok = rg.sb("V_tok", [128, S], BF16)
    for c in range(NCH):
        ctx.dma("gpsimd", pk_sb[:, c * 128:(c + 1) * 128], pk[c * 128:(c + 1) * 128, :])
        ctx.dma("gpsimd", pv_sb[:, c * 128:(c + 1) * 128], pv[c * 128:(c + 1) * 128, :])
    for hh in range(8):
        for c in range(NCH):
            ctx.dma("gpsimd", pq_sb[:, (hh * NCH + c) * 128:(hh * NCH + c + 1) * 128],
                    pq[hh, c * 128:(c + 1) * 128, :])
        ctx.dma("gpsimd", po_sb[:, hh * D:(hh + 1) * D], poT[hh, :, :])

    xb = rg.sb("p2_xb", [128, NCH * TT], BF16)
    sqk = rg.sb("p2_sqk", [128, TT])
    maxrow = rg.sb("p2_maxrow", [1, 8])
    ps_k = rg.banks[4][:, 0:TT]
    ps_k2 = rg.banks[5][0:1, 0:TT]
    ps_v = rg.banks[6][:, 0:128]
    ones_col = rg.sb("ones_col", [128, 1])
    ctx.op("vector", lambda e: e.memset(maxrow[:, :], 0.0), writes=[maxrow[:, :]])
    for it in range(NT_FULL):
        t0 = it * TT
        for c in range(NCH):
            ctx.dma("sync", xb[:, c * TT:(c + 1) * TT],
                    xn0b[c * 128:(c + 1) * 128, t0:t0 + TT])
        for c in range(NCH):
            ctx.mm(ps_k, pk_sb[:, c * 128:(c + 1) * 128], xb[:, c * TT:(c + 1) * TT],
                   c == 0, c == NCH - 1)
        ctx.op("scalar", lambda e, t0=t0: e.activation(K_sb[:, t0:t0 + TT], ps_k, AF.Copy),
               reads=[ps_k], writes=[K_sb[:, t0:t0 + TT]])
        ctx.op("scalar", lambda e: e.activation(sqk[:, :], ps_k, AF.Square),
               reads=[ps_k], writes=[sqk[:, :]])
        ctx.mm(ps_k2, ones_col[:, :], sqk[:, :], True, True)
        ctx.op("vector", lambda e: e.reduce_max(maxrow[0:1, 1:2], ps_k2, AX.X),
               reads=[ps_k2], writes=[maxrow[0:1, 1:2]])
        ctx.op("vector", lambda e: e.tensor_max(maxrow[0:1, 0:1], maxrow[0:1, 0:1], maxrow[0:1, 1:2]),
               reads=[maxrow[0:1, 0:2]], writes=[maxrow[0:1, 0:1]])
        for tc in range(TT // 128):
            for c in range(NCH):
                ctx.mm(ps_v, xb[:, c * TT + tc * 128: c * TT + (tc + 1) * 128],
                       pv_sb[:, c * 128:(c + 1) * 128], c == 0, c == NCH - 1)
            kc_g = it * (TT // 128) + tc
            ctx.op("scalar", lambda e, kc_g=kc_g: e.activation(
                V_tok[:, kc_g * 128:(kc_g + 1) * 128], ps_v, AF.Copy),
                reads=[ps_v], writes=[V_tok[:, kc_g * 128:(kc_g + 1) * 128]])
    ctx.op("scalar", lambda e: e.activation(maxrow[0:1, 2:3], maxrow[0:1, 0:1], AF.Sqrt),
           reads=[maxrow[0:1, 0:1]], writes=[maxrow[0:1, 2:3]])
    ctx.barrier()

    O_all = rg.sb("p2_O_all", [128, 8 * TT], BF16)
    qb = rg.sb("p2_qb", [128, TT], BF16)
    sqq = rg.sb("p2_sqq", [128, TT])
    p_b = rg.sb("p2_pb", [128, 4 * TT], BF16)
    rowq = rg.sb("p2_rowq", [1, TT])
    sc1 = rg.sb("p2_sc1", [1, 4])
    negb_sb = rg.sb("p2_negb", [128, 1])
    o_sb = rg.sb("p2_osb", [128, TT])
    recrow = rg.sb("p2_rec", [1, TT])
    ysb = rg.sb("p2_ysb", [128, TT])
    ps_q = rg.banks[0][:, 0:TT]
    ps_scores = [rg.banks[i][:, 0:TT] for i in (1, 2, 6)]
    ps_sigs = [rg.banks[3][0:1, 0:TT], rg.banks[3][32:33, 0:TT]]
    ps_os = [rg.banks[4][:, 0:TT], rg.banks[7][:, 0:TT]]
    ps_m_row = rg.banks[5][0:1, 0:TT]
    ps_m_col = rg.banks[5][:, 0:1]
    ps_m_full = rg.banks[5][:, 0:TT]
    ones_row = rg.sb("ones_row", [1, 128])
    ones_col_b = rg.sb("ones_col_b", [128, 1], BF16)

    NKC = S // 128

    pend_b0 = [None]

    def b0_epilogue(hh):
        ps_sig, ps_o = ps_sigs[hh % 2], ps_os[hh % 2]
        ctx.op("scalar", lambda e, ps_o=ps_o: e.activation(o_sb[:, :], ps_o, AF.Copy),
               reads=[ps_o], writes=[o_sb[:, :]])
        ctx.op("vector", lambda e, ps_sig=ps_sig: e.reciprocal(recrow[0:1, :], ps_sig),
               reads=[ps_sig], writes=[recrow[0:1, :]])
        ctx.mm(ps_m_full, ones_row[0:1, :], recrow[0:1, :], True, True)
        ctx.op("vector", lambda e, hh=hh: e.tensor_mul(
            O_all[:, hh * TT:(hh + 1) * TT], o_sb[:, :], ps_m_full),
            reads=[o_sb[:, :], ps_m_full], writes=[O_all[:, hh * TT:(hh + 1) * TT]])

    for qt in range(NT_OWN):
        q0 = S + qt * TT
        for c in range(NCH):
            ctx.dma("sync", xb[:, c * TT:(c + 1) * TT],
                    xn0b[c * 128:(c + 1) * 128, q0:q0 + TT])
        for hh in range(8):
            for c in range(NCH):
                ctx.mm(ps_q, pq_sb[:, (hh * NCH + c) * 128:(hh * NCH + c + 1) * 128],
                       xb[:, c * TT:(c + 1) * TT], c == 0, c == NCH - 1)
            ctx.op("scalar", lambda e: e.activation(qb[:, :], ps_q, AF.Copy),
                   reads=[ps_q], writes=[qb[:, :]])
            ctx.op("scalar", lambda e: e.activation(sqq[:, :], ps_q, AF.Square),
                   reads=[ps_q], writes=[sqq[:, :]])
            ctx.mm(ps_m_row, ones_col[:, :], sqq[:, :], True, True)
            ctx.op("scalar", lambda e: e.activation(rowq[0:1, :], ps_m_row, AF.Sqrt),
                   reads=[ps_m_row], writes=[rowq[0:1, :]])
            ctx.op("vector", lambda e: e.reduce_max(sc1[0:1, 0:1], rowq[0:1, :], AX.X),
                   reads=[rowq[0:1, :]], writes=[sc1[0:1, 0:1]])
            ctx.op("vector", lambda e: e.tensor_mul(sc1[0:1, 1:2], sc1[0:1, 0:1], maxrow[0:1, 2:3]),
                   reads=[sc1[0:1, 0:1], maxrow[0:1, 2:3]], writes=[sc1[0:1, 1:2]])
            ctx.op("vector", lambda e: e.tensor_scalar_mul(sc1[0:1, 2:3], sc1[0:1, 1:2], -1.0),
                   reads=[sc1[0:1, 1:2]], writes=[sc1[0:1, 2:3]])
            ctx.mm(ps_m_col, ones_row[0:1, :], sc1[0:1, 2:3], True, True)
            ctx.op("scalar", lambda e: e.activation(negb_sb[:, :], ps_m_col, AF.Copy),
                   reads=[ps_m_col], writes=[negb_sb[:, :]])

            ps_sig, ps_o = ps_sigs[hh % 2], ps_os[hh % 2]

            def do_sums(kc, ps_sig=ps_sig, ps_o=ps_o):
                pbs = p_b[:, (kc % 4) * TT:(kc % 4 + 1) * TT]
                ctx.mm(ps_sig, ones_col_b[:, :], pbs, kc == 0, kc == NKC - 1)
                ctx.mm(ps_o, V_tok[:, kc * 128:(kc + 1) * 128], pbs, kc == 0, kc == NKC - 1)
            for kc in range(NKC):
                psbank = ps_scores[kc % 3]
                pbslice = p_b[:, (kc % 4) * TT:(kc % 4 + 1) * TT]
                ctx.mm(psbank, K_sb[:, kc * 128:(kc + 1) * 128], qb[:, :], True, True)
                ctx.op("scalar", lambda e, psbank=psbank, pbslice=pbslice: e.activation(
                    pbslice, psbank, AF.Exp, bias=negb_sb[:, 0:1]),
                    reads=[psbank, negb_sb[:, 0:1]], writes=[pbslice])
                if kc >= 1:
                    do_sums(kc - 1)
            do_sums(NKC - 1)
            if pend_b0[0] is not None:
                b0_epilogue(pend_b0[0])
            pend_b0[0] = hh
        b0_epilogue(pend_b0[0])
        pend_b0[0] = None
        for dc in range(NCH):
            for hh in range(8):
                ctx.mm(ps_q, po_sb[:, hh * D + dc * 128: hh * D + (dc + 1) * 128],
                       O_all[:, hh * TT:(hh + 1) * TT], hh == 0, hh == 7)
            ctx.op("scalar", lambda e: e.activation(ysb[:, :], ps_q, AF.Copy),
                   reads=[ps_q], writes=[ysb[:, :]])
            ctx.dma("scalar", s0[dc * 128:(dc + 1) * 128, qt * TT:(qt + 1) * TT], ysb[:, :])
    ctx.barrier()


def prep_b0_lnattn(inputs, c):
    ci = core_info(c)
    b, hh = ci["b"], ci["h"]
    x0 = np.asarray(inputs["L0_attn_x"][b], np.float32)
    x0T = np.ascontiguousarray(x0.T)
    own = x0T[:, hh * SH:(hh + 1) * SH]
    x0T_ext = np.ascontiguousarray(np.concatenate([x0T, own], axis=1))
    return {
        "x0T": x0T_ext,
        "pk": np.ascontiguousarray(np.asarray(inputs["L0_attn_proj_k"], np.float32)),
        "pv": np.ascontiguousarray(np.asarray(inputs["L0_attn_proj_v"], np.float32)),
        "pq": np.ascontiguousarray(np.asarray(inputs["L0_attn_proj_q"], np.float32)),
        "poT": np.ascontiguousarray(np.asarray(inputs["L0_attn_proj_o"], np.float32).transpose(0, 2, 1)),
    }




L0F_HALO = 513
NV0 = 70


def _emit_b0_scalar(ctx, rg, qsrow, ksrow, qk_scale):
    sc = rg.sb("b0_sc", [1, 4])
    negb0 = rg.sb("negb0", [128, 1])
    ones_row = rg.sb("ones_row", [1, 128])
    ps = rg.banks[7][:, 0:1]
    ctx.op("vector", lambda e: e.reduce_max(sc[0:1, 0:1], qsrow[0:1, :], AX.X, apply_absolute_value=True),
           reads=[qsrow[0:1, :]], writes=[sc[0:1, 0:1]])
    ctx.op("vector", lambda e: e.reduce_max(sc[0:1, 1:2], ksrow[0:1, :], AX.X, apply_absolute_value=True),
           reads=[ksrow[0:1, :]], writes=[sc[0:1, 1:2]])
    ctx.op("vector", lambda e: e.tensor_mul(sc[0:1, 2:3], sc[0:1, 0:1], sc[0:1, 1:2]),
           reads=[sc[0:1, 0:2]], writes=[sc[0:1, 2:3]])
    ctx.op("vector", lambda e: e.tensor_scalar_mul(sc[0:1, 3:4], sc[0:1, 2:3], -float(qk_scale)),
           reads=[sc[0:1, 2:3]], writes=[sc[0:1, 3:4]])
    ctx.mm(ps, ones_row[0:1, :], sc[0:1, 3:4], True, True)
    ctx.op("scalar", lambda e: e.activation(negb0[:, :], ps, AF.Copy),
           reads=[ps], writes=[negb0[:, :]])
    return negb0


def _emit_l2norm_rope(ctx, rg, raw_bf, T, scale_col, rnb_bank, h2_bank, rot_bank,
                      amap, bmap, out_bf, tag, nch=NCH):
    hselA = rg.sb("hselA", [128, 2])
    hselB = rg.sb("hselB", [2, 128])
    rotm = rg.sb("rotm", [128, 128], BF16)
    sqt_ = rg.sb(f"{tag}_sq", [128, TT])
    rnr_ = rg.sb(f"{tag}_rn", [2, 2 * TT])
    t1_ = rg.sb(f"{tag}_t1", [128, TT])
    t2_ = rg.sb(f"{tag}_t2", [128, TT])
    qtl_ = rg.sb(f"{tag}_qt", [128, TT], BF16)
    sqt = sqt_[:, 0:T]; t1 = t1_[:, 0:T]; t2 = t2_[:, 0:T]; qtl = qtl_[:, 0:T]
    rnr = rnr_
    eps12 = rg.sb("eps12", [1, 1])
    for c in range(nch):
        rslice = raw_bf[:, c * T:(c + 1) * T]
        ctx.op("scalar", lambda e, rslice=rslice: e.activation(sqt[:, :], rslice, AF.Square),
               reads=[rslice], writes=[sqt[:, :]])
        ctx.mm(h2_bank[0:2, 0:T], hselA[:, :], sqt[:, :], True, True)
        ctx.op("scalar", lambda e: e.activation(rnr[0:2, 0:T], h2_bank[0:2, 0:T], AF.Sqrt),
               reads=[h2_bank[0:2, 0:T]], writes=[rnr[0:2, 0:T]])
        ctx.op("vector", lambda e: e.tensor_scalar_max(rnr[0:2, TT:TT + T], rnr[0:2, 0:T], 1e-12),
               reads=[rnr[0:2, 0:T]], writes=[rnr[0:2, TT:TT + T]])
        ctx.op("vector", lambda e: e.reciprocal(rnr[0:2, 0:T], rnr[0:2, TT:TT + T]),
               reads=[rnr[0:2, TT:TT + T]], writes=[rnr[0:2, 0:T]])
        ctx.mm(rnb_bank[:, 0:T], hselB[:, :], rnr[0:2, 0:T], True, True)
        ctx.op("vector", lambda e, rslice=rslice: e.scalar_tensor_tensor(
            qtl[:, :], rslice, scale_col, rnb_bank[:, 0:T], ALU.mult, ALU.mult),
            reads=[rslice, scale_col, rnb_bank[:, 0:T]], writes=[qtl[:, :]])
        ctx.mm(rot_bank[:, 0:T], rotm[:, :], qtl[:, :], True, True)
        ctx.op("vector", lambda e, c=c: e.tensor_mul(t1[:, :], qtl[:, :], amap[c]),
               reads=[qtl[:, :], amap[c]], writes=[t1[:, :]])
        ctx.op("vector", lambda e, c=c: e.tensor_mul(t2[:, :], rot_bank[:, 0:T], bmap[c]),
               reads=[rot_bank[:, 0:T], bmap[c]], writes=[t2[:, :]])
        ctx.op("vector", lambda e, c=c: e.tensor_add(out_bf[:, c * T:(c + 1) * T], t1[:, :], t2[:, :]),
               reads=[t1[:, :], t2[:, :]], writes=[out_bf[:, c * T:(c + 1) * T]])


def build_l0f_attn():
    return build_lf_attn(8, 256)


def build_l1f_attn():
    return build_lf_attn(4, 512)


def build_lf_attn(NI, W):
    HW = L0F_HALO
    NKC = (W + TT) // 128
    INNER = NI * 128
    nc = bass.Bass("TRN2", target_bir_lowering=False)
    es = ExitStack()
    ctx = Ctx(nc, es)
    rg = Regs(nc, es)
    io = {}
    for nm, shp, dt_, kind in [("xw", [D, HW + SH], F32, "ExternalInput"),
                               ("wqkv", [D, 3 * INNER], F32, "ExternalInput"),
                               ("wo", [INNER, D], F32, "ExternalInput"),
                               ("ropes", [4, 128, HW + SH], F32, "ExternalInput"),
                               ("qsrow", [1, 64], F32, "ExternalInput"),
                               ("ksrow", [1, 64], F32, "ExternalInput"),
                               ("vecs_in", [128, NV0], F32, "ExternalInput"),
                               ("hselA_in", [128, 2], F32, "ExternalInput"),
                               ("hselB_in", [2, 128], F32, "ExternalInput"),
                               ("rotm_in", [128, 128], F32, "ExternalInput"),
                               ("idneg_in", [128, 128], F32, "ExternalInput"),
                               ("masks_in", [2, NKC, 128, TT], F32, "ExternalInput"),
                               ("xo", [D, SH], F32, "ExternalOutput")]:
        io[nm] = nc.dram_tensor(nm, shp, dt_, kind=kind).ap()
    emit_setup(ctx, rg)
    io["xw_halo"] = io["xw"][:, 0:HW]
    io["xw"] = io["xw"][:, HW:]
    emit_lf_attn(ctx, rg, io, NI, W)
    ctx.finalize()
    return nc, es


def emit_lf_attn(ctx, rg, io, NI, W, lset=0):
    HW = L0F_HALO
    NKC = (W + TT) // 128
    NPV = W // 128
    INNER = NI * 128
    NT = SH // TT
    xw, wqkv, wo, ropes = io["xw"], io["wqkv"], io["wo"], io["ropes"]
    xw_halo = io["xw_halo"]
    qsrow, ksrow = io["qsrow"], io["ksrow"]
    vecs_d, hselA_d, hselB_d = io["vecs_in"], io["hselA_in"], io["hselB_in"]
    rotm_d, idneg_d, masks_d = io["rotm_in"], io["idneg_in"], io["masks_in"]
    xo = io["xo"]
    vecs = rg.sb("vecs", [128, NV0])
    hselA = rg.sb("hselA", [128, 2])
    hselB = rg.sb("hselB", [2, 128])
    rotm = rg.sb("rotm", [128, 128], BF16)
    idneg = rg.sb("idneg", [128, 128], BF16)
    eps12 = rg.sb("eps12", [1, 1])
    qsr = rg.sb("qsr", [1, 64])
    ksr = rg.sb("ksr", [1, 64])
    masks = rg.sb(f"masks_sb_{NI}", [128, 2 * NKC * TT], BF16)
    ctx.op("vector", lambda e: e.memset(eps12[:, :], 1e-12), writes=[eps12[:, :]])
    ctx.dma("sync", vecs[:, :], vecs_d)
    ctx.dma("sync", hselA[:, :], hselA_d)
    ctx.dma("sync", hselB[:, :], hselB_d)
    ctx.dma("sync", qsr[:, :], qsrow)
    ctx.dma("sync", ksr[:, :], ksrow)
    if io.get("load_consts", True):
        ctx.dma("gpsimd", rotm[:, :], rotm_d)
        ctx.dma("gpsimd", idneg[:, :], idneg_d)
        for mi in range(2):
            for kc in range(NKC):
                ctx.dma("gpsimd", masks[:, (mi * NKC + kc) * TT:(mi * NKC + kc + 1) * TT],
                        masks_d[mi, kc, :, :])
    negb0 = _emit_b0_scalar(ctx, rg, qsr, ksr, 8.0)
    wq_sb = rg.sb("wq_sb", [128, NCH * 3072], BF16)
    wo_sb = rg.sb("wo_sb", [128, 8 * 1024], BF16)
    for c in range(NCH):
        ctx.dma("gpsimd", wq_sb[:, c * 3 * INNER:(c + 1) * 3 * INNER], wqkv[c * 128:(c + 1) * 128, :])
    for c in range(NI):
        ctx.dma("gpsimd", wo_sb[:, c * 1024:(c + 1) * 1024], wo[c * 128:(c + 1) * 128, :])
    ropebuf = rg.sb("ropebuf", [128, 4 * TT])

    qraw = rg.sb("lf_qraw", [128, 8 * TT], BF16)
    kraw = rg.sb("lf_kraw", [128, 8 * TT], BF16)
    qcur = qraw
    kcur = kraw
    vcur = rg.sb("lf_vcur", [128, (TT // 128) * 1024], BF16)
    kprev = rg.sb("lf_kprev", [128, 8 * 256 if NI == 8 else 4 * 512], BF16)
    vprev = rg.sb("lf_vprev", [128, 2048], BF16)
    p_b = rg.sb("lf_pb", [128, 4 * TT], BF16)
    o_f = rg.sb("lf_of", [128, 2 * TT])
    recr = rg.sb("lf_rec", [1, 2 * TT])
    Oall = rg.sb("lf_Oall", [128, NI * TT], BF16)
    osum = o_f[:, 0:TT]
    xot = rg.sb("lf_xot", [128, TT])
    ones_col_b = rg.sb("ones_col_b", [128, 1])
    ones_row = rg.sb("ones_row", [1, 128])
    onesb = rg.sb("ones_col_bb", [128, 1], BF16)
    ctx.op("vector", lambda e: e.memset(onesb[:, :], 1.0), writes=[onesb[:, :]])

    def load_ropes(lo, width):
        aps = {}
        for ti in range(4):
            dst = ropebuf[:, ti * TT: ti * TT + width]
            ctx.dma("sync", dst, ropes[ti, :, lo:lo + width])
            aps[ti] = dst
        return aps

    def qkv_project(xsb_ap_of, T, want_q=True):
        W3 = 3 * INNER
        bs = [0]
        for base_oc, dst in (((NI, kraw),) if not want_q else ((0, qraw), (NI, kraw))):
            for oc in range(NI):
                bank = rg.banks[bs[0] % 6][:, 0:T]
                bs[0] += 1
                for c in range(NCH):
                    ctx.mm(bank, wq_sb[:, c * W3 + (base_oc + oc) * 128: c * W3 + (base_oc + oc + 1) * 128],
                           xsb_ap_of(c), c == 0, c == NCH - 1)
                ctx.op("scalar", lambda e, bank=bank, dst=dst, oc=oc, T=T: e.activation(
                    dst[:, oc * T:(oc + 1) * T], bank, AF.Copy),
                    reads=[bank], writes=[dst[:, oc * T:(oc + 1) * T]])
        ntc = T // 128
        nvh = max(1, INNER // 512)
        vw = min(512, INNER)
        for tc in range(ntc):
            for vh in range(nvh):
                bank = rg.banks[bs[0] % 6][:, 0:vw]
                bs[0] += 1
                for c in range(NCH):
                    ctx.mm(bank, xsb_ap_of(c)[:, tc * 128:(tc + 1) * 128],
                           wq_sb[:, c * W3 + 2 * INNER + vh * vw: c * W3 + 2 * INNER + (vh + 1) * vw],
                           c == 0, c == NCH - 1)
                ctx.op("scalar", lambda e, bank=bank, tc=tc, vh=vh: e.activation(
                    vcur[:, tc * INNER + vh * vw: tc * INNER + (vh + 1) * vw], bank, AF.Copy),
                    reads=[bank], writes=[vcur[:, tc * INNER + vh * vw: tc * INNER + (vh + 1) * vw]])

    Tp = W
    xt = rg.sb("lf_xt", [128, NCH * (TT + 1)])
    xsb = rg.sb("lf_xsb", [128, NCH * TT], BF16)
    for c in range(NCH):
        ctx.dma("sync", xt[:, c * (TT + 1): c * (TT + 1) + Tp + 1],
                xw_halo[c * 128:(c + 1) * 128, HW - W - 1:HW])
    for c in range(NCH):
        src = xt[:, c * (TT + 1) + 1: c * (TT + 1) + Tp + 1] if c < 4 else \
              xt[:, c * (TT + 1): c * (TT + 1) + Tp]
        ctx.op("gpsimd", lambda e, c=c, src=src: e.tensor_copy(xsb[:, c * Tp:(c + 1) * Tp], src),
               reads=[src], writes=[xsb[:, c * Tp:(c + 1) * Tp]])
    qkv_project(lambda c: xsb[:, c * Tp:(c + 1) * Tp], Tp, want_q=False)
    rp = load_ropes(HW - W, Tp)
    amap = [rp[2]] * NI
    bmap = [rp[3]] * NI
    _emit_l2norm_rope(ctx, rg, kraw, Tp, vecs[:, 69:70], rg.banks[4], rg.banks[5],
                      rg.banks[6], amap, bmap, kprev, "lr", nch=NI)
    for i in range(NPV):
        ctx.op("gpsimd", lambda e, i=i: e.tensor_copy(vprev[:, i * INNER:(i + 1) * INNER],
                                                      vcur[:, i * INNER:(i + 1) * INNER]),
               reads=[vcur[:, i * INNER:(i + 1) * INNER]], writes=[vprev[:, i * INNER:(i + 1) * INNER]])

    xt2 = rg.sb("lf_xt2", [128, NCH * (TT + 1)])
    for it in range(NT):
        t0 = it * TT
        xt = (rg.sb("lf_xt", [128, NCH * (TT + 1)]), xt2)[it % 2]
        xsb = rg.sb("lf_xsb", [128, NCH * TT], BF16)
        for c in range(NCH):
            if it == 0:
                ctx.dma("sync", xt[:, c * (TT + 1):c * (TT + 1) + 1],
                        xw_halo[c * 128:(c + 1) * 128, HW - 1:HW],
                        allow_slow_non_contiguous=True)
                ctx.dma("sync", xt[:, c * (TT + 1) + 1:(c + 1) * (TT + 1)],
                        xw[c * 128:(c + 1) * 128, 0:TT])
            else:
                ctx.dma("sync", xt[:, c * (TT + 1):(c + 1) * (TT + 1)],
                        xw[c * 128:(c + 1) * 128, t0 - 1: t0 + TT])
        for c in range(NCH):
            src = xt[:, c * (TT + 1) + 1:(c + 1) * (TT + 1)] if c < 4 else \
                  xt[:, c * (TT + 1):(c + 1) * (TT + 1) - 1]
            ctx.op("gpsimd", lambda e, c=c, src=src: e.tensor_copy(xsb[:, c * TT:(c + 1) * TT], src),
                   reads=[src], writes=[xsb[:, c * TT:(c + 1) * TT]])
        qkv_project(lambda c: xsb[:, c * TT:(c + 1) * TT], TT, want_q=True)
        rp = load_ropes(HW + t0, TT)
        _emit_l2norm_rope(ctx, rg, qraw, TT, vecs[:, 68:69], rg.banks[4], rg.banks[5],
                          rg.banks[6], [rp[0]] * NI, [rp[1]] * NI, qcur, "lr", nch=NI)
        _emit_l2norm_rope(ctx, rg, kraw, TT, vecs[:, 69:70], rg.banks[4], rg.banks[5],
                          rg.banks[6], [rp[2]] * NI, [rp[3]] * NI, kcur, "lr", nch=NI)
        mbase = NKC * TT if it == 0 else 0
        _emit_lf_attn_tile(ctx, rg, NI, W, mbase, qcur, kcur, vcur, kprev, vprev,
                           p_b, o_f, recr, Oall, masks, idneg, negb0, onesb)
        for oc in range(NCH):
            bank = rg.banks[oc % 6][:, 0:TT]
            for c in range(NI):
                ctx.mm(bank, wo_sb[:, c * 1024 + oc * 128: c * 1024 + (oc + 1) * 128],
                       Oall[:, c * TT:(c + 1) * TT], c == 0, c == NI - 1)
            ctx.op("scalar", lambda e, bank=bank: e.activation(osum[:, :], bank, AF.Copy),
                   reads=[bank], writes=[osum[:, :]])
            xres = xt[:, oc * (TT + 1) + 1:(oc + 1) * (TT + 1)]
            ctx.op("vector", lambda e, oc=oc, xres=xres: e.tensor_add(xot[:, :], osum[:, :], xres),
                   reads=[osum[:, :], xres], writes=[xot[:, :]])
            ctx.dma("scalar", xo[oc * 128:(oc + 1) * 128, t0:t0 + TT], xot[:, :])
        for c in range(NI):
            ctx.op("gpsimd", lambda e, c=c: e.tensor_copy(
                kprev[:, c * W:(c + 1) * W], kcur[:, c * TT + TT - W:(c + 1) * TT]),
                reads=[kcur[:, c * TT + TT - W:(c + 1) * TT]], writes=[kprev[:, c * W:(c + 1) * W]])
        for i in range(NPV):
            src_v = vcur[:, (TT // 128 - NPV + i) * INNER:(TT // 128 - NPV + i + 1) * INNER]
            ctx.op("gpsimd", lambda e, i=i, src_v=src_v: e.tensor_copy(
                vprev[:, i * INNER:(i + 1) * INNER], src_v),
                reads=[src_v], writes=[vprev[:, i * INNER:(i + 1) * INNER]])
    ctx.barrier()


def _emit_lf_attn_tile(ctx, rg, NI, W, mbase, qcur, kcur, vcur, kprev, vprev,
                       p_b, o_f, recr, Oall, masks, idneg, negb0, onesb):
    ones_row = rg.sb("ones_row", [1, 128])
    INNER = NI * 128
    NKC = (W + TT) // 128
    NPK = W // 128

    def epilogue(hc, hp, sig, pvb, sb_bank):
        r0 = 64 * hp
        ctx.op("scalar", lambda e, pvb=pvb, hp=hp: e.activation(
            o_f[0:64, hp * TT:(hp + 1) * TT], pvb, AF.Copy),
            reads=[pvb], writes=[o_f[0:64, hp * TT:(hp + 1) * TT]])
        ctx.op("vector", lambda e, sig=sig, hp=hp: e.reciprocal(
            recr[0:1, hp * TT:(hp + 1) * TT], sig),
            reads=[sig], writes=[recr[0:1, hp * TT:(hp + 1) * TT]])
        rb = rg.banks[4 + 3 * hp][0:64, 0:TT]
        ctx.mm(rb, ones_row[0:1, 0:64], recr[0:1, hp * TT:(hp + 1) * TT], True, True)
        dst = Oall[r0:r0 + 64, hc * TT:(hc + 1) * TT]
        ctx.op("vector", lambda e, rb=rb, dst=dst, hp=hp: e.tensor_mul(
            dst, o_f[0:64, hp * TT:(hp + 1) * TT], rb),
            reads=[o_f[0:64, hp * TT:(hp + 1) * TT], rb], writes=[dst])

    pend = [None]
    for hh in range(2 * NI):
        hc, hp = hh // 2, hh % 2
        r0 = 64 * hp
        sb_bank = rg.banks[4 + 3 * hp][:, 0:TT]
        sig = rg.banks[6][64 * hp:64 * hp + 1, 0:TT]
        pvb = rg.banks[5][64 * hp:64 * hp + 64, 0:TT]
        qh = qcur[r0:r0 + 64, hc * TT:(hc + 1) * TT]

        def vl_of(kc):
            if kc < NPK:
                return vprev[:, kc * INNER + hc * 128 + r0: kc * INNER + hc * 128 + r0 + 64]
            return vcur[:, (kc - NPK) * INNER + hc * 128 + r0: (kc - NPK) * INNER + hc * 128 + r0 + 64]

        def do_sums2(kc):
            pbs = p_b[:, ((hh % 2) * 2 + kc % 2) * TT:((hh % 2) * 2 + kc % 2 + 1) * TT]
            ctx.mm(sig, onesb[:, :], pbs, kc == 0, kc == NKC - 1)
            ctx.mm(pvb, vl_of(kc), pbs, kc == 0, kc == NKC - 1)
        for kc in range(NKC):
            if kc < NPK:
                klhs = kprev[r0:r0 + 64, hc * W + kc * 128: hc * W + (kc + 1) * 128]
            else:
                klhs = kcur[r0:r0 + 64, hc * TT + (kc - NPK) * 128: hc * TT + (kc - NPK + 1) * 128]
            mslice = masks[:, mbase + kc * TT: mbase + (kc + 1) * TT]
            pbuf = p_b[:, ((hh % 2) * 2 + kc % 2) * TT:((hh % 2) * 2 + kc % 2 + 1) * TT]
            ctx.mm(sb_bank, klhs, qh, True, False)
            ctx.mm(sb_bank, idneg[:, :], mslice, False, True)
            ctx.op("scalar", lambda e, sb_bank=sb_bank, pbuf=pbuf: e.activation(
                pbuf, sb_bank, AF.Exp, bias=negb0[:, 0:1]),
                reads=[sb_bank, negb0[:, 0:1]], writes=[pbuf])
            if kc >= 1:
                do_sums2(kc - 1)
        do_sums2(NKC - 1)
        if pend[0] is not None:
            epilogue(*pend[0])
        pend[0] = (hc, hp, sig, pvb, sb_bank)
    epilogue(*pend[0])
    pend[0] = None


def build_l0f_ffn():
    nc = bass.Bass("TRN2", target_bir_lowering=False)
    es = ExitStack()
    ctx = Ctx(nc, es)
    rg = Regs(nc, es)
    io = {}
    for nm, shp, dt_, kind in [("xw1", [D, 1 + SH], F32, "ExternalInput"),
                               ("w1p", [D, 5632], F32, "ExternalInput"),
                               ("w2p", [2816, 1024], F32, "ExternalInput"),
                               ("vecs_in", [128, NV0], F32, "ExternalInput"),
                               ("xo", [D, SH], F32, "ExternalOutput")]:
        io[nm] = nc.dram_tensor(nm, shp, dt_, kind=kind).ap()
    emit_setup(ctx, rg)
    emit_lf_ffn(ctx, rg, io)
    ctx.finalize()
    return nc, es


def emit_lf_ffn(ctx, rg, io):
    NT = SH // TT
    NIC2 = 22
    xw1, w1p, w2p, vecs_d, xo = io["xw1"], io["w1p"], io["w2p"], io["vecs_in"], io["xo"]
    xt2 = rg.sb("ff_xt2", [128, NCH * (TT + 1)])
    xn2 = rg.sb("ff_xn2", [128, NCH * TT], BF16)

    emit_setup(ctx, rg)
    vecs = rg.sb("vecs", [128, NV0])
    ctx.dma("sync", vecs[:, :], vecs_d)
    w1b = rg.sb("ff_w1b", [128, 4 * 8 * 128], BF16)
    w2b = rg.sb("ff_w2b", [128, 2 * 1024], BF16)

    ab = rg.sb("ff_ab", [128, NIC2 * TT], BF16)
    gb = rg.sb("ff_gb", [128, TT], BF16)
    ysum = rg.sb("ff_ysum", [128, TT])
    xot = rg.sb("ff_xot", [128, TT])
    st = make_ln_state(rg, "ffln", TT, [4, 5, 6, 7])
    for it in [NT - 1] + list(range(NT - 1)):
        t0 = it * TT
        xt = (rg.sb("ff_xt", [128, NCH * (TT + 1)]), xt2)[it % 2]
        xn = (rg.sb("ff_xn", [128, NCH * TT], BF16), xn2)[it % 2]
        for c in range(NCH):
            ctx.dma("sync", xt[:, c * (TT + 1):(c + 1) * (TT + 1)],
                    xw1[c * 128:(c + 1) * 128, t0:t0 + TT + 1])
        x_shift = [xt[:, c * (TT + 1) + 1:(c + 1) * (TT + 1)] if c < 4 else
                   xt[:, c * (TT + 1):(c + 1) * (TT + 1) - 1] for c in range(NCH)]
        gamma_cols = [vecs[:, c:c + 1] for c in range(NCH)]
        beta_cols = [vecs[:, 8 + c:8 + c + 1] for c in range(NCH)]
        xn_aps = [xn[:, c * TT:(c + 1) * TT] for c in range(NCH)]
        emit_ln_tile(ctx, rg, st, TT, x_shift, None, xn_aps,
                     gamma_cols=gamma_cols, beta_cols=beta_cols)
        for c2 in range(NIC2):
            sa, sg = (2 * c2) % 4, (2 * c2 + 1) % 4
            wa = w1b[:, sa * 1024:(sa + 1) * 1024]
            wg = w1b[:, sg * 1024:(sg + 1) * 1024]
            ctx.dma("sync", wa.rearrange("p (a c) -> p a c", a=8, c=128),
                    w1p[:, c2 * 128:(c2 + 1) * 128].rearrange("(a b) c -> b a c", a=8, b=128))
            ctx.dma("sync", wg.rearrange("p (a c) -> p a c", a=8, c=128),
                    w1p[:, (NIC2 + c2) * 128:(NIC2 + c2 + 1) * 128].rearrange("(a b) c -> b a c", a=8, b=128))
            bank_a = rg.banks[(2 * c2) % 6][:, 0:TT]
            bank_g = rg.banks[(2 * c2 + 1) % 6][:, 0:TT]
            for c in range(NCH):
                ctx.mm(bank_a, wa[:, c * 128:(c + 1) * 128], xn_aps[c], c == 0, c == NCH - 1)
            for c in range(NCH):
                ctx.mm(bank_g, wg[:, c * 128:(c + 1) * 128], xn_aps[c], c == 0, c == NCH - 1)
            acol = vecs[:, 16 + c2:16 + c2 + 1]
            gcol = vecs[:, 16 + NIC2 + c2:16 + NIC2 + c2 + 1]
            aslice = ab[:, c2 * TT:(c2 + 1) * TT]
            ctx.op("vector", lambda e, bank_a=bank_a, aslice=aslice, acol=acol: e.tensor_scalar_add(
                aslice, bank_a, acol), reads=[bank_a, acol], writes=[aslice])
            ctx.op("scalar", lambda e, bank_g=bank_g, gcol=gcol: e.activation(
                gb[:, :], bank_g, AF.Gelu, bias=gcol), reads=[bank_g, gcol], writes=[gb[:, :]])
            ctx.op("vector", lambda e, aslice=aslice: e.tensor_mul(aslice, aslice, gb[:, :]),
                   reads=[aslice, gb[:, :]], writes=[aslice])
        for c2 in range(NIC2):
            wslot = w2b[:, (c2 % 2) * 1024:(c2 % 2 + 1) * 1024]
            ctx.dma("sync", wslot, w2p[c2 * 128:(c2 + 1) * 128, :])
            for oc in range(NCH):
                ctx.mm(rg.banks[oc][:, 0:TT], wslot[:, oc * 128:(oc + 1) * 128],
                       ab[:, c2 * TT:(c2 + 1) * TT], c2 == 0, c2 == NIC2 - 1)
        for oc in range(NCH):
            bank = rg.banks[oc][:, 0:TT]
            bcol = vecs[:, 60 + oc:60 + oc + 1]
            ctx.op("vector", lambda e, bank=bank, bcol=bcol: e.tensor_scalar_add(
                ysum[:, :], bank, bcol), reads=[bank, bcol], writes=[ysum[:, :]])
            xres = xt[:, oc * (TT + 1) + 1:(oc + 1) * (TT + 1)]
            ctx.op("vector", lambda e, xres=xres: e.tensor_add(xot[:, :], ysum[:, :], xres),
                   reads=[ysum[:, :], xres], writes=[xot[:, :]])
            ctx.dma("scalar", xo[oc * 128:(oc + 1) * 128, t0:t0 + TT], xot[:, :])
    ctx.barrier()


def _l0f_consts(inputs, li):
    vecs = np.zeros((128, NV0), np.float32)
    ln_g = np.asarray(inputs["L0_ffn_ln_g"][li], np.float32)
    ln_b = np.asarray(inputs["L0_ffn_ln_b"][li], np.float32)
    b1 = np.asarray(inputs["L0_ffn_b1"][li], np.float32)
    b2 = np.asarray(inputs["L0_ffn_b2"][li], np.float32)
    qs = np.asarray(inputs["L0_ffn_q_scale"][li], np.float32)
    ks = np.asarray(inputs["L0_ffn_k_scale"][li], np.float32)
    for c in range(8):
        vecs[:, c] = ln_g[c * 128:(c + 1) * 128]
        vecs[:, 8 + c] = ln_b[c * 128:(c + 1) * 128]
        vecs[:, 60 + c] = b2[c * 128:(c + 1) * 128]
    b1p = np.zeros(5632, np.float32)
    b1p[0:2730] = b1[0:2730]
    b1p[2816:2816 + 2730] = b1[2730:5460]
    for c in range(44):
        vecs[:, 16 + c] = b1p[c * 128:(c + 1) * 128]
    vecs[:, 68] = np.tile(qs, 2)
    vecs[:, 69] = np.tile(ks, 2)
    W1 = np.asarray(inputs["L0_ffn_W1"][li], np.float32)
    w1p = np.zeros((D, 5632), np.float32)
    w1p[:, 0:2730] = W1[:, 0:2730]
    w1p[:, 2816:2816 + 2730] = W1[:, 2730:5460]
    W2 = np.asarray(inputs["L0_ffn_W2"][li], np.float32)
    w2p = np.zeros((2816, 1024), np.float32)
    w2p[0:2730, :] = W2
    return vecs, np.ascontiguousarray(w1p), np.ascontiguousarray(w2p)


_HSEL_A = None


def _l0f_static(c):
    ci = core_info(c)
    own0 = ci["h"] * SH
    aq, bq, ak, bk = rope_tables_l0(own0)
    ropes = np.stack([aq, bq, ak, bk])
    hselA = np.zeros((128, 2), np.float32)
    hselA[0:64, 0] = 1.0
    hselA[64:128, 1] = 1.0
    hselB = np.zeros((2, 128), np.float32)
    hselB[0, 0:64] = 1.0
    hselB[1, 64:128] = 1.0
    rotm = rot_matrix(64, 2)
    idneg = (np.eye(128, dtype=np.float32) * -1e5)
    masks = l0f_masks(ci["has_prev"])
    return dict(ropes=ropes, hselA_in=hselA, hselB_in=hselB, rotm_in=rotm,
                idneg_in=idneg, masks_in=masks)


def rope_tables_l0(own0):
    pos = np.arange(-L0F_HALO, SH) + own0
    pos = np.clip(pos, 0, None)
    aq, bq, ak, bk = rope_tables(S, 64, 256, pos, 8.0)
    return [np.ascontiguousarray(np.tile(t, (2, 1))) for t in (aq, bq, ak, bk)]


def rope_tables(n_total, dim_head, window, positions, qk_scale):
    half = np.arange(0, dim_head, 2, dtype=np.float64)
    inv_freq = 1.0 / 10000.0 ** (half / dim_head)
    t = positions.astype(np.float64)
    freqs = t[:, None] * inv_freq[None, :]
    freqs = np.concatenate([freqs, freqs], -1)
    base = (half + 0.4 * dim_head) / (1.4 * dim_head)
    power = (t - n_total // 2) / float(window // 2)
    scale = base[None, :] ** power[:, None]
    scale = np.concatenate([scale, scale], -1)
    cos, sin = np.cos(freqs), np.sin(freqs)
    aq = (cos * scale * qk_scale).T
    bq = (sin * scale * qk_scale).T
    ak = (cos / scale).T
    bk = (sin / scale).T
    return [np.asarray(x, np.float32) for x in (aq, bq, ak, bk)]


def rot_matrix(dim_head, nheads_per_chunk):
    dh = dim_head
    m = np.zeros((dh, dh), np.float32)
    for dcoord in range(dh // 2):
        m[dcoord + dh // 2, dcoord] = -1.0
        m[dcoord, dcoord + dh // 2] = 1.0
    full = np.zeros((128, 128), np.float32)
    for hh2 in range(nheads_per_chunk):
        full[hh2 * dh:(hh2 + 1) * dh, hh2 * dh:(hh2 + 1) * dh] = m
    return full


def l0f_masks(has_prev):
    out = np.zeros((2, 6, 128, TT), np.float32)
    q_rel = np.arange(TT)
    for kc in range(6):
        k_rel = np.arange(128) + kc * 128 - 256
        dmat = q_rel[None, :] - k_rel[:, None]
        invalid = (dmat < 0) | (dmat > 256)
        out[0, kc] = invalid
        inv0 = invalid.copy()
        if not has_prev and kc < 2:
            inv0[:, :] = 1.0
        out[1, kc] = inv0
    return np.ascontiguousarray(out)


def prep_l0f_attn(inputs, c, li):
    st_ = _l0f_static(c)
    vecs, _, _ = _l0f_consts(inputs, li)
    return dict(
        wqkv=np.ascontiguousarray(np.asarray(inputs["L0_ffn_Wqkv"][li], np.float32)),
        wo=np.ascontiguousarray(np.asarray(inputs["L0_ffn_Wo"][li], np.float32)),
        qsrow=np.asarray(inputs["L0_ffn_q_scale"][li], np.float32).reshape(1, 64),
        ksrow=np.asarray(inputs["L0_ffn_k_scale"][li], np.float32).reshape(1, 64),
        vecs_in=vecs, **st_)


def prep_l0f_ffn(inputs, c, li):
    vecs, w1p, w2p = _l0f_consts(inputs, li)
    return dict(w1p=w1p, w2p=w2p, vecs_in=vecs)


def build_combine():
    nc = bass.Bass("TRN2", target_bir_lowering=False)
    es = ExitStack()
    ctx = Ctx(nc, es)
    rg = Regs(nc, es)
    io = {}
    for nm, kind in [("xT", "ExternalInput"), ("sT", "ExternalInput"),
                     ("gT", "ExternalInput"), ("xo", "ExternalOutput")]:
        io[nm] = nc.dram_tensor(nm, [D, SH], F32, kind=kind).ap()
    emit_combine(ctx, rg, io)
    ctx.finalize()
    return nc, es


def emit_combine(ctx, rg, io):
    NT = SH // TT
    xT, sT, gT, xo = io["xT"], io["sT"], io["gT"], io["xo"]
    xb_ = rg.sb("cb_x", [128, TT])
    sb_ = rg.sb("cb_s", [128, TT])
    gb_ = rg.sb("cb_g", [128, TT])
    sg_ = rg.sb("cb_sg", [128, TT])
    ot_ = rg.sb("cb_o", [128, TT])
    for it in range(NT * NCH):
        c, ti = it % NCH, it // NCH
        t0 = ti * TT
        r = slice(c * 128, (c + 1) * 128)
        ctx.dma("sync", xb_[:, :], xT[r, t0:t0 + TT])
        ctx.dma("sync", sb_[:, :], sT[r, t0:t0 + TT])
        ctx.dma("sync", gb_[:, :], gT[r, t0:t0 + TT])
        ctx.op("scalar", lambda e: e.activation(sg_[:, :], gb_[:, :], AF.Silu),
               reads=[gb_[:, :]], writes=[sg_[:, :]])
        ctx.op("vector", lambda e: e.tensor_mul(sg_[:, :], sg_[:, :], sb_[:, :]),
               reads=[sg_[:, :], sb_[:, :]], writes=[sg_[:, :]])
        ctx.op("vector", lambda e: e.tensor_add(ot_[:, :], sg_[:, :], xb_[:, :]),
               reads=[sg_[:, :], xb_[:, :]], writes=[ot_[:, :]])
        ctx.dma("scalar", xo[r, t0:t0 + TT], ot_[:, :])
    ctx.barrier()


def build_ln(ncols):
    nc = bass.Bass("TRN2", target_bir_lowering=False)
    es = ExitStack()
    ctx = Ctx(nc, es)
    rg = Regs(nc, es)
    io = {}
    io["xT"] = nc.dram_tensor("xT", [D, ncols], F32, kind="ExternalInput").ap()
    io["xnf"] = nc.dram_tensor("xnf", [D, ncols], F32, kind="ExternalOutput").ap()
    io["xnb"] = nc.dram_tensor("xnb", [D, ncols], BF16, kind="ExternalOutput").ap()
    emit_setup(ctx, rg)
    emit_ln_pass(ctx, rg, io, ncols)
    ctx.finalize()
    return nc, es


def emit_ln_pass(ctx, rg, io, ncols):
    nt = ncols // TT
    xT, xnf, xnb = io["xT"], io["xnf"], io["xnb"]
    st = make_ln_state(rg, "gln", TT, [0, 1, 2, 3])
    xs = rg.sb("gln_x", [128, NCH * TT])
    of = rg.sb("gln_of", [128, NCH * TT])
    ob = rg.sb("gln_ob", [128, NCH * TT], BF16)
    for it in range(nt):
        t0 = it * TT
        x_aps = []
        for c in range(NCH):
            ap = xs[:, c * TT:(c + 1) * TT]
            ctx.dma("sync", ap, xT[c * 128:(c + 1) * 128, t0:t0 + TT])
            x_aps.append(ap)
        of_aps = [of[:, c * TT:(c + 1) * TT] for c in range(NCH)]
        ob_aps = [ob[:, c * TT:(c + 1) * TT] for c in range(NCH)]
        emit_ln_tile(ctx, rg, st, TT, x_aps, of_aps, ob_aps)
        for c in range(NCH):
            ctx.dma("scalar", xnf[c * 128:(c + 1) * 128, t0:t0 + TT], of_aps[c])
            ctx.dma("gpsimd", xnb[c * 128:(c + 1) * 128, t0:t0 + TT], ob_aps[c])
    ctx.barrier()


NV1 = 84


def build_l1_pre():
    nc = bass.Bass("TRN2", target_bir_lowering=False)
    es = ExitStack()
    ctx = Ctx(nc, es)
    rg = Regs(nc, es)
    NI2 = 16
    io = {}
    for nm, shp, dt_, kind in [("xnw", [D, 1024 + SH], F32, "ExternalInput"),
                               ("w_upT", [D, 4096], F32, "ExternalInput"),
                               ("wqd", [NI2, 128, 128], F32, "ExternalInput"),
                               ("wkd", [NI2, 128, 128], F32, "ExternalInput"),
                               ("wvd", [NI2, 128, 128], F32, "ExternalInput"),
                               ("wigT", [6144, 4], F32, "ExternalInput"),
                               ("wfgT", [6144, 4], F32, "ExternalInput"),
                               ("vecs_in", [128, NV1], F32, "ExternalInput"),
                               ("xc_o", [2048, SH], BF16, "ExternalOutput"),
                               ("z_o", [2048, SH], BF16, "ExternalOutput"),
                               ("qf_o", [2048, SH], BF16, "ExternalOutput"),
                               ("kf_o", [2048, SH], BF16, "ExternalOutput"),
                               ("vtok_o", [SH, 2048], BF16, "ExternalOutput"),
                               ("ktok_o", [SH, 2048], BF16, "ExternalOutput"),
                               ("arow_o", [4, SH], F32, "ExternalOutput"),
                               ("mrow_o", [4, SH], F32, "ExternalOutput"),
                               ("lcrow_o", [4, SH], F32, "ExternalOutput"),
                               ("gscal_o", [4, 8], F32, "ExternalOutput"),
                               ("Mc_o", [2048, 512], F32, "ExternalOutput"),
                               ("ncv_o", [128, 16], F32, "ExternalOutput")]:
        io[nm] = nc.dram_tensor(nm, shp, dt_, kind=kind).ap()
    emit_setup(ctx, rg)
    emit_l1_pre(ctx, rg, io)
    ctx.finalize()
    return nc, es


def emit_l1_pre(ctx, rg, io):
    NT = SH // TT
    NI2 = 16
    xnw, w_upT = io["xnw"], io["w_upT"]
    wqd, wkd, wvd, wigT, wfgT = io["wqd"], io["wkd"], io["wvd"], io["wigT"], io["wfgT"]
    vecs_d = io["vecs_in"]
    xc_o, z_o, qf_o, kf_o = io["xc_o"], io["z_o"], io["qf_o"], io["kf_o"]
    vtok_o, ktok_o = io["vtok_o"], io["ktok_o"]
    arow_o, mrow_o, lcrow_o, gscal_o = io["arow_o"], io["mrow_o"], io["lcrow_o"], io["gscal_o"]
    Mc_o, ncv_o = io["Mc_o"], io["ncv_o"]
    emit_setup(ctx, rg)
    vecs = rg.sb("vecs", [128, NV1])
    ctx.dma("sync", vecs[:, :], vecs_d)
    wup_sb = rg.sb("wup_sb", [128, NCH * 4096], BF16)
    for c in range(NCH):
        ctx.dma("gpsimd", wup_sb[:, c * 4096:(c + 1) * 4096], w_upT[c * 128:(c + 1) * 128, :])
    wq_sb = rg.sb("l1_wq", [128, NI2 * 128], BF16)
    wk_sb = rg.sb("l1_wk", [128, NI2 * 128], BF16)
    wv_sb = rg.sb("l1_wv", [128, NI2 * 128], BF16)
    for c in range(NI2):
        ctx.dma("gpsimd", wq_sb[:, c * 128:(c + 1) * 128], wqd[c, :, :])
        ctx.dma("gpsimd", wk_sb[:, c * 128:(c + 1) * 128], wkd[c, :, :])
        ctx.dma("gpsimd", wv_sb[:, c * 128:(c + 1) * 128], wvd[c, :, :])
    wig_sb = rg.sb("l1_wig", [128, 48 * 4], BF16)
    wfg_sb = rg.sb("l1_wfg", [128, 48 * 4], BF16)
    for c in range(48):
        ctx.dma("gpsimd", wig_sb[:, c * 4:(c + 1) * 4], wigT[c * 128:(c + 1) * 128, :])
        ctx.dma("gpsimd", wfg_sb[:, c * 4:(c + 1) * 4], wfgT[c * 128:(c + 1) * 128, :])
    idf = rg.sb("idf", [128, 128])
    onesq = rg.sb("onesq", [128, 128])
    ctx.op("vector", lambda e: e.memset(onesq[:, :], 1.0), writes=[onesq[:, :]])
    ctx.op("gpsimd", lambda e: e.affine_select(idf[:, :], onesq[:, :], [[-1, 128]],
                                               ALU.is_equal, 0.0, channel_multiplier=1),
           reads=[onesq[:, :]], writes=[idf[:, :]])

    xs = rg.sb("l1_xs", [128, NCH * TT], BF16)
    xh3 = rg.sb("l1_xh3", [128, NCH * 3], BF16)
    xm = rg.sb("l1_xm", [128, NI2 * (TT + 3)], BF16)
    ev1 = rg.sb("l1_ev1", [128, TT])
    ev2 = rg.sb("l1_ev2", [128, TT])
    xcb = rg.sb("l1_xcb", [128, NI2 * TT], BF16)
    qkvb = rg.sb("l1_qkvb", [128, 3 * TT], BF16)
    evb = rg.sb("l1_evb", [128, TT], BF16)
    tokb = rg.sb("l1_tokb", [128, 128], BF16)
    grows = rg.sb("l1_grows", [4, 6 * TT])
    ones4t = rg.sb("l1_ones4t", [4, TT])
    carry = rg.sb("l1_carry", [4, 4])
    gs = rg.sb("l1_gs", [4, 8])
    ones_col = rg.sb("ones_col", [128, 1])
    G = lambda i: grows[0:4, i * TT:(i + 1) * TT]
    ctx.op("vector", lambda e: e.memset(ones4t[0:4, :], 1.0), writes=[ones4t[0:4, :]])
    ctx.op("vector", lambda e: e.memset(carry[0:4, :], 0.0), writes=[carry[0:4, :]])
    ctx.op("vector", lambda e: e.memset(gs[0:4, :], 0.0), writes=[gs[0:4, :]])
    ctx.op("vector", lambda e: e.tensor_scalar_add(carry[0:4, 1:2], carry[0:4, 1:2], -3e38),
           reads=[carry[0:4, 1:2]], writes=[carry[0:4, 1:2]])

    for it in range(NT):
        t0 = it * TT
        for c in range(NCH):
            ctx.dma("gpsimd", xs[:, c * TT:(c + 1) * TT],
                    xnw[c * 128:(c + 1) * 128, 1024 + t0:1024 + t0 + TT])
        if it == 0:
            for c in range(NCH):
                ctx.dma("gpsimd", xh3[:, c * 3:(c + 1) * 3],
                        xnw[c * 128:(c + 1) * 128, 1021:1024])
            for oc in range(NI2):
                bank = rg.banks[0][:, 0:3]
                for c in range(NCH):
                    ctx.mm(bank, wup_sb[:, c * 4096 + oc * 128: c * 4096 + (oc + 1) * 128],
                           xh3[:, c * 3:(c + 1) * 3], c == 0, c == NCH - 1)
                ctx.op("scalar", lambda e, bank=bank, oc=oc: e.activation(
                    xm[:, oc * (TT + 3): oc * (TT + 3) + 3], bank, AF.Copy),
                    reads=[bank], writes=[xm[:, oc * (TT + 3): oc * (TT + 3) + 3]])
        else:
            for oc in range(NI2):
                srcv = xm[:, oc * (TT + 3) + TT: (oc + 1) * (TT + 3)]
                dstv = xm[:, oc * (TT + 3): oc * (TT + 3) + 3]
                ctx.op("gpsimd", lambda e, srcv=srcv, dstv=dstv: e.tensor_copy(dstv, srcv),
                       reads=[srcv], writes=[dstv])
        for oc in range(NI2):
            bank = rg.banks[oc % 2][:, 0:TT]
            for c in range(NCH):
                ctx.mm(bank, wup_sb[:, c * 4096 + oc * 128: c * 4096 + (oc + 1) * 128],
                       xs[:, c * TT:(c + 1) * TT], c == 0, c == NCH - 1)
            dstm = xm[:, oc * (TT + 3) + 3:(oc + 1) * (TT + 3)]
            ctx.op("scalar", lambda e, bank=bank, dstm=dstm: e.activation(dstm, bank, AF.Copy),
                   reads=[bank], writes=[dstm])
        for oc in range(NI2):
            bank = rg.banks[oc % 2][:, 0:TT]
            for c in range(NCH):
                ctx.mm(bank, wup_sb[:, c * 4096 + 2048 + oc * 128: c * 4096 + 2048 + (oc + 1) * 128],
                       xs[:, c * TT:(c + 1) * TT], c == 0, c == NCH - 1)
            ctx.op("scalar", lambda e, bank=bank: e.activation(evb[:, :], bank, AF.Copy),
                   reads=[bank], writes=[evb[:, :]])
            ctx.dma("scalar", z_o[oc * 128:(oc + 1) * 128, t0:t0 + TT], evb[:, :])
        for oc in range(NI2):
            xm_c = xm[:, oc * (TT + 3):(oc + 1) * (TT + 3)]
            ctx.op("vector", lambda e, xm_c=xm_c, oc=oc: e.tensor_scalar_mul(
                ev1[:, :], xm_c[:, 0:TT], vecs[:, oc * 4:oc * 4 + 1]),
                reads=[xm_c[:, 0:TT], vecs[:, oc * 4:oc * 4 + 1]], writes=[ev1[:, :]])
            for j in range(1, 4):
                ctx.op("vector", lambda e, xm_c=xm_c, oc=oc, j=j: e.tensor_scalar_mul(
                    ev2[:, :], xm_c[:, j:j + TT], vecs[:, oc * 4 + j:oc * 4 + j + 1]),
                    reads=[xm_c[:, j:j + TT], vecs[:, oc * 4 + j:oc * 4 + j + 1]], writes=[ev2[:, :]])
                ctx.op("vector", lambda e: e.tensor_add(ev1[:, :], ev1[:, :], ev2[:, :]),
                       reads=[ev1[:, :], ev2[:, :]], writes=[ev1[:, :]])
            ctx.op("vector", lambda e, oc=oc: e.tensor_scalar_add(ev1[:, :], ev1[:, :],
                                                                  vecs[:, 64 + oc:64 + oc + 1]),
                   reads=[ev1[:, :], vecs[:, 64 + oc:64 + oc + 1]], writes=[ev1[:, :]])
            dst = xcb[:, oc * TT:(oc + 1) * TT]
            ctx.op("scalar", lambda e, dst=dst: e.activation(dst, ev1[:, :], AF.Silu),
                   reads=[ev1[:, :]], writes=[dst])
            ctx.dma("scalar", xc_o[oc * 128:(oc + 1) * 128, t0:t0 + TT], dst)
        big = rg.banks[7][0:4, 0:TT]
        bfg = rg.banks[3][0:4, 0:TT]
        for oc in range(NI2):
            bq = rg.banks[4][:, 0:TT]
            bk = rg.banks[5][:, 0:TT]
            bv = rg.banks[6][:, 0:TT]
            xm_body = xm[:, oc * (TT + 3) + 3:(oc + 1) * (TT + 3)]
            ctx.mm(bq, wq_sb[:, oc * 128:(oc + 1) * 128], xcb[:, oc * TT:(oc + 1) * TT], True, True)
            ctx.mm(bk, wk_sb[:, oc * 128:(oc + 1) * 128], xcb[:, oc * TT:(oc + 1) * TT], True, True)
            ctx.mm(bv, wv_sb[:, oc * 128:(oc + 1) * 128], xm_body, True, True)
            for bank, qo in ((bq, 0), (bk, 1), (bv, 2)):
                dstb = qkvb[:, qo * TT:(qo + 1) * TT]
                ctx.op("scalar", lambda e, bank=bank, dstb=dstb: e.activation(dstb, bank, AF.Copy),
                       reads=[bank], writes=[dstb])
            ctx.dma("scalar", qf_o[oc * 128:(oc + 1) * 128, t0:t0 + TT], qkvb[:, 0:TT])
            ctx.dma("scalar", kf_o[oc * 128:(oc + 1) * 128, t0:t0 + TT], qkvb[:, TT:2 * TT])
            for qo, base in ((0, 0), (1, NI2), (2, 2 * NI2)):
                idx = base + oc
                st_ = (oc == 0 and qo == 0)
                sp_ = (oc == NI2 - 1 and qo == 2)
                ctx.mm(big, wig_sb[:, idx * 4:(idx + 1) * 4], qkvb[:, qo * TT:(qo + 1) * TT], st_, sp_)
                ctx.mm(bfg, wfg_sb[:, idx * 4:(idx + 1) * 4], qkvb[:, qo * TT:(qo + 1) * TT], st_, sp_)
            for tc in range(TT // 128):
                bank = rg.banks[(tc + oc) % 2][:, 0:128]
                ctx.mm(bank, xcb[:, oc * TT + tc * 128: oc * TT + (tc + 1) * 128],
                       wk_sb[:, oc * 128:(oc + 1) * 128], True, True)
                ctx.op("scalar", lambda e, bank=bank: e.activation(tokb[:, :], bank, AF.Copy),
                       reads=[bank], writes=[tokb[:, :]])
                ctx.dma("scalar", ktok_o[t0 + tc * 128:t0 + (tc + 1) * 128, oc * 128:(oc + 1) * 128],
                        tokb[:, :])
                bank2 = rg.banks[2][:, 0:128]
                xm_sl = xm[:, oc * (TT + 3) + 3 + tc * 128: oc * (TT + 3) + 3 + (tc + 1) * 128]
                ctx.mm(bank2, xm_sl, wv_sb[:, oc * 128:(oc + 1) * 128], True, True)
                ctx.op("scalar", lambda e, bank2=bank2: e.activation(tokb[:, :], bank2, AF.Copy),
                       reads=[bank2], writes=[tokb[:, :]])
                ctx.dma("scalar", vtok_o[t0 + tc * 128:t0 + (tc + 1) * 128, oc * 128:(oc + 1) * 128],
                        tokb[:, :])
        ctx.op("scalar", lambda e, big=big: e.activation(G(0), big, AF.Copy),
               reads=[big], writes=[G(0)])
        ctx.op("scalar", lambda e, bfg=bfg: e.activation(G(1), bfg, AF.Copy),
               reads=[bfg], writes=[G(1)])
        ctx.op("vector", lambda e: e.tensor_scalar_add(G(0), G(0), vecs[0:4, 80:81]),
               reads=[G(0), vecs[0:4, 80:81]], writes=[G(0)])
        ctx.op("vector", lambda e: e.tensor_scalar_add(G(1), G(1), vecs[0:4, 81:82]),
               reads=[G(1), vecs[0:4, 81:82]], writes=[G(1)])
        ctx.op("scalar", lambda e: e.activation(G(2), G(1), AF.Abs), reads=[G(1)], writes=[G(2)])
        ctx.op("scalar", lambda e: e.activation(G(2), G(2), AF.Exp, scale=-1.0),
               reads=[G(2)], writes=[G(2)])
        ctx.op("scalar", lambda e: e.activation(G(2), G(2), AF.Ln, bias=ones_col[0:4, 0:1]),
               reads=[G(2), ones_col[0:4, 0:1]], writes=[G(2)])
        ctx.op("vector", lambda e: e.tensor_scalar_min(G(1), G(1), 0.0),
               reads=[G(1)], writes=[G(1)])
        ctx.op("vector", lambda e: e.tensor_sub(G(2), G(1), G(2)),
               reads=[G(1), G(2)], writes=[G(2)])
        ctx.op("vector", lambda e: e.tensor_tensor_scan(G(3), ones4t[0:4, :], G(2),
                                                        carry[0:4, 0:1], ALU.mult, ALU.add),
               reads=[ones4t[0:4, :], G(2), carry[0:4, 0:1]], writes=[G(3)])
        ctx.op("vector", lambda e: e.tensor_sub(G(4), G(0), G(3)),
               reads=[G(0), G(3)], writes=[G(4)])
        ctx.op("vector", lambda e: e.tensor_tensor_scan(G(5), ones4t[0:4, :], G(4),
                                                        carry[0:4, 1:2], ALU.mult, ALU.max),
               reads=[ones4t[0:4, :], G(4), carry[0:4, 1:2]], writes=[G(5)])
        ctx.op("gpsimd", lambda e: e.tensor_copy(carry[0:4, 0:1], grows[0:4, 4 * TT - 1:4 * TT]),
               reads=[grows[0:4, 4 * TT - 1:4 * TT]], writes=[carry[0:4, 0:1]])
        ctx.op("gpsimd", lambda e: e.tensor_copy(carry[0:4, 1:2], grows[0:4, 6 * TT - 1:6 * TT]),
               reads=[grows[0:4, 6 * TT - 1:6 * TT]], writes=[carry[0:4, 1:2]])
        ctx.dma("scalar", lcrow_o[0:4, t0:t0 + TT], G(3))
        ctx.dma("scalar", arow_o[0:4, t0:t0 + TT], G(4))
        ctx.dma("scalar", mrow_o[0:4, t0:t0 + TT], G(5))
    ctx.op("gpsimd", lambda e: e.tensor_copy(gs[0:4, 0:1], carry[0:4, 0:1]),
           reads=[carry[0:4, 0:1]], writes=[gs[0:4, 0:1]])
    ctx.op("gpsimd", lambda e: e.tensor_copy(gs[0:4, 1:2], carry[0:4, 1:2]),
           reads=[carry[0:4, 1:2]], writes=[gs[0:4, 1:2]])
    ctx.dma("scalar", gscal_o[0:4, :], gs[0:4, :])
    ctx.op("vector", lambda e: e.tensor_scalar_mul(carry[0:4, 2:3], carry[0:4, 1:2], -1.0),
           reads=[carry[0:4, 1:2]], writes=[carry[0:4, 2:3]])
    idf = rg.sb("idf", [128, 128])
    onesq = rg.sb("onesq", [128, 128])
    ctx.op("vector", lambda e: e.memset(onesq[:, :], 1.0), writes=[onesq[:, :]])
    ctx.op("gpsimd", lambda e: e.affine_select(idf[:, :], onesq[:, :], [[-1, 128]],
                                               ALU.is_equal, 0.0, channel_multiplier=1),
           reads=[onesq[:, :]], writes=[idf[:, :]])
    au = rg.sb("l1_au", [4, 128])
    kt_sb = rg.sb("l1_kt", [128, 512], BF16)
    vt_sb = rg.sb("l1_vt", [128, 512], BF16)
    ku_sb = rg.sb("l1_ku", [128, 512], BF16)
    mc_ev = rg.sb("l1_mcev", [128, 512])
    ncrow = rg.sb("l1_ncrow", [1, 512])
    ncv_sb = rg.sb("l1_ncv", [128, 16])
    ucol1 = rg.sb("l1_ucol1", [128, 4])
    onesb = rg.sb("ones_col_b", [128, 1], BF16)
    for h in range(4):
        for tc in range(SH // 128):
            ctx.dma("sync", kt_sb[:, :], ktok_o[tc * 128:(tc + 1) * 128, h * 512:(h + 1) * 512])
            ctx.dma("sync", vt_sb[:, :], vtok_o[tc * 128:(tc + 1) * 128, h * 512:(h + 1) * 512])
            ctx.dma("sync", au[0:4, 0:128], arow_o[0:4, tc * 128:(tc + 1) * 128])
            ctx.op("scalar", lambda e: e.activation(au[0:4, :], au[0:4, :], AF.Exp,
                                                    bias=carry[0:4, 2:3]),
                   reads=[au[0:4, :], carry[0:4, 2:3]], writes=[au[0:4, :]])
            bankt = rg.banks[5][:, 0:4]
            ctx.op("tensor", lambda e, bankt=bankt: e.transpose(bankt, au[0:4, :], idf[0:4, 0:4]),
                   reads=[au[0:4, :], idf[0:4, 0:4]], writes=[bankt])
            ctx.op("scalar", lambda e, bankt=bankt: e.activation(ucol1[:, :], bankt, AF.Copy),
                   reads=[bankt], writes=[ucol1[:, :]])
            ctx.op("vector", lambda e, h=h: e.tensor_scalar_mul(ku_sb[:, :], kt_sb[:, :],
                                                                ucol1[:, h:h + 1]),
                   reads=[kt_sb[:, :], ucol1[:, h:h + 1]], writes=[ku_sb[:, :]])
            for dc in range(4):
                ctx.mm(rg.banks[dc][:, 0:512], ku_sb[:, dc * 128:(dc + 1) * 128], vt_sb[:, :],
                       tc == 0, tc == SH // 128 - 1)
            ctx.mm(rg.banks[4][0:1, 0:512], onesb[:, :], ku_sb[:, :], tc == 0, tc == SH // 128 - 1)
        for dc in range(4):
            ctx.op("scalar", lambda e, dc=dc: e.activation(mc_ev[:, :], rg.banks[dc][:, 0:512], AF.Copy),
                   reads=[rg.banks[dc][:, 0:512]], writes=[mc_ev[:, :]])
            ctx.dma("scalar", Mc_o[h * 512 + dc * 128: h * 512 + (dc + 1) * 128, :], mc_ev[:, :])
        ctx.op("scalar", lambda e: e.activation(ncrow[0:1, :], rg.banks[4][0:1, 0:512], AF.Copy),
               reads=[rg.banks[4][0:1, 0:512]], writes=[ncrow[0:1, :]])
        for dc in range(4):
            bank = rg.banks[5][:, 0:1]
            ctx.op("tensor", lambda e, dc=dc, bank=bank: e.transpose(
                bank, ncrow[0:1, dc * 128:(dc + 1) * 128], idf[0:1, 0:1]),
                reads=[ncrow[0:1, dc * 128:(dc + 1) * 128], idf[0:1, 0:1]], writes=[bank])
            ctx.op("scalar", lambda e, h=h, dc=dc, bank=bank: e.activation(
                ncv_sb[:, h * 4 + dc:h * 4 + dc + 1], bank, AF.Copy),
                reads=[bank], writes=[ncv_sb[:, h * 4 + dc:h * 4 + dc + 1]])
    ctx.dma("scalar", ncv_o[:, :], ncv_sb[:, :])
    ctx.barrier()


NV2 = 32


def build_l1_mlstm():
    nc = bass.Bass("TRN2", target_bir_lowering=False)
    es = ExitStack()
    ctx = Ctx(nc, es)
    rg = Regs(nc, es)
    io = {}
    for nm, shp, dt_, kind in [("qf_i", [2048, SH], BF16, "ExternalInput"),
                               ("kf_i", [2048, SH], BF16, "ExternalInput"),
                               ("vtok_i", [SH, 2048], BF16, "ExternalInput"),
                               ("xc_i", [2048, SH], BF16, "ExternalInput"),
                               ("z_i", [2048, SH], BF16, "ExternalInput"),
                               ("arow_i", [4, SH], F32, "ExternalInput"),
                               ("mrow_i", [4, SH], F32, "ExternalInput"),
                               ("lcrow_i", [4, SH], F32, "ExternalInput"),
                               ("gscal_i", [4, 8], F32, "ExternalInput"),
                               ("Mc_p", [2048, 512], F32, "ExternalInput"),
                               ("ncv_p", [128, 16], F32, "ExternalInput"),
                               ("pscal", [4, 8], F32, "ExternalInput"),
                               ("w_downT", [2048, 1024], F32, "ExternalInput"),
                               ("vecs_in", [128, NV2], F32, "ExternalInput"),
                               ("sel4_in", [4, 512], F32, "ExternalInput"),
                               ("mdiag_in", [4, 128, TT], F32, "ExternalInput"),
                               ("s1_o", [D, SH], F32, "ExternalOutput")]:
        io[nm] = nc.dram_tensor(nm, shp, dt_, kind=kind).ap()
    emit_setup(ctx, rg)
    emit_l1_mlstm(ctx, rg, io)
    ctx.finalize()
    return nc, es


def emit_l1_mlstm(ctx, rg, io):
    NT = SH // TT
    NI2 = 16
    qf, kf, vtok = io["qf_i"], io["kf_i"], io["vtok_i"]
    xc_i, z_i = io["xc_i"], io["z_i"]
    arow, mrow, lcrow, gscal = io["arow_i"], io["mrow_i"], io["lcrow_i"], io["gscal_i"]
    Mc_p, ncv_p, pscal = io["Mc_p"], io["ncv_p"], io["pscal"]
    w_downT, vecs_d, sel4_d, mdiag_d = io["w_downT"], io["vecs_in"], io["sel4_in"], io["mdiag_in"]
    s1_o = io["s1_o"]
    emit_setup(ctx, rg)
    vecs = rg.sb("vecs", [128, NV2])
    sel4 = rg.sb("sel4", [4, 512])
    mdiag = rg.sb("mdiag", [128, 4 * TT], BF16)
    ctx.dma("sync", vecs[:, :], vecs_d)
    ctx.dma("sync", sel4[0:4, :], sel4_d)
    for i in range(4):
        ctx.dma("gpsimd", mdiag[:, i * TT:(i + 1) * TT], mdiag_d[i, :, :])
    idf = rg.sb("idf", [128, 128])
    onesq = rg.sb("onesq", [128, 128])
    ctx.op("vector", lambda e: e.memset(onesq[:, :], 1.0), writes=[onesq[:, :]])
    ctx.op("gpsimd", lambda e: e.affine_select(idf[:, :], onesq[:, :], [[-1, 128]],
                                               ALU.is_equal, 0.0, channel_multiplier=1),
           reads=[onesq[:, :]], writes=[idf[:, :]])
    wd_sb = rg.sb("wd_sb", [128, NI2 * 1024], BF16)
    for c in range(NI2):
        ctx.dma("gpsimd", wd_sb[:, c * 1024:(c + 1) * 1024], w_downT[c * 128:(c + 1) * 128, :])
    ncb = rg.sb("ncb", [128, 16], BF16)
    ctx.dma("gpsimd", ncb[:, :], ncv_p)

    a_sb = rg.sb("m_a", [4, SH])
    m_sb = rg.sb("m_m", [4, SH])
    lc_sb = rg.sb("m_lc", [4, SH])
    gsc = rg.sb("m_gsc", [4, 8])
    psc = rg.sb("m_psc", [4, 8])
    ctx.dma("sync", a_sb[0:4, :], arow)
    ctx.dma("sync", m_sb[0:4, :], mrow)
    ctx.dma("sync", lc_sb[0:4, :], lcrow)
    ctx.dma("sync", gsc[0:4, :], gscal)
    ctx.dma("sync", psc[0:4, :], pscal)

    sc2 = rg.sb("m_sc2", [4, 16])
    rows4 = rg.sb("m_rows4", [4, 4 * TT])
    R4 = lambda i: rows4[0:4, i * TT:(i + 1) * TT]
    ucols = rg.sb("m_ucols", [128, (SH // 128) * 4])
    qh_sb = rg.sb("m_qh", [128, 4 * TT], BF16)
    qsp = rg.sb("m_qsp", [128, 4 * TT], BF16)
    khc = rg.sb("m_khc", [128, 2 * 4 * 128], BF16)
    vt_sb = rg.sb("m_vt", [128, 1024], BF16)
    mc_sb = rg.sb("m_mc", [128, 4 * 512], BF16)
    pb_sb = rg.sb("m_pb", [128, 2 * TT], BF16)
    sob = rg.sb("m_sob", [128, 2])
    hn_f = rg.sb("m_hnf", [128, 4 * TT])
    rowb = rg.sb("m_rowb", [1, 6 * TT])
    hst = rg.sb("m_hst", [128, NI2 * TT], BF16)
    xcz = rg.sb("m_xcz", [128, 2 * NI2 * TT], BF16)
    ev1 = rg.sb("m_ev1", [128, TT])
    ysum = rg.sb("m_ysum", [128, TT])
    ones_col = rg.sb("ones_col", [128, 1])
    ones_row = rg.sb("ones_row", [1, 128])
    onesb = rg.sb("ones_col_b", [128, 1], BF16)
    eps5 = rg.sb("eps5", [1, 1])
    R8 = lambda i: rowb[0:1, i * TT:(i + 1) * TT]

    for qt in range(NT):
        t0 = qt * TT
        qend = t0 + TT
        nkc = qend // 128
        for c in range(NI2):
            ctx.dma("sync", xcz[:, c * TT:(c + 1) * TT], xc_i[c * 128:(c + 1) * 128, t0:t0 + TT])
            ctx.dma("sync", xcz[:, (NI2 + c) * TT:(NI2 + c + 1) * TT],
                    z_i[c * 128:(c + 1) * 128, t0:t0 + TT])
        cqc = sc2[0:4, 0:1]
        ctx.op("vector", lambda e, qend=qend: e.tensor_scalar(
            cqc, m_sb[0:4, qend - 1:qend], psc[0:4, 0:1], psc[0:4, 1:2], ALU.subtract, ALU.max),
            reads=[m_sb[0:4, qend - 1:qend], psc[0:4, 0:2]], writes=[cqc])
        ctx.op("vector", lambda e: e.tensor_scalar_mul(sc2[0:4, 1:2], cqc, -1.0),
               reads=[cqc], writes=[sc2[0:4, 1:2]])
        ctx.op("vector", lambda e: e.tensor_add(sc2[0:4, 4:5], cqc, psc[0:4, 0:1]),
               reads=[cqc, psc[0:4, 0:1]], writes=[sc2[0:4, 4:5]])
        ctx.op("vector", lambda e: e.tensor_scalar_mul(sc2[0:4, 4:5], sc2[0:4, 4:5], -1.0),
               reads=[sc2[0:4, 4:5]], writes=[sc2[0:4, 4:5]])
        for tc in range(nkc):
            ctx.op("scalar", lambda e, tc=tc: e.activation(R4(3)[0:4, 0:128],
                                                           a_sb[0:4, tc * 128:(tc + 1) * 128],
                                                           AF.Exp, bias=sc2[0:4, 4:5]),
                   reads=[a_sb[0:4, tc * 128:(tc + 1) * 128], sc2[0:4, 4:5]],
                   writes=[R4(3)[0:4, 0:128]])
            ctx.op("vector", lambda e: e.tensor_scalar_mul(R4(3)[0:4, 0:128], R4(3)[0:4, 0:128],
                                                           1.0 / math.sqrt(512.0)),
                   reads=[R4(3)[0:4, 0:128]], writes=[R4(3)[0:4, 0:128]])
            bankt = rg.banks[7][:, 0:4]
            ctx.op("tensor", lambda e, bankt=bankt: e.transpose(bankt, R4(3)[0:4, 0:128], idf[0:4, 0:4]),
                   reads=[R4(3)[0:4, 0:128], idf[0:4, 0:4]], writes=[bankt])
            ctx.op("scalar", lambda e, tc=tc, bankt=bankt: e.activation(
                ucols[:, tc * 4:(tc + 1) * 4], bankt, AF.Copy),
                reads=[bankt], writes=[ucols[:, tc * 4:(tc + 1) * 4]])
        ctx.op("scalar", lambda e: e.activation(sc2[0:4, 3:4], cqc, AF.Exp,
                                                scale=-1.0, bias=psc[0:4, 1:2]),
               reads=[cqc, psc[0:4, 1:2]], writes=[sc2[0:4, 3:4]])
        ctx.op("vector", lambda e: e.tensor_scalar_mul(sc2[0:4, 3:4], sc2[0:4, 3:4],
                                                       1.0 / math.sqrt(512.0)),
               reads=[sc2[0:4, 3:4]], writes=[sc2[0:4, 3:4]])
        ctx.op("vector", lambda e, t0=t0: e.tensor_scalar(
            R4(0), m_sb[0:4, t0:t0 + TT], psc[0:4, 0:1], psc[0:4, 1:2], ALU.subtract, ALU.max),
            reads=[m_sb[0:4, t0:t0 + TT], psc[0:4, 0:2]], writes=[R4(0)])
        ctx.op("scalar", lambda e: e.activation(R4(1), R4(0), AF.Exp, bias=sc2[0:4, 1:2]),
               reads=[R4(0), sc2[0:4, 1:2]], writes=[R4(1)])
        ctx.op("vector", lambda e, t0=t0: e.tensor_scalar_add(R4(2), lc_sb[0:4, t0:t0 + TT],
                                                              psc[0:4, 0:1]),
               reads=[lc_sb[0:4, t0:t0 + TT], psc[0:4, 0:1]], writes=[R4(2)])
        ctx.op("scalar", lambda e: e.activation(R4(2), R4(2), AF.Exp, scale=-1.0, bias=sc2[0:4, 1:2]),
               reads=[R4(2), sc2[0:4, 1:2]], writes=[R4(2)])
        for h in range(4):
            for dc in range(4):
                ctx.dma("sync", qh_sb[:, dc * TT:(dc + 1) * TT],
                        qf[h * 512 + dc * 128: h * 512 + (dc + 1) * 128, t0:t0 + TT])
                ctx.dma("gpsimd", mc_sb[:, dc * 512:(dc + 1) * 512],
                        Mc_p[h * 512 + dc * 128: h * 512 + (dc + 1) * 128, :])
            bsel = rg.banks[7][:, 0:1]
            ctx.mm(bsel, sel4[0:4, h * 128:(h + 1) * 128], sc2[0:4, 3:4], True, True)
            ctx.op("scalar", lambda e, bsel=bsel: e.activation(sob[:, 0:1], bsel, AF.Copy),
                   reads=[bsel], writes=[sob[:, 0:1]])
            for dc in range(4):
                ctx.op("vector", lambda e, dc=dc: e.tensor_scalar_mul(
                    qsp[:, dc * TT:(dc + 1) * TT], qh_sb[:, dc * TT:(dc + 1) * TT], sob[:, 0:1]),
                    reads=[qh_sb[:, dc * TT:(dc + 1) * TT], sob[:, 0:1]],
                    writes=[qsp[:, dc * TT:(dc + 1) * TT]])
            def msums(kc):
                pb = pb_sb[:, (kc % 2) * TT:(kc % 2 + 1) * TT]
                vts = vt_sb[:, (kc % 2) * 512:(kc % 2) * 512 + 512]
                ctx.mm(rg.banks[4][0:1, 0:TT], onesb[:, :], pb, kc == 0, False)
                ctx.dma("sync", vts, vtok[kc * 128:(kc + 1) * 128, h * 512:(h + 1) * 512])
                for dcv in range(4):
                    ctx.mm(rg.banks[dcv][:, 0:TT], vts[:, dcv * 128:(dcv + 1) * 128], pb,
                           kc == 0, False)
            for kc in range(nkc):
                khs = khc[:, (kc % 2) * 512:(kc % 2) * 512 + 512]
                ctx.dma("sync", khs.rearrange("p (a c) -> p a c", a=4, c=128),
                        kf[h * 512:(h + 1) * 512, kc * 128:(kc + 1) * 128].rearrange(
                            "(a b) c -> b a c", a=4, b=128))
                sbank = rg.banks[5 + kc % 2][:, 0:TT]
                for dc in range(4):
                    ctx.mm(sbank, khs[:, dc * 128:(dc + 1) * 128],
                           qh_sb[:, dc * TT:(dc + 1) * TT], dc == 0, dc == 3)
                ucol = ucols[:, kc * 4 + h:kc * 4 + h + 1]
                pb = pb_sb[:, (kc % 2) * TT:(kc % 2 + 1) * TT]
                ctx.op("vector", lambda e, sbank=sbank, pb=pb, ucol=ucol: e.tensor_scalar_mul(
                    pb, sbank, ucol), reads=[sbank, ucol], writes=[pb])
                if kc >= nkc - 4:
                    md = mdiag[:, (kc - (nkc - 4)) * TT:(kc - (nkc - 4) + 1) * TT]
                    ctx.op("vector", lambda e, pb=pb, md=md: e.tensor_mul(pb, pb, md),
                           reads=[pb, md], writes=[pb])
                if kc >= 1:
                    msums(kc - 1)
            msums(nkc - 1)
            for dcv in range(4):
                for dq in range(4):
                    ctx.mm(rg.banks[dcv][:, 0:TT],
                           mc_sb[:, dq * 512 + dcv * 128: dq * 512 + (dcv + 1) * 128],
                           qsp[:, dq * TT:(dq + 1) * TT], False, dq == 3)
            for dq in range(4):
                ctx.mm(rg.banks[4][0:1, 0:TT], ncb[:, h * 4 + dq:h * 4 + dq + 1],
                       qsp[:, dq * TT:(dq + 1) * TT], False, dq == 3)
            brow = rg.banks[7][0:1, 0:TT]
            ctx.mm(brow, idf[0:4, h:h + 1], R4(1), True, True)
            ctx.op("scalar", lambda e, brow=brow: e.activation(R8(1), brow, AF.Copy),
                   reads=[brow], writes=[R8(1)])
            ctx.mm(brow, idf[0:4, h:h + 1], R4(2), True, True)
            ctx.op("scalar", lambda e, brow=brow: e.activation(R8(2), brow, AF.Copy),
                   reads=[brow], writes=[R8(2)])
            ctx.op("scalar", lambda e: e.activation(R8(3), rg.banks[4][0:1, 0:TT], AF.Abs),
                   reads=[rg.banks[4][0:1, 0:TT]], writes=[R8(3)])
            ctx.op("vector", lambda e: e.tensor_max(R8(3), R8(3), R8(2)),
                   reads=[R8(3), R8(2)], writes=[R8(3)])
            ctx.op("vector", lambda e: e.scalar_tensor_tensor(R8(3), R8(1), 1e-6, R8(3),
                                                              ALU.mult, ALU.add),
                   reads=[R8(1), R8(3)], writes=[R8(3)])
            ctx.op("vector", lambda e: e.reciprocal(R8(4), R8(3)),
                   reads=[R8(3)], writes=[R8(4)])
            bfac = rg.banks[7][:, 0:TT]
            ctx.mm(bfac, ones_row[0:1, :], R8(4), True, True)
            for dcv in range(4):
                ctx.op("scalar", lambda e, dcv=dcv: e.activation(
                    hn_f[:, dcv * TT:(dcv + 1) * TT], rg.banks[dcv][:, 0:TT], AF.Copy),
                    reads=[rg.banks[dcv][:, 0:TT]], writes=[hn_f[:, dcv * TT:(dcv + 1) * TT]])
                ctx.op("vector", lambda e, dcv=dcv: e.tensor_mul(
                    hn_f[:, dcv * TT:(dcv + 1) * TT], hn_f[:, dcv * TT:(dcv + 1) * TT], bfac),
                    reads=[hn_f[:, dcv * TT:(dcv + 1) * TT], bfac],
                    writes=[hn_f[:, dcv * TT:(dcv + 1) * TT]])
            bsum = rg.banks[4][0:1, 0:TT]
            bsq = rg.banks[5][0:1, 0:TT]
            for dcv in range(4):
                ctx.mm(bsum, ones_col[:, :], hn_f[:, dcv * TT:(dcv + 1) * TT], dcv == 0, dcv == 3)
                ctx.op("scalar", lambda e, dcv=dcv: e.activation(ev1[:, :], hn_f[:, dcv * TT:(dcv + 1) * TT],
                                                                 AF.Square),
                       reads=[hn_f[:, dcv * TT:(dcv + 1) * TT]], writes=[ev1[:, :]])
                ctx.mm(bsq, ones_col[:, :], ev1[:, :], dcv == 0, dcv == 3)
            ctx.op("vector", lambda e: e.tensor_scalar_mul(R8(1), bsum, 1.0 / 512.0),
                   reads=[bsum], writes=[R8(1)])
            ctx.op("vector", lambda e: e.tensor_scalar_mul(R8(2), bsq, 1.0 / 512.0),
                   reads=[bsq], writes=[R8(2)])
            ctx.op("vector", lambda e: e.tensor_mul(R8(3), R8(1), R8(1)),
                   reads=[R8(1)], writes=[R8(3)])
            ctx.op("vector", lambda e: e.tensor_sub(R8(2), R8(2), R8(3)),
                   reads=[R8(2), R8(3)], writes=[R8(2)])
            ctx.op("scalar", lambda e: e.activation(R8(3), R8(2), AF.Sqrt, bias=eps5[0:1, 0:1]),
                   reads=[R8(2), eps5[0:1, 0:1]], writes=[R8(3)])
            ctx.op("vector", lambda e: e.reciprocal(R8(4), R8(3)),
                   reads=[R8(3)], writes=[R8(4)])
            ctx.op("vector", lambda e: e.tensor_mul(R8(5), R8(1), R8(4)),
                   reads=[R8(1), R8(4)], writes=[R8(5)])
            brb = rg.banks[4][:, 0:TT]
            bzb = rg.banks[5][:, 0:TT]
            ctx.mm(brb, ones_row[0:1, :], R8(4), True, True)
            ctx.mm(bzb, ones_row[0:1, :], R8(5), True, True)
            for dcv in range(4):
                hslice = hn_f[:, dcv * TT:(dcv + 1) * TT]
                ctx.op("vector", lambda e, hslice=hslice: e.tensor_mul(hslice, hslice, brb),
                       reads=[hslice, brb], writes=[hslice])
                ctx.op("vector", lambda e, hslice=hslice: e.tensor_sub(hslice, hslice, bzb),
                       reads=[hslice, bzb], writes=[hslice])
                lnw = vecs[:, h * 4 + dcv:h * 4 + dcv + 1]
                ctx.op("vector", lambda e, hslice=hslice, lnw=lnw: e.tensor_scalar_mul(hslice, hslice, lnw),
                       reads=[hslice, lnw], writes=[hslice])
                ic = h * 4 + dcv
                xc_s = xcz[:, ic * TT:(ic + 1) * TT]
                z_s = xcz[:, (NI2 + ic) * TT:(NI2 + ic + 1) * TT]
                skipc = vecs[:, 16 + ic:16 + ic + 1]
                ctx.op("vector", lambda e, hslice=hslice, xc_s=xc_s, skipc=skipc: e.scalar_tensor_tensor(
                    hslice, xc_s, skipc, hslice, ALU.mult, ALU.add),
                    reads=[hslice, xc_s, skipc], writes=[hslice])
                ctx.op("scalar", lambda e, z_s=z_s: e.activation(ev1[:, :], z_s, AF.Silu),
                       reads=[z_s], writes=[ev1[:, :]])
                dsth = hst[:, ic * TT:(ic + 1) * TT]
                ctx.op("vector", lambda e, hslice=hslice, dsth=dsth: e.tensor_mul(dsth, hslice, ev1[:, :]),
                       reads=[hslice, ev1[:, :]], writes=[dsth])
        for ec in range(NCH):
            bank = rg.banks[ec % 2][:, 0:TT]
            for jc in range(NI2):
                ctx.mm(bank, wd_sb[:, jc * 1024 + ec * 128: jc * 1024 + (ec + 1) * 128],
                       hst[:, jc * TT:(jc + 1) * TT], jc == 0, jc == NI2 - 1)
            ctx.op("scalar", lambda e, bank=bank: e.activation(ysum[:, :], bank, AF.Copy),
                   reads=[bank], writes=[ysum[:, :]])
            ctx.dma("scalar", s1_o[ec * 128:(ec + 1) * 128, t0:t0 + TT], ysum[:, :])
    ctx.barrier()


def prep_l1_pre(inputs):
    NI2 = 16
    vecs = np.zeros((128, NV1), np.float32)
    cw = np.asarray(inputs["L1_attn_conv_w"], np.float32)[:, 0, :]
    cb = np.asarray(inputs["L1_attn_conv_b"], np.float32)
    for c in range(NI2):
        for j in range(4):
            vecs[:, c * 4 + j] = cw[c * 128:(c + 1) * 128, j]
        vecs[:, 64 + c] = cb[c * 128:(c + 1) * 128]
    vecs[0:4, 80] = np.asarray(inputs["L1_attn_b_ig"], np.float32)
    vecs[0:4, 81] = np.asarray(inputs["L1_attn_b_fg"], np.float32)

    def blockdiag(w):
        out = np.zeros((NI2, 128, 128), np.float32)
        for c in range(NI2):
            for a2 in range(32):
                blk = w[32 * c + a2]
                out[c, a2 * 4:(a2 + 1) * 4, a2 * 4:(a2 + 1) * 4] = blk.T
        return out
    wqd = blockdiag(np.asarray(inputs["L1_attn_w_q"], np.float32))
    wkd = blockdiag(np.asarray(inputs["L1_attn_w_k"], np.float32))
    wvd = blockdiag(np.asarray(inputs["L1_attn_w_v"], np.float32))
    return dict(
        w_upT=np.ascontiguousarray(np.asarray(inputs["L1_attn_w_up"], np.float32).T),
        wqd=wqd, wkd=wkd, wvd=wvd,
        wigT=np.ascontiguousarray(np.asarray(inputs["L1_attn_w_ig"], np.float32).T),
        wfgT=np.ascontiguousarray(np.asarray(inputs["L1_attn_w_fg"], np.float32).T),
        vecs_in=vecs)


def prep_l1_mlstm(inputs, pre_out, partner_pre_out, has_prev):
    NI2 = 16
    vecs = np.zeros((128, NV2), np.float32)
    lnw = np.asarray(inputs["L1_attn_ln_w"], np.float32)
    skip = np.asarray(inputs["L1_attn_skip"], np.float32)
    for h in range(4):
        for dc in range(4):
            vecs[:, h * 4 + dc] = lnw[h, dc * 128:(dc + 1) * 128]
    for c in range(NI2):
        vecs[:, 16 + c] = skip[c * 128:(c + 1) * 128]
    sel4 = np.zeros((4, 512), np.float32)
    for h in range(4):
        sel4[h, h * 128:(h + 1) * 128] = 1.0
    mdiag = np.zeros((4, 128, TT), np.float32)
    for p in range(4):
        k_rel = np.arange(128) + p * 128
        q_rel = np.arange(TT)
        mdiag[p] = (k_rel[:, None] <= q_rel[None, :]).astype(np.float32)
    if has_prev:
        Mc_p = partner_pre_out["Mc_o"]
        ncv_p = partner_pre_out["ncv_o"]
        pscal = np.zeros((4, 8), np.float32)
        pscal[:, 0] = partner_pre_out["gscal_o"][:, 0]
        pscal[:, 1] = partner_pre_out["gscal_o"][:, 1]
    else:
        Mc_p = np.zeros((2048, 512), np.float32)
        ncv_p = np.zeros((128, 16), np.float32)
        pscal = np.zeros((4, 8), np.float32)
        pscal[:, 1] = -1e30
    return dict(
        qf_i=pre_out["qf_o"], kf_i=pre_out["kf_o"], vtok_i=pre_out["vtok_o"],
        xc_i=pre_out["xc_o"], z_i=pre_out["z_o"],
        arow_i=pre_out["arow_o"], mrow_i=pre_out["mrow_o"], lcrow_i=pre_out["lcrow_o"],
        gscal_i=pre_out["gscal_o"], Mc_p=np.ascontiguousarray(Mc_p),
        ncv_p=np.ascontiguousarray(ncv_p), pscal=pscal,
        w_downT=np.ascontiguousarray(np.asarray(inputs["L1_attn_w_down"], np.float32).T),
        vecs_in=vecs, sel4_in=sel4, mdiag_in=mdiag)


def _l1f_consts(inputs, li):
    vecs = np.zeros((128, NV0), np.float32)
    ln_g = np.asarray(inputs["L1_ffn_ln_g"][li], np.float32)
    ln_b = np.asarray(inputs["L1_ffn_ln_b"][li], np.float32)
    b1 = np.asarray(inputs["L1_ffn_b1"][li], np.float32)
    b2 = np.asarray(inputs["L1_ffn_b2"][li], np.float32)
    qs = np.asarray(inputs["L1_ffn_q_scale"][li], np.float32)
    ks = np.asarray(inputs["L1_ffn_k_scale"][li], np.float32)
    for c in range(8):
        vecs[:, c] = ln_g[c * 128:(c + 1) * 128]
        vecs[:, 8 + c] = ln_b[c * 128:(c + 1) * 128]
        vecs[:, 60 + c] = b2[c * 128:(c + 1) * 128]
    b1p = np.zeros(5632, np.float32)
    b1p[0:2730] = b1[0:2730]
    b1p[2816:2816 + 2730] = b1[2730:5460]
    for c in range(44):
        vecs[:, 16 + c] = b1p[c * 128:(c + 1) * 128]
    vecs[:, 68] = np.tile(qs, 2)
    vecs[:, 69] = np.tile(ks, 2)
    W1 = np.asarray(inputs["L1_ffn_w1"][li], np.float32)
    w1p = np.zeros((D, 5632), np.float32)
    w1p[:, 0:2730] = W1[:, 0:2730]
    w1p[:, 2816:2816 + 2730] = W1[:, 2730:5460]
    W2 = np.asarray(inputs["L1_ffn_w2"][li], np.float32)
    w2p = np.zeros((2816, 1024), np.float32)
    w2p[0:2730, :] = W2
    return vecs, np.ascontiguousarray(w1p), np.ascontiguousarray(w2p)


def l1f_masks(has_prev):
    NKC = 8
    out = np.zeros((2, NKC, 128, TT), np.float32)
    q_rel = np.arange(TT)
    for kc in range(NKC):
        k_rel = np.arange(128) + kc * 128 - 512
        dmat = q_rel[None, :] - k_rel[:, None]
        invalid = (dmat < 0) | (dmat > 512)
        out[0, kc] = invalid
        inv0 = invalid.copy()
        if not has_prev and kc < 4:
            inv0[:, :] = 1.0
        out[1, kc] = inv0
    return np.ascontiguousarray(out)


def rope_tables_l1(own0):
    pos = np.arange(-L0F_HALO, SH) + own0
    pos = np.clip(pos, 0, None)
    aq, bq, ak, bk = rope_tables(S, 64, 512, pos, 8.0)
    return [np.ascontiguousarray(np.tile(t, (2, 1))) for t in (aq, bq, ak, bk)]


def prep_l1f_attn(inputs, c, li):
    ci = core_info(c)
    own0 = ci["h"] * SH
    aq, bq, ak, bk = rope_tables_l1(own0)
    ropes = np.stack([aq, bq, ak, bk])
    hselA = np.zeros((128, 2), np.float32)
    hselA[0:64, 0] = 1.0
    hselA[64:128, 1] = 1.0
    hselB = np.zeros((2, 128), np.float32)
    hselB[0, 0:64] = 1.0
    hselB[1, 64:128] = 1.0
    vecs, _, _ = _l1f_consts(inputs, li)
    return dict(
        wqkv=np.ascontiguousarray(np.asarray(inputs["L1_ffn_wqkv"][li], np.float32)),
        wo=np.ascontiguousarray(np.asarray(inputs["L1_ffn_wo"][li], np.float32)),
        qsrow=np.asarray(inputs["L1_ffn_q_scale"][li], np.float32).reshape(1, 64),
        ksrow=np.asarray(inputs["L1_ffn_k_scale"][li], np.float32).reshape(1, 64),
        vecs_in=vecs, ropes=ropes, hselA_in=hselA, hselB_in=hselB,
        rotm_in=rot_matrix(64, 2), idneg_in=np.eye(128, dtype=np.float32) * -1e5,
        masks_in=l1f_masks(ci["has_prev"]))


def prep_l1f_ffn(inputs, c, li):
    vecs, w1p, w2p = _l1f_consts(inputs, li)
    return dict(w1p=w1p, w2p=w2p, vecs_in=vecs)


_BUILD_CACHE = {}


def _get(name, builder):
    if name not in _BUILD_CACHE:
        _BUILD_CACHE[name] = builder()
    return _BUILD_CACHE[name]


def _run(name, builder, in_maps):
    nc, _ = _get(name, builder)
    res = run_bass_kernel_spmd(nc, in_maps, core_ids=list(range(8)))
    return res.results


def kernel(**inputs):
    cores = list(range(8))
    cis = [core_info(c) for c in cores]
    r1 = _run("b0", build_b0_lnattn, [prep_b0_lnattn(inputs, c) for c in cores])
    xn0b = [r1[c]["xn0b"] for c in cores]
    xn0f = [r1[c]["xn0f"] for c in cores]
    s0 = [r1[c]["s0"] for c in cores]
    xcur = [np.ascontiguousarray(xn0f[c][:, S:]) for c in cores]
    for li in range(6):
        ins = []
        for c in cores:
            hh = cis[c]["h"]
            if hh == 1:
                halo = xcur[c - 1][:, SH - L0F_HALO:]
            else:
                halo = np.zeros((D, L0F_HALO), np.float32)
            xw = np.concatenate([halo, xcur[c]], axis=1)
            m = prep_l0f_attn(inputs, c, li)
            m["xw"] = np.ascontiguousarray(xw)
            ins.append(m)
        ra = _run("l0f_attn", build_l0f_attn, ins)
        xp = [ra[c]["xo"] for c in cores]
        ins = []
        for c in cores:
            hh = cis[c]["h"]
            h1 = xp[c - 1][:, SH - 1:SH] if hh == 1 else np.zeros((D, 1), np.float32)
            xw1 = np.concatenate([h1, xp[c]], axis=1)
            m = prep_l0f_ffn(inputs, c, li)
            m["xw1"] = np.ascontiguousarray(xw1)
            ins.append(m)
        rf = _run("l0f_ffn", build_l0f_ffn, ins)
        xcur = [np.ascontiguousarray(rf[c]["xo"]) for c in cores]
    ins = []
    for c in cores:
        hh = cis[c]["h"]
        x0T_own = np.ascontiguousarray(
            np.asarray(inputs["L0_attn_x"][cis[c]["b"]], np.float32).T[:, hh * SH:(hh + 1) * SH])
        ins.append(dict(xT=x0T_own, sT=np.ascontiguousarray(s0[c]), gT=xcur[c]))
    rc = _run("combine", build_combine, ins)
    x1 = [np.ascontiguousarray(rc[c]["xo"]) for c in cores]
    ins = []
    for c in cores:
        hh = cis[c]["h"]
        halo = x1[c - 1][:, SH - 513:] if hh == 1 else np.zeros((D, 513), np.float32)
        xw = np.concatenate([np.zeros((D, 511), np.float32), halo, x1[c]], axis=1)
        ins.append(dict(xT=np.ascontiguousarray(xw)))
    rl = _run("ln_b1", lambda: build_ln(1024 + SH), ins)
    xn1 = [rl[c]["xnf"] for c in cores]
    ins = []
    for c in cores:
        m = prep_l1_pre(inputs)
        m["xnw"] = np.ascontiguousarray(xn1[c])
        ins.append(m)
    rp = _run("l1_pre", build_l1_pre, ins)
    ins = []
    for c in cores:
        hh = cis[c]["h"]
        partner = rp[c - 1] if hh == 1 else None
        ins.append(prep_l1_mlstm(inputs, rp[c], partner, hh == 1))
    rm = _run("l1_mlstm", build_l1_mlstm, ins)
    s1 = [rm[c]["s1_o"] for c in cores]
    xcur = [np.ascontiguousarray(xn1[c][:, 1024:]) for c in cores]
    for li in range(2):
        ins = []
        for c in cores:
            hh = cis[c]["h"]
            if li == 0:
                xw = np.ascontiguousarray(xn1[c][:, 1024 - L0F_HALO:])
            else:
                halo = xcur[c - 1][:, SH - L0F_HALO:] if hh == 1 else np.zeros((D, L0F_HALO), np.float32)
                xw = np.ascontiguousarray(np.concatenate([halo, xcur[c]], axis=1))
            m = prep_l1f_attn(inputs, c, li)
            m["xw"] = xw
            ins.append(m)
        ra = _run("l1f_attn", build_l1f_attn, ins)
        xp = [ra[c]["xo"] for c in cores]
        ins = []
        for c in cores:
            hh = cis[c]["h"]
            h1 = xp[c - 1][:, SH - 1:SH] if hh == 1 else np.zeros((D, 1), np.float32)
            xw1 = np.concatenate([h1, xp[c]], axis=1)
            m = prep_l1f_ffn(inputs, c, li)
            m["xw1"] = np.ascontiguousarray(xw1)
            ins.append(m)
        rf = _run("l0f_ffn", build_l0f_ffn, ins)
        xcur = [np.ascontiguousarray(rf[c]["xo"]) for c in cores]
    ins = []
    for c in cores:
        ins.append(dict(xT=x1[c], sT=np.ascontiguousarray(s1[c]), gT=xcur[c]))
    rc2 = _run("combine", build_combine, ins)
    out = np.empty((B, S, D), np.float32)
    for c in cores:
        b, hh = cis[c]["b"], cis[c]["h"]
        out[b, hh * SH:(hh + 1) * SH, :] = rc2[c]["xo"].T
    return out


L0H = L0F_HALO
RG_PAIRS = [[0, 1], [2, 3], [4, 5], [6, 7]]
RG_ALL = [list(range(8))]


def _pack_table():
    t = [("pk", (D, 128)), ("pv", (D, 128)), ("pq", (8, D, 128)), ("poT", (8, 128, D)),
         ("rotm", (128, 128)), ("idneg", (128, 128))]
    for li in range(6):
        t += [(f"wqkv0_{li}", (D, 3072)), (f"wo0_{li}", (1024, D)),
              (f"w1p0_{li}", (D, 5632)), (f"w2p0_{li}", (2816, 1024))]
    t += [("w_upT", (D, 4096)), ("wqd", (16, 128, 128)), ("wkd", (16, 128, 128)),
          ("wvd", (16, 128, 128)), ("wigT", (6144, 4)), ("wfgT", (6144, 4)),
          ("w_downT", (2048, 1024))]
    for li in range(2):
        t += [(f"wqkv1_{li}", (D, 1536)), (f"wo1_{li}", (512, D)),
              (f"w1p1_{li}", (D, 5632)), (f"w2p1_{li}", (2816, 1024))]
    offs = {}
    off = 0
    for nm, shp in t:
        offs[nm] = off
        off += int(np.prod(shp))
    pad = (-off) % (8 * 128 * 4096)
    total = off + pad
    return t, offs, total


def _wview(wb_handle, offs, nm, shape):
    dims = []
    strides = []
    st = 1
    for s_ in reversed(shape):
        strides.append(st)
        st *= s_
    strides = list(reversed(strides))
    for s_, stx in zip(shape, strides):
        dims.append([stx, s_])
    return bass.AP(wb_handle, offs[nm], dims)


def build_mega():
    nc = bass.Bass("TRN2", target_bir_lowering=False)
    es = ExitStack()
    ctx = Ctx(nc, es)
    rg = Regs(nc, es)
    ptab, offs, WTOT = _pack_table()
    LS = WTOT // 8

    EI = lambda nm, shp, dt_=F32: nc.dram_tensor(nm, list(shp), dt_, kind="ExternalInput").ap()
    x0T = EI("x0T", [D, S + SH])
    wshard = EI("wshard", [1, LS])
    vecs0 = EI("vecs0", [6, 128, NV0])
    vecs1f = EI("vecs1f", [2, 128, NV0])
    vecsl1 = EI("vecsl1", [128, NV1])
    vecsm = EI("vecsm", [128, NV2])
    qsk0 = EI("qsk0", [6, 2, 64])
    qsk1 = EI("qsk1", [2, 2, 64])
    hselA_d = EI("hselA_d", [128, 2])
    hselB_d = EI("hselB_d", [2, 128])
    sel4_d = EI("sel4_d", [4, 512])
    masks0 = EI("masks0", [2, 6, 128, TT])
    masks1 = EI("masks1", [2, 8, 128, TT])
    mdiag_d = EI("mdiag_d", [4, 128, TT])
    ropes0 = EI("ropes0", [4, 128, L0H + SH])
    ropes1 = EI("ropes1", [4, 128, L0H + SH])
    pcc = EI("pcc", [128, 4])
    out = nc.dram_tensor("out", [D, SH], F32, kind="ExternalOutput").ap()

    IN = lambda nm, shp, dt_=F32: nc.dram_tensor(nm, list(shp), dt_, kind="Internal")
    wsh_bf_t = IN("wsh_bf", [1, LS], BF16)
    wbig_t = IN("wbig", [8 * LS], BF16)
    xn0f_t = IN("xn0f_s", [D, S + SH])
    xn0b_t = IN("xn0b_s", [D, S + SH], BF16)
    s0_t = IN("s0_s", [D, SH])
    xH_t = IN("xH_s", [D, L0H])
    xB_t = IN("xB_s", [D, 1 + SH])
    xC_t = IN("xC_s", [D, SH])
    x1_t = IN("x1_s", [D, SH])
    xn1f_t = IN("xn1f_s", [D, 1024 + SH])
    xn1b_t = IN("xn1b_s", [D, 1024 + SH], BF16)
    xc_t = IN("xc_s", [2048, SH], BF16)
    z_t = IN("z_s", [2048, SH], BF16)
    qf_t = IN("qf_s", [2048, SH], BF16)
    kf_t = IN("kf_s", [2048, SH], BF16)
    vtok_t = IN("vtok_s", [SH, 2048], BF16)
    ktok_t = IN("ktok_s", [SH, 2048], BF16)
    arow_t = IN("arow_s", [4, SH])
    mrow_t = IN("mrow_s", [4, SH])
    lcrow_t = IN("lcrow_s", [4, SH])
    gscal_t = IN("gscal_s", [4, 8])
    Mc_t = IN("Mc_s", [2048, 512])
    ncv_t = IN("ncv_s", [128, 16])
    pscal_t = IN("pscal_s", [4, 8])
    s1_t = IN("s1_s", [D, SH])
    agA_in = IN("agA_in", [D, L0H])
    agA_out = IN("agA_out", [2 * D, L0H])
    agB_in = IN("agB_in", [D, 1])
    agB_out = IN("agB_out", [2 * D, 1])
    agS_mc_out = IN("agS_mc_out", [2 * 2048, 512])
    agS_nc_out = IN("agS_nc_out", [2 * 128, 16])
    agS_gs_out = IN("agS_gs_out", [2 * 4, 8])

    wb = wbig_t
    WV = lambda nm, shp: _wview(wb, offs, nm, shp)

    emit_setup(ctx, rg)
    valid = rg.sb("pc_sb", [128, 4])
    ctx.dma("sync", valid[:, :], pcc)
    zt = rg.sb("zt", [128, 512])
    ctx.op("vector", lambda e: e.memset(zt[:, :], 0.0), writes=[zt[:, :]])
    hb = rg.sb("hg_hb", [128, L0H])
    rg.sb("masks_sb_8", [128, 2 * 6 * TT], BF16)
    rg.sb("rotm", [128, 128], BF16)
    rg.sb("idneg", [128, 128], BF16)

    def halo_mul(dst, srcd, width, tag=""):
        kw = {"allow_slow_non_contiguous": True} if width == 1 else {}
        for c in range(NCH):
            ctx.dma("sync", hb[:, 0:width], srcd[c * 128:(c + 1) * 128, 0:width], **kw)
            ctx.op("vector", lambda e, c=c, width=width: e.tensor_scalar_mul(
                hb[:, 0:width], hb[:, 0:width], valid[:, 0:1]),
                reads=[hb[:, 0:width], valid[:, 0:1]], writes=[hb[:, 0:width]])
            ctx.dma("scalar", dst[c * 128:(c + 1) * 128, 0:width], hb[:, 0:width], **kw)

    rg.push()
    nchunk = LS // (128 * 4096)
    for i in range(nchunk):
        lo = i * 128 * 4096
        ctx.dma("gpsimd", bass.AP(wsh_bf_t, lo, [[4096, 128], [1, 4096]]),
                bass.AP(wshard.tensor, lo, [[4096, 128], [1, 4096]]))
    ctx.collective("AllGather", ALU.bypass, RG_ALL,
                   [wsh_bf_t.ap()], [bass.AP(wbig_t, 0, [[LS, 8], [1, LS]])])
    rg.pop()
    ctx.barrier()

    rg.push()
    io = dict(x0T=x0T, pk=WV("pk", (D, 128)), pv=WV("pv", (D, 128)),
              pq=WV("pq", (8, D, 128)), poT=WV("poT", (8, 128, D)),
              xn0f=xn0f_t.ap(), xn0b=xn0b_t.ap(), s0=s0_t.ap())
    emit_b0_lnattn(ctx, rg, io)
    rg.pop()
    ctx.barrier()

    for li in range(6):
        if li == 0:
            halo_mul(xH_t.ap(), xn0f_t.ap()[:, SH - L0H:SH], L0H)
            body_ap = xn0f_t.ap()[:, S:S + SH]
        else:
            halo_mul(xH_t.ap(), agA_out.ap()[0:D, :], L0H)
            body_ap = xC_t.ap()
        ctx.barrier()
        rg.push()
        io = dict(xw=body_ap, xw_halo=xH_t.ap(),
                  wqkv=WV(f"wqkv0_{li}", (D, 3072)), wo=WV(f"wo0_{li}", (1024, D)),
                  ropes=ropes0, qsrow=qsk0[li, 0:1, :], ksrow=qsk0[li, 1:2, :],
                  vecs_in=vecs0[li], hselA_in=hselA_d, hselB_in=hselB_d,
                  rotm_in=WV("rotm", (128, 128)), idneg_in=WV("idneg", (128, 128)),
                  masks_in=masks0, xo=xB_t.ap()[:, 1:1 + SH], load_consts=(li == 0))
        emit_lf_attn(ctx, rg, io, 8, 256)
        rg.pop()
        ctx.dma("gpsimd", agA_in.ap(), xB_t.ap()[:, 1 + SH - L0H:1 + SH])
        ctx.collective("AllGather", ALU.bypass, RG_PAIRS, [agA_in.ap()], [agA_out.ap()])
        halo_mul(xB_t.ap()[:, 0:1], agA_out.ap()[0:D, L0H - 1:L0H], 1)
        ctx.barrier()
        rg.push()
        io = dict(xw1=xB_t.ap(), w1p=WV(f"w1p0_{li}", (D, 5632)), w2p=WV(f"w2p0_{li}", (2816, 1024)),
                  vecs_in=vecs0[li], xo=xC_t.ap())
        emit_lf_ffn(ctx, rg, io)
        rg.pop()
        if li < 5:
            ctx.dma("gpsimd", agA_in.ap(), xC_t.ap()[:, SH - L0H:SH])
            ctx.collective("AllGather", ALU.bypass, RG_PAIRS, [agA_in.ap()], [agA_out.ap()])
        ctx.barrier()

    rg.push()
    io = dict(xT=x0T[:, S:S + SH], sT=s0_t.ap(), gT=xC_t.ap(), xo=x1_t.ap())
    emit_combine(ctx, rg, io)
    rg.pop()
    ctx.dma("gpsimd", agA_in.ap(), x1_t.ap()[:, SH - L0H:SH])
    ctx.collective("AllGather", ALU.bypass, RG_PAIRS, [agA_in.ap()], [agA_out.ap()])
    ctx.barrier()

    lnin_t = IN("lnin_s", [D, 1024 + SH])
    for c in range(NCH):
        ctx.dma("scalar", lnin_t.ap()[c * 128:(c + 1) * 128, 0:512], zt[:, :])
    halo_mul(lnin_t.ap()[:, 511:1024], agA_out.ap()[0:D, :], L0H, tag="hg3")
    for c in range(NCH):
        ctx.dma("gpsimd", lnin_t.ap()[c * 128:(c + 1) * 128, 1024:1024 + SH],
                x1_t.ap()[c * 128:(c + 1) * 128, :])
    ctx.barrier()
    rg.push()
    io = dict(xT=lnin_t.ap(), xnf=xn1f_t.ap(), xnb=xn1b_t.ap())
    emit_ln_pass(ctx, rg, io, 1024 + SH)
    rg.pop()
    ctx.barrier()

    rg.push()
    io = dict(xnw=xn1f_t.ap(), w_upT=WV("w_upT", (D, 4096)),
              wqd=WV("wqd", (16, 128, 128)), wkd=WV("wkd", (16, 128, 128)),
              wvd=WV("wvd", (16, 128, 128)), wigT=WV("wigT", (6144, 4)),
              wfgT=WV("wfgT", (6144, 4)), vecs_in=vecsl1,
              xc_o=xc_t.ap(), z_o=z_t.ap(), qf_o=qf_t.ap(), kf_o=kf_t.ap(),
              vtok_o=vtok_t.ap(), ktok_o=ktok_t.ap(), arow_o=arow_t.ap(),
              mrow_o=mrow_t.ap(), lcrow_o=lcrow_t.ap(), gscal_o=gscal_t.ap(),
              Mc_o=Mc_t.ap(), ncv_o=ncv_t.ap())
    emit_l1_pre(ctx, rg, io)
    rg.pop()
    ctx.collective("AllGather", ALU.bypass, RG_PAIRS, [Mc_t.ap()], [agS_mc_out.ap()])
    ctx.collective("AllGather", ALU.bypass, RG_PAIRS, [ncv_t.ap()], [agS_nc_out.ap()])
    ctx.collective("AllGather", ALU.bypass, RG_PAIRS, [gscal_t.ap()], [agS_gs_out.ap()])
    pg = rg.sb("pg", [4, 8])
    ctx.dma("sync", pg[0:4, :], agS_gs_out.ap()[0:4, :])
    ctx.op("vector", lambda e: e.tensor_scalar_mul(pg[0:4, 0:2], pg[0:4, 0:2], valid[0:4, 0:1]),
           reads=[pg[0:4, 0:2], valid[0:4, 0:1]], writes=[pg[0:4, 0:2]])
    ctx.op("vector", lambda e: e.tensor_scalar_mul(pg[0:4, 2:3], valid[0:4, 1:2], -1e30),
           reads=[valid[0:4, 1:2]], writes=[pg[0:4, 2:3]])
    ctx.op("vector", lambda e: e.tensor_add(pg[0:4, 1:2], pg[0:4, 1:2], pg[0:4, 2:3]),
           reads=[pg[0:4, 1:3]], writes=[pg[0:4, 1:2]])
    ctx.dma("scalar", pscal_t.ap()[0:4, :], pg[0:4, 0:8])
    ctx.barrier()

    rg.push()
    io = dict(qf_i=qf_t.ap(), kf_i=kf_t.ap(), vtok_i=vtok_t.ap(), xc_i=xc_t.ap(),
              z_i=z_t.ap(), arow_i=arow_t.ap(), mrow_i=mrow_t.ap(), lcrow_i=lcrow_t.ap(),
              gscal_i=gscal_t.ap(), Mc_p=agS_mc_out.ap()[0:2048, :],
              ncv_p=agS_nc_out.ap()[0:128, :], pscal=pscal_t.ap(),
              w_downT=WV("w_downT", (2048, 1024)), vecs_in=vecsm, sel4_in=sel4_d,
              mdiag_in=mdiag_d, s1_o=s1_t.ap())
    emit_l1_mlstm(ctx, rg, io)
    rg.pop()
    ctx.barrier()

    for li in range(2):
        if li == 0:
            xw_ap = xn1f_t.ap()[:, 1024:1024 + SH]
            xwh_ap = xn1f_t.ap()[:, 1024 - L0H:1024]
        else:
            halo_mul(xH_t.ap(), agA_out.ap()[0:D, :], L0H)
            ctx.barrier()
            xw_ap = xC_t.ap()
            xwh_ap = xH_t.ap()
        rg.push()
        io = dict(xw=xw_ap, xw_halo=xwh_ap,
                  wqkv=WV(f"wqkv1_{li}", (D, 1536)), wo=WV(f"wo1_{li}", (512, D)),
                  ropes=ropes1, qsrow=qsk1[li, 0:1, :], ksrow=qsk1[li, 1:2, :],
                  vecs_in=vecs1f[li], hselA_in=hselA_d, hselB_in=hselB_d,
                  rotm_in=WV("rotm", (128, 128)), idneg_in=WV("idneg", (128, 128)),
                  masks_in=masks1, xo=xB_t.ap()[:, 1:1 + SH], load_consts=True)
        emit_lf_attn(ctx, rg, io, 4, 512)
        rg.pop()
        ctx.dma("gpsimd", agA_in.ap(), xB_t.ap()[:, 1 + SH - L0H:1 + SH])
        ctx.collective("AllGather", ALU.bypass, RG_PAIRS, [agA_in.ap()], [agA_out.ap()])
        halo_mul(xB_t.ap()[:, 0:1], agA_out.ap()[0:D, L0H - 1:L0H], 1)
        ctx.barrier()
        rg.push()
        io = dict(xw1=xB_t.ap(), w1p=WV(f"w1p1_{li}", (D, 5632)), w2p=WV(f"w2p1_{li}", (2816, 1024)),
                  vecs_in=vecs1f[li], xo=xC_t.ap())
        emit_lf_ffn(ctx, rg, io)
        rg.pop()
        if li == 0:
            ctx.dma("gpsimd", agA_in.ap(), xC_t.ap()[:, SH - L0H:SH])
            ctx.collective("AllGather", ALU.bypass, RG_PAIRS, [agA_in.ap()], [agA_out.ap()])
        ctx.barrier()

    rg.push()
    io = dict(xT=x1_t.ap(), sT=s1_t.ap(), gT=xC_t.ap(), xo=out)
    emit_combine(ctx, rg, io)
    rg.pop()
    ctx.barrier()
    ctx.finalize()
    return nc, es


def build_flat_pack(inputs):
    ptab, offs, WTOT = _pack_table()
    flat = np.zeros(WTOT, np.float32)

    def put(nm, arr):
        a = np.ascontiguousarray(np.asarray(arr, np.float32)).ravel()
        flat[offs[nm]:offs[nm] + a.size] = a
    put("pk", inputs["L0_attn_proj_k"])
    put("pv", inputs["L0_attn_proj_v"])
    put("pq", inputs["L0_attn_proj_q"])
    put("poT", np.asarray(inputs["L0_attn_proj_o"], np.float32).transpose(0, 2, 1))
    put("rotm", rot_matrix(64, 2))
    put("idneg", np.eye(128, dtype=np.float32) * -1e5)
    for li in range(6):
        put(f"wqkv0_{li}", inputs["L0_ffn_Wqkv"][li])
        put(f"wo0_{li}", inputs["L0_ffn_Wo"][li])
        _, w1p, w2p = _l0f_consts(inputs, li)
        put(f"w1p0_{li}", w1p)
        put(f"w2p0_{li}", w2p)
    put("w_upT", np.asarray(inputs["L1_attn_w_up"], np.float32).T)
    l1p = prep_l1_pre(inputs)
    put("wqd", l1p["wqd"])
    put("wkd", l1p["wkd"])
    put("wvd", l1p["wvd"])
    put("wigT", l1p["wigT"])
    put("wfgT", l1p["wfgT"])
    put("w_downT", np.asarray(inputs["L1_attn_w_down"], np.float32).T)
    for li in range(2):
        put(f"wqkv1_{li}", inputs["L1_ffn_wqkv"][li])
        put(f"wo1_{li}", inputs["L1_ffn_wo"][li])
        _, w1p, w2p = _l1f_consts(inputs, li)
        put(f"w1p1_{li}", w1p)
        put(f"w2p1_{li}", w2p)
    return flat


def prep_mega(inputs, c, flat):
    ci = core_info(c)
    b, hh = ci["b"], ci["h"]
    _, _, WTOT = _pack_table()
    LS = WTOT // 8
    shard = np.ascontiguousarray(flat[c * LS:(c + 1) * LS].reshape(1, LS))

    x0 = np.asarray(inputs["L0_attn_x"][b], np.float32)
    x0T = np.ascontiguousarray(x0.T)
    own = x0T[:, hh * SH:(hh + 1) * SH]
    x0T_ext = np.ascontiguousarray(np.concatenate([x0T, own], axis=1))

    vecs0 = np.stack([_l0f_consts(inputs, li)[0] for li in range(6)])
    vecs1f = np.stack([_l1f_consts(inputs, li)[0] for li in range(2)])
    qsk0 = np.stack([np.stack([np.asarray(inputs["L0_ffn_q_scale"][li], np.float32),
                               np.asarray(inputs["L0_ffn_k_scale"][li], np.float32)])
                     for li in range(6)])
    qsk1 = np.stack([np.stack([np.asarray(inputs["L1_ffn_q_scale"][li], np.float32),
                               np.asarray(inputs["L1_ffn_k_scale"][li], np.float32)])
                     for li in range(2)])
    hselA = np.zeros((128, 2), np.float32)
    hselA[0:64, 0] = 1.0
    hselA[64:128, 1] = 1.0
    hselB = np.zeros((2, 128), np.float32)
    hselB[0, 0:64] = 1.0
    hselB[1, 64:128] = 1.0
    sel4 = np.zeros((4, 512), np.float32)
    for h in range(4):
        sel4[h, h * 128:(h + 1) * 128] = 1.0
    mdiag = np.zeros((4, 128, TT), np.float32)
    for p in range(4):
        k_rel = np.arange(128) + p * 128
        q_rel = np.arange(TT)
        mdiag[p] = (k_rel[:, None] <= q_rel[None, :]).astype(np.float32)
    own0 = hh * SH
    r0 = np.stack(rope_tables_l0(own0))
    r1 = np.stack(rope_tables_l1(own0))
    pc = np.zeros((128, 4), np.float32)
    pc[:, 0] = 1.0 if ci["has_prev"] else 0.0
    pc[:, 1] = 0.0 if ci["has_prev"] else 1.0
    return dict(x0T=x0T_ext, wshard=shard, vecs0=vecs0, vecs1f=vecs1f,
                vecsl1=prep_l1_pre(inputs)["vecs_in"], vecsm=prep_l1_mlstm_vecs(inputs),
                qsk0=qsk0, qsk1=qsk1, hselA_d=hselA, hselB_d=hselB, sel4_d=sel4,
                masks0=l0f_masks(ci["has_prev"]), masks1=l1f_masks(ci["has_prev"]),
                mdiag_d=mdiag, ropes0=r0, ropes1=r1, pcc=pc)


def prep_l1_mlstm_vecs(inputs):
    NI2 = 16
    vecs = np.zeros((128, NV2), np.float32)
    lnw = np.asarray(inputs["L1_attn_ln_w"], np.float32)
    skip = np.asarray(inputs["L1_attn_skip"], np.float32)
    for h in range(4):
        for dc in range(4):
            vecs[:, h * 4 + dc] = lnw[h, dc * 128:(dc + 1) * 128]
    for c2 in range(NI2):
        vecs[:, 16 + c2] = skip[c2 * 128:(c2 + 1) * 128]
    return vecs


_MEGA_CACHE = {}


def kernel(**inputs):
    if "nc" not in _MEGA_CACHE:
        _MEGA_CACHE["nc"], _MEGA_CACHE["es"] = build_mega()
    nc = _MEGA_CACHE["nc"]
    flat = build_flat_pack(inputs)
    in_maps = [prep_mega(inputs, c, flat) for c in range(8)]
    res = run_bass_kernel_spmd(nc, in_maps, core_ids=list(range(8)))
    out = np.empty((B, S, D), np.float32)
    for c in range(8):
        ci = core_info(c)
        out[ci["b"], ci["h"] * SH:(ci["h"] + 1) * SH, :] = res.results[c]["out"].T
    return out


# version 09
# speedup vs baseline: 1.1517x; 1.1517x over previous
import math
import numpy as np
import ml_dtypes
from contextlib import ExitStack

import concourse.bass as bass
import concourse.mybir as mybir
from concourse.bass_utils import run_bass_kernel_spmd

F32 = mybir.dt.float32
BF16 = mybir.dt.bfloat16
AF = mybir.ActivationFunctionType
ALU = mybir.AluOpType
AX = mybir.AxisListType
bfnp = ml_dtypes.bfloat16

D = 1024
B = 4
S = 8192
SH = S // 2
NCH = D // 128
TT = 512

ENGS = ("sync", "tensor", "scalar", "vector", "gpsimd")
SAME_ENG_FIFO = False


def core_info(c):
    return {"b": c // 2, "h": c % 2, "has_prev": (c % 2 == 1)}


def _access(ap):
    t = ap.tensor
    name = t.name
    shape = list(t.shape)
    dims = ap.ap.to_list() if hasattr(ap.ap, "to_list") else list(ap.ap)
    off = ap.offset
    is_dram = "dram" in type(t).__name__.lower()
    if is_dram or len(shape) != 2:
        span = 1
        for st_, cnt in dims:
            span += abs(st_) * (cnt - 1)
        return (name, "flat", (off, off + span))
    F = shape[1]
    p0, f0 = off // F, off % F
    if dims and (dims[0][0] == F or dims[0][1] == 1):
        pcnt = dims[0][1]
        rest = dims[1:]
    else:
        pcnt = 1
        rest = dims
    fspan = 1
    for st_, cnt in rest:
        fspan += abs(st_) * (cnt - 1)
    return (name, "sb", (p0, p0 + pcnt, f0, f0 + fspan))


def _overlap(a, b):
    ka, ra = a
    kb, rb = b
    if ka != kb:
        return True
    if ka == "flat":
        return ra[0] < rb[1] and rb[0] < ra[1]
    return (ra[0] < rb[1] and rb[0] < ra[1] and
            ra[2] < rb[3] and rb[2] < ra[3])


def _covers(a, b):
    ka, ra = a
    kb, rb = b
    if ka != kb:
        return False
    if ka == "flat":
        return ra[0] <= rb[0] and ra[1] >= rb[1]
    return (ra[0] <= rb[0] and ra[1] >= rb[1] and
            ra[2] <= rb[2] and ra[3] >= rb[3])


class Ctx:
    def __init__(self, nc, es):
        self.nc = nc
        self.es = es
        self.streams = {e: [] for e in ENGS}
        self.sem = {e: es.enter_context(nc.semaphore(f"sem_{e}")) for e in ENGS}
        self.cnt = {e: 0 for e in ENGS}
        self.NSLOT = 8
        self.qsem = {q: [es.enter_context(nc.semaphore(f"qsem_{q}{i}"))
                         for i in range(self.NSLOT)]
                     for q in ("sync", "scalar", "gpsimd")}
        self.qcnt = {q: [0] * self.NSLOT for q in self.qsem}
        self.qidx = {q: 0 for q in self.qsem}
        self.water = {}
        self.acc = {}

    def _wait(self, eng, token):
        if token is None:
            return
        (kind, name), val = token
        if kind == "e":
            if name == eng and (name == "tensor" or SAME_ENG_FIFO):
                return
            sem = self.sem[name]
        else:
            q, slot = name
            sem = self.qsem[q][slot]
        key = (eng, kind, name)
        if self.water.get(key, 0) >= val:
            return
        self.water[key] = val

        def thunk(e, sem=sem, val=val):
            e.wait_ge(sem, val)
        self.streams[eng].append(thunk)

    def _deps(self, eng, reads, writes):
        toks = []
        for ap in reads:
            nm, kind, rect = _access(ap)
            for rec in self.acc.get(nm, ()):
                if rec["write"] and _overlap((kind, rect), rec["rect"]):
                    toks.append(rec["tok"])
        for ap in writes:
            nm, kind, rect = _access(ap)
            for rec in self.acc.get(nm, ()):
                if _overlap((kind, rect), rec["rect"]):
                    toks.append(rec["tok"])
        for t in toks:
            self._wait(eng, t)

    def _record(self, token, reads, writes):
        for ap in writes:
            nm, kind, rect = _access(ap)
            lst = self.acc.setdefault(nm, [])
            lst[:] = [r for r in lst if not _covers((kind, rect), r["rect"])]
            lst.append({"rect": (kind, rect), "tok": token, "write": True})
        for ap in reads:
            nm, kind, rect = _access(ap)
            lst = self.acc.setdefault(nm, [])
            lst.append({"rect": (kind, rect), "tok": token, "write": False})
            if len(lst) > 96:
                ws = [r for r in lst if r["write"]]
                rs = {}
                for r in lst:
                    if not r["write"]:
                        k = r["tok"][0]
                        if k not in rs or r["tok"][1] > rs[k]["tok"][1]:
                            rs[k] = r
                full = ("flat", (0, 1 << 60)) if kind == "flat" else ("sb", (0, 1 << 30, 0, 1 << 30))
                for r in rs.values():
                    r["rect"] = full
                lst[:] = ws + list(rs.values())

    def op(self, eng, fn, reads=(), writes=()):
        self._deps(eng, reads, writes)
        self.cnt[eng] += 1
        val = self.cnt[eng]
        sem = self.sem[eng]

        def thunk(e, fn=fn, sem=sem):
            fn(e).then_inc(sem, 1)
        self.streams[eng].append(thunk)
        tok = (("e", eng), val)
        self._record(tok, reads, writes)
        return tok

    def mm(self, out, lhsT, rhs, start, stop, extra_reads=()):
        reads = [lhsT, rhs] + list(extra_reads)
        return self.op("tensor",
                       lambda e: e.matmul(out, lhsT, rhs, start=start, stop=stop),
                       reads=reads, writes=[out])

    def dma(self, q, out, in_, **kw):
        self._deps(q, [in_], [out])
        slot = self.qidx[q]
        self.qidx[q] = (slot + 1) % self.NSLOT
        if self.qcnt[q][slot] > 0:
            self._wait(q, (("q", (q, slot)), self.qcnt[q][slot]))
        self.qcnt[q][slot] += 16
        qsem = self.qsem[q][slot]

        def thunk(e, out=out, in_=in_, qsem=qsem, kw=kw):
            e.dma_start(out=out, in_=in_, **kw).then_inc(qsem, 16)
        self.streams[q].append(thunk)
        tok = (("q", (q, slot)), self.qcnt[q][slot])
        self._record(tok, [in_], [out])
        return tok

    def collective(self, kind, op_, rg_groups, ins, outs):
        self._deps("gpsimd", ins, outs)
        self.cnt["gpsimd"] += 1
        val = self.cnt["gpsimd"]
        sem = self.sem["gpsimd"]

        def thunk(e, ins=ins, outs=outs):
            e.collective_compute(kind, op_, replica_groups=rg_groups,
                                 ins=[a.opt() for a in ins],
                                 outs=[a.opt() for a in outs]).then_inc(sem, 1)
        self.streams["gpsimd"].append(thunk)
        tok = (("e", "gpsimd"), val)
        self._record(tok, ins, outs)
        return tok

    def barrier(self):
        snap_e = dict(self.cnt)
        for eng in ENGS:
            for p in ENGS:
                if snap_e[p] > 0 and not (p == eng == "tensor"):
                    self._wait(eng, (("e", p), snap_e[p]))
            for q in self.qcnt:
                for slot in range(self.NSLOT):
                    v = self.qcnt[q][slot]
                    if v > 0:
                        self._wait(eng, (("q", (q, slot)), v))

    def finalize(self):
        with self.nc.Block() as block:
            for eng in ENGS:
                stream = self.streams[eng]

                def run(e, stream=stream):
                    for f in stream:
                        f(e)
                getattr(block, eng)(run)


class Regs:
    def __init__(self, nc, es):
        self.nc, self.es = nc, es
        self.map = {}
        self.scopes = []
        self.uid = 0
        self.banks = [es.enter_context(nc.psum_tensor(f"psb{i}", [128, 512], F32))
                      for i in range(8)]

    def push(self):
        self.uid += 1
        self.scopes.append((ExitStack(), [], self.uid))

    def pop(self):
        es2, names, _ = self.scopes.pop()
        es2.close()
        for n in names:
            del self.map[n]

    def sb(self, name, shape, dtype=F32):
        if name not in self.map:
            if self.scopes:
                es2, names, uid = self.scopes[-1]
                self.map[name] = es2.enter_context(
                    self.nc.sbuf_tensor(f"{name}__{uid}", list(shape), dtype))
                names.append(name)
            else:
                self.map[name] = self.es.enter_context(
                    self.nc.sbuf_tensor(name, list(shape), dtype))
        return self.map[name]


def emit_setup(ctx, rg):
    for nm, shape, dt_, v in [("ones_col", [128, 1], F32, 1.0),
                              ("ones_row", [1, 128], F32, 1.0),
                              ("ones_col_b", [128, 1], BF16, 1.0),
                              ("eps5", [1, 1], F32, 1e-5)]:
        t = rg.sb(nm, shape, dt_)
        ctx.op("vector", lambda e, t=t, v=v: e.memset(t[:, :], v), writes=[t[:, :]])


def emit_ln_tile(ctx, rg, st, T, x_aps, out_f_aps, out_b_aps,
                 gamma_cols=None, beta_cols=None):
    sq, tmp, row = st["sq"], st["tmp"], st["row"]
    ps_sum, ps_sq, ps_rb, ps_zb = st["ps_sum"], st["ps_sq"], st["ps_rb"], st["ps_zb"]
    ones_col = rg.sb("ones_col", [128, 1])
    ones_row = rg.sb("ones_row", [1, 128])
    eps5 = rg.sb("eps5", [1, 1])
    MU, SA, SB = 0, T, 2 * T

    def R(a, b_):
        return row[0:1, a:b_]

    for c in range(NCH):
        ctx.op("scalar", lambda e, c=c: e.activation(sq[:, :], x_aps[c], AF.Square),
               reads=[x_aps[c]], writes=[sq[:, :]])
        ctx.mm(ps_sum, ones_col[:, :], x_aps[c], c == 0, c == NCH - 1)
        ctx.mm(ps_sq, ones_col[:, :], sq[:, :], c == 0, c == NCH - 1)
    ctx.op("vector", lambda e: e.tensor_scalar_mul(R(MU, MU + T), ps_sum, 1.0 / D),
           reads=[ps_sum], writes=[R(MU, MU + T)])
    ctx.op("vector", lambda e: e.tensor_scalar_mul(R(SA, SA + T), ps_sq, 1.0 / D),
           reads=[ps_sq], writes=[R(SA, SA + T)])
    ctx.op("vector", lambda e: e.tensor_mul(R(SB, SB + T), R(MU, MU + T), R(MU, MU + T)),
           reads=[R(MU, MU + T)], writes=[R(SB, SB + T)])
    ctx.op("vector", lambda e: e.tensor_sub(R(SA, SA + T), R(SA, SA + T), R(SB, SB + T)),
           reads=[R(SA, SA + T), R(SB, SB + T)], writes=[R(SA, SA + T)])
    ctx.op("scalar", lambda e: e.activation(R(SB, SB + T), R(SA, SA + T), AF.Sqrt, bias=eps5[0:1, 0:1]),
           reads=[R(SA, SA + T), eps5[0:1, 0:1]], writes=[R(SB, SB + T)])
    ctx.op("vector", lambda e: e.reciprocal(R(SA, SA + T), R(SB, SB + T)),
           reads=[R(SB, SB + T)], writes=[R(SA, SA + T)])
    ctx.op("vector", lambda e: e.tensor_mul(R(SB, SB + T), R(MU, MU + T), R(SA, SA + T)),
           reads=[R(MU, MU + T), R(SA, SA + T)], writes=[R(SB, SB + T)])
    ctx.mm(ps_rb, ones_row[0:1, :], R(SA, SA + T), True, True)
    ctx.mm(ps_zb, ones_row[0:1, :], R(SB, SB + T), True, True)

    toks_out = []
    for c in range(NCH):
        ctx.op("vector", lambda e, c=c: e.tensor_mul(tmp[:, :], x_aps[c], ps_rb),
               reads=[x_aps[c], ps_rb], writes=[tmp[:, :]])
        dst = out_f_aps[c] if out_f_aps is not None else out_b_aps[c]
        if gamma_cols is not None:
            ctx.op("vector", lambda e: e.tensor_sub(tmp[:, :], tmp[:, :], ps_zb),
                   reads=[tmp[:, :], ps_zb], writes=[tmp[:, :]])
            tcx = ctx.op("vector", lambda e, c=c, dst=dst: e.tensor_scalar(
                dst, tmp[:, :], gamma_cols[c], beta_cols[c], ALU.mult, ALU.add),
                reads=[tmp[:, :], gamma_cols[c], beta_cols[c]], writes=[dst])
        else:
            tcx = ctx.op("vector", lambda e, c=c, dst=dst: e.tensor_sub(dst, tmp[:, :], ps_zb),
                         reads=[tmp[:, :], ps_zb], writes=[dst])
        if out_f_aps is not None and out_b_aps is not None:
            tcx = ctx.op("gpsimd", lambda e, c=c: e.tensor_copy(out_b_aps[c], out_f_aps[c]),
                         reads=[out_f_aps[c]], writes=[out_b_aps[c]])
        toks_out.append(tcx)
    return toks_out


def make_ln_state(rg, tag, T, bank_ids):
    return {
        "sq": rg.sb(f"{tag}_sq", [128, T]),
        "tmp": rg.sb(f"{tag}_tmp", [128, T]),
        "row": rg.sb(f"{tag}_row", [1, 3 * T]),
        "ps_sum": rg.banks[bank_ids[0]][0:1, 0:T],
        "ps_sq": rg.banks[bank_ids[1]][0:1, 0:T],
        "ps_rb": rg.banks[bank_ids[2]][:, 0:T],
        "ps_zb": rg.banks[bank_ids[3]][:, 0:T],
    }


def build_b0_lnattn():
    nc = bass.Bass("TRN2", target_bir_lowering=False)
    es = ExitStack()
    ctx = Ctx(nc, es)
    rg = Regs(nc, es)
    io = {}
    for nm, shp, dt_, kind in [("x0T", [D, S + SH], F32, "ExternalInput"),
                               ("pk", [D, 128], F32, "ExternalInput"),
                               ("pv", [D, 128], F32, "ExternalInput"),
                               ("pq", [8, D, 128], F32, "ExternalInput"),
                               ("poT", [8, 128, D], F32, "ExternalInput"),
                               ("xn0f", [D, S + SH], F32, "ExternalOutput"),
                               ("xn0b", [D, S + SH], BF16, "ExternalOutput"),
                               ("s0", [D, SH], F32, "ExternalOutput")]:
        io[nm] = nc.dram_tensor(nm, shp, dt_, kind=kind).ap()
    emit_setup(ctx, rg)
    emit_b0_lnattn(ctx, rg, io)
    ctx.finalize()
    return nc, es


def emit_b0_lnattn(ctx, rg, io):
    NT_OWN = SH // TT
    NT_FULL = S // TT
    x0T, pk, pv, pq, poT = io["x0T"], io["pk"], io["pv"], io["pq"], io["poT"]
    xn0f, xn0b, s0 = io["xn0f"], io["xn0b"], io["s0"]

    st = make_ln_state(rg, "p1", TT, [0, 1, 2, 3])
    xs = rg.sb("p1_x", [128, NCH * TT])
    of = rg.sb("p1_of", [128, NCH * TT])
    ob = rg.sb("p1_ob", [128, NCH * TT], BF16)
    for it in range(NT_FULL + NT_OWN):
        t0 = it * TT
        x_aps = []
        for c in range(NCH):
            ap = xs[:, c * TT:(c + 1) * TT]
            ctx.dma("sync", ap, x0T[c * 128:(c + 1) * 128, t0:t0 + TT])
            x_aps.append(ap)
        of_aps = [of[:, c * TT:(c + 1) * TT] for c in range(NCH)]
        ob_aps = [ob[:, c * TT:(c + 1) * TT] for c in range(NCH)]
        emit_ln_tile(ctx, rg, st, TT, x_aps, of_aps, ob_aps)
        for c in range(NCH):
            ctx.dma("scalar", xn0f[c * 128:(c + 1) * 128, t0:t0 + TT], of_aps[c])
            ctx.dma("gpsimd", xn0b[c * 128:(c + 1) * 128, t0:t0 + TT], ob_aps[c])
    ctx.barrier()

    pk_sb = rg.sb("pk_sb", [128, NCH * 128], BF16)
    pv_sb = rg.sb("pv_sb", [128, NCH * 128], BF16)
    pq_sb = rg.sb("pq_sb", [128, 8 * NCH * 128], BF16)
    po_sb = rg.sb("po_sb", [128, 8 * D], BF16)
    K_sb = rg.sb("K_sb", [128, S], BF16)
    V_tok = rg.sb("V_tok", [128, S], BF16)
    for c in range(NCH):
        ctx.dma("gpsimd", pk_sb[:, c * 128:(c + 1) * 128], pk[c * 128:(c + 1) * 128, :])
        ctx.dma("gpsimd", pv_sb[:, c * 128:(c + 1) * 128], pv[c * 128:(c + 1) * 128, :])
    for hh in range(8):
        for c in range(NCH):
            ctx.dma("gpsimd", pq_sb[:, (hh * NCH + c) * 128:(hh * NCH + c + 1) * 128],
                    pq[hh, c * 128:(c + 1) * 128, :])
        ctx.dma("gpsimd", po_sb[:, hh * D:(hh + 1) * D], poT[hh, :, :])

    xb = rg.sb("p2_xb", [128, NCH * TT], BF16)
    sqk = rg.sb("p2_sqk", [128, TT])
    maxrow = rg.sb("p2_maxrow", [1, 8])
    ps_k = rg.banks[4][:, 0:TT]
    ps_k2 = rg.banks[5][0:1, 0:TT]
    ps_v = rg.banks[6][:, 0:128]
    ones_col = rg.sb("ones_col", [128, 1])
    ctx.op("vector", lambda e: e.memset(maxrow[:, :], 0.0), writes=[maxrow[:, :]])
    for it in range(NT_FULL):
        t0 = it * TT
        for c in range(NCH):
            ctx.dma("sync", xb[:, c * TT:(c + 1) * TT],
                    xn0b[c * 128:(c + 1) * 128, t0:t0 + TT])
        for c in range(NCH):
            ctx.mm(ps_k, pk_sb[:, c * 128:(c + 1) * 128], xb[:, c * TT:(c + 1) * TT],
                   c == 0, c == NCH - 1)
        ctx.op("scalar", lambda e, t0=t0: e.activation(K_sb[:, t0:t0 + TT], ps_k, AF.Copy),
               reads=[ps_k], writes=[K_sb[:, t0:t0 + TT]])
        ctx.op("scalar", lambda e: e.activation(sqk[:, :], ps_k, AF.Square),
               reads=[ps_k], writes=[sqk[:, :]])
        ctx.mm(ps_k2, ones_col[:, :], sqk[:, :], True, True)
        ctx.op("vector", lambda e: e.reduce_max(maxrow[0:1, 1:2], ps_k2, AX.X),
               reads=[ps_k2], writes=[maxrow[0:1, 1:2]])
        ctx.op("vector", lambda e: e.tensor_max(maxrow[0:1, 0:1], maxrow[0:1, 0:1], maxrow[0:1, 1:2]),
               reads=[maxrow[0:1, 0:2]], writes=[maxrow[0:1, 0:1]])
        for tc in range(TT // 128):
            for c in range(NCH):
                ctx.mm(ps_v, xb[:, c * TT + tc * 128: c * TT + (tc + 1) * 128],
                       pv_sb[:, c * 128:(c + 1) * 128], c == 0, c == NCH - 1)
            kc_g = it * (TT // 128) + tc
            ctx.op("scalar", lambda e, kc_g=kc_g: e.activation(
                V_tok[:, kc_g * 128:(kc_g + 1) * 128], ps_v, AF.Copy),
                reads=[ps_v], writes=[V_tok[:, kc_g * 128:(kc_g + 1) * 128]])
    ctx.op("scalar", lambda e: e.activation(maxrow[0:1, 2:3], maxrow[0:1, 0:1], AF.Sqrt),
           reads=[maxrow[0:1, 0:1]], writes=[maxrow[0:1, 2:3]])
    ctx.barrier()

    O_all = rg.sb("p2_O_all", [128, 8 * TT], BF16)
    qb = rg.sb("p2_qb", [128, TT], BF16)
    sqq = rg.sb("p2_sqq", [128, TT])
    p_b = rg.sb("p2_pb", [128, 4 * TT], BF16)
    rowq = rg.sb("p2_rowq", [1, TT])
    sc1 = rg.sb("p2_sc1", [1, 4])
    negb_sb = rg.sb("p2_negb", [128, 1])
    o_sb = rg.sb("p2_osb", [128, TT])
    recrow = rg.sb("p2_rec", [1, TT])
    ysb = rg.sb("p2_ysb", [128, TT])
    ps_q = rg.banks[0][:, 0:TT]
    ps_scores = [rg.banks[i][:, 0:TT] for i in (1, 2, 6)]
    ps_sigs = [rg.banks[3][0:1, 0:TT], rg.banks[3][32:33, 0:TT]]
    ps_os = [rg.banks[4][:, 0:TT], rg.banks[7][:, 0:TT]]
    ps_m_row = rg.banks[5][0:1, 0:TT]
    ps_m_col = rg.banks[5][:, 0:1]
    ps_m_full = rg.banks[5][:, 0:TT]
    ones_row = rg.sb("ones_row", [1, 128])
    ones_col_b = rg.sb("ones_col_b", [128, 1], BF16)

    NKC = S // 128

    pend_b0 = [None]

    def b0_epilogue(hh):
        ps_sig, ps_o = ps_sigs[hh % 2], ps_os[hh % 2]
        ctx.op("scalar", lambda e, ps_o=ps_o: e.activation(o_sb[:, :], ps_o, AF.Copy),
               reads=[ps_o], writes=[o_sb[:, :]])
        ctx.op("vector", lambda e, ps_sig=ps_sig: e.reciprocal(recrow[0:1, :], ps_sig),
               reads=[ps_sig], writes=[recrow[0:1, :]])
        ctx.mm(ps_m_full, ones_row[0:1, :], recrow[0:1, :], True, True)
        ctx.op("vector", lambda e, hh=hh: e.tensor_mul(
            O_all[:, hh * TT:(hh + 1) * TT], o_sb[:, :], ps_m_full),
            reads=[o_sb[:, :], ps_m_full], writes=[O_all[:, hh * TT:(hh + 1) * TT]])

    for qt in range(NT_OWN):
        q0 = S + qt * TT
        for c in range(NCH):
            ctx.dma("sync", xb[:, c * TT:(c + 1) * TT],
                    xn0b[c * 128:(c + 1) * 128, q0:q0 + TT])
        for hh in range(8):
            for c in range(NCH):
                ctx.mm(ps_q, pq_sb[:, (hh * NCH + c) * 128:(hh * NCH + c + 1) * 128],
                       xb[:, c * TT:(c + 1) * TT], c == 0, c == NCH - 1)
            ctx.op("scalar", lambda e: e.activation(qb[:, :], ps_q, AF.Copy),
                   reads=[ps_q], writes=[qb[:, :]])
            ctx.op("scalar", lambda e: e.activation(sqq[:, :], ps_q, AF.Square),
                   reads=[ps_q], writes=[sqq[:, :]])
            ctx.mm(ps_m_row, ones_col[:, :], sqq[:, :], True, True)
            ctx.op("scalar", lambda e: e.activation(rowq[0:1, :], ps_m_row, AF.Sqrt),
                   reads=[ps_m_row], writes=[rowq[0:1, :]])
            ctx.op("vector", lambda e: e.reduce_max(sc1[0:1, 0:1], rowq[0:1, :], AX.X),
                   reads=[rowq[0:1, :]], writes=[sc1[0:1, 0:1]])
            ctx.op("vector", lambda e: e.tensor_mul(sc1[0:1, 1:2], sc1[0:1, 0:1], maxrow[0:1, 2:3]),
                   reads=[sc1[0:1, 0:1], maxrow[0:1, 2:3]], writes=[sc1[0:1, 1:2]])
            ctx.op("vector", lambda e: e.tensor_scalar_mul(sc1[0:1, 2:3], sc1[0:1, 1:2], -1.0),
                   reads=[sc1[0:1, 1:2]], writes=[sc1[0:1, 2:3]])
            ctx.mm(ps_m_col, ones_row[0:1, :], sc1[0:1, 2:3], True, True)
            ctx.op("scalar", lambda e: e.activation(negb_sb[:, :], ps_m_col, AF.Copy),
                   reads=[ps_m_col], writes=[negb_sb[:, :]])

            ps_sig, ps_o = ps_sigs[hh % 2], ps_os[hh % 2]

            def do_sums(kc, ps_sig=ps_sig, ps_o=ps_o):
                pbs = p_b[:, (kc % 4) * TT:(kc % 4 + 1) * TT]
                ctx.mm(ps_sig, ones_col_b[:, :], pbs, kc == 0, kc == NKC - 1)
                ctx.mm(ps_o, V_tok[:, kc * 128:(kc + 1) * 128], pbs, kc == 0, kc == NKC - 1)
            for kc in range(NKC):
                psbank = ps_scores[kc % 3]
                pbslice = p_b[:, (kc % 4) * TT:(kc % 4 + 1) * TT]
                ctx.mm(psbank, K_sb[:, kc * 128:(kc + 1) * 128], qb[:, :], True, True)
                ctx.op("scalar", lambda e, psbank=psbank, pbslice=pbslice: e.activation(
                    pbslice, psbank, AF.Exp, bias=negb_sb[:, 0:1]),
                    reads=[psbank, negb_sb[:, 0:1]], writes=[pbslice])
                if kc >= 1:
                    do_sums(kc - 1)
            do_sums(NKC - 1)
            if pend_b0[0] is not None:
                b0_epilogue(pend_b0[0])
            pend_b0[0] = hh
        b0_epilogue(pend_b0[0])
        pend_b0[0] = None
        for dc in range(NCH):
            for hh in range(8):
                ctx.mm(ps_q, po_sb[:, hh * D + dc * 128: hh * D + (dc + 1) * 128],
                       O_all[:, hh * TT:(hh + 1) * TT], hh == 0, hh == 7)
            ctx.op("scalar", lambda e: e.activation(ysb[:, :], ps_q, AF.Copy),
                   reads=[ps_q], writes=[ysb[:, :]])
            ctx.dma("scalar", s0[dc * 128:(dc + 1) * 128, qt * TT:(qt + 1) * TT], ysb[:, :])
    ctx.barrier()


def prep_b0_lnattn(inputs, c):
    ci = core_info(c)
    b, hh = ci["b"], ci["h"]
    x0 = np.asarray(inputs["L0_attn_x"][b], np.float32)
    x0T = np.ascontiguousarray(x0.T)
    own = x0T[:, hh * SH:(hh + 1) * SH]
    x0T_ext = np.ascontiguousarray(np.concatenate([x0T, own], axis=1))
    return {
        "x0T": x0T_ext,
        "pk": np.ascontiguousarray(np.asarray(inputs["L0_attn_proj_k"], np.float32)),
        "pv": np.ascontiguousarray(np.asarray(inputs["L0_attn_proj_v"], np.float32)),
        "pq": np.ascontiguousarray(np.asarray(inputs["L0_attn_proj_q"], np.float32)),
        "poT": np.ascontiguousarray(np.asarray(inputs["L0_attn_proj_o"], np.float32).transpose(0, 2, 1)),
    }




L0F_HALO = 513
NV0 = 70


def _emit_b0_scalar(ctx, rg, qsrow, ksrow, qk_scale):
    sc = rg.sb("b0_sc", [1, 4])
    negb0 = rg.sb("negb0", [128, 1])
    ones_row = rg.sb("ones_row", [1, 128])
    ps = rg.banks[7][:, 0:1]
    ctx.op("vector", lambda e: e.reduce_max(sc[0:1, 0:1], qsrow[0:1, :], AX.X, apply_absolute_value=True),
           reads=[qsrow[0:1, :]], writes=[sc[0:1, 0:1]])
    ctx.op("vector", lambda e: e.reduce_max(sc[0:1, 1:2], ksrow[0:1, :], AX.X, apply_absolute_value=True),
           reads=[ksrow[0:1, :]], writes=[sc[0:1, 1:2]])
    ctx.op("vector", lambda e: e.tensor_mul(sc[0:1, 2:3], sc[0:1, 0:1], sc[0:1, 1:2]),
           reads=[sc[0:1, 0:2]], writes=[sc[0:1, 2:3]])
    ctx.op("vector", lambda e: e.tensor_scalar_mul(sc[0:1, 3:4], sc[0:1, 2:3], -float(qk_scale)),
           reads=[sc[0:1, 2:3]], writes=[sc[0:1, 3:4]])
    ctx.mm(ps, ones_row[0:1, :], sc[0:1, 3:4], True, True)
    ctx.op("scalar", lambda e: e.activation(negb0[:, :], ps, AF.Copy),
           reads=[ps], writes=[negb0[:, :]])
    return negb0


def _emit_l2norm_rope(ctx, rg, raw_bf, T, scale_col, rnb_bank, h2_bank, rot_bank,
                      amap, bmap, out_bf, tag, nch=NCH):
    hselA = rg.sb("hselA", [128, 2])
    hselB = rg.sb("hselB", [2, 128])
    rotm = rg.sb("rotm", [128, 128], BF16)
    sqt_ = rg.sb(f"{tag}_sq", [128, TT])
    rnr_ = rg.sb(f"{tag}_rn", [2, 2 * TT])
    t1_ = rg.sb(f"{tag}_t1", [128, TT])
    t2_ = rg.sb(f"{tag}_t2", [128, TT])
    qtl_ = rg.sb(f"{tag}_qt", [128, TT], BF16)
    sqt = sqt_[:, 0:T]; t1 = t1_[:, 0:T]; t2 = t2_[:, 0:T]; qtl = qtl_[:, 0:T]
    rnr = rnr_
    eps12 = rg.sb("eps12", [1, 1])
    for c in range(nch):
        rslice = raw_bf[:, c * T:(c + 1) * T]
        ctx.op("scalar", lambda e, rslice=rslice: e.activation(sqt[:, :], rslice, AF.Square),
               reads=[rslice], writes=[sqt[:, :]])
        ctx.mm(h2_bank[0:2, 0:T], hselA[:, :], sqt[:, :], True, True)
        ctx.op("scalar", lambda e: e.activation(rnr[0:2, 0:T], h2_bank[0:2, 0:T], AF.Sqrt),
               reads=[h2_bank[0:2, 0:T]], writes=[rnr[0:2, 0:T]])
        ctx.op("vector", lambda e: e.tensor_scalar_max(rnr[0:2, TT:TT + T], rnr[0:2, 0:T], 1e-12),
               reads=[rnr[0:2, 0:T]], writes=[rnr[0:2, TT:TT + T]])
        ctx.op("vector", lambda e: e.reciprocal(rnr[0:2, 0:T], rnr[0:2, TT:TT + T]),
               reads=[rnr[0:2, TT:TT + T]], writes=[rnr[0:2, 0:T]])
        ctx.mm(rnb_bank[:, 0:T], hselB[:, :], rnr[0:2, 0:T], True, True)
        ctx.op("vector", lambda e, rslice=rslice: e.scalar_tensor_tensor(
            qtl[:, :], rslice, scale_col, rnb_bank[:, 0:T], ALU.mult, ALU.mult),
            reads=[rslice, scale_col, rnb_bank[:, 0:T]], writes=[qtl[:, :]])
        ctx.mm(rot_bank[:, 0:T], rotm[:, :], qtl[:, :], True, True)
        ctx.op("vector", lambda e, c=c: e.tensor_mul(t1[:, :], qtl[:, :], amap[c]),
               reads=[qtl[:, :], amap[c]], writes=[t1[:, :]])
        ctx.op("vector", lambda e, c=c: e.tensor_mul(t2[:, :], rot_bank[:, 0:T], bmap[c]),
               reads=[rot_bank[:, 0:T], bmap[c]], writes=[t2[:, :]])
        ctx.op("vector", lambda e, c=c: e.tensor_add(out_bf[:, c * T:(c + 1) * T], t1[:, :], t2[:, :]),
               reads=[t1[:, :], t2[:, :]], writes=[out_bf[:, c * T:(c + 1) * T]])


def build_l0f_attn():
    return build_lf_attn(8, 256)


def build_l1f_attn():
    return build_lf_attn(4, 512)


def build_lf_attn(NI, W):
    HW = L0F_HALO
    NKC = (W + TT) // 128
    INNER = NI * 128
    nc = bass.Bass("TRN2", target_bir_lowering=False)
    es = ExitStack()
    ctx = Ctx(nc, es)
    rg = Regs(nc, es)
    io = {}
    for nm, shp, dt_, kind in [("xw", [D, HW + SH], F32, "ExternalInput"),
                               ("wqkv", [D, 3 * INNER], F32, "ExternalInput"),
                               ("wo", [INNER, D], F32, "ExternalInput"),
                               ("ropes", [4, 128, HW + SH], F32, "ExternalInput"),
                               ("qsrow", [1, 64], F32, "ExternalInput"),
                               ("ksrow", [1, 64], F32, "ExternalInput"),
                               ("vecs_in", [128, NV0], F32, "ExternalInput"),
                               ("hselA_in", [128, 2], F32, "ExternalInput"),
                               ("hselB_in", [2, 128], F32, "ExternalInput"),
                               ("rotm_in", [128, 128], F32, "ExternalInput"),
                               ("idneg_in", [128, 128], F32, "ExternalInput"),
                               ("masks_in", [2, NKC, 128, TT], F32, "ExternalInput"),
                               ("xo", [D, SH], F32, "ExternalOutput")]:
        io[nm] = nc.dram_tensor(nm, shp, dt_, kind=kind).ap()
    emit_setup(ctx, rg)
    io["xw_halo"] = io["xw"][:, 0:HW]
    io["xw"] = io["xw"][:, HW:]
    emit_lf_attn(ctx, rg, io, NI, W)
    ctx.finalize()
    return nc, es


def emit_lf_attn(ctx, rg, io, NI, W, lset=0):
    HW = L0F_HALO
    NKC = (W + TT) // 128
    NPV = W // 128
    INNER = NI * 128
    NT = SH // TT
    xw, wqkv, wo, ropes = io["xw"], io["wqkv"], io["wo"], io["ropes"]
    xw_halo = io["xw_halo"]
    qsrow, ksrow = io["qsrow"], io["ksrow"]
    vecs_d, hselA_d, hselB_d = io["vecs_in"], io["hselA_in"], io["hselB_in"]
    rotm_d, idneg_d, masks_d = io["rotm_in"], io["idneg_in"], io["masks_in"]
    xo = io["xo"]
    vecs = rg.sb("vecs", [128, NV0])
    hselA = rg.sb("hselA", [128, 2])
    hselB = rg.sb("hselB", [2, 128])
    rotm = rg.sb("rotm", [128, 128], BF16)
    idneg = rg.sb("idneg", [128, 128], BF16)
    eps12 = rg.sb("eps12", [1, 1])
    qsr = rg.sb("qsr", [1, 64])
    ksr = rg.sb("ksr", [1, 64])
    masks = rg.sb(f"masks_sb_{NI}", [128, 2 * NKC * TT], BF16)
    ctx.op("vector", lambda e: e.memset(eps12[:, :], 1e-12), writes=[eps12[:, :]])
    ctx.dma("sync", vecs[:, :], vecs_d)
    ctx.dma("sync", hselA[:, :], hselA_d)
    ctx.dma("sync", hselB[:, :], hselB_d)
    ctx.dma("sync", qsr[:, :], qsrow)
    ctx.dma("sync", ksr[:, :], ksrow)
    if io.get("load_consts", True):
        ctx.dma("gpsimd", rotm[:, :], rotm_d)
        ctx.dma("gpsimd", idneg[:, :], idneg_d)
        for mi in range(2):
            for kc in range(NKC):
                ctx.dma("gpsimd", masks[:, (mi * NKC + kc) * TT:(mi * NKC + kc + 1) * TT],
                        masks_d[mi, kc, :, :])
    negb0 = _emit_b0_scalar(ctx, rg, qsr, ksr, 8.0)
    wq_sb = rg.sb("wq_sb", [128, NCH * 3072], BF16)
    wo_sb = rg.sb("wo_sb", [128, 8 * 1024], BF16)
    for c in range(NCH):
        ctx.dma("gpsimd", wq_sb[:, c * 3 * INNER:(c + 1) * 3 * INNER], wqkv[c * 128:(c + 1) * 128, :])
    for c in range(NI):
        ctx.dma("gpsimd", wo_sb[:, c * 1024:(c + 1) * 1024], wo[c * 128:(c + 1) * 128, :])
    ropebuf = rg.sb("ropebuf", [128, 4 * TT])

    qraw = rg.sb("lf_qraw", [128, 8 * TT], BF16)
    kraw = rg.sb("lf_kraw", [128, 8 * TT], BF16)
    qcur = qraw
    kcur = kraw
    VG = 2 * NI * 65
    vcur = rg.sb("lf_vcur", [128, (TT // 128) * VG], BF16)
    kprev = rg.sb("lf_kprev", [128, 8 * 256 if NI == 8 else 4 * 512], BF16)
    vprev = rg.sb("lf_vprev", [128, (W // 128) * VG], BF16)
    _vt = vcur[:, 0:1].tensor
    _vrl = (TT // 128) * VG
    ctx.op("vector", lambda e: e.memset(
        bass.AP(_vt, 64, [[_vrl, 128], [65, 2 * NI * (TT // 128)], [1, 1]]), 1.0),
        writes=[vcur[:, :]])
    _pt = vprev[:, 0:1].tensor
    _prl = (W // 128) * VG
    ctx.op("vector", lambda e: e.memset(
        bass.AP(_pt, 64, [[_prl, 128], [65, 2 * NI * (W // 128)], [1, 1]]), 1.0),
        writes=[vprev[:, :]])
    p_b = rg.sb("lf_pb", [128, 4 * TT], BF16)
    o_f = rg.sb("lf_of", [128, 2 * TT])
    recr = rg.sb("lf_rec", [1, 2 * TT])
    Oall = rg.sb("lf_Oall", [128, NI * TT], BF16)
    osum = o_f[:, 0:TT]
    xot = rg.sb("lf_xot", [128, TT])
    ones_col_b = rg.sb("ones_col_b", [128, 1])
    ones_row = rg.sb("ones_row", [1, 128])
    onesb = rg.sb("ones_col_bb", [128, 1], BF16)
    ctx.op("vector", lambda e: e.memset(onesb[:, :], 1.0), writes=[onesb[:, :]])

    def load_ropes(lo, width):
        aps = {}
        for ti in range(4):
            dst = ropebuf[:, ti * TT: ti * TT + width]
            ctx.dma("sync", dst, ropes[ti, :, lo:lo + width])
            aps[ti] = dst
        return aps

    def qkv_project(xsb_ap_of, T, want_q=True):
        W3 = 3 * INNER
        bs = [0]
        for base_oc, dst in (((NI, kraw),) if not want_q else ((0, qraw), (NI, kraw))):
            for oc in range(NI):
                bank = rg.banks[bs[0] % 6][:, 0:T]
                bs[0] += 1
                for c in range(NCH):
                    ctx.mm(bank, wq_sb[:, c * W3 + (base_oc + oc) * 128: c * W3 + (base_oc + oc + 1) * 128],
                           xsb_ap_of(c), c == 0, c == NCH - 1)
                ctx.op("scalar", lambda e, bank=bank, dst=dst, oc=oc, T=T: e.activation(
                    dst[:, oc * T:(oc + 1) * T], bank, AF.Copy),
                    reads=[bank], writes=[dst[:, oc * T:(oc + 1) * T]])
        ntc = T // 128
        nvh = max(1, INNER // 512)
        vw = min(512, INNER)
        vrow = vcur[:, 0:1].tensor
        vrl = (TT // 128) * VG
        for tc in range(ntc):
            for vh in range(nvh):
                bank = rg.banks[bs[0] % 6][:, 0:vw]
                bs[0] += 1
                for c in range(NCH):
                    ctx.mm(bank, xsb_ap_of(c)[:, tc * 128:(tc + 1) * 128],
                           wq_sb[:, c * W3 + 2 * INNER + vh * vw: c * W3 + 2 * INNER + (vh + 1) * vw],
                           c == 0, c == NCH - 1)
                dst3 = bass.AP(vrow, vcur[:, 0:1].offset + tc * VG + vh * 8 * 65,
                               [[vrl, 128], [65, 8], [1, 64]])
                cov = vcur[:, tc * VG + vh * 8 * 65: tc * VG + vh * 8 * 65 + 8 * 65 - 1]
                ctx.op("scalar", lambda e, bank=bank, dst3=dst3: e.activation(
                    dst3, bank, AF.Copy), reads=[bank], writes=[cov])

    Tp = W
    xt = rg.sb("lf_xt", [128, NCH * (TT + 1)])
    xsb = rg.sb("lf_xsb", [128, NCH * TT], BF16)
    for c in range(NCH):
        ctx.dma("sync", xt[:, c * (TT + 1): c * (TT + 1) + Tp + 1],
                xw_halo[c * 128:(c + 1) * 128, HW - W - 1:HW])
    for c in range(NCH):
        src = xt[:, c * (TT + 1) + 1: c * (TT + 1) + Tp + 1] if c < 4 else \
              xt[:, c * (TT + 1): c * (TT + 1) + Tp]
        ctx.op("gpsimd", lambda e, c=c, src=src: e.tensor_copy(xsb[:, c * Tp:(c + 1) * Tp], src),
               reads=[src], writes=[xsb[:, c * Tp:(c + 1) * Tp]])
    qkv_project(lambda c: xsb[:, c * Tp:(c + 1) * Tp], Tp, want_q=False)
    rp = load_ropes(HW - W, Tp)
    amap = [rp[2]] * NI
    bmap = [rp[3]] * NI
    _emit_l2norm_rope(ctx, rg, kraw, Tp, vecs[:, 69:70], rg.banks[4], rg.banks[5],
                      rg.banks[6], amap, bmap, kprev, "lr", nch=NI)
    for i in range(NPV):
        ctx.op("gpsimd", lambda e, i=i: e.tensor_copy(vprev[:, i * VG:(i + 1) * VG],
                                                      vcur[:, i * VG:(i + 1) * VG]),
               reads=[vcur[:, i * VG:(i + 1) * VG]], writes=[vprev[:, i * VG:(i + 1) * VG]])

    xt2 = rg.sb("lf_xt2", [128, NCH * (TT + 1)])
    for it in range(NT):
        t0 = it * TT
        xt = (rg.sb("lf_xt", [128, NCH * (TT + 1)]), xt2)[it % 2]
        xsb = rg.sb("lf_xsb", [128, NCH * TT], BF16)
        for c in range(NCH):
            if it == 0:
                ctx.dma("sync", xt[:, c * (TT + 1):c * (TT + 1) + 1],
                        xw_halo[c * 128:(c + 1) * 128, HW - 1:HW],
                        allow_slow_non_contiguous=True)
                ctx.dma("sync", xt[:, c * (TT + 1) + 1:(c + 1) * (TT + 1)],
                        xw[c * 128:(c + 1) * 128, 0:TT])
            else:
                ctx.dma("sync", xt[:, c * (TT + 1):(c + 1) * (TT + 1)],
                        xw[c * 128:(c + 1) * 128, t0 - 1: t0 + TT])
        for c in range(NCH):
            src = xt[:, c * (TT + 1) + 1:(c + 1) * (TT + 1)] if c < 4 else \
                  xt[:, c * (TT + 1):(c + 1) * (TT + 1) - 1]
            ctx.op("gpsimd", lambda e, c=c, src=src: e.tensor_copy(xsb[:, c * TT:(c + 1) * TT], src),
                   reads=[src], writes=[xsb[:, c * TT:(c + 1) * TT]])
        qkv_project(lambda c: xsb[:, c * TT:(c + 1) * TT], TT, want_q=True)
        rp = load_ropes(HW + t0, TT)
        _emit_l2norm_rope(ctx, rg, qraw, TT, vecs[:, 68:69], rg.banks[4], rg.banks[5],
                          rg.banks[6], [rp[0]] * NI, [rp[1]] * NI, qcur, "lr", nch=NI)
        _emit_l2norm_rope(ctx, rg, kraw, TT, vecs[:, 69:70], rg.banks[4], rg.banks[5],
                          rg.banks[6], [rp[2]] * NI, [rp[3]] * NI, kcur, "lr", nch=NI)
        mbase = NKC * TT if it == 0 else 0
        _emit_lf_attn_tile(ctx, rg, NI, W, mbase, qcur, kcur, vcur, kprev, vprev,
                           p_b, o_f, recr, Oall, masks, idneg, negb0, onesb)
        for oc in range(NCH):
            bank = rg.banks[oc % 6][:, 0:TT]
            for c in range(NI):
                ctx.mm(bank, wo_sb[:, c * 1024 + oc * 128: c * 1024 + (oc + 1) * 128],
                       Oall[:, c * TT:(c + 1) * TT], c == 0, c == NI - 1)
            ctx.op("scalar", lambda e, bank=bank: e.activation(osum[:, :], bank, AF.Copy),
                   reads=[bank], writes=[osum[:, :]])
            xres = xt[:, oc * (TT + 1) + 1:(oc + 1) * (TT + 1)]
            ctx.op("vector", lambda e, oc=oc, xres=xres: e.tensor_add(xot[:, :], osum[:, :], xres),
                   reads=[osum[:, :], xres], writes=[xot[:, :]])
            ctx.dma("scalar", xo[oc * 128:(oc + 1) * 128, t0:t0 + TT], xot[:, :])
        for c in range(NI):
            ctx.op("gpsimd", lambda e, c=c: e.tensor_copy(
                kprev[:, c * W:(c + 1) * W], kcur[:, c * TT + TT - W:(c + 1) * TT]),
                reads=[kcur[:, c * TT + TT - W:(c + 1) * TT]], writes=[kprev[:, c * W:(c + 1) * W]])
        for i in range(NPV):
            src_v = vcur[:, (TT // 128 - NPV + i) * VG:(TT // 128 - NPV + i + 1) * VG]
            ctx.op("gpsimd", lambda e, i=i, src_v=src_v: e.tensor_copy(
                vprev[:, i * VG:(i + 1) * VG], src_v),
                reads=[src_v], writes=[vprev[:, i * VG:(i + 1) * VG]])
    ctx.barrier()


def _emit_lf_attn_tile(ctx, rg, NI, W, mbase, qcur, kcur, vcur, kprev, vprev,
                       p_b, o_f, recr, Oall, masks, idneg, negb0, onesb):
    ones_row = rg.sb("ones_row", [1, 128])
    INNER = NI * 128
    VG = 2 * NI * 65
    NKC = (W + TT) // 128
    NPK = W // 128

    def epilogue(hc, hp, sig, pvb, sb_bank):
        r0 = 64 * hp
        ctx.op("scalar", lambda e, pvb=pvb, hp=hp: e.activation(
            o_f[0:64, hp * TT:(hp + 1) * TT], pvb[0:64, 0:TT], AF.Copy),
            reads=[pvb[0:64, 0:TT]], writes=[o_f[0:64, hp * TT:(hp + 1) * TT]])
        ctx.op("vector", lambda e, sig=sig, hp=hp: e.reciprocal(
            recr[0:1, hp * TT:(hp + 1) * TT], sig),
            reads=[sig], writes=[recr[0:1, hp * TT:(hp + 1) * TT]])
        rb = rg.banks[4 + 3 * hp][0:64, 0:TT]
        ctx.mm(rb, ones_row[0:1, 0:64], recr[0:1, hp * TT:(hp + 1) * TT], True, True)
        dst = Oall[r0:r0 + 64, hc * TT:(hc + 1) * TT]
        ctx.op("vector", lambda e, rb=rb, dst=dst, hp=hp: e.tensor_mul(
            dst, o_f[0:64, hp * TT:(hp + 1) * TT], rb),
            reads=[o_f[0:64, hp * TT:(hp + 1) * TT], rb], writes=[dst])

    pend = [None]
    for hh in range(2 * NI):
        hc, hp = hh // 2, hh % 2
        r0 = 64 * hp
        sb_bank = rg.banks[4 + 3 * hp][:, 0:TT]
        pvb = rg.banks[5 + hp][0:65, 0:TT]
        sig = rg.banks[5 + hp][64:65, 0:TT]
        qh = qcur[r0:r0 + 64, hc * TT:(hc + 1) * TT]

        g65 = (hc * 2 + hp) * 65

        def vl_of(kc):
            if kc < NPK:
                return vprev[:, kc * VG + g65: kc * VG + g65 + 65]
            return vcur[:, (kc - NPK) * VG + g65: (kc - NPK) * VG + g65 + 65]

        def do_sums2(kc):
            pbs = p_b[:, ((hh % 2) * 2 + kc % 2) * TT:((hh % 2) * 2 + kc % 2 + 1) * TT]
            ctx.mm(pvb, vl_of(kc), pbs, kc == 0, kc == NKC - 1)
        for kc in range(NKC):
            if kc < NPK:
                klhs = kprev[r0:r0 + 64, hc * W + kc * 128: hc * W + (kc + 1) * 128]
            else:
                klhs = kcur[r0:r0 + 64, hc * TT + (kc - NPK) * 128: hc * TT + (kc - NPK + 1) * 128]
            mslice = masks[:, mbase + kc * TT: mbase + (kc + 1) * TT]
            pbuf = p_b[:, ((hh % 2) * 2 + kc % 2) * TT:((hh % 2) * 2 + kc % 2 + 1) * TT]
            ctx.mm(sb_bank, klhs, qh, True, False)
            ctx.mm(sb_bank, idneg[:, :], mslice, False, True)
            ctx.op("scalar", lambda e, sb_bank=sb_bank, pbuf=pbuf: e.activation(
                pbuf, sb_bank, AF.Exp, bias=negb0[:, 0:1]),
                reads=[sb_bank, negb0[:, 0:1]], writes=[pbuf])
            if kc >= 1:
                do_sums2(kc - 1)
        do_sums2(NKC - 1)
        if pend[0] is not None:
            epilogue(*pend[0])
        pend[0] = (hc, hp, sig, pvb, sb_bank)
    epilogue(*pend[0])
    pend[0] = None


def build_l0f_ffn():
    nc = bass.Bass("TRN2", target_bir_lowering=False)
    es = ExitStack()
    ctx = Ctx(nc, es)
    rg = Regs(nc, es)
    io = {}
    for nm, shp, dt_, kind in [("xw1", [D, 1 + SH], F32, "ExternalInput"),
                               ("w1p", [D, 5632], F32, "ExternalInput"),
                               ("w2p", [2816, 1024], F32, "ExternalInput"),
                               ("vecs_in", [128, NV0], F32, "ExternalInput"),
                               ("xo", [D, SH], F32, "ExternalOutput")]:
        io[nm] = nc.dram_tensor(nm, shp, dt_, kind=kind).ap()
    emit_setup(ctx, rg)
    emit_lf_ffn(ctx, rg, io)
    ctx.finalize()
    return nc, es


def emit_lf_ffn(ctx, rg, io):
    NT = SH // TT
    NIC2 = 22
    xw1, w1p, w2p, vecs_d, xo = io["xw1"], io["w1p"], io["w2p"], io["vecs_in"], io["xo"]
    xt2 = rg.sb("ff_xt2", [128, NCH * (TT + 1)])
    xn2 = rg.sb("ff_xn2", [128, NCH * TT], BF16)

    emit_setup(ctx, rg)
    vecs = rg.sb("vecs", [128, NV0])
    ctx.dma("sync", vecs[:, :], vecs_d)
    w1b = rg.sb("ff_w1b", [128, 4 * 8 * 128], BF16)
    w2b = rg.sb("ff_w2b", [128, 2 * 1024], BF16)

    ab = rg.sb("ff_ab", [128, NIC2 * TT], BF16)
    gb = rg.sb("ff_gb", [128, TT], BF16)
    ysum = rg.sb("ff_ysum", [128, TT])
    xot = rg.sb("ff_xot", [128, TT])
    st = make_ln_state(rg, "ffln", TT, [4, 5, 6, 7])
    for it in [NT - 1] + list(range(NT - 1)):
        t0 = it * TT
        xt = (rg.sb("ff_xt", [128, NCH * (TT + 1)]), xt2)[it % 2]
        xn = (rg.sb("ff_xn", [128, NCH * TT], BF16), xn2)[it % 2]
        for c in range(NCH):
            ctx.dma("sync", xt[:, c * (TT + 1):(c + 1) * (TT + 1)],
                    xw1[c * 128:(c + 1) * 128, t0:t0 + TT + 1])
        x_shift = [xt[:, c * (TT + 1) + 1:(c + 1) * (TT + 1)] if c < 4 else
                   xt[:, c * (TT + 1):(c + 1) * (TT + 1) - 1] for c in range(NCH)]
        gamma_cols = [vecs[:, c:c + 1] for c in range(NCH)]
        beta_cols = [vecs[:, 8 + c:8 + c + 1] for c in range(NCH)]
        xn_aps = [xn[:, c * TT:(c + 1) * TT] for c in range(NCH)]
        emit_ln_tile(ctx, rg, st, TT, x_shift, None, xn_aps,
                     gamma_cols=gamma_cols, beta_cols=beta_cols)
        for c2 in range(NIC2):
            sa, sg = (2 * c2) % 4, (2 * c2 + 1) % 4
            wa = w1b[:, sa * 1024:(sa + 1) * 1024]
            wg = w1b[:, sg * 1024:(sg + 1) * 1024]
            ctx.dma("sync", wa.rearrange("p (a c) -> p a c", a=8, c=128),
                    w1p[:, c2 * 128:(c2 + 1) * 128].rearrange("(a b) c -> b a c", a=8, b=128))
            ctx.dma("sync", wg.rearrange("p (a c) -> p a c", a=8, c=128),
                    w1p[:, (NIC2 + c2) * 128:(NIC2 + c2 + 1) * 128].rearrange("(a b) c -> b a c", a=8, b=128))
            bank_a = rg.banks[(2 * c2) % 6][:, 0:TT]
            bank_g = rg.banks[(2 * c2 + 1) % 6][:, 0:TT]
            for c in range(NCH):
                ctx.mm(bank_a, wa[:, c * 128:(c + 1) * 128], xn_aps[c], c == 0, c == NCH - 1)
            for c in range(NCH):
                ctx.mm(bank_g, wg[:, c * 128:(c + 1) * 128], xn_aps[c], c == 0, c == NCH - 1)
            acol = vecs[:, 16 + c2:16 + c2 + 1]
            gcol = vecs[:, 16 + NIC2 + c2:16 + NIC2 + c2 + 1]
            aslice = ab[:, c2 * TT:(c2 + 1) * TT]
            ctx.op("vector", lambda e, bank_a=bank_a, aslice=aslice, acol=acol: e.tensor_scalar_add(
                aslice, bank_a, acol), reads=[bank_a, acol], writes=[aslice])
            ctx.op("scalar", lambda e, bank_g=bank_g, gcol=gcol: e.activation(
                gb[:, :], bank_g, AF.Gelu, bias=gcol), reads=[bank_g, gcol], writes=[gb[:, :]])
            ctx.op("vector", lambda e, aslice=aslice: e.tensor_mul(aslice, aslice, gb[:, :]),
                   reads=[aslice, gb[:, :]], writes=[aslice])
        for c2 in range(NIC2):
            wslot = w2b[:, (c2 % 2) * 1024:(c2 % 2 + 1) * 1024]
            ctx.dma("sync", wslot, w2p[c2 * 128:(c2 + 1) * 128, :])
            for oc in range(NCH):
                ctx.mm(rg.banks[oc][:, 0:TT], wslot[:, oc * 128:(oc + 1) * 128],
                       ab[:, c2 * TT:(c2 + 1) * TT], c2 == 0, c2 == NIC2 - 1)
        for oc in range(NCH):
            bank = rg.banks[oc][:, 0:TT]
            bcol = vecs[:, 60 + oc:60 + oc + 1]
            ctx.op("vector", lambda e, bank=bank, bcol=bcol: e.tensor_scalar_add(
                ysum[:, :], bank, bcol), reads=[bank, bcol], writes=[ysum[:, :]])
            xres = xt[:, oc * (TT + 1) + 1:(oc + 1) * (TT + 1)]
            ctx.op("vector", lambda e, xres=xres: e.tensor_add(xot[:, :], ysum[:, :], xres),
                   reads=[ysum[:, :], xres], writes=[xot[:, :]])
            ctx.dma("scalar", xo[oc * 128:(oc + 1) * 128, t0:t0 + TT], xot[:, :])
    ctx.barrier()


def _l0f_consts(inputs, li):
    vecs = np.zeros((128, NV0), np.float32)
    ln_g = np.asarray(inputs["L0_ffn_ln_g"][li], np.float32)
    ln_b = np.asarray(inputs["L0_ffn_ln_b"][li], np.float32)
    b1 = np.asarray(inputs["L0_ffn_b1"][li], np.float32)
    b2 = np.asarray(inputs["L0_ffn_b2"][li], np.float32)
    qs = np.asarray(inputs["L0_ffn_q_scale"][li], np.float32)
    ks = np.asarray(inputs["L0_ffn_k_scale"][li], np.float32)
    for c in range(8):
        vecs[:, c] = ln_g[c * 128:(c + 1) * 128]
        vecs[:, 8 + c] = ln_b[c * 128:(c + 1) * 128]
        vecs[:, 60 + c] = b2[c * 128:(c + 1) * 128]
    b1p = np.zeros(5632, np.float32)
    b1p[0:2730] = b1[0:2730]
    b1p[2816:2816 + 2730] = b1[2730:5460]
    for c in range(44):
        vecs[:, 16 + c] = b1p[c * 128:(c + 1) * 128]
    vecs[:, 68] = np.tile(qs, 2)
    vecs[:, 69] = np.tile(ks, 2)
    W1 = np.asarray(inputs["L0_ffn_W1"][li], np.float32)
    w1p = np.zeros((D, 5632), np.float32)
    w1p[:, 0:2730] = W1[:, 0:2730]
    w1p[:, 2816:2816 + 2730] = W1[:, 2730:5460]
    W2 = np.asarray(inputs["L0_ffn_W2"][li], np.float32)
    w2p = np.zeros((2816, 1024), np.float32)
    w2p[0:2730, :] = W2
    return vecs, np.ascontiguousarray(w1p), np.ascontiguousarray(w2p)


_HSEL_A = None


def _l0f_static(c):
    ci = core_info(c)
    own0 = ci["h"] * SH
    aq, bq, ak, bk = rope_tables_l0(own0)
    ropes = np.stack([aq, bq, ak, bk])
    hselA = np.zeros((128, 2), np.float32)
    hselA[0:64, 0] = 1.0
    hselA[64:128, 1] = 1.0
    hselB = np.zeros((2, 128), np.float32)
    hselB[0, 0:64] = 1.0
    hselB[1, 64:128] = 1.0
    rotm = rot_matrix(64, 2)
    idneg = (np.eye(128, dtype=np.float32) * -1e5)
    masks = l0f_masks(ci["has_prev"])
    return dict(ropes=ropes, hselA_in=hselA, hselB_in=hselB, rotm_in=rotm,
                idneg_in=idneg, masks_in=masks)


def rope_tables_l0(own0):
    pos = np.arange(-L0F_HALO, SH) + own0
    pos = np.clip(pos, 0, None)
    aq, bq, ak, bk = rope_tables(S, 64, 256, pos, 8.0)
    return [np.ascontiguousarray(np.tile(t, (2, 1))) for t in (aq, bq, ak, bk)]


def rope_tables(n_total, dim_head, window, positions, qk_scale):
    half = np.arange(0, dim_head, 2, dtype=np.float64)
    inv_freq = 1.0 / 10000.0 ** (half / dim_head)
    t = positions.astype(np.float64)
    freqs = t[:, None] * inv_freq[None, :]
    freqs = np.concatenate([freqs, freqs], -1)
    base = (half + 0.4 * dim_head) / (1.4 * dim_head)
    power = (t - n_total // 2) / float(window // 2)
    scale = base[None, :] ** power[:, None]
    scale = np.concatenate([scale, scale], -1)
    cos, sin = np.cos(freqs), np.sin(freqs)
    aq = (cos * scale * qk_scale).T
    bq = (sin * scale * qk_scale).T
    ak = (cos / scale).T
    bk = (sin / scale).T
    return [np.asarray(x, np.float32) for x in (aq, bq, ak, bk)]


def rot_matrix(dim_head, nheads_per_chunk):
    dh = dim_head
    m = np.zeros((dh, dh), np.float32)
    for dcoord in range(dh // 2):
        m[dcoord + dh // 2, dcoord] = -1.0
        m[dcoord, dcoord + dh // 2] = 1.0
    full = np.zeros((128, 128), np.float32)
    for hh2 in range(nheads_per_chunk):
        full[hh2 * dh:(hh2 + 1) * dh, hh2 * dh:(hh2 + 1) * dh] = m
    return full


def l0f_masks(has_prev):
    out = np.zeros((2, 6, 128, TT), np.float32)
    q_rel = np.arange(TT)
    for kc in range(6):
        k_rel = np.arange(128) + kc * 128 - 256
        dmat = q_rel[None, :] - k_rel[:, None]
        invalid = (dmat < 0) | (dmat > 256)
        out[0, kc] = invalid
        inv0 = invalid.copy()
        if not has_prev and kc < 2:
            inv0[:, :] = 1.0
        out[1, kc] = inv0
    return np.ascontiguousarray(out)


def prep_l0f_attn(inputs, c, li):
    st_ = _l0f_static(c)
    vecs, _, _ = _l0f_consts(inputs, li)
    return dict(
        wqkv=np.ascontiguousarray(np.asarray(inputs["L0_ffn_Wqkv"][li], np.float32)),
        wo=np.ascontiguousarray(np.asarray(inputs["L0_ffn_Wo"][li], np.float32)),
        qsrow=np.asarray(inputs["L0_ffn_q_scale"][li], np.float32).reshape(1, 64),
        ksrow=np.asarray(inputs["L0_ffn_k_scale"][li], np.float32).reshape(1, 64),
        vecs_in=vecs, **st_)


def prep_l0f_ffn(inputs, c, li):
    vecs, w1p, w2p = _l0f_consts(inputs, li)
    return dict(w1p=w1p, w2p=w2p, vecs_in=vecs)


def build_combine():
    nc = bass.Bass("TRN2", target_bir_lowering=False)
    es = ExitStack()
    ctx = Ctx(nc, es)
    rg = Regs(nc, es)
    io = {}
    for nm, kind in [("xT", "ExternalInput"), ("sT", "ExternalInput"),
                     ("gT", "ExternalInput"), ("xo", "ExternalOutput")]:
        io[nm] = nc.dram_tensor(nm, [D, SH], F32, kind=kind).ap()
    emit_combine(ctx, rg, io)
    ctx.finalize()
    return nc, es


def emit_combine(ctx, rg, io):
    NT = SH // TT
    xT, sT, gT, xo = io["xT"], io["sT"], io["gT"], io["xo"]
    xb_ = rg.sb("cb_x", [128, TT])
    sb_ = rg.sb("cb_s", [128, TT])
    gb_ = rg.sb("cb_g", [128, TT])
    sg_ = rg.sb("cb_sg", [128, TT])
    ot_ = rg.sb("cb_o", [128, TT])
    for it in range(NT * NCH):
        c, ti = it % NCH, it // NCH
        t0 = ti * TT
        r = slice(c * 128, (c + 1) * 128)
        ctx.dma("sync", xb_[:, :], xT[r, t0:t0 + TT])
        ctx.dma("sync", sb_[:, :], sT[r, t0:t0 + TT])
        ctx.dma("sync", gb_[:, :], gT[r, t0:t0 + TT])
        ctx.op("scalar", lambda e: e.activation(sg_[:, :], gb_[:, :], AF.Silu),
               reads=[gb_[:, :]], writes=[sg_[:, :]])
        ctx.op("vector", lambda e: e.tensor_mul(sg_[:, :], sg_[:, :], sb_[:, :]),
               reads=[sg_[:, :], sb_[:, :]], writes=[sg_[:, :]])
        ctx.op("vector", lambda e: e.tensor_add(ot_[:, :], sg_[:, :], xb_[:, :]),
               reads=[sg_[:, :], xb_[:, :]], writes=[ot_[:, :]])
        ctx.dma("scalar", xo[r, t0:t0 + TT], ot_[:, :])
    ctx.barrier()


def build_ln(ncols):
    nc = bass.Bass("TRN2", target_bir_lowering=False)
    es = ExitStack()
    ctx = Ctx(nc, es)
    rg = Regs(nc, es)
    io = {}
    io["xT"] = nc.dram_tensor("xT", [D, ncols], F32, kind="ExternalInput").ap()
    io["xnf"] = nc.dram_tensor("xnf", [D, ncols], F32, kind="ExternalOutput").ap()
    io["xnb"] = nc.dram_tensor("xnb", [D, ncols], BF16, kind="ExternalOutput").ap()
    emit_setup(ctx, rg)
    emit_ln_pass(ctx, rg, io, ncols)
    ctx.finalize()
    return nc, es


def emit_ln_pass(ctx, rg, io, ncols):
    nt = ncols // TT
    xT, xnf, xnb = io["xT"], io["xnf"], io["xnb"]
    st = make_ln_state(rg, "gln", TT, [0, 1, 2, 3])
    xs = rg.sb("gln_x", [128, NCH * TT])
    of = rg.sb("gln_of", [128, NCH * TT])
    ob = rg.sb("gln_ob", [128, NCH * TT], BF16)
    for it in range(nt):
        t0 = it * TT
        x_aps = []
        for c in range(NCH):
            ap = xs[:, c * TT:(c + 1) * TT]
            ctx.dma("sync", ap, xT[c * 128:(c + 1) * 128, t0:t0 + TT])
            x_aps.append(ap)
        of_aps = [of[:, c * TT:(c + 1) * TT] for c in range(NCH)]
        ob_aps = [ob[:, c * TT:(c + 1) * TT] for c in range(NCH)]
        emit_ln_tile(ctx, rg, st, TT, x_aps, of_aps, ob_aps)
        for c in range(NCH):
            ctx.dma("scalar", xnf[c * 128:(c + 1) * 128, t0:t0 + TT], of_aps[c])
            ctx.dma("gpsimd", xnb[c * 128:(c + 1) * 128, t0:t0 + TT], ob_aps[c])
    ctx.barrier()


NV1 = 84


def build_l1_pre():
    nc = bass.Bass("TRN2", target_bir_lowering=False)
    es = ExitStack()
    ctx = Ctx(nc, es)
    rg = Regs(nc, es)
    NI2 = 16
    io = {}
    for nm, shp, dt_, kind in [("xnw", [D, 1024 + SH], F32, "ExternalInput"),
                               ("w_upT", [D, 4096], F32, "ExternalInput"),
                               ("wqd", [NI2, 128, 128], F32, "ExternalInput"),
                               ("wkd", [NI2, 128, 128], F32, "ExternalInput"),
                               ("wvd", [NI2, 128, 128], F32, "ExternalInput"),
                               ("wigT", [6144, 4], F32, "ExternalInput"),
                               ("wfgT", [6144, 4], F32, "ExternalInput"),
                               ("vecs_in", [128, NV1], F32, "ExternalInput"),
                               ("xc_o", [2048, SH], BF16, "ExternalOutput"),
                               ("z_o", [2048, SH], BF16, "ExternalOutput"),
                               ("qf_o", [2048, SH], BF16, "ExternalOutput"),
                               ("kf_o", [2048, SH], BF16, "ExternalOutput"),
                               ("vtok_o", [SH, 2048], BF16, "ExternalOutput"),
                               ("ktok_o", [SH, 2048], BF16, "ExternalOutput"),
                               ("arow_o", [4, SH], F32, "ExternalOutput"),
                               ("mrow_o", [4, SH], F32, "ExternalOutput"),
                               ("lcrow_o", [4, SH], F32, "ExternalOutput"),
                               ("gscal_o", [4, 8], F32, "ExternalOutput"),
                               ("Mc_o", [2048, 512], F32, "ExternalOutput"),
                               ("ncv_o", [128, 16], F32, "ExternalOutput")]:
        io[nm] = nc.dram_tensor(nm, shp, dt_, kind=kind).ap()
    emit_setup(ctx, rg)
    emit_l1_pre(ctx, rg, io)
    ctx.finalize()
    return nc, es


def emit_l1_pre(ctx, rg, io):
    NT = SH // TT
    NI2 = 16
    xnw, w_upT = io["xnw"], io["w_upT"]
    wqd, wkd, wvd, wigT, wfgT = io["wqd"], io["wkd"], io["wvd"], io["wigT"], io["wfgT"]
    vecs_d = io["vecs_in"]
    xc_o, z_o, qf_o, kf_o = io["xc_o"], io["z_o"], io["qf_o"], io["kf_o"]
    vtok_o, ktok_o = io["vtok_o"], io["ktok_o"]
    arow_o, mrow_o, lcrow_o, gscal_o = io["arow_o"], io["mrow_o"], io["lcrow_o"], io["gscal_o"]
    Mc_o, ncv_o = io["Mc_o"], io["ncv_o"]
    emit_setup(ctx, rg)
    vecs = rg.sb("vecs", [128, NV1])
    ctx.dma("sync", vecs[:, :], vecs_d)
    wup_sb = rg.sb("wup_sb", [128, NCH * 4096], BF16)
    for c in range(NCH):
        ctx.dma("gpsimd", wup_sb[:, c * 4096:(c + 1) * 4096], w_upT[c * 128:(c + 1) * 128, :])
    wq_sb = rg.sb("l1_wq", [128, NI2 * 128], BF16)
    wk_sb = rg.sb("l1_wk", [128, NI2 * 128], BF16)
    wv_sb = rg.sb("l1_wv", [128, NI2 * 128], BF16)
    for c in range(NI2):
        ctx.dma("gpsimd", wq_sb[:, c * 128:(c + 1) * 128], wqd[c, :, :])
        ctx.dma("gpsimd", wk_sb[:, c * 128:(c + 1) * 128], wkd[c, :, :])
        ctx.dma("gpsimd", wv_sb[:, c * 128:(c + 1) * 128], wvd[c, :, :])
    wig_sb = rg.sb("l1_wig", [128, 48 * 4], BF16)
    wfg_sb = rg.sb("l1_wfg", [128, 48 * 4], BF16)
    for c in range(48):
        ctx.dma("gpsimd", wig_sb[:, c * 4:(c + 1) * 4], wigT[c * 128:(c + 1) * 128, :])
        ctx.dma("gpsimd", wfg_sb[:, c * 4:(c + 1) * 4], wfgT[c * 128:(c + 1) * 128, :])
    idf = rg.sb("idf", [128, 128])
    onesq = rg.sb("onesq", [128, 128])
    ctx.op("vector", lambda e: e.memset(onesq[:, :], 1.0), writes=[onesq[:, :]])
    ctx.op("gpsimd", lambda e: e.affine_select(idf[:, :], onesq[:, :], [[-1, 128]],
                                               ALU.is_equal, 0.0, channel_multiplier=1),
           reads=[onesq[:, :]], writes=[idf[:, :]])

    xs = rg.sb("l1_xs", [128, NCH * TT], BF16)
    xh3 = rg.sb("l1_xh3", [128, NCH * 3], BF16)
    xm = rg.sb("l1_xm", [128, NI2 * (TT + 3)], BF16)
    ev1 = rg.sb("l1_ev1", [128, TT])
    ev2 = rg.sb("l1_ev2", [128, TT])
    xcb = rg.sb("l1_xcb", [128, NI2 * TT], BF16)
    qkvb = rg.sb("l1_qkvb", [128, 3 * TT], BF16)
    evb = rg.sb("l1_evb", [128, TT], BF16)
    tokb = rg.sb("l1_tokb", [128, 128], BF16)
    grows = rg.sb("l1_grows", [4, 6 * TT])
    ones4t = rg.sb("l1_ones4t", [4, TT])
    carry = rg.sb("l1_carry", [4, 4])
    gs = rg.sb("l1_gs", [4, 8])
    ones_col = rg.sb("ones_col", [128, 1])
    G = lambda i: grows[0:4, i * TT:(i + 1) * TT]
    ctx.op("vector", lambda e: e.memset(ones4t[0:4, :], 1.0), writes=[ones4t[0:4, :]])
    ctx.op("vector", lambda e: e.memset(carry[0:4, :], 0.0), writes=[carry[0:4, :]])
    ctx.op("vector", lambda e: e.memset(gs[0:4, :], 0.0), writes=[gs[0:4, :]])
    ctx.op("vector", lambda e: e.tensor_scalar_add(carry[0:4, 1:2], carry[0:4, 1:2], -3e38),
           reads=[carry[0:4, 1:2]], writes=[carry[0:4, 1:2]])

    for it in range(NT):
        t0 = it * TT
        for c in range(NCH):
            ctx.dma("gpsimd", xs[:, c * TT:(c + 1) * TT],
                    xnw[c * 128:(c + 1) * 128, 1024 + t0:1024 + t0 + TT])
        if it == 0:
            for c in range(NCH):
                ctx.dma("gpsimd", xh3[:, c * 3:(c + 1) * 3],
                        xnw[c * 128:(c + 1) * 128, 1021:1024])
            for oc in range(NI2):
                bank = rg.banks[0][:, 0:3]
                for c in range(NCH):
                    ctx.mm(bank, wup_sb[:, c * 4096 + oc * 128: c * 4096 + (oc + 1) * 128],
                           xh3[:, c * 3:(c + 1) * 3], c == 0, c == NCH - 1)
                ctx.op("scalar", lambda e, bank=bank, oc=oc: e.activation(
                    xm[:, oc * (TT + 3): oc * (TT + 3) + 3], bank, AF.Copy),
                    reads=[bank], writes=[xm[:, oc * (TT + 3): oc * (TT + 3) + 3]])
        else:
            for oc in range(NI2):
                srcv = xm[:, oc * (TT + 3) + TT: (oc + 1) * (TT + 3)]
                dstv = xm[:, oc * (TT + 3): oc * (TT + 3) + 3]
                ctx.op("gpsimd", lambda e, srcv=srcv, dstv=dstv: e.tensor_copy(dstv, srcv),
                       reads=[srcv], writes=[dstv])
        for oc in range(NI2):
            bank = rg.banks[oc % 2][:, 0:TT]
            for c in range(NCH):
                ctx.mm(bank, wup_sb[:, c * 4096 + oc * 128: c * 4096 + (oc + 1) * 128],
                       xs[:, c * TT:(c + 1) * TT], c == 0, c == NCH - 1)
            dstm = xm[:, oc * (TT + 3) + 3:(oc + 1) * (TT + 3)]
            ctx.op("scalar", lambda e, bank=bank, dstm=dstm: e.activation(dstm, bank, AF.Copy),
                   reads=[bank], writes=[dstm])
        for oc in range(NI2):
            bank = rg.banks[oc % 2][:, 0:TT]
            for c in range(NCH):
                ctx.mm(bank, wup_sb[:, c * 4096 + 2048 + oc * 128: c * 4096 + 2048 + (oc + 1) * 128],
                       xs[:, c * TT:(c + 1) * TT], c == 0, c == NCH - 1)
            ctx.op("scalar", lambda e, bank=bank: e.activation(evb[:, :], bank, AF.Copy),
                   reads=[bank], writes=[evb[:, :]])
            ctx.dma("scalar", z_o[oc * 128:(oc + 1) * 128, t0:t0 + TT], evb[:, :])
        for oc in range(NI2):
            xm_c = xm[:, oc * (TT + 3):(oc + 1) * (TT + 3)]
            ctx.op("vector", lambda e, xm_c=xm_c, oc=oc: e.tensor_scalar_mul(
                ev1[:, :], xm_c[:, 0:TT], vecs[:, oc * 4:oc * 4 + 1]),
                reads=[xm_c[:, 0:TT], vecs[:, oc * 4:oc * 4 + 1]], writes=[ev1[:, :]])
            for j in range(1, 4):
                ctx.op("vector", lambda e, xm_c=xm_c, oc=oc, j=j: e.tensor_scalar_mul(
                    ev2[:, :], xm_c[:, j:j + TT], vecs[:, oc * 4 + j:oc * 4 + j + 1]),
                    reads=[xm_c[:, j:j + TT], vecs[:, oc * 4 + j:oc * 4 + j + 1]], writes=[ev2[:, :]])
                ctx.op("vector", lambda e: e.tensor_add(ev1[:, :], ev1[:, :], ev2[:, :]),
                       reads=[ev1[:, :], ev2[:, :]], writes=[ev1[:, :]])
            ctx.op("vector", lambda e, oc=oc: e.tensor_scalar_add(ev1[:, :], ev1[:, :],
                                                                  vecs[:, 64 + oc:64 + oc + 1]),
                   reads=[ev1[:, :], vecs[:, 64 + oc:64 + oc + 1]], writes=[ev1[:, :]])
            dst = xcb[:, oc * TT:(oc + 1) * TT]
            ctx.op("scalar", lambda e, dst=dst: e.activation(dst, ev1[:, :], AF.Silu),
                   reads=[ev1[:, :]], writes=[dst])
            ctx.dma("scalar", xc_o[oc * 128:(oc + 1) * 128, t0:t0 + TT], dst)
        big = rg.banks[7][0:4, 0:TT]
        bfg = rg.banks[3][0:4, 0:TT]
        for oc in range(NI2):
            bq = rg.banks[4][:, 0:TT]
            bk = rg.banks[5][:, 0:TT]
            bv = rg.banks[6][:, 0:TT]
            xm_body = xm[:, oc * (TT + 3) + 3:(oc + 1) * (TT + 3)]
            ctx.mm(bq, wq_sb[:, oc * 128:(oc + 1) * 128], xcb[:, oc * TT:(oc + 1) * TT], True, True)
            ctx.mm(bk, wk_sb[:, oc * 128:(oc + 1) * 128], xcb[:, oc * TT:(oc + 1) * TT], True, True)
            ctx.mm(bv, wv_sb[:, oc * 128:(oc + 1) * 128], xm_body, True, True)
            for bank, qo in ((bq, 0), (bk, 1), (bv, 2)):
                dstb = qkvb[:, qo * TT:(qo + 1) * TT]
                ctx.op("scalar", lambda e, bank=bank, dstb=dstb: e.activation(dstb, bank, AF.Copy),
                       reads=[bank], writes=[dstb])
            ctx.dma("scalar", qf_o[oc * 128:(oc + 1) * 128, t0:t0 + TT], qkvb[:, 0:TT])
            ctx.dma("scalar", kf_o[oc * 128:(oc + 1) * 128, t0:t0 + TT], qkvb[:, TT:2 * TT])
            for qo, base in ((0, 0), (1, NI2), (2, 2 * NI2)):
                idx = base + oc
                st_ = (oc == 0 and qo == 0)
                sp_ = (oc == NI2 - 1 and qo == 2)
                ctx.mm(big, wig_sb[:, idx * 4:(idx + 1) * 4], qkvb[:, qo * TT:(qo + 1) * TT], st_, sp_)
                ctx.mm(bfg, wfg_sb[:, idx * 4:(idx + 1) * 4], qkvb[:, qo * TT:(qo + 1) * TT], st_, sp_)
            for tc in range(TT // 128):
                bank = rg.banks[(tc + oc) % 2][:, 0:128]
                ctx.mm(bank, xcb[:, oc * TT + tc * 128: oc * TT + (tc + 1) * 128],
                       wk_sb[:, oc * 128:(oc + 1) * 128], True, True)
                ctx.op("scalar", lambda e, bank=bank: e.activation(tokb[:, :], bank, AF.Copy),
                       reads=[bank], writes=[tokb[:, :]])
                ctx.dma("scalar", ktok_o[t0 + tc * 128:t0 + (tc + 1) * 128, oc * 128:(oc + 1) * 128],
                        tokb[:, :])
                bank2 = rg.banks[2][:, 0:128]
                xm_sl = xm[:, oc * (TT + 3) + 3 + tc * 128: oc * (TT + 3) + 3 + (tc + 1) * 128]
                ctx.mm(bank2, xm_sl, wv_sb[:, oc * 128:(oc + 1) * 128], True, True)
                ctx.op("scalar", lambda e, bank2=bank2: e.activation(tokb[:, :], bank2, AF.Copy),
                       reads=[bank2], writes=[tokb[:, :]])
                ctx.dma("scalar", vtok_o[t0 + tc * 128:t0 + (tc + 1) * 128, oc * 128:(oc + 1) * 128],
                        tokb[:, :])
        ctx.op("scalar", lambda e, big=big: e.activation(G(0), big, AF.Copy),
               reads=[big], writes=[G(0)])
        ctx.op("scalar", lambda e, bfg=bfg: e.activation(G(1), bfg, AF.Copy),
               reads=[bfg], writes=[G(1)])
        ctx.op("vector", lambda e: e.tensor_scalar_add(G(0), G(0), vecs[0:4, 80:81]),
               reads=[G(0), vecs[0:4, 80:81]], writes=[G(0)])
        ctx.op("vector", lambda e: e.tensor_scalar_add(G(1), G(1), vecs[0:4, 81:82]),
               reads=[G(1), vecs[0:4, 81:82]], writes=[G(1)])
        ctx.op("scalar", lambda e: e.activation(G(2), G(1), AF.Abs), reads=[G(1)], writes=[G(2)])
        ctx.op("scalar", lambda e: e.activation(G(2), G(2), AF.Exp, scale=-1.0),
               reads=[G(2)], writes=[G(2)])
        ctx.op("scalar", lambda e: e.activation(G(2), G(2), AF.Ln, bias=ones_col[0:4, 0:1]),
               reads=[G(2), ones_col[0:4, 0:1]], writes=[G(2)])
        ctx.op("vector", lambda e: e.tensor_scalar_min(G(1), G(1), 0.0),
               reads=[G(1)], writes=[G(1)])
        ctx.op("vector", lambda e: e.tensor_sub(G(2), G(1), G(2)),
               reads=[G(1), G(2)], writes=[G(2)])
        ctx.op("vector", lambda e: e.tensor_tensor_scan(G(3), ones4t[0:4, :], G(2),
                                                        carry[0:4, 0:1], ALU.mult, ALU.add),
               reads=[ones4t[0:4, :], G(2), carry[0:4, 0:1]], writes=[G(3)])
        ctx.op("vector", lambda e: e.tensor_sub(G(4), G(0), G(3)),
               reads=[G(0), G(3)], writes=[G(4)])
        ctx.op("vector", lambda e: e.tensor_tensor_scan(G(5), ones4t[0:4, :], G(4),
                                                        carry[0:4, 1:2], ALU.mult, ALU.max),
               reads=[ones4t[0:4, :], G(4), carry[0:4, 1:2]], writes=[G(5)])
        ctx.op("gpsimd", lambda e: e.tensor_copy(carry[0:4, 0:1], grows[0:4, 4 * TT - 1:4 * TT]),
               reads=[grows[0:4, 4 * TT - 1:4 * TT]], writes=[carry[0:4, 0:1]])
        ctx.op("gpsimd", lambda e: e.tensor_copy(carry[0:4, 1:2], grows[0:4, 6 * TT - 1:6 * TT]),
               reads=[grows[0:4, 6 * TT - 1:6 * TT]], writes=[carry[0:4, 1:2]])
        ctx.dma("scalar", lcrow_o[0:4, t0:t0 + TT], G(3))
        ctx.dma("scalar", arow_o[0:4, t0:t0 + TT], G(4))
        ctx.dma("scalar", mrow_o[0:4, t0:t0 + TT], G(5))
    ctx.op("gpsimd", lambda e: e.tensor_copy(gs[0:4, 0:1], carry[0:4, 0:1]),
           reads=[carry[0:4, 0:1]], writes=[gs[0:4, 0:1]])
    ctx.op("gpsimd", lambda e: e.tensor_copy(gs[0:4, 1:2], carry[0:4, 1:2]),
           reads=[carry[0:4, 1:2]], writes=[gs[0:4, 1:2]])
    ctx.dma("scalar", gscal_o[0:4, :], gs[0:4, :])
    ctx.op("vector", lambda e: e.tensor_scalar_mul(carry[0:4, 2:3], carry[0:4, 1:2], -1.0),
           reads=[carry[0:4, 1:2]], writes=[carry[0:4, 2:3]])
    idf = rg.sb("idf", [128, 128])
    onesq = rg.sb("onesq", [128, 128])
    ctx.op("vector", lambda e: e.memset(onesq[:, :], 1.0), writes=[onesq[:, :]])
    ctx.op("gpsimd", lambda e: e.affine_select(idf[:, :], onesq[:, :], [[-1, 128]],
                                               ALU.is_equal, 0.0, channel_multiplier=1),
           reads=[onesq[:, :]], writes=[idf[:, :]])
    au = rg.sb("l1_au", [4, 128])
    kt_sb = rg.sb("l1_kt", [128, 512], BF16)
    vt_sb = rg.sb("l1_vt", [128, 512], BF16)
    ku_sb = rg.sb("l1_ku", [128, 512], BF16)
    mc_ev = rg.sb("l1_mcev", [128, 512])
    ncrow = rg.sb("l1_ncrow", [1, 512])
    ncv_sb = rg.sb("l1_ncv", [128, 16])
    ucol1 = rg.sb("l1_ucol1", [128, 4])
    onesb = rg.sb("ones_col_b", [128, 1], BF16)
    for h in range(4):
        for tc in range(SH // 128):
            ctx.dma("sync", kt_sb[:, :], ktok_o[tc * 128:(tc + 1) * 128, h * 512:(h + 1) * 512])
            ctx.dma("sync", vt_sb[:, :], vtok_o[tc * 128:(tc + 1) * 128, h * 512:(h + 1) * 512])
            ctx.dma("sync", au[0:4, 0:128], arow_o[0:4, tc * 128:(tc + 1) * 128])
            ctx.op("scalar", lambda e: e.activation(au[0:4, :], au[0:4, :], AF.Exp,
                                                    bias=carry[0:4, 2:3]),
                   reads=[au[0:4, :], carry[0:4, 2:3]], writes=[au[0:4, :]])
            bankt = rg.banks[5][:, 0:4]
            ctx.op("tensor", lambda e, bankt=bankt: e.transpose(bankt, au[0:4, :], idf[0:4, 0:4]),
                   reads=[au[0:4, :], idf[0:4, 0:4]], writes=[bankt])
            ctx.op("scalar", lambda e, bankt=bankt: e.activation(ucol1[:, :], bankt, AF.Copy),
                   reads=[bankt], writes=[ucol1[:, :]])
            ctx.op("vector", lambda e, h=h: e.tensor_scalar_mul(ku_sb[:, :], kt_sb[:, :],
                                                                ucol1[:, h:h + 1]),
                   reads=[kt_sb[:, :], ucol1[:, h:h + 1]], writes=[ku_sb[:, :]])
            for dc in range(4):
                ctx.mm(rg.banks[dc][:, 0:512], ku_sb[:, dc * 128:(dc + 1) * 128], vt_sb[:, :],
                       tc == 0, tc == SH // 128 - 1)
            ctx.mm(rg.banks[4][0:1, 0:512], onesb[:, :], ku_sb[:, :], tc == 0, tc == SH // 128 - 1)
        for dc in range(4):
            ctx.op("scalar", lambda e, dc=dc: e.activation(mc_ev[:, :], rg.banks[dc][:, 0:512], AF.Copy),
                   reads=[rg.banks[dc][:, 0:512]], writes=[mc_ev[:, :]])
            ctx.dma("scalar", Mc_o[h * 512 + dc * 128: h * 512 + (dc + 1) * 128, :], mc_ev[:, :])
        ctx.op("scalar", lambda e: e.activation(ncrow[0:1, :], rg.banks[4][0:1, 0:512], AF.Copy),
               reads=[rg.banks[4][0:1, 0:512]], writes=[ncrow[0:1, :]])
        for dc in range(4):
            bank = rg.banks[5][:, 0:1]
            ctx.op("tensor", lambda e, dc=dc, bank=bank: e.transpose(
                bank, ncrow[0:1, dc * 128:(dc + 1) * 128], idf[0:1, 0:1]),
                reads=[ncrow[0:1, dc * 128:(dc + 1) * 128], idf[0:1, 0:1]], writes=[bank])
            ctx.op("scalar", lambda e, h=h, dc=dc, bank=bank: e.activation(
                ncv_sb[:, h * 4 + dc:h * 4 + dc + 1], bank, AF.Copy),
                reads=[bank], writes=[ncv_sb[:, h * 4 + dc:h * 4 + dc + 1]])
    ctx.dma("scalar", ncv_o[:, :], ncv_sb[:, :])
    ctx.barrier()


NV2 = 32


def build_l1_mlstm():
    nc = bass.Bass("TRN2", target_bir_lowering=False)
    es = ExitStack()
    ctx = Ctx(nc, es)
    rg = Regs(nc, es)
    io = {}
    for nm, shp, dt_, kind in [("qf_i", [2048, SH], BF16, "ExternalInput"),
                               ("kf_i", [2048, SH], BF16, "ExternalInput"),
                               ("vtok_i", [SH, 2048], BF16, "ExternalInput"),
                               ("xc_i", [2048, SH], BF16, "ExternalInput"),
                               ("z_i", [2048, SH], BF16, "ExternalInput"),
                               ("arow_i", [4, SH], F32, "ExternalInput"),
                               ("mrow_i", [4, SH], F32, "ExternalInput"),
                               ("lcrow_i", [4, SH], F32, "ExternalInput"),
                               ("gscal_i", [4, 8], F32, "ExternalInput"),
                               ("Mc_p", [2048, 512], F32, "ExternalInput"),
                               ("ncv_p", [128, 16], F32, "ExternalInput"),
                               ("pscal", [4, 8], F32, "ExternalInput"),
                               ("w_downT", [2048, 1024], F32, "ExternalInput"),
                               ("vecs_in", [128, NV2], F32, "ExternalInput"),
                               ("sel4_in", [4, 512], F32, "ExternalInput"),
                               ("mdiag_in", [4, 128, TT], F32, "ExternalInput"),
                               ("s1_o", [D, SH], F32, "ExternalOutput")]:
        io[nm] = nc.dram_tensor(nm, shp, dt_, kind=kind).ap()
    emit_setup(ctx, rg)
    emit_l1_mlstm(ctx, rg, io)
    ctx.finalize()
    return nc, es


def emit_l1_mlstm(ctx, rg, io):
    NT = SH // TT
    NI2 = 16
    qf, kf, vtok = io["qf_i"], io["kf_i"], io["vtok_i"]
    xc_i, z_i = io["xc_i"], io["z_i"]
    arow, mrow, lcrow, gscal = io["arow_i"], io["mrow_i"], io["lcrow_i"], io["gscal_i"]
    Mc_p, ncv_p, pscal = io["Mc_p"], io["ncv_p"], io["pscal"]
    w_downT, vecs_d, sel4_d, mdiag_d = io["w_downT"], io["vecs_in"], io["sel4_in"], io["mdiag_in"]
    s1_o = io["s1_o"]
    emit_setup(ctx, rg)
    vecs = rg.sb("vecs", [128, NV2])
    sel4 = rg.sb("sel4", [4, 512])
    mdiag = rg.sb("mdiag", [128, 4 * TT], BF16)
    ctx.dma("sync", vecs[:, :], vecs_d)
    ctx.dma("sync", sel4[0:4, :], sel4_d)
    for i in range(4):
        ctx.dma("gpsimd", mdiag[:, i * TT:(i + 1) * TT], mdiag_d[i, :, :])
    idf = rg.sb("idf", [128, 128])
    onesq = rg.sb("onesq", [128, 128])
    ctx.op("vector", lambda e: e.memset(onesq[:, :], 1.0), writes=[onesq[:, :]])
    ctx.op("gpsimd", lambda e: e.affine_select(idf[:, :], onesq[:, :], [[-1, 128]],
                                               ALU.is_equal, 0.0, channel_multiplier=1),
           reads=[onesq[:, :]], writes=[idf[:, :]])
    wd_sb = rg.sb("wd_sb", [128, NI2 * 1024], BF16)
    for c in range(NI2):
        ctx.dma("gpsimd", wd_sb[:, c * 1024:(c + 1) * 1024], w_downT[c * 128:(c + 1) * 128, :])
    ncb = rg.sb("ncb", [128, 16], BF16)
    ctx.dma("gpsimd", ncb[:, :], ncv_p)

    a_sb = rg.sb("m_a", [4, SH])
    m_sb = rg.sb("m_m", [4, SH])
    lc_sb = rg.sb("m_lc", [4, SH])
    gsc = rg.sb("m_gsc", [4, 8])
    psc = rg.sb("m_psc", [4, 8])
    ctx.dma("sync", a_sb[0:4, :], arow)
    ctx.dma("sync", m_sb[0:4, :], mrow)
    ctx.dma("sync", lc_sb[0:4, :], lcrow)
    ctx.dma("sync", gsc[0:4, :], gscal)
    ctx.dma("sync", psc[0:4, :], pscal)

    sc2 = rg.sb("m_sc2", [4, 16])
    rows4 = rg.sb("m_rows4", [4, 4 * TT])
    R4 = lambda i: rows4[0:4, i * TT:(i + 1) * TT]
    ucols = rg.sb("m_ucols", [128, (SH // 128) * 4])
    qh_sb = rg.sb("m_qh", [128, 4 * TT], BF16)
    qsp = rg.sb("m_qsp", [128, 4 * TT], BF16)
    khc = rg.sb("m_khc", [128, 2 * 4 * 128], BF16)
    vt_sb = rg.sb("m_vt", [128, 1024], BF16)
    mc_sb = rg.sb("m_mc", [128, 4 * 512], BF16)
    pb_sb = rg.sb("m_pb", [128, 2 * TT], BF16)
    sob = rg.sb("m_sob", [128, 2])
    hn_f = rg.sb("m_hnf", [128, 4 * TT])
    rowb = rg.sb("m_rowb", [1, 6 * TT])
    hst = rg.sb("m_hst", [128, NI2 * TT], BF16)
    xcz = rg.sb("m_xcz", [128, 2 * NI2 * TT], BF16)
    ev1 = rg.sb("m_ev1", [128, TT])
    ysum = rg.sb("m_ysum", [128, TT])
    ones_col = rg.sb("ones_col", [128, 1])
    ones_row = rg.sb("ones_row", [1, 128])
    onesb = rg.sb("ones_col_b", [128, 1], BF16)
    eps5 = rg.sb("eps5", [1, 1])
    R8 = lambda i: rowb[0:1, i * TT:(i + 1) * TT]

    for qt in range(NT):
        t0 = qt * TT
        qend = t0 + TT
        nkc = qend // 128
        for c in range(NI2):
            ctx.dma("sync", xcz[:, c * TT:(c + 1) * TT], xc_i[c * 128:(c + 1) * 128, t0:t0 + TT])
            ctx.dma("sync", xcz[:, (NI2 + c) * TT:(NI2 + c + 1) * TT],
                    z_i[c * 128:(c + 1) * 128, t0:t0 + TT])
        cqc = sc2[0:4, 0:1]
        ctx.op("vector", lambda e, qend=qend: e.tensor_scalar(
            cqc, m_sb[0:4, qend - 1:qend], psc[0:4, 0:1], psc[0:4, 1:2], ALU.subtract, ALU.max),
            reads=[m_sb[0:4, qend - 1:qend], psc[0:4, 0:2]], writes=[cqc])
        ctx.op("vector", lambda e: e.tensor_scalar_mul(sc2[0:4, 1:2], cqc, -1.0),
               reads=[cqc], writes=[sc2[0:4, 1:2]])
        ctx.op("vector", lambda e: e.tensor_add(sc2[0:4, 4:5], cqc, psc[0:4, 0:1]),
               reads=[cqc, psc[0:4, 0:1]], writes=[sc2[0:4, 4:5]])
        ctx.op("vector", lambda e: e.tensor_scalar_mul(sc2[0:4, 4:5], sc2[0:4, 4:5], -1.0),
               reads=[sc2[0:4, 4:5]], writes=[sc2[0:4, 4:5]])
        for tc in range(nkc):
            ctx.op("scalar", lambda e, tc=tc: e.activation(R4(3)[0:4, 0:128],
                                                           a_sb[0:4, tc * 128:(tc + 1) * 128],
                                                           AF.Exp, bias=sc2[0:4, 4:5]),
                   reads=[a_sb[0:4, tc * 128:(tc + 1) * 128], sc2[0:4, 4:5]],
                   writes=[R4(3)[0:4, 0:128]])
            ctx.op("vector", lambda e: e.tensor_scalar_mul(R4(3)[0:4, 0:128], R4(3)[0:4, 0:128],
                                                           1.0 / math.sqrt(512.0)),
                   reads=[R4(3)[0:4, 0:128]], writes=[R4(3)[0:4, 0:128]])
            bankt = rg.banks[7][:, 0:4]
            ctx.op("tensor", lambda e, bankt=bankt: e.transpose(bankt, R4(3)[0:4, 0:128], idf[0:4, 0:4]),
                   reads=[R4(3)[0:4, 0:128], idf[0:4, 0:4]], writes=[bankt])
            ctx.op("scalar", lambda e, tc=tc, bankt=bankt: e.activation(
                ucols[:, tc * 4:(tc + 1) * 4], bankt, AF.Copy),
                reads=[bankt], writes=[ucols[:, tc * 4:(tc + 1) * 4]])
        ctx.op("scalar", lambda e: e.activation(sc2[0:4, 3:4], cqc, AF.Exp,
                                                scale=-1.0, bias=psc[0:4, 1:2]),
               reads=[cqc, psc[0:4, 1:2]], writes=[sc2[0:4, 3:4]])
        ctx.op("vector", lambda e: e.tensor_scalar_mul(sc2[0:4, 3:4], sc2[0:4, 3:4],
                                                       1.0 / math.sqrt(512.0)),
               reads=[sc2[0:4, 3:4]], writes=[sc2[0:4, 3:4]])
        ctx.op("vector", lambda e, t0=t0: e.tensor_scalar(
            R4(0), m_sb[0:4, t0:t0 + TT], psc[0:4, 0:1], psc[0:4, 1:2], ALU.subtract, ALU.max),
            reads=[m_sb[0:4, t0:t0 + TT], psc[0:4, 0:2]], writes=[R4(0)])
        ctx.op("scalar", lambda e: e.activation(R4(1), R4(0), AF.Exp, bias=sc2[0:4, 1:2]),
               reads=[R4(0), sc2[0:4, 1:2]], writes=[R4(1)])
        ctx.op("vector", lambda e, t0=t0: e.tensor_scalar_add(R4(2), lc_sb[0:4, t0:t0 + TT],
                                                              psc[0:4, 0:1]),
               reads=[lc_sb[0:4, t0:t0 + TT], psc[0:4, 0:1]], writes=[R4(2)])
        ctx.op("scalar", lambda e: e.activation(R4(2), R4(2), AF.Exp, scale=-1.0, bias=sc2[0:4, 1:2]),
               reads=[R4(2), sc2[0:4, 1:2]], writes=[R4(2)])
        for h in range(4):
            for dc in range(4):
                ctx.dma("sync", qh_sb[:, dc * TT:(dc + 1) * TT],
                        qf[h * 512 + dc * 128: h * 512 + (dc + 1) * 128, t0:t0 + TT])
                ctx.dma("gpsimd", mc_sb[:, dc * 512:(dc + 1) * 512],
                        Mc_p[h * 512 + dc * 128: h * 512 + (dc + 1) * 128, :])
            bsel = rg.banks[7][:, 0:1]
            ctx.mm(bsel, sel4[0:4, h * 128:(h + 1) * 128], sc2[0:4, 3:4], True, True)
            ctx.op("scalar", lambda e, bsel=bsel: e.activation(sob[:, 0:1], bsel, AF.Copy),
                   reads=[bsel], writes=[sob[:, 0:1]])
            for dc in range(4):
                ctx.op("vector", lambda e, dc=dc: e.tensor_scalar_mul(
                    qsp[:, dc * TT:(dc + 1) * TT], qh_sb[:, dc * TT:(dc + 1) * TT], sob[:, 0:1]),
                    reads=[qh_sb[:, dc * TT:(dc + 1) * TT], sob[:, 0:1]],
                    writes=[qsp[:, dc * TT:(dc + 1) * TT]])
            def msums(kc):
                pb = pb_sb[:, (kc % 2) * TT:(kc % 2 + 1) * TT]
                vts = vt_sb[:, (kc % 2) * 512:(kc % 2) * 512 + 512]
                ctx.mm(rg.banks[4][0:1, 0:TT], onesb[:, :], pb, kc == 0, False)
                ctx.dma("sync", vts, vtok[kc * 128:(kc + 1) * 128, h * 512:(h + 1) * 512])
                for dcv in range(4):
                    ctx.mm(rg.banks[dcv][:, 0:TT], vts[:, dcv * 128:(dcv + 1) * 128], pb,
                           kc == 0, False)
            for kc in range(nkc):
                khs = khc[:, (kc % 2) * 512:(kc % 2) * 512 + 512]
                ctx.dma("sync", khs.rearrange("p (a c) -> p a c", a=4, c=128),
                        kf[h * 512:(h + 1) * 512, kc * 128:(kc + 1) * 128].rearrange(
                            "(a b) c -> b a c", a=4, b=128))
                sbank = rg.banks[5 + kc % 2][:, 0:TT]
                for dc in range(4):
                    ctx.mm(sbank, khs[:, dc * 128:(dc + 1) * 128],
                           qh_sb[:, dc * TT:(dc + 1) * TT], dc == 0, dc == 3)
                ucol = ucols[:, kc * 4 + h:kc * 4 + h + 1]
                pb = pb_sb[:, (kc % 2) * TT:(kc % 2 + 1) * TT]
                ctx.op("vector", lambda e, sbank=sbank, pb=pb, ucol=ucol: e.tensor_scalar_mul(
                    pb, sbank, ucol), reads=[sbank, ucol], writes=[pb])
                if kc >= nkc - 4:
                    md = mdiag[:, (kc - (nkc - 4)) * TT:(kc - (nkc - 4) + 1) * TT]
                    ctx.op("vector", lambda e, pb=pb, md=md: e.tensor_mul(pb, pb, md),
                           reads=[pb, md], writes=[pb])
                if kc >= 1:
                    msums(kc - 1)
            msums(nkc - 1)
            for dcv in range(4):
                for dq in range(4):
                    ctx.mm(rg.banks[dcv][:, 0:TT],
                           mc_sb[:, dq * 512 + dcv * 128: dq * 512 + (dcv + 1) * 128],
                           qsp[:, dq * TT:(dq + 1) * TT], False, dq == 3)
            for dq in range(4):
                ctx.mm(rg.banks[4][0:1, 0:TT], ncb[:, h * 4 + dq:h * 4 + dq + 1],
                       qsp[:, dq * TT:(dq + 1) * TT], False, dq == 3)
            brow = rg.banks[7][0:1, 0:TT]
            ctx.mm(brow, idf[0:4, h:h + 1], R4(1), True, True)
            ctx.op("scalar", lambda e, brow=brow: e.activation(R8(1), brow, AF.Copy),
                   reads=[brow], writes=[R8(1)])
            ctx.mm(brow, idf[0:4, h:h + 1], R4(2), True, True)
            ctx.op("scalar", lambda e, brow=brow: e.activation(R8(2), brow, AF.Copy),
                   reads=[brow], writes=[R8(2)])
            ctx.op("scalar", lambda e: e.activation(R8(3), rg.banks[4][0:1, 0:TT], AF.Abs),
                   reads=[rg.banks[4][0:1, 0:TT]], writes=[R8(3)])
            ctx.op("vector", lambda e: e.tensor_max(R8(3), R8(3), R8(2)),
                   reads=[R8(3), R8(2)], writes=[R8(3)])
            ctx.op("vector", lambda e: e.scalar_tensor_tensor(R8(3), R8(1), 1e-6, R8(3),
                                                              ALU.mult, ALU.add),
                   reads=[R8(1), R8(3)], writes=[R8(3)])
            ctx.op("vector", lambda e: e.reciprocal(R8(4), R8(3)),
                   reads=[R8(3)], writes=[R8(4)])
            bfac = rg.banks[7][:, 0:TT]
            ctx.mm(bfac, ones_row[0:1, :], R8(4), True, True)
            for dcv in range(4):
                ctx.op("scalar", lambda e, dcv=dcv: e.activation(
                    hn_f[:, dcv * TT:(dcv + 1) * TT], rg.banks[dcv][:, 0:TT], AF.Copy),
                    reads=[rg.banks[dcv][:, 0:TT]], writes=[hn_f[:, dcv * TT:(dcv + 1) * TT]])
                ctx.op("vector", lambda e, dcv=dcv: e.tensor_mul(
                    hn_f[:, dcv * TT:(dcv + 1) * TT], hn_f[:, dcv * TT:(dcv + 1) * TT], bfac),
                    reads=[hn_f[:, dcv * TT:(dcv + 1) * TT], bfac],
                    writes=[hn_f[:, dcv * TT:(dcv + 1) * TT]])
            bsum = rg.banks[4][0:1, 0:TT]
            bsq = rg.banks[5][0:1, 0:TT]
            for dcv in range(4):
                ctx.mm(bsum, ones_col[:, :], hn_f[:, dcv * TT:(dcv + 1) * TT], dcv == 0, dcv == 3)
                ctx.op("scalar", lambda e, dcv=dcv: e.activation(ev1[:, :], hn_f[:, dcv * TT:(dcv + 1) * TT],
                                                                 AF.Square),
                       reads=[hn_f[:, dcv * TT:(dcv + 1) * TT]], writes=[ev1[:, :]])
                ctx.mm(bsq, ones_col[:, :], ev1[:, :], dcv == 0, dcv == 3)
            ctx.op("vector", lambda e: e.tensor_scalar_mul(R8(1), bsum, 1.0 / 512.0),
                   reads=[bsum], writes=[R8(1)])
            ctx.op("vector", lambda e: e.tensor_scalar_mul(R8(2), bsq, 1.0 / 512.0),
                   reads=[bsq], writes=[R8(2)])
            ctx.op("vector", lambda e: e.tensor_mul(R8(3), R8(1), R8(1)),
                   reads=[R8(1)], writes=[R8(3)])
            ctx.op("vector", lambda e: e.tensor_sub(R8(2), R8(2), R8(3)),
                   reads=[R8(2), R8(3)], writes=[R8(2)])
            ctx.op("scalar", lambda e: e.activation(R8(3), R8(2), AF.Sqrt, bias=eps5[0:1, 0:1]),
                   reads=[R8(2), eps5[0:1, 0:1]], writes=[R8(3)])
            ctx.op("vector", lambda e: e.reciprocal(R8(4), R8(3)),
                   reads=[R8(3)], writes=[R8(4)])
            ctx.op("vector", lambda e: e.tensor_mul(R8(5), R8(1), R8(4)),
                   reads=[R8(1), R8(4)], writes=[R8(5)])
            brb = rg.banks[4][:, 0:TT]
            bzb = rg.banks[5][:, 0:TT]
            ctx.mm(brb, ones_row[0:1, :], R8(4), True, True)
            ctx.mm(bzb, ones_row[0:1, :], R8(5), True, True)
            for dcv in range(4):
                hslice = hn_f[:, dcv * TT:(dcv + 1) * TT]
                ctx.op("vector", lambda e, hslice=hslice: e.tensor_mul(hslice, hslice, brb),
                       reads=[hslice, brb], writes=[hslice])
                ctx.op("vector", lambda e, hslice=hslice: e.tensor_sub(hslice, hslice, bzb),
                       reads=[hslice, bzb], writes=[hslice])
                lnw = vecs[:, h * 4 + dcv:h * 4 + dcv + 1]
                ctx.op("vector", lambda e, hslice=hslice, lnw=lnw: e.tensor_scalar_mul(hslice, hslice, lnw),
                       reads=[hslice, lnw], writes=[hslice])
                ic = h * 4 + dcv
                xc_s = xcz[:, ic * TT:(ic + 1) * TT]
                z_s = xcz[:, (NI2 + ic) * TT:(NI2 + ic + 1) * TT]
                skipc = vecs[:, 16 + ic:16 + ic + 1]
                ctx.op("vector", lambda e, hslice=hslice, xc_s=xc_s, skipc=skipc: e.scalar_tensor_tensor(
                    hslice, xc_s, skipc, hslice, ALU.mult, ALU.add),
                    reads=[hslice, xc_s, skipc], writes=[hslice])
                ctx.op("scalar", lambda e, z_s=z_s: e.activation(ev1[:, :], z_s, AF.Silu),
                       reads=[z_s], writes=[ev1[:, :]])
                dsth = hst[:, ic * TT:(ic + 1) * TT]
                ctx.op("vector", lambda e, hslice=hslice, dsth=dsth: e.tensor_mul(dsth, hslice, ev1[:, :]),
                       reads=[hslice, ev1[:, :]], writes=[dsth])
        for ec in range(NCH):
            bank = rg.banks[ec % 2][:, 0:TT]
            for jc in range(NI2):
                ctx.mm(bank, wd_sb[:, jc * 1024 + ec * 128: jc * 1024 + (ec + 1) * 128],
                       hst[:, jc * TT:(jc + 1) * TT], jc == 0, jc == NI2 - 1)
            ctx.op("scalar", lambda e, bank=bank: e.activation(ysum[:, :], bank, AF.Copy),
                   reads=[bank], writes=[ysum[:, :]])
            ctx.dma("scalar", s1_o[ec * 128:(ec + 1) * 128, t0:t0 + TT], ysum[:, :])
    ctx.barrier()


def prep_l1_pre(inputs):
    NI2 = 16
    vecs = np.zeros((128, NV1), np.float32)
    cw = np.asarray(inputs["L1_attn_conv_w"], np.float32)[:, 0, :]
    cb = np.asarray(inputs["L1_attn_conv_b"], np.float32)
    for c in range(NI2):
        for j in range(4):
            vecs[:, c * 4 + j] = cw[c * 128:(c + 1) * 128, j]
        vecs[:, 64 + c] = cb[c * 128:(c + 1) * 128]
    vecs[0:4, 80] = np.asarray(inputs["L1_attn_b_ig"], np.float32)
    vecs[0:4, 81] = np.asarray(inputs["L1_attn_b_fg"], np.float32)

    def blockdiag(w):
        out = np.zeros((NI2, 128, 128), np.float32)
        for c in range(NI2):
            for a2 in range(32):
                blk = w[32 * c + a2]
                out[c, a2 * 4:(a2 + 1) * 4, a2 * 4:(a2 + 1) * 4] = blk.T
        return out
    wqd = blockdiag(np.asarray(inputs["L1_attn_w_q"], np.float32))
    wkd = blockdiag(np.asarray(inputs["L1_attn_w_k"], np.float32))
    wvd = blockdiag(np.asarray(inputs["L1_attn_w_v"], np.float32))
    return dict(
        w_upT=np.ascontiguousarray(np.asarray(inputs["L1_attn_w_up"], np.float32).T),
        wqd=wqd, wkd=wkd, wvd=wvd,
        wigT=np.ascontiguousarray(np.asarray(inputs["L1_attn_w_ig"], np.float32).T),
        wfgT=np.ascontiguousarray(np.asarray(inputs["L1_attn_w_fg"], np.float32).T),
        vecs_in=vecs)


def prep_l1_mlstm(inputs, pre_out, partner_pre_out, has_prev):
    NI2 = 16
    vecs = np.zeros((128, NV2), np.float32)
    lnw = np.asarray(inputs["L1_attn_ln_w"], np.float32)
    skip = np.asarray(inputs["L1_attn_skip"], np.float32)
    for h in range(4):
        for dc in range(4):
            vecs[:, h * 4 + dc] = lnw[h, dc * 128:(dc + 1) * 128]
    for c in range(NI2):
        vecs[:, 16 + c] = skip[c * 128:(c + 1) * 128]
    sel4 = np.zeros((4, 512), np.float32)
    for h in range(4):
        sel4[h, h * 128:(h + 1) * 128] = 1.0
    mdiag = np.zeros((4, 128, TT), np.float32)
    for p in range(4):
        k_rel = np.arange(128) + p * 128
        q_rel = np.arange(TT)
        mdiag[p] = (k_rel[:, None] <= q_rel[None, :]).astype(np.float32)
    if has_prev:
        Mc_p = partner_pre_out["Mc_o"]
        ncv_p = partner_pre_out["ncv_o"]
        pscal = np.zeros((4, 8), np.float32)
        pscal[:, 0] = partner_pre_out["gscal_o"][:, 0]
        pscal[:, 1] = partner_pre_out["gscal_o"][:, 1]
    else:
        Mc_p = np.zeros((2048, 512), np.float32)
        ncv_p = np.zeros((128, 16), np.float32)
        pscal = np.zeros((4, 8), np.float32)
        pscal[:, 1] = -1e30
    return dict(
        qf_i=pre_out["qf_o"], kf_i=pre_out["kf_o"], vtok_i=pre_out["vtok_o"],
        xc_i=pre_out["xc_o"], z_i=pre_out["z_o"],
        arow_i=pre_out["arow_o"], mrow_i=pre_out["mrow_o"], lcrow_i=pre_out["lcrow_o"],
        gscal_i=pre_out["gscal_o"], Mc_p=np.ascontiguousarray(Mc_p),
        ncv_p=np.ascontiguousarray(ncv_p), pscal=pscal,
        w_downT=np.ascontiguousarray(np.asarray(inputs["L1_attn_w_down"], np.float32).T),
        vecs_in=vecs, sel4_in=sel4, mdiag_in=mdiag)


def _l1f_consts(inputs, li):
    vecs = np.zeros((128, NV0), np.float32)
    ln_g = np.asarray(inputs["L1_ffn_ln_g"][li], np.float32)
    ln_b = np.asarray(inputs["L1_ffn_ln_b"][li], np.float32)
    b1 = np.asarray(inputs["L1_ffn_b1"][li], np.float32)
    b2 = np.asarray(inputs["L1_ffn_b2"][li], np.float32)
    qs = np.asarray(inputs["L1_ffn_q_scale"][li], np.float32)
    ks = np.asarray(inputs["L1_ffn_k_scale"][li], np.float32)
    for c in range(8):
        vecs[:, c] = ln_g[c * 128:(c + 1) * 128]
        vecs[:, 8 + c] = ln_b[c * 128:(c + 1) * 128]
        vecs[:, 60 + c] = b2[c * 128:(c + 1) * 128]
    b1p = np.zeros(5632, np.float32)
    b1p[0:2730] = b1[0:2730]
    b1p[2816:2816 + 2730] = b1[2730:5460]
    for c in range(44):
        vecs[:, 16 + c] = b1p[c * 128:(c + 1) * 128]
    vecs[:, 68] = np.tile(qs, 2)
    vecs[:, 69] = np.tile(ks, 2)
    W1 = np.asarray(inputs["L1_ffn_w1"][li], np.float32)
    w1p = np.zeros((D, 5632), np.float32)
    w1p[:, 0:2730] = W1[:, 0:2730]
    w1p[:, 2816:2816 + 2730] = W1[:, 2730:5460]
    W2 = np.asarray(inputs["L1_ffn_w2"][li], np.float32)
    w2p = np.zeros((2816, 1024), np.float32)
    w2p[0:2730, :] = W2
    return vecs, np.ascontiguousarray(w1p), np.ascontiguousarray(w2p)


def l1f_masks(has_prev):
    NKC = 8
    out = np.zeros((2, NKC, 128, TT), np.float32)
    q_rel = np.arange(TT)
    for kc in range(NKC):
        k_rel = np.arange(128) + kc * 128 - 512
        dmat = q_rel[None, :] - k_rel[:, None]
        invalid = (dmat < 0) | (dmat > 512)
        out[0, kc] = invalid
        inv0 = invalid.copy()
        if not has_prev and kc < 4:
            inv0[:, :] = 1.0
        out[1, kc] = inv0
    return np.ascontiguousarray(out)


def rope_tables_l1(own0):
    pos = np.arange(-L0F_HALO, SH) + own0
    pos = np.clip(pos, 0, None)
    aq, bq, ak, bk = rope_tables(S, 64, 512, pos, 8.0)
    return [np.ascontiguousarray(np.tile(t, (2, 1))) for t in (aq, bq, ak, bk)]


def prep_l1f_attn(inputs, c, li):
    ci = core_info(c)
    own0 = ci["h"] * SH
    aq, bq, ak, bk = rope_tables_l1(own0)
    ropes = np.stack([aq, bq, ak, bk])
    hselA = np.zeros((128, 2), np.float32)
    hselA[0:64, 0] = 1.0
    hselA[64:128, 1] = 1.0
    hselB = np.zeros((2, 128), np.float32)
    hselB[0, 0:64] = 1.0
    hselB[1, 64:128] = 1.0
    vecs, _, _ = _l1f_consts(inputs, li)
    return dict(
        wqkv=np.ascontiguousarray(np.asarray(inputs["L1_ffn_wqkv"][li], np.float32)),
        wo=np.ascontiguousarray(np.asarray(inputs["L1_ffn_wo"][li], np.float32)),
        qsrow=np.asarray(inputs["L1_ffn_q_scale"][li], np.float32).reshape(1, 64),
        ksrow=np.asarray(inputs["L1_ffn_k_scale"][li], np.float32).reshape(1, 64),
        vecs_in=vecs, ropes=ropes, hselA_in=hselA, hselB_in=hselB,
        rotm_in=rot_matrix(64, 2), idneg_in=np.eye(128, dtype=np.float32) * -1e5,
        masks_in=l1f_masks(ci["has_prev"]))


def prep_l1f_ffn(inputs, c, li):
    vecs, w1p, w2p = _l1f_consts(inputs, li)
    return dict(w1p=w1p, w2p=w2p, vecs_in=vecs)


_BUILD_CACHE = {}


def _get(name, builder):
    if name not in _BUILD_CACHE:
        _BUILD_CACHE[name] = builder()
    return _BUILD_CACHE[name]


def _run(name, builder, in_maps):
    nc, _ = _get(name, builder)
    res = run_bass_kernel_spmd(nc, in_maps, core_ids=list(range(8)))
    return res.results


def kernel(**inputs):
    cores = list(range(8))
    cis = [core_info(c) for c in cores]
    r1 = _run("b0", build_b0_lnattn, [prep_b0_lnattn(inputs, c) for c in cores])
    xn0b = [r1[c]["xn0b"] for c in cores]
    xn0f = [r1[c]["xn0f"] for c in cores]
    s0 = [r1[c]["s0"] for c in cores]
    xcur = [np.ascontiguousarray(xn0f[c][:, S:]) for c in cores]
    for li in range(6):
        ins = []
        for c in cores:
            hh = cis[c]["h"]
            if hh == 1:
                halo = xcur[c - 1][:, SH - L0F_HALO:]
            else:
                halo = np.zeros((D, L0F_HALO), np.float32)
            xw = np.concatenate([halo, xcur[c]], axis=1)
            m = prep_l0f_attn(inputs, c, li)
            m["xw"] = np.ascontiguousarray(xw)
            ins.append(m)
        ra = _run("l0f_attn", build_l0f_attn, ins)
        xp = [ra[c]["xo"] for c in cores]
        ins = []
        for c in cores:
            hh = cis[c]["h"]
            h1 = xp[c - 1][:, SH - 1:SH] if hh == 1 else np.zeros((D, 1), np.float32)
            xw1 = np.concatenate([h1, xp[c]], axis=1)
            m = prep_l0f_ffn(inputs, c, li)
            m["xw1"] = np.ascontiguousarray(xw1)
            ins.append(m)
        rf = _run("l0f_ffn", build_l0f_ffn, ins)
        xcur = [np.ascontiguousarray(rf[c]["xo"]) for c in cores]
    ins = []
    for c in cores:
        hh = cis[c]["h"]
        x0T_own = np.ascontiguousarray(
            np.asarray(inputs["L0_attn_x"][cis[c]["b"]], np.float32).T[:, hh * SH:(hh + 1) * SH])
        ins.append(dict(xT=x0T_own, sT=np.ascontiguousarray(s0[c]), gT=xcur[c]))
    rc = _run("combine", build_combine, ins)
    x1 = [np.ascontiguousarray(rc[c]["xo"]) for c in cores]
    ins = []
    for c in cores:
        hh = cis[c]["h"]
        halo = x1[c - 1][:, SH - 513:] if hh == 1 else np.zeros((D, 513), np.float32)
        xw = np.concatenate([np.zeros((D, 511), np.float32), halo, x1[c]], axis=1)
        ins.append(dict(xT=np.ascontiguousarray(xw)))
    rl = _run("ln_b1", lambda: build_ln(1024 + SH), ins)
    xn1 = [rl[c]["xnf"] for c in cores]
    ins = []
    for c in cores:
        m = prep_l1_pre(inputs)
        m["xnw"] = np.ascontiguousarray(xn1[c])
        ins.append(m)
    rp = _run("l1_pre", build_l1_pre, ins)
    ins = []
    for c in cores:
        hh = cis[c]["h"]
        partner = rp[c - 1] if hh == 1 else None
        ins.append(prep_l1_mlstm(inputs, rp[c], partner, hh == 1))
    rm = _run("l1_mlstm", build_l1_mlstm, ins)
    s1 = [rm[c]["s1_o"] for c in cores]
    xcur = [np.ascontiguousarray(xn1[c][:, 1024:]) for c in cores]
    for li in range(2):
        ins = []
        for c in cores:
            hh = cis[c]["h"]
            if li == 0:
                xw = np.ascontiguousarray(xn1[c][:, 1024 - L0F_HALO:])
            else:
                halo = xcur[c - 1][:, SH - L0F_HALO:] if hh == 1 else np.zeros((D, L0F_HALO), np.float32)
                xw = np.ascontiguousarray(np.concatenate([halo, xcur[c]], axis=1))
            m = prep_l1f_attn(inputs, c, li)
            m["xw"] = xw
            ins.append(m)
        ra = _run("l1f_attn", build_l1f_attn, ins)
        xp = [ra[c]["xo"] for c in cores]
        ins = []
        for c in cores:
            hh = cis[c]["h"]
            h1 = xp[c - 1][:, SH - 1:SH] if hh == 1 else np.zeros((D, 1), np.float32)
            xw1 = np.concatenate([h1, xp[c]], axis=1)
            m = prep_l1f_ffn(inputs, c, li)
            m["xw1"] = np.ascontiguousarray(xw1)
            ins.append(m)
        rf = _run("l0f_ffn", build_l0f_ffn, ins)
        xcur = [np.ascontiguousarray(rf[c]["xo"]) for c in cores]
    ins = []
    for c in cores:
        ins.append(dict(xT=x1[c], sT=np.ascontiguousarray(s1[c]), gT=xcur[c]))
    rc2 = _run("combine", build_combine, ins)
    out = np.empty((B, S, D), np.float32)
    for c in cores:
        b, hh = cis[c]["b"], cis[c]["h"]
        out[b, hh * SH:(hh + 1) * SH, :] = rc2[c]["xo"].T
    return out


L0H = L0F_HALO
RG_PAIRS = [[0, 1], [2, 3], [4, 5], [6, 7]]
RG_ALL = [list(range(8))]


def _pack_table():
    t = [("pk", (D, 128)), ("pv", (D, 128)), ("pq", (8, D, 128)), ("poT", (8, 128, D)),
         ("rotm", (128, 128)), ("idneg", (128, 128))]
    for li in range(6):
        t += [(f"wqkv0_{li}", (D, 3072)), (f"wo0_{li}", (1024, D)),
              (f"w1p0_{li}", (D, 5632)), (f"w2p0_{li}", (2816, 1024))]
    t += [("w_upT", (D, 4096)), ("wqd", (16, 128, 128)), ("wkd", (16, 128, 128)),
          ("wvd", (16, 128, 128)), ("wigT", (6144, 4)), ("wfgT", (6144, 4)),
          ("w_downT", (2048, 1024))]
    for li in range(2):
        t += [(f"wqkv1_{li}", (D, 1536)), (f"wo1_{li}", (512, D)),
              (f"w1p1_{li}", (D, 5632)), (f"w2p1_{li}", (2816, 1024))]
    offs = {}
    off = 0
    for nm, shp in t:
        offs[nm] = off
        off += int(np.prod(shp))
    pad = (-off) % (8 * 128 * 4096)
    total = off + pad
    return t, offs, total


def _wview(wb_handle, offs, nm, shape):
    dims = []
    strides = []
    st = 1
    for s_ in reversed(shape):
        strides.append(st)
        st *= s_
    strides = list(reversed(strides))
    for s_, stx in zip(shape, strides):
        dims.append([stx, s_])
    return bass.AP(wb_handle, offs[nm], dims)


def build_mega():
    nc = bass.Bass("TRN2", target_bir_lowering=False)
    es = ExitStack()
    ctx = Ctx(nc, es)
    rg = Regs(nc, es)
    ptab, offs, WTOT = _pack_table()
    LS = WTOT // 8

    EI = lambda nm, shp, dt_=F32: nc.dram_tensor(nm, list(shp), dt_, kind="ExternalInput").ap()
    x0T = EI("x0T", [D, S + SH])
    wshard = EI("wshard", [1, LS])
    vecs0 = EI("vecs0", [6, 128, NV0])
    vecs1f = EI("vecs1f", [2, 128, NV0])
    vecsl1 = EI("vecsl1", [128, NV1])
    vecsm = EI("vecsm", [128, NV2])
    qsk0 = EI("qsk0", [6, 2, 64])
    qsk1 = EI("qsk1", [2, 2, 64])
    hselA_d = EI("hselA_d", [128, 2])
    hselB_d = EI("hselB_d", [2, 128])
    sel4_d = EI("sel4_d", [4, 512])
    masks0 = EI("masks0", [2, 6, 128, TT])
    masks1 = EI("masks1", [2, 8, 128, TT])
    mdiag_d = EI("mdiag_d", [4, 128, TT])
    ropes0 = EI("ropes0", [4, 128, L0H + SH])
    ropes1 = EI("ropes1", [4, 128, L0H + SH])
    pcc = EI("pcc", [128, 4])
    out = nc.dram_tensor("out", [D, SH], F32, kind="ExternalOutput").ap()

    IN = lambda nm, shp, dt_=F32: nc.dram_tensor(nm, list(shp), dt_, kind="Internal")
    wsh_bf_t = IN("wsh_bf", [1, LS], BF16)
    wbig_t = IN("wbig", [8 * LS], BF16)
    xn0f_t = IN("xn0f_s", [D, S + SH])
    xn0b_t = IN("xn0b_s", [D, S + SH], BF16)
    s0_t = IN("s0_s", [D, SH])
    xH_t = IN("xH_s", [D, L0H])
    xB_t = IN("xB_s", [D, 1 + SH])
    xC_t = IN("xC_s", [D, SH])
    x1_t = IN("x1_s", [D, SH])
    xn1f_t = IN("xn1f_s", [D, 1024 + SH])
    xn1b_t = IN("xn1b_s", [D, 1024 + SH], BF16)
    xc_t = IN("xc_s", [2048, SH], BF16)
    z_t = IN("z_s", [2048, SH], BF16)
    qf_t = IN("qf_s", [2048, SH], BF16)
    kf_t = IN("kf_s", [2048, SH], BF16)
    vtok_t = IN("vtok_s", [SH, 2048], BF16)
    ktok_t = IN("ktok_s", [SH, 2048], BF16)
    arow_t = IN("arow_s", [4, SH])
    mrow_t = IN("mrow_s", [4, SH])
    lcrow_t = IN("lcrow_s", [4, SH])
    gscal_t = IN("gscal_s", [4, 8])
    Mc_t = IN("Mc_s", [2048, 512])
    ncv_t = IN("ncv_s", [128, 16])
    pscal_t = IN("pscal_s", [4, 8])
    s1_t = IN("s1_s", [D, SH])
    agA_in = IN("agA_in", [D, L0H])
    agA_out = IN("agA_out", [2 * D, L0H])
    agB_in = IN("agB_in", [D, 1])
    agB_out = IN("agB_out", [2 * D, 1])
    agS_mc_out = IN("agS_mc_out", [2 * 2048, 512])
    agS_nc_out = IN("agS_nc_out", [2 * 128, 16])
    agS_gs_out = IN("agS_gs_out", [2 * 4, 8])

    wb = wbig_t
    WV = lambda nm, shp: _wview(wb, offs, nm, shp)

    emit_setup(ctx, rg)
    valid = rg.sb("pc_sb", [128, 4])
    ctx.dma("sync", valid[:, :], pcc)
    zt = rg.sb("zt", [128, 512])
    ctx.op("vector", lambda e: e.memset(zt[:, :], 0.0), writes=[zt[:, :]])
    hb = rg.sb("hg_hb", [128, L0H])
    rg.sb("masks_sb_8", [128, 2 * 6 * TT], BF16)
    rg.sb("rotm", [128, 128], BF16)
    rg.sb("idneg", [128, 128], BF16)

    def halo_mul(dst, srcd, width, tag=""):
        kw = {"allow_slow_non_contiguous": True} if width == 1 else {}
        for c in range(NCH):
            ctx.dma("sync", hb[:, 0:width], srcd[c * 128:(c + 1) * 128, 0:width], **kw)
            ctx.op("vector", lambda e, c=c, width=width: e.tensor_scalar_mul(
                hb[:, 0:width], hb[:, 0:width], valid[:, 0:1]),
                reads=[hb[:, 0:width], valid[:, 0:1]], writes=[hb[:, 0:width]])
            ctx.dma("scalar", dst[c * 128:(c + 1) * 128, 0:width], hb[:, 0:width], **kw)

    rg.push()
    nchunk = LS // (128 * 4096)
    for i in range(nchunk):
        lo = i * 128 * 4096
        ctx.dma("gpsimd", bass.AP(wsh_bf_t, lo, [[4096, 128], [1, 4096]]),
                bass.AP(wshard.tensor, lo, [[4096, 128], [1, 4096]]))
    ctx.collective("AllGather", ALU.bypass, RG_ALL,
                   [wsh_bf_t.ap()], [bass.AP(wbig_t, 0, [[LS, 8], [1, LS]])])
    rg.pop()
    ctx.barrier()

    rg.push()
    io = dict(x0T=x0T, pk=WV("pk", (D, 128)), pv=WV("pv", (D, 128)),
              pq=WV("pq", (8, D, 128)), poT=WV("poT", (8, 128, D)),
              xn0f=xn0f_t.ap(), xn0b=xn0b_t.ap(), s0=s0_t.ap())
    emit_b0_lnattn(ctx, rg, io)
    rg.pop()
    ctx.barrier()

    for li in range(6):
        if li == 0:
            halo_mul(xH_t.ap(), xn0f_t.ap()[:, SH - L0H:SH], L0H)
            body_ap = xn0f_t.ap()[:, S:S + SH]
        else:
            halo_mul(xH_t.ap(), agA_out.ap()[0:D, :], L0H)
            body_ap = xC_t.ap()
        ctx.barrier()
        rg.push()
        io = dict(xw=body_ap, xw_halo=xH_t.ap(),
                  wqkv=WV(f"wqkv0_{li}", (D, 3072)), wo=WV(f"wo0_{li}", (1024, D)),
                  ropes=ropes0, qsrow=qsk0[li, 0:1, :], ksrow=qsk0[li, 1:2, :],
                  vecs_in=vecs0[li], hselA_in=hselA_d, hselB_in=hselB_d,
                  rotm_in=WV("rotm", (128, 128)), idneg_in=WV("idneg", (128, 128)),
                  masks_in=masks0, xo=xB_t.ap()[:, 1:1 + SH], load_consts=(li == 0))
        emit_lf_attn(ctx, rg, io, 8, 256)
        rg.pop()
        ctx.dma("gpsimd", agA_in.ap(), xB_t.ap()[:, 1 + SH - L0H:1 + SH])
        ctx.collective("AllGather", ALU.bypass, RG_PAIRS, [agA_in.ap()], [agA_out.ap()])
        halo_mul(xB_t.ap()[:, 0:1], agA_out.ap()[0:D, L0H - 1:L0H], 1)
        ctx.barrier()
        rg.push()
        io = dict(xw1=xB_t.ap(), w1p=WV(f"w1p0_{li}", (D, 5632)), w2p=WV(f"w2p0_{li}", (2816, 1024)),
                  vecs_in=vecs0[li], xo=xC_t.ap())
        emit_lf_ffn(ctx, rg, io)
        rg.pop()
        if li < 5:
            ctx.dma("gpsimd", agA_in.ap(), xC_t.ap()[:, SH - L0H:SH])
            ctx.collective("AllGather", ALU.bypass, RG_PAIRS, [agA_in.ap()], [agA_out.ap()])
        ctx.barrier()

    rg.push()
    io = dict(xT=x0T[:, S:S + SH], sT=s0_t.ap(), gT=xC_t.ap(), xo=x1_t.ap())
    emit_combine(ctx, rg, io)
    rg.pop()
    ctx.dma("gpsimd", agA_in.ap(), x1_t.ap()[:, SH - L0H:SH])
    ctx.collective("AllGather", ALU.bypass, RG_PAIRS, [agA_in.ap()], [agA_out.ap()])
    ctx.barrier()

    lnin_t = IN("lnin_s", [D, 1024 + SH])
    for c in range(NCH):
        ctx.dma("scalar", lnin_t.ap()[c * 128:(c + 1) * 128, 0:512], zt[:, :])
    halo_mul(lnin_t.ap()[:, 511:1024], agA_out.ap()[0:D, :], L0H, tag="hg3")
    for c in range(NCH):
        ctx.dma("gpsimd", lnin_t.ap()[c * 128:(c + 1) * 128, 1024:1024 + SH],
                x1_t.ap()[c * 128:(c + 1) * 128, :])
    ctx.barrier()
    rg.push()
    io = dict(xT=lnin_t.ap(), xnf=xn1f_t.ap(), xnb=xn1b_t.ap())
    emit_ln_pass(ctx, rg, io, 1024 + SH)
    rg.pop()
    ctx.barrier()

    rg.push()
    io = dict(xnw=xn1f_t.ap(), w_upT=WV("w_upT", (D, 4096)),
              wqd=WV("wqd", (16, 128, 128)), wkd=WV("wkd", (16, 128, 128)),
              wvd=WV("wvd", (16, 128, 128)), wigT=WV("wigT", (6144, 4)),
              wfgT=WV("wfgT", (6144, 4)), vecs_in=vecsl1,
              xc_o=xc_t.ap(), z_o=z_t.ap(), qf_o=qf_t.ap(), kf_o=kf_t.ap(),
              vtok_o=vtok_t.ap(), ktok_o=ktok_t.ap(), arow_o=arow_t.ap(),
              mrow_o=mrow_t.ap(), lcrow_o=lcrow_t.ap(), gscal_o=gscal_t.ap(),
              Mc_o=Mc_t.ap(), ncv_o=ncv_t.ap())
    emit_l1_pre(ctx, rg, io)
    rg.pop()
    ctx.collective("AllGather", ALU.bypass, RG_PAIRS, [Mc_t.ap()], [agS_mc_out.ap()])
    ctx.collective("AllGather", ALU.bypass, RG_PAIRS, [ncv_t.ap()], [agS_nc_out.ap()])
    ctx.collective("AllGather", ALU.bypass, RG_PAIRS, [gscal_t.ap()], [agS_gs_out.ap()])
    pg = rg.sb("pg", [4, 8])
    ctx.dma("sync", pg[0:4, :], agS_gs_out.ap()[0:4, :])
    ctx.op("vector", lambda e: e.tensor_scalar_mul(pg[0:4, 0:2], pg[0:4, 0:2], valid[0:4, 0:1]),
           reads=[pg[0:4, 0:2], valid[0:4, 0:1]], writes=[pg[0:4, 0:2]])
    ctx.op("vector", lambda e: e.tensor_scalar_mul(pg[0:4, 2:3], valid[0:4, 1:2], -1e30),
           reads=[valid[0:4, 1:2]], writes=[pg[0:4, 2:3]])
    ctx.op("vector", lambda e: e.tensor_add(pg[0:4, 1:2], pg[0:4, 1:2], pg[0:4, 2:3]),
           reads=[pg[0:4, 1:3]], writes=[pg[0:4, 1:2]])
    ctx.dma("scalar", pscal_t.ap()[0:4, :], pg[0:4, 0:8])
    ctx.barrier()

    rg.push()
    io = dict(qf_i=qf_t.ap(), kf_i=kf_t.ap(), vtok_i=vtok_t.ap(), xc_i=xc_t.ap(),
              z_i=z_t.ap(), arow_i=arow_t.ap(), mrow_i=mrow_t.ap(), lcrow_i=lcrow_t.ap(),
              gscal_i=gscal_t.ap(), Mc_p=agS_mc_out.ap()[0:2048, :],
              ncv_p=agS_nc_out.ap()[0:128, :], pscal=pscal_t.ap(),
              w_downT=WV("w_downT", (2048, 1024)), vecs_in=vecsm, sel4_in=sel4_d,
              mdiag_in=mdiag_d, s1_o=s1_t.ap())
    emit_l1_mlstm(ctx, rg, io)
    rg.pop()
    ctx.barrier()

    for li in range(2):
        if li == 0:
            xw_ap = xn1f_t.ap()[:, 1024:1024 + SH]
            xwh_ap = xn1f_t.ap()[:, 1024 - L0H:1024]
        else:
            halo_mul(xH_t.ap(), agA_out.ap()[0:D, :], L0H)
            ctx.barrier()
            xw_ap = xC_t.ap()
            xwh_ap = xH_t.ap()
        rg.push()
        io = dict(xw=xw_ap, xw_halo=xwh_ap,
                  wqkv=WV(f"wqkv1_{li}", (D, 1536)), wo=WV(f"wo1_{li}", (512, D)),
                  ropes=ropes1, qsrow=qsk1[li, 0:1, :], ksrow=qsk1[li, 1:2, :],
                  vecs_in=vecs1f[li], hselA_in=hselA_d, hselB_in=hselB_d,
                  rotm_in=WV("rotm", (128, 128)), idneg_in=WV("idneg", (128, 128)),
                  masks_in=masks1, xo=xB_t.ap()[:, 1:1 + SH], load_consts=True)
        emit_lf_attn(ctx, rg, io, 4, 512)
        rg.pop()
        ctx.dma("gpsimd", agA_in.ap(), xB_t.ap()[:, 1 + SH - L0H:1 + SH])
        ctx.collective("AllGather", ALU.bypass, RG_PAIRS, [agA_in.ap()], [agA_out.ap()])
        halo_mul(xB_t.ap()[:, 0:1], agA_out.ap()[0:D, L0H - 1:L0H], 1)
        ctx.barrier()
        rg.push()
        io = dict(xw1=xB_t.ap(), w1p=WV(f"w1p1_{li}", (D, 5632)), w2p=WV(f"w2p1_{li}", (2816, 1024)),
                  vecs_in=vecs1f[li], xo=xC_t.ap())
        emit_lf_ffn(ctx, rg, io)
        rg.pop()
        if li == 0:
            ctx.dma("gpsimd", agA_in.ap(), xC_t.ap()[:, SH - L0H:SH])
            ctx.collective("AllGather", ALU.bypass, RG_PAIRS, [agA_in.ap()], [agA_out.ap()])
        ctx.barrier()

    rg.push()
    io = dict(xT=x1_t.ap(), sT=s1_t.ap(), gT=xC_t.ap(), xo=out)
    emit_combine(ctx, rg, io)
    rg.pop()
    ctx.barrier()
    ctx.finalize()
    return nc, es


def build_flat_pack(inputs):
    ptab, offs, WTOT = _pack_table()
    flat = np.zeros(WTOT, np.float32)

    def put(nm, arr):
        a = np.ascontiguousarray(np.asarray(arr, np.float32)).ravel()
        flat[offs[nm]:offs[nm] + a.size] = a
    put("pk", inputs["L0_attn_proj_k"])
    put("pv", inputs["L0_attn_proj_v"])
    put("pq", inputs["L0_attn_proj_q"])
    put("poT", np.asarray(inputs["L0_attn_proj_o"], np.float32).transpose(0, 2, 1))
    put("rotm", rot_matrix(64, 2))
    put("idneg", np.eye(128, dtype=np.float32) * -1e5)
    for li in range(6):
        put(f"wqkv0_{li}", inputs["L0_ffn_Wqkv"][li])
        put(f"wo0_{li}", inputs["L0_ffn_Wo"][li])
        _, w1p, w2p = _l0f_consts(inputs, li)
        put(f"w1p0_{li}", w1p)
        put(f"w2p0_{li}", w2p)
    put("w_upT", np.asarray(inputs["L1_attn_w_up"], np.float32).T)
    l1p = prep_l1_pre(inputs)
    put("wqd", l1p["wqd"])
    put("wkd", l1p["wkd"])
    put("wvd", l1p["wvd"])
    put("wigT", l1p["wigT"])
    put("wfgT", l1p["wfgT"])
    put("w_downT", np.asarray(inputs["L1_attn_w_down"], np.float32).T)
    for li in range(2):
        put(f"wqkv1_{li}", inputs["L1_ffn_wqkv"][li])
        put(f"wo1_{li}", inputs["L1_ffn_wo"][li])
        _, w1p, w2p = _l1f_consts(inputs, li)
        put(f"w1p1_{li}", w1p)
        put(f"w2p1_{li}", w2p)
    return flat


def prep_mega(inputs, c, flat):
    ci = core_info(c)
    b, hh = ci["b"], ci["h"]
    _, _, WTOT = _pack_table()
    LS = WTOT // 8
    shard = np.ascontiguousarray(flat[c * LS:(c + 1) * LS].reshape(1, LS))

    x0 = np.asarray(inputs["L0_attn_x"][b], np.float32)
    x0T = np.ascontiguousarray(x0.T)
    own = x0T[:, hh * SH:(hh + 1) * SH]
    x0T_ext = np.ascontiguousarray(np.concatenate([x0T, own], axis=1))

    vecs0 = np.stack([_l0f_consts(inputs, li)[0] for li in range(6)])
    vecs1f = np.stack([_l1f_consts(inputs, li)[0] for li in range(2)])
    qsk0 = np.stack([np.stack([np.asarray(inputs["L0_ffn_q_scale"][li], np.float32),
                               np.asarray(inputs["L0_ffn_k_scale"][li], np.float32)])
                     for li in range(6)])
    qsk1 = np.stack([np.stack([np.asarray(inputs["L1_ffn_q_scale"][li], np.float32),
                               np.asarray(inputs["L1_ffn_k_scale"][li], np.float32)])
                     for li in range(2)])
    hselA = np.zeros((128, 2), np.float32)
    hselA[0:64, 0] = 1.0
    hselA[64:128, 1] = 1.0
    hselB = np.zeros((2, 128), np.float32)
    hselB[0, 0:64] = 1.0
    hselB[1, 64:128] = 1.0
    sel4 = np.zeros((4, 512), np.float32)
    for h in range(4):
        sel4[h, h * 128:(h + 1) * 128] = 1.0
    mdiag = np.zeros((4, 128, TT), np.float32)
    for p in range(4):
        k_rel = np.arange(128) + p * 128
        q_rel = np.arange(TT)
        mdiag[p] = (k_rel[:, None] <= q_rel[None, :]).astype(np.float32)
    own0 = hh * SH
    r0 = np.stack(rope_tables_l0(own0))
    r1 = np.stack(rope_tables_l1(own0))
    pc = np.zeros((128, 4), np.float32)
    pc[:, 0] = 1.0 if ci["has_prev"] else 0.0
    pc[:, 1] = 0.0 if ci["has_prev"] else 1.0
    return dict(x0T=x0T_ext, wshard=shard, vecs0=vecs0, vecs1f=vecs1f,
                vecsl1=prep_l1_pre(inputs)["vecs_in"], vecsm=prep_l1_mlstm_vecs(inputs),
                qsk0=qsk0, qsk1=qsk1, hselA_d=hselA, hselB_d=hselB, sel4_d=sel4,
                masks0=l0f_masks(ci["has_prev"]), masks1=l1f_masks(ci["has_prev"]),
                mdiag_d=mdiag, ropes0=r0, ropes1=r1, pcc=pc)


def prep_l1_mlstm_vecs(inputs):
    NI2 = 16
    vecs = np.zeros((128, NV2), np.float32)
    lnw = np.asarray(inputs["L1_attn_ln_w"], np.float32)
    skip = np.asarray(inputs["L1_attn_skip"], np.float32)
    for h in range(4):
        for dc in range(4):
            vecs[:, h * 4 + dc] = lnw[h, dc * 128:(dc + 1) * 128]
    for c2 in range(NI2):
        vecs[:, 16 + c2] = skip[c2 * 128:(c2 + 1) * 128]
    return vecs


_MEGA_CACHE = {}


def kernel(**inputs):
    if "nc" not in _MEGA_CACHE:
        _MEGA_CACHE["nc"], _MEGA_CACHE["es"] = build_mega()
    nc = _MEGA_CACHE["nc"]
    flat = build_flat_pack(inputs)
    in_maps = [prep_mega(inputs, c, flat) for c in range(8)]
    res = run_bass_kernel_spmd(nc, in_maps, core_ids=list(range(8)))
    out = np.empty((B, S, D), np.float32)
    for c in range(8):
        ci = core_info(c)
        out[ci["b"], ci["h"] * SH:(ci["h"] + 1) * SH, :] = res.results[c]["out"].T
    return out


# version 10
# speedup vs baseline: 1.1639x; 1.0106x over previous
import math
import numpy as np
import ml_dtypes
from contextlib import ExitStack

import concourse.bass as bass
import concourse.mybir as mybir
from concourse.bass_utils import run_bass_kernel_spmd

F32 = mybir.dt.float32
BF16 = mybir.dt.bfloat16
AF = mybir.ActivationFunctionType
ALU = mybir.AluOpType
AX = mybir.AxisListType
bfnp = ml_dtypes.bfloat16

D = 1024
B = 4
S = 8192
SH = S // 2
NCH = D // 128
TT = 512

ENGS = ("sync", "tensor", "scalar", "vector", "gpsimd")
SAME_ENG_FIFO = False


def core_info(c):
    return {"b": c // 2, "h": c % 2, "has_prev": (c % 2 == 1)}


def _access(ap):
    t = ap.tensor
    name = t.name
    shape = list(t.shape)
    dims = ap.ap.to_list() if hasattr(ap.ap, "to_list") else list(ap.ap)
    off = ap.offset
    is_dram = "dram" in type(t).__name__.lower()
    if is_dram or len(shape) != 2:
        span = 1
        for st_, cnt in dims:
            span += abs(st_) * (cnt - 1)
        return (name, "flat", (off, off + span))
    F = shape[1]
    p0, f0 = off // F, off % F
    if dims and (dims[0][0] == F or dims[0][1] == 1):
        pcnt = dims[0][1]
        rest = dims[1:]
    else:
        pcnt = 1
        rest = dims
    fspan = 1
    for st_, cnt in rest:
        fspan += abs(st_) * (cnt - 1)
    return (name, "sb", (p0, p0 + pcnt, f0, f0 + fspan))


def _overlap(a, b):
    ka, ra = a
    kb, rb = b
    if ka != kb:
        return True
    if ka == "flat":
        return ra[0] < rb[1] and rb[0] < ra[1]
    return (ra[0] < rb[1] and rb[0] < ra[1] and
            ra[2] < rb[3] and rb[2] < ra[3])


def _covers(a, b):
    ka, ra = a
    kb, rb = b
    if ka != kb:
        return False
    if ka == "flat":
        return ra[0] <= rb[0] and ra[1] >= rb[1]
    return (ra[0] <= rb[0] and ra[1] >= rb[1] and
            ra[2] <= rb[2] and ra[3] >= rb[3])


class Ctx:
    def __init__(self, nc, es):
        self.nc = nc
        self.es = es
        self.streams = {e: [] for e in ENGS}
        self.sem = {e: es.enter_context(nc.semaphore(f"sem_{e}")) for e in ENGS}
        self.cnt = {e: 0 for e in ENGS}
        self.NSLOT = 8
        self.qsem = {q: [es.enter_context(nc.semaphore(f"qsem_{q}{i}"))
                         for i in range(self.NSLOT)]
                     for q in ("sync", "scalar", "gpsimd")}
        self.qcnt = {q: [0] * self.NSLOT for q in self.qsem}
        self.qidx = {q: 0 for q in self.qsem}
        self.ccsem = es.enter_context(nc.semaphore("sem_cc"))
        self.ccnt = 0
        self.water = {}
        self.acc = {}

    def _wait(self, eng, token):
        if token is None:
            return
        (kind, name), val = token
        if kind == "c":
            sem = self.ccsem
        elif kind == "e":
            if name == eng and (name == "tensor" or SAME_ENG_FIFO):
                return
            sem = self.sem[name]
        else:
            q, slot = name
            sem = self.qsem[q][slot]
        key = (eng, kind, name)
        if self.water.get(key, 0) >= val:
            return
        self.water[key] = val

        def thunk(e, sem=sem, val=val):
            e.wait_ge(sem, val)
        self.streams[eng].append(thunk)

    def _deps(self, eng, reads, writes):
        toks = []
        for ap in reads:
            nm, kind, rect = _access(ap)
            for rec in self.acc.get(nm, ()):
                if rec["write"] and _overlap((kind, rect), rec["rect"]):
                    toks.append(rec["tok"])
        for ap in writes:
            nm, kind, rect = _access(ap)
            for rec in self.acc.get(nm, ()):
                if _overlap((kind, rect), rec["rect"]):
                    toks.append(rec["tok"])
        for t in toks:
            self._wait(eng, t)

    def _record(self, token, reads, writes):
        for ap in writes:
            nm, kind, rect = _access(ap)
            lst = self.acc.setdefault(nm, [])
            lst[:] = [r for r in lst if not _covers((kind, rect), r["rect"])]
            lst.append({"rect": (kind, rect), "tok": token, "write": True})
        for ap in reads:
            nm, kind, rect = _access(ap)
            lst = self.acc.setdefault(nm, [])
            lst.append({"rect": (kind, rect), "tok": token, "write": False})
            if len(lst) > 96:
                ws = [r for r in lst if r["write"]]
                rs = {}
                for r in lst:
                    if not r["write"]:
                        k = r["tok"][0]
                        if k not in rs or r["tok"][1] > rs[k]["tok"][1]:
                            rs[k] = r
                full = ("flat", (0, 1 << 60)) if kind == "flat" else ("sb", (0, 1 << 30, 0, 1 << 30))
                for r in rs.values():
                    r["rect"] = full
                lst[:] = ws + list(rs.values())

    def op(self, eng, fn, reads=(), writes=()):
        self._deps(eng, reads, writes)
        self.cnt[eng] += 1
        val = self.cnt[eng]
        sem = self.sem[eng]

        def thunk(e, fn=fn, sem=sem):
            fn(e).then_inc(sem, 1)
        self.streams[eng].append(thunk)
        tok = (("e", eng), val)
        self._record(tok, reads, writes)
        return tok

    def mm(self, out, lhsT, rhs, start, stop, extra_reads=()):
        reads = [lhsT, rhs] + list(extra_reads)
        return self.op("tensor",
                       lambda e: e.matmul(out, lhsT, rhs, start=start, stop=stop),
                       reads=reads, writes=[out])

    def dma(self, q, out, in_, **kw):
        self._deps(q, [in_], [out])
        slot = self.qidx[q]
        self.qidx[q] = (slot + 1) % self.NSLOT
        if self.qcnt[q][slot] > 0:
            self._wait(q, (("q", (q, slot)), self.qcnt[q][slot]))
        self.qcnt[q][slot] += 16
        qsem = self.qsem[q][slot]

        def thunk(e, out=out, in_=in_, qsem=qsem, kw=kw):
            e.dma_start(out=out, in_=in_, **kw).then_inc(qsem, 16)
        self.streams[q].append(thunk)
        tok = (("q", (q, slot)), self.qcnt[q][slot])
        self._record(tok, [in_], [out])
        return tok

    def collective(self, kind, op_, rg_groups, ins, outs):
        self._deps("gpsimd", ins, outs)
        self.ccnt += 1
        val = self.ccnt
        sem = self.ccsem

        def thunk(e, ins=ins, outs=outs):
            e.collective_compute(kind, op_, replica_groups=rg_groups,
                                 ins=[a.opt() for a in ins],
                                 outs=[a.opt() for a in outs]).then_inc(sem, 1)
        self.streams["gpsimd"].append(thunk)
        tok = (("c", "cc"), val)
        self._record(tok, ins, outs)
        return tok

    def barrier(self, cc=False):
        snap_e = dict(self.cnt)
        for eng in ENGS:
            for p in ENGS:
                if snap_e[p] > 0 and not (p == eng == "tensor"):
                    self._wait(eng, (("e", p), snap_e[p]))
            for q in self.qcnt:
                for slot in range(self.NSLOT):
                    v = self.qcnt[q][slot]
                    if v > 0:
                        self._wait(eng, (("q", (q, slot)), v))
            if cc and self.ccnt > 0:
                self._wait(eng, (("c", "cc"), self.ccnt))

    def finalize(self):
        with self.nc.Block() as block:
            for eng in ENGS:
                stream = self.streams[eng]

                def run(e, stream=stream):
                    for f in stream:
                        f(e)
                getattr(block, eng)(run)


class Regs:
    def __init__(self, nc, es):
        self.nc, self.es = nc, es
        self.map = {}
        self.scopes = []
        self.uid = 0
        self.banks = [es.enter_context(nc.psum_tensor(f"psb{i}", [128, 512], F32))
                      for i in range(8)]

    def push(self):
        self.uid += 1
        self.scopes.append((ExitStack(), [], self.uid))

    def pop(self):
        es2, names, _ = self.scopes.pop()
        es2.close()
        for n in names:
            del self.map[n]

    def sb(self, name, shape, dtype=F32):
        if name not in self.map:
            if self.scopes:
                es2, names, uid = self.scopes[-1]
                self.map[name] = es2.enter_context(
                    self.nc.sbuf_tensor(f"{name}__{uid}", list(shape), dtype))
                names.append(name)
            else:
                self.map[name] = self.es.enter_context(
                    self.nc.sbuf_tensor(name, list(shape), dtype))
        return self.map[name]


def emit_setup(ctx, rg):
    for nm, shape, dt_, v in [("ones_col", [128, 1], F32, 1.0),
                              ("ones_row", [1, 128], F32, 1.0),
                              ("ones_col_b", [128, 1], BF16, 1.0),
                              ("eps5", [1, 1], F32, 1e-5)]:
        t = rg.sb(nm, shape, dt_)
        ctx.op("vector", lambda e, t=t, v=v: e.memset(t[:, :], v), writes=[t[:, :]])


def emit_ln_tile(ctx, rg, st, T, x_aps, out_f_aps, out_b_aps,
                 gamma_cols=None, beta_cols=None):
    sq, tmp, row = st["sq"], st["tmp"], st["row"]
    ps_sum, ps_sq, ps_rb, ps_zb = st["ps_sum"], st["ps_sq"], st["ps_rb"], st["ps_zb"]
    ones_col = rg.sb("ones_col", [128, 1])
    ones_row = rg.sb("ones_row", [1, 128])
    eps5 = rg.sb("eps5", [1, 1])
    MU, SA, SB = 0, T, 2 * T

    def R(a, b_):
        return row[0:1, a:b_]

    for c in range(NCH):
        ctx.op("scalar", lambda e, c=c: e.activation(sq[:, :], x_aps[c], AF.Square),
               reads=[x_aps[c]], writes=[sq[:, :]])
        ctx.mm(ps_sum, ones_col[:, :], x_aps[c], c == 0, c == NCH - 1)
        ctx.mm(ps_sq, ones_col[:, :], sq[:, :], c == 0, c == NCH - 1)
    ctx.op("vector", lambda e: e.tensor_scalar_mul(R(MU, MU + T), ps_sum, 1.0 / D),
           reads=[ps_sum], writes=[R(MU, MU + T)])
    ctx.op("vector", lambda e: e.tensor_scalar_mul(R(SA, SA + T), ps_sq, 1.0 / D),
           reads=[ps_sq], writes=[R(SA, SA + T)])
    ctx.op("vector", lambda e: e.tensor_mul(R(SB, SB + T), R(MU, MU + T), R(MU, MU + T)),
           reads=[R(MU, MU + T)], writes=[R(SB, SB + T)])
    ctx.op("vector", lambda e: e.tensor_sub(R(SA, SA + T), R(SA, SA + T), R(SB, SB + T)),
           reads=[R(SA, SA + T), R(SB, SB + T)], writes=[R(SA, SA + T)])
    ctx.op("scalar", lambda e: e.activation(R(SB, SB + T), R(SA, SA + T), AF.Sqrt, bias=eps5[0:1, 0:1]),
           reads=[R(SA, SA + T), eps5[0:1, 0:1]], writes=[R(SB, SB + T)])
    ctx.op("vector", lambda e: e.reciprocal(R(SA, SA + T), R(SB, SB + T)),
           reads=[R(SB, SB + T)], writes=[R(SA, SA + T)])
    ctx.op("vector", lambda e: e.tensor_mul(R(SB, SB + T), R(MU, MU + T), R(SA, SA + T)),
           reads=[R(MU, MU + T), R(SA, SA + T)], writes=[R(SB, SB + T)])
    ctx.mm(ps_rb, ones_row[0:1, :], R(SA, SA + T), True, True)
    ctx.mm(ps_zb, ones_row[0:1, :], R(SB, SB + T), True, True)

    toks_out = []
    for c in range(NCH):
        ctx.op("vector", lambda e, c=c: e.tensor_mul(tmp[:, :], x_aps[c], ps_rb),
               reads=[x_aps[c], ps_rb], writes=[tmp[:, :]])
        dst = out_f_aps[c] if out_f_aps is not None else out_b_aps[c]
        if gamma_cols is not None:
            ctx.op("vector", lambda e: e.tensor_sub(tmp[:, :], tmp[:, :], ps_zb),
                   reads=[tmp[:, :], ps_zb], writes=[tmp[:, :]])
            tcx = ctx.op("vector", lambda e, c=c, dst=dst: e.tensor_scalar(
                dst, tmp[:, :], gamma_cols[c], beta_cols[c], ALU.mult, ALU.add),
                reads=[tmp[:, :], gamma_cols[c], beta_cols[c]], writes=[dst])
        else:
            tcx = ctx.op("vector", lambda e, c=c, dst=dst: e.tensor_sub(dst, tmp[:, :], ps_zb),
                         reads=[tmp[:, :], ps_zb], writes=[dst])
        if out_f_aps is not None and out_b_aps is not None:
            tcx = ctx.op("gpsimd", lambda e, c=c: e.tensor_copy(out_b_aps[c], out_f_aps[c]),
                         reads=[out_f_aps[c]], writes=[out_b_aps[c]])
        toks_out.append(tcx)
    return toks_out


def make_ln_state(rg, tag, T, bank_ids):
    return {
        "sq": rg.sb(f"{tag}_sq", [128, T]),
        "tmp": rg.sb(f"{tag}_tmp", [128, T]),
        "row": rg.sb(f"{tag}_row", [1, 3 * T]),
        "ps_sum": rg.banks[bank_ids[0]][0:1, 0:T],
        "ps_sq": rg.banks[bank_ids[1]][0:1, 0:T],
        "ps_rb": rg.banks[bank_ids[2]][:, 0:T],
        "ps_zb": rg.banks[bank_ids[3]][:, 0:T],
    }


def build_b0_lnattn():
    nc = bass.Bass("TRN2", target_bir_lowering=False)
    es = ExitStack()
    ctx = Ctx(nc, es)
    rg = Regs(nc, es)
    io = {}
    for nm, shp, dt_, kind in [("x0T", [D, S + SH], F32, "ExternalInput"),
                               ("pk", [D, 128], F32, "ExternalInput"),
                               ("pv", [D, 128], F32, "ExternalInput"),
                               ("pq", [8, D, 128], F32, "ExternalInput"),
                               ("poT", [8, 128, D], F32, "ExternalInput"),
                               ("xn0f", [D, S + SH], F32, "ExternalOutput"),
                               ("xn0b", [D, S + SH], BF16, "ExternalOutput"),
                               ("s0", [D, SH], F32, "ExternalOutput")]:
        io[nm] = nc.dram_tensor(nm, shp, dt_, kind=kind).ap()
    emit_setup(ctx, rg)
    emit_b0_lnattn(ctx, rg, io)
    ctx.finalize()
    return nc, es


def emit_b0_lnattn(ctx, rg, io):
    NT_OWN = SH // TT
    NT_FULL = S // TT
    x0T, pk, pv, pq, poT = io["x0T"], io["pk"], io["pv"], io["pq"], io["poT"]
    xn0f, xn0b, s0 = io["xn0f"], io["xn0b"], io["s0"]

    st = make_ln_state(rg, "p1", TT, [0, 1, 2, 3])
    xs = rg.sb("p1_x", [128, NCH * TT])
    of = rg.sb("p1_of", [128, NCH * TT])
    ob = rg.sb("p1_ob", [128, NCH * TT], BF16)
    for it in range(NT_FULL + NT_OWN):
        t0 = it * TT
        x_aps = []
        for c in range(NCH):
            ap = xs[:, c * TT:(c + 1) * TT]
            ctx.dma("sync", ap, x0T[c * 128:(c + 1) * 128, t0:t0 + TT])
            x_aps.append(ap)
        of_aps = [of[:, c * TT:(c + 1) * TT] for c in range(NCH)]
        ob_aps = [ob[:, c * TT:(c + 1) * TT] for c in range(NCH)]
        emit_ln_tile(ctx, rg, st, TT, x_aps, of_aps, ob_aps)
        for c in range(NCH):
            ctx.dma("scalar", xn0f[c * 128:(c + 1) * 128, t0:t0 + TT], of_aps[c])
            ctx.dma("gpsimd", xn0b[c * 128:(c + 1) * 128, t0:t0 + TT], ob_aps[c])
    ctx.barrier()

    pk_sb = rg.sb("pk_sb", [128, NCH * 128], BF16)
    pv_sb = rg.sb("pv_sb", [128, NCH * 128], BF16)
    pq_sb = rg.sb("pq_sb", [128, 8 * NCH * 128], BF16)
    po_sb = rg.sb("po_sb", [128, 8 * D], BF16)
    K_sb = rg.sb("K_sb", [128, S], BF16)
    V_tok = rg.sb("V_tok", [128, S], BF16)
    for c in range(NCH):
        ctx.dma("gpsimd", pk_sb[:, c * 128:(c + 1) * 128], pk[c * 128:(c + 1) * 128, :])
        ctx.dma("gpsimd", pv_sb[:, c * 128:(c + 1) * 128], pv[c * 128:(c + 1) * 128, :])
    for hh in range(8):
        for c in range(NCH):
            ctx.dma("gpsimd", pq_sb[:, (hh * NCH + c) * 128:(hh * NCH + c + 1) * 128],
                    pq[hh, c * 128:(c + 1) * 128, :])
        ctx.dma("gpsimd", po_sb[:, hh * D:(hh + 1) * D], poT[hh, :, :])

    xb = rg.sb("p2_xb", [128, NCH * TT], BF16)
    sqk = rg.sb("p2_sqk", [128, TT])
    maxrow = rg.sb("p2_maxrow", [1, 8])
    ps_k = rg.banks[4][:, 0:TT]
    ps_k2 = rg.banks[5][0:1, 0:TT]
    ps_v = rg.banks[6][:, 0:128]
    ones_col = rg.sb("ones_col", [128, 1])
    ctx.op("vector", lambda e: e.memset(maxrow[:, :], 0.0), writes=[maxrow[:, :]])
    for it in range(NT_FULL):
        t0 = it * TT
        for c in range(NCH):
            ctx.dma("sync", xb[:, c * TT:(c + 1) * TT],
                    xn0b[c * 128:(c + 1) * 128, t0:t0 + TT])
        for c in range(NCH):
            ctx.mm(ps_k, pk_sb[:, c * 128:(c + 1) * 128], xb[:, c * TT:(c + 1) * TT],
                   c == 0, c == NCH - 1)
        ctx.op("scalar", lambda e, t0=t0: e.activation(K_sb[:, t0:t0 + TT], ps_k, AF.Copy),
               reads=[ps_k], writes=[K_sb[:, t0:t0 + TT]])
        ctx.op("scalar", lambda e: e.activation(sqk[:, :], ps_k, AF.Square),
               reads=[ps_k], writes=[sqk[:, :]])
        ctx.mm(ps_k2, ones_col[:, :], sqk[:, :], True, True)
        ctx.op("vector", lambda e: e.reduce_max(maxrow[0:1, 1:2], ps_k2, AX.X),
               reads=[ps_k2], writes=[maxrow[0:1, 1:2]])
        ctx.op("vector", lambda e: e.tensor_max(maxrow[0:1, 0:1], maxrow[0:1, 0:1], maxrow[0:1, 1:2]),
               reads=[maxrow[0:1, 0:2]], writes=[maxrow[0:1, 0:1]])
        for tc in range(TT // 128):
            for c in range(NCH):
                ctx.mm(ps_v, xb[:, c * TT + tc * 128: c * TT + (tc + 1) * 128],
                       pv_sb[:, c * 128:(c + 1) * 128], c == 0, c == NCH - 1)
            kc_g = it * (TT // 128) + tc
            ctx.op("scalar", lambda e, kc_g=kc_g: e.activation(
                V_tok[:, kc_g * 128:(kc_g + 1) * 128], ps_v, AF.Copy),
                reads=[ps_v], writes=[V_tok[:, kc_g * 128:(kc_g + 1) * 128]])
    ctx.op("scalar", lambda e: e.activation(maxrow[0:1, 2:3], maxrow[0:1, 0:1], AF.Sqrt),
           reads=[maxrow[0:1, 0:1]], writes=[maxrow[0:1, 2:3]])
    ctx.barrier()

    O_all = rg.sb("p2_O_all", [128, 8 * TT], BF16)
    qb = rg.sb("p2_qb", [128, TT], BF16)
    sqq = rg.sb("p2_sqq", [128, TT])
    p_b = rg.sb("p2_pb", [128, 4 * TT], BF16)
    rowq = rg.sb("p2_rowq", [1, TT])
    sc1 = rg.sb("p2_sc1", [1, 4])
    negb_sb = rg.sb("p2_negb", [128, 1])
    o_sb = rg.sb("p2_osb", [128, TT])
    recrow = rg.sb("p2_rec", [1, TT])
    ysb = rg.sb("p2_ysb", [128, TT])
    ps_q = rg.banks[0][:, 0:TT]
    ps_scores = [rg.banks[i][:, 0:TT] for i in (1, 2, 6)]
    ps_sigs = [rg.banks[3][0:1, 0:TT], rg.banks[3][32:33, 0:TT]]
    ps_os = [rg.banks[4][:, 0:TT], rg.banks[7][:, 0:TT]]
    ps_m_row = rg.banks[5][0:1, 0:TT]
    ps_m_col = rg.banks[5][:, 0:1]
    ps_m_full = rg.banks[5][:, 0:TT]
    ones_row = rg.sb("ones_row", [1, 128])
    ones_col_b = rg.sb("ones_col_b", [128, 1], BF16)

    NKC = S // 128

    pend_b0 = [None]

    def b0_epilogue(hh):
        ps_sig, ps_o = ps_sigs[hh % 2], ps_os[hh % 2]
        ctx.op("scalar", lambda e, ps_o=ps_o: e.activation(o_sb[:, :], ps_o, AF.Copy),
               reads=[ps_o], writes=[o_sb[:, :]])
        ctx.op("vector", lambda e, ps_sig=ps_sig: e.reciprocal(recrow[0:1, :], ps_sig),
               reads=[ps_sig], writes=[recrow[0:1, :]])
        ctx.mm(ps_m_full, ones_row[0:1, :], recrow[0:1, :], True, True)
        ctx.op("vector", lambda e, hh=hh: e.tensor_mul(
            O_all[:, hh * TT:(hh + 1) * TT], o_sb[:, :], ps_m_full),
            reads=[o_sb[:, :], ps_m_full], writes=[O_all[:, hh * TT:(hh + 1) * TT]])

    for qt in range(NT_OWN):
        q0 = S + qt * TT
        for c in range(NCH):
            ctx.dma("sync", xb[:, c * TT:(c + 1) * TT],
                    xn0b[c * 128:(c + 1) * 128, q0:q0 + TT])
        for hh in range(8):
            for c in range(NCH):
                ctx.mm(ps_q, pq_sb[:, (hh * NCH + c) * 128:(hh * NCH + c + 1) * 128],
                       xb[:, c * TT:(c + 1) * TT], c == 0, c == NCH - 1)
            ctx.op("scalar", lambda e: e.activation(qb[:, :], ps_q, AF.Copy),
                   reads=[ps_q], writes=[qb[:, :]])
            ctx.op("scalar", lambda e: e.activation(sqq[:, :], ps_q, AF.Square),
                   reads=[ps_q], writes=[sqq[:, :]])
            ctx.mm(ps_m_row, ones_col[:, :], sqq[:, :], True, True)
            ctx.op("scalar", lambda e: e.activation(rowq[0:1, :], ps_m_row, AF.Sqrt),
                   reads=[ps_m_row], writes=[rowq[0:1, :]])
            ctx.op("vector", lambda e: e.reduce_max(sc1[0:1, 0:1], rowq[0:1, :], AX.X),
                   reads=[rowq[0:1, :]], writes=[sc1[0:1, 0:1]])
            ctx.op("vector", lambda e: e.tensor_mul(sc1[0:1, 1:2], sc1[0:1, 0:1], maxrow[0:1, 2:3]),
                   reads=[sc1[0:1, 0:1], maxrow[0:1, 2:3]], writes=[sc1[0:1, 1:2]])
            ctx.op("vector", lambda e: e.tensor_scalar_mul(sc1[0:1, 2:3], sc1[0:1, 1:2], -1.0),
                   reads=[sc1[0:1, 1:2]], writes=[sc1[0:1, 2:3]])
            ctx.mm(ps_m_col, ones_row[0:1, :], sc1[0:1, 2:3], True, True)
            ctx.op("scalar", lambda e: e.activation(negb_sb[:, :], ps_m_col, AF.Copy),
                   reads=[ps_m_col], writes=[negb_sb[:, :]])

            ps_sig, ps_o = ps_sigs[hh % 2], ps_os[hh % 2]

            def do_sums(kc, ps_sig=ps_sig, ps_o=ps_o):
                pbs = p_b[:, (kc % 4) * TT:(kc % 4 + 1) * TT]
                ctx.mm(ps_sig, ones_col_b[:, :], pbs, kc == 0, kc == NKC - 1)
                ctx.mm(ps_o, V_tok[:, kc * 128:(kc + 1) * 128], pbs, kc == 0, kc == NKC - 1)
            for kc in range(NKC):
                psbank = ps_scores[kc % 3]
                pbslice = p_b[:, (kc % 4) * TT:(kc % 4 + 1) * TT]
                ctx.mm(psbank, K_sb[:, kc * 128:(kc + 1) * 128], qb[:, :], True, True)
                ctx.op("scalar", lambda e, psbank=psbank, pbslice=pbslice: e.activation(
                    pbslice, psbank, AF.Exp, bias=negb_sb[:, 0:1]),
                    reads=[psbank, negb_sb[:, 0:1]], writes=[pbslice])
                if kc >= 1:
                    do_sums(kc - 1)
            do_sums(NKC - 1)
            if pend_b0[0] is not None:
                b0_epilogue(pend_b0[0])
            pend_b0[0] = hh
        b0_epilogue(pend_b0[0])
        pend_b0[0] = None
        for dc in range(NCH):
            for hh in range(8):
                ctx.mm(ps_q, po_sb[:, hh * D + dc * 128: hh * D + (dc + 1) * 128],
                       O_all[:, hh * TT:(hh + 1) * TT], hh == 0, hh == 7)
            ctx.op("scalar", lambda e: e.activation(ysb[:, :], ps_q, AF.Copy),
                   reads=[ps_q], writes=[ysb[:, :]])
            ctx.dma("scalar", s0[dc * 128:(dc + 1) * 128, qt * TT:(qt + 1) * TT], ysb[:, :])
    ctx.barrier()


def prep_b0_lnattn(inputs, c):
    ci = core_info(c)
    b, hh = ci["b"], ci["h"]
    x0 = np.asarray(inputs["L0_attn_x"][b], np.float32)
    x0T = np.ascontiguousarray(x0.T)
    own = x0T[:, hh * SH:(hh + 1) * SH]
    x0T_ext = np.ascontiguousarray(np.concatenate([x0T, own], axis=1))
    return {
        "x0T": x0T_ext,
        "pk": np.ascontiguousarray(np.asarray(inputs["L0_attn_proj_k"], np.float32)),
        "pv": np.ascontiguousarray(np.asarray(inputs["L0_attn_proj_v"], np.float32)),
        "pq": np.ascontiguousarray(np.asarray(inputs["L0_attn_proj_q"], np.float32)),
        "poT": np.ascontiguousarray(np.asarray(inputs["L0_attn_proj_o"], np.float32).transpose(0, 2, 1)),
    }




L0F_HALO = 513
NV0 = 70


def _emit_b0_scalar(ctx, rg, qsrow, ksrow, qk_scale):
    sc = rg.sb("b0_sc", [1, 4])
    negb0 = rg.sb("negb0", [128, 1])
    ones_row = rg.sb("ones_row", [1, 128])
    ps = rg.banks[7][:, 0:1]
    ctx.op("vector", lambda e: e.reduce_max(sc[0:1, 0:1], qsrow[0:1, :], AX.X, apply_absolute_value=True),
           reads=[qsrow[0:1, :]], writes=[sc[0:1, 0:1]])
    ctx.op("vector", lambda e: e.reduce_max(sc[0:1, 1:2], ksrow[0:1, :], AX.X, apply_absolute_value=True),
           reads=[ksrow[0:1, :]], writes=[sc[0:1, 1:2]])
    ctx.op("vector", lambda e: e.tensor_mul(sc[0:1, 2:3], sc[0:1, 0:1], sc[0:1, 1:2]),
           reads=[sc[0:1, 0:2]], writes=[sc[0:1, 2:3]])
    ctx.op("vector", lambda e: e.tensor_scalar_mul(sc[0:1, 3:4], sc[0:1, 2:3], -float(qk_scale)),
           reads=[sc[0:1, 2:3]], writes=[sc[0:1, 3:4]])
    ctx.mm(ps, ones_row[0:1, :], sc[0:1, 3:4], True, True)
    ctx.op("scalar", lambda e: e.activation(negb0[:, :], ps, AF.Copy),
           reads=[ps], writes=[negb0[:, :]])
    return negb0


def _emit_l2norm_rope(ctx, rg, raw_bf, T, scale_col, rnb_bank, h2_bank, rot_bank,
                      amap, bmap, out_bf, tag, nch=NCH):
    hselA = rg.sb("hselA", [128, 2])
    hselB = rg.sb("hselB", [2, 128])
    rotm = rg.sb("rotm", [128, 128], BF16)
    sqt_ = rg.sb(f"{tag}_sq", [128, TT])
    rnr_ = rg.sb(f"{tag}_rn", [2, 2 * TT])
    t1_ = rg.sb(f"{tag}_t1", [128, TT])
    t2_ = rg.sb(f"{tag}_t2", [128, TT])
    qtl_ = rg.sb(f"{tag}_qt", [128, TT], BF16)
    sqt = sqt_[:, 0:T]; t1 = t1_[:, 0:T]; t2 = t2_[:, 0:T]; qtl = qtl_[:, 0:T]
    rnr = rnr_
    eps12 = rg.sb("eps12", [1, 1])
    for c in range(nch):
        rslice = raw_bf[:, c * T:(c + 1) * T]
        ctx.op("scalar", lambda e, rslice=rslice: e.activation(sqt[:, :], rslice, AF.Square),
               reads=[rslice], writes=[sqt[:, :]])
        ctx.mm(h2_bank[0:2, 0:T], hselA[:, :], sqt[:, :], True, True)
        ctx.op("scalar", lambda e: e.activation(rnr[0:2, 0:T], h2_bank[0:2, 0:T], AF.Sqrt),
               reads=[h2_bank[0:2, 0:T]], writes=[rnr[0:2, 0:T]])
        ctx.op("vector", lambda e: e.tensor_scalar_max(rnr[0:2, TT:TT + T], rnr[0:2, 0:T], 1e-12),
               reads=[rnr[0:2, 0:T]], writes=[rnr[0:2, TT:TT + T]])
        ctx.op("vector", lambda e: e.reciprocal(rnr[0:2, 0:T], rnr[0:2, TT:TT + T]),
               reads=[rnr[0:2, TT:TT + T]], writes=[rnr[0:2, 0:T]])
        ctx.mm(rnb_bank[:, 0:T], hselB[:, :], rnr[0:2, 0:T], True, True)
        ctx.op("vector", lambda e, rslice=rslice: e.scalar_tensor_tensor(
            qtl[:, :], rslice, scale_col, rnb_bank[:, 0:T], ALU.mult, ALU.mult),
            reads=[rslice, scale_col, rnb_bank[:, 0:T]], writes=[qtl[:, :]])
        ctx.mm(rot_bank[:, 0:T], rotm[:, :], qtl[:, :], True, True)
        ctx.op("vector", lambda e, c=c: e.tensor_mul(t1[:, :], qtl[:, :], amap[c]),
               reads=[qtl[:, :], amap[c]], writes=[t1[:, :]])
        ctx.op("vector", lambda e, c=c: e.tensor_mul(t2[:, :], rot_bank[:, 0:T], bmap[c]),
               reads=[rot_bank[:, 0:T], bmap[c]], writes=[t2[:, :]])
        ctx.op("vector", lambda e, c=c: e.tensor_add(out_bf[:, c * T:(c + 1) * T], t1[:, :], t2[:, :]),
               reads=[t1[:, :], t2[:, :]], writes=[out_bf[:, c * T:(c + 1) * T]])


def build_l0f_attn():
    return build_lf_attn(8, 256)


def build_l1f_attn():
    return build_lf_attn(4, 512)


def build_lf_attn(NI, W):
    HW = L0F_HALO
    NKC = (W + TT) // 128
    INNER = NI * 128
    nc = bass.Bass("TRN2", target_bir_lowering=False)
    es = ExitStack()
    ctx = Ctx(nc, es)
    rg = Regs(nc, es)
    io = {}
    for nm, shp, dt_, kind in [("xw", [D, HW + SH], F32, "ExternalInput"),
                               ("wqkv", [D, 3 * INNER], F32, "ExternalInput"),
                               ("wo", [INNER, D], F32, "ExternalInput"),
                               ("ropes", [4, 128, HW + SH], F32, "ExternalInput"),
                               ("qsrow", [1, 64], F32, "ExternalInput"),
                               ("ksrow", [1, 64], F32, "ExternalInput"),
                               ("vecs_in", [128, NV0], F32, "ExternalInput"),
                               ("hselA_in", [128, 2], F32, "ExternalInput"),
                               ("hselB_in", [2, 128], F32, "ExternalInput"),
                               ("rotm_in", [128, 128], F32, "ExternalInput"),
                               ("idneg_in", [128, 128], F32, "ExternalInput"),
                               ("masks_in", [2, NKC, 128, TT], F32, "ExternalInput"),
                               ("xo", [D, SH], F32, "ExternalOutput")]:
        io[nm] = nc.dram_tensor(nm, shp, dt_, kind=kind).ap()
    emit_setup(ctx, rg)
    io["xw_halo"] = io["xw"][:, 0:HW]
    io["xw"] = io["xw"][:, HW:]
    emit_lf_attn(ctx, rg, io, NI, W)
    ctx.finalize()
    return nc, es


def emit_lf_attn(ctx, rg, io, NI, W, lset=0):
    HW = L0F_HALO
    NKC = (W + TT) // 128
    NPV = W // 128
    INNER = NI * 128
    NT = SH // TT
    xw, wqkv, wo, ropes = io["xw"], io["wqkv"], io["wo"], io["ropes"]
    xw_halo = io["xw_halo"]
    qsrow, ksrow = io["qsrow"], io["ksrow"]
    vecs_d, hselA_d, hselB_d = io["vecs_in"], io["hselA_in"], io["hselB_in"]
    rotm_d, idneg_d, masks_d = io["rotm_in"], io["idneg_in"], io["masks_in"]
    xo = io["xo"]
    vecs = rg.sb("vecs", [128, NV0])
    hselA = rg.sb("hselA", [128, 2])
    hselB = rg.sb("hselB", [2, 128])
    rotm = rg.sb("rotm", [128, 128], BF16)
    idneg = rg.sb("idneg", [128, 128], BF16)
    eps12 = rg.sb("eps12", [1, 1])
    qsr = rg.sb("qsr", [1, 64])
    ksr = rg.sb("ksr", [1, 64])
    masks = rg.sb(f"masks_sb_{NI}", [128, 2 * NKC * TT], BF16)
    ctx.op("vector", lambda e: e.memset(eps12[:, :], 1e-12), writes=[eps12[:, :]])
    ctx.dma("sync", vecs[:, :], vecs_d)
    ctx.dma("sync", hselA[:, :], hselA_d)
    ctx.dma("sync", hselB[:, :], hselB_d)
    ctx.dma("sync", qsr[:, :], qsrow)
    ctx.dma("sync", ksr[:, :], ksrow)
    if io.get("load_consts", True):
        ctx.dma("gpsimd", rotm[:, :], rotm_d)
        ctx.dma("gpsimd", idneg[:, :], idneg_d)
        for mi in range(2):
            for kc in range(NKC):
                ctx.dma("gpsimd", masks[:, (mi * NKC + kc) * TT:(mi * NKC + kc + 1) * TT],
                        masks_d[mi, kc, :, :])
    negb0 = _emit_b0_scalar(ctx, rg, qsr, ksr, 8.0)
    wq_sb = rg.sb("wq_sb", [128, NCH * 3072], BF16)
    wo_sb = rg.sb("wo_sb", [128, 8 * 1024], BF16)
    for c in range(NCH):
        ctx.dma("gpsimd", wq_sb[:, c * 3 * INNER:(c + 1) * 3 * INNER], wqkv[c * 128:(c + 1) * 128, :])
    for c in range(NI):
        ctx.dma("gpsimd", wo_sb[:, c * 1024:(c + 1) * 1024], wo[c * 128:(c + 1) * 128, :])
    ropebuf = rg.sb("ropebuf", [128, 4 * TT])

    qraw = rg.sb("lf_qraw", [128, 8 * TT], BF16)
    kraw = rg.sb("lf_kraw", [128, 8 * TT], BF16)
    qcur = qraw
    kcur = kraw
    VG = 2 * NI * 65
    vcur = rg.sb("lf_vcur", [128, (TT // 128) * VG], BF16)
    kprev = rg.sb("lf_kprev", [128, 8 * 256 if NI == 8 else 4 * 512], BF16)
    vprev = rg.sb("lf_vprev", [128, (W // 128) * VG], BF16)
    _vt = vcur[:, 0:1].tensor
    _vrl = (TT // 128) * VG
    ctx.op("vector", lambda e: e.memset(
        bass.AP(_vt, 64, [[_vrl, 128], [65, 2 * NI * (TT // 128)], [1, 1]]), 1.0),
        writes=[vcur[:, :]])
    _pt = vprev[:, 0:1].tensor
    _prl = (W // 128) * VG
    ctx.op("vector", lambda e: e.memset(
        bass.AP(_pt, 64, [[_prl, 128], [65, 2 * NI * (W // 128)], [1, 1]]), 1.0),
        writes=[vprev[:, :]])
    p_b = rg.sb("lf_pb", [128, 4 * TT], BF16)
    o_f = rg.sb("lf_of", [128, 2 * TT])
    recr = rg.sb("lf_rec", [1, 2 * TT])
    Oall = rg.sb("lf_Oall", [128, NI * TT], BF16)
    osum = o_f[:, 0:TT]
    xot = rg.sb("lf_xot", [128, TT])
    ones_col_b = rg.sb("ones_col_b", [128, 1])
    ones_row = rg.sb("ones_row", [1, 128])
    onesb = rg.sb("ones_col_bb", [128, 1], BF16)
    ctx.op("vector", lambda e: e.memset(onesb[:, :], 1.0), writes=[onesb[:, :]])

    def load_ropes(lo, width):
        aps = {}
        for ti in range(4):
            dst = ropebuf[:, ti * TT: ti * TT + width]
            ctx.dma("sync", dst, ropes[ti, :, lo:lo + width])
            aps[ti] = dst
        return aps

    def qkv_project(xsb_ap_of, T, want_q=True):
        W3 = 3 * INNER
        bs = [0]
        for base_oc, dst in (((NI, kraw),) if not want_q else ((0, qraw), (NI, kraw))):
            for oc in range(NI):
                bank = rg.banks[bs[0] % 6][:, 0:T]
                bs[0] += 1
                for c in range(NCH):
                    ctx.mm(bank, wq_sb[:, c * W3 + (base_oc + oc) * 128: c * W3 + (base_oc + oc + 1) * 128],
                           xsb_ap_of(c), c == 0, c == NCH - 1)
                ctx.op("scalar", lambda e, bank=bank, dst=dst, oc=oc, T=T: e.activation(
                    dst[:, oc * T:(oc + 1) * T], bank, AF.Copy),
                    reads=[bank], writes=[dst[:, oc * T:(oc + 1) * T]])
        ntc = T // 128
        nvh = max(1, INNER // 512)
        vw = min(512, INNER)
        vrow = vcur[:, 0:1].tensor
        vrl = (TT // 128) * VG
        for tc in range(ntc):
            for vh in range(nvh):
                bank = rg.banks[bs[0] % 6][:, 0:vw]
                bs[0] += 1
                for c in range(NCH):
                    ctx.mm(bank, xsb_ap_of(c)[:, tc * 128:(tc + 1) * 128],
                           wq_sb[:, c * W3 + 2 * INNER + vh * vw: c * W3 + 2 * INNER + (vh + 1) * vw],
                           c == 0, c == NCH - 1)
                dst3 = bass.AP(vrow, vcur[:, 0:1].offset + tc * VG + vh * 8 * 65,
                               [[vrl, 128], [65, 8], [1, 64]])
                cov = vcur[:, tc * VG + vh * 8 * 65: tc * VG + vh * 8 * 65 + 8 * 65 - 1]
                ctx.op("scalar", lambda e, bank=bank, dst3=dst3: e.activation(
                    dst3, bank, AF.Copy), reads=[bank], writes=[cov])

    Tp = W
    xt = rg.sb("lf_xt", [128, NCH * (TT + 1)])
    xsb = rg.sb("lf_xsb", [128, NCH * TT], BF16)
    for c in range(NCH):
        ctx.dma("sync", xt[:, c * (TT + 1): c * (TT + 1) + Tp + 1],
                xw_halo[c * 128:(c + 1) * 128, HW - W - 1:HW])
    for c in range(NCH):
        src = xt[:, c * (TT + 1) + 1: c * (TT + 1) + Tp + 1] if c < 4 else \
              xt[:, c * (TT + 1): c * (TT + 1) + Tp]
        ctx.op("gpsimd", lambda e, c=c, src=src: e.tensor_copy(xsb[:, c * Tp:(c + 1) * Tp], src),
               reads=[src], writes=[xsb[:, c * Tp:(c + 1) * Tp]])
    qkv_project(lambda c: xsb[:, c * Tp:(c + 1) * Tp], Tp, want_q=False)
    rp = load_ropes(HW - W, Tp)
    amap = [rp[2]] * NI
    bmap = [rp[3]] * NI
    _emit_l2norm_rope(ctx, rg, kraw, Tp, vecs[:, 69:70], rg.banks[4], rg.banks[5],
                      rg.banks[6], amap, bmap, kprev, "lr", nch=NI)
    for i in range(NPV):
        ctx.op("gpsimd", lambda e, i=i: e.tensor_copy(vprev[:, i * VG:(i + 1) * VG],
                                                      vcur[:, i * VG:(i + 1) * VG]),
               reads=[vcur[:, i * VG:(i + 1) * VG]], writes=[vprev[:, i * VG:(i + 1) * VG]])

    xt2 = rg.sb("lf_xt2", [128, NCH * (TT + 1)])
    for it in range(NT):
        t0 = it * TT
        xt = (rg.sb("lf_xt", [128, NCH * (TT + 1)]), xt2)[it % 2]
        xsb = rg.sb("lf_xsb", [128, NCH * TT], BF16)
        for c in range(NCH):
            if it == 0:
                ctx.dma("sync", xt[:, c * (TT + 1):c * (TT + 1) + 1],
                        xw_halo[c * 128:(c + 1) * 128, HW - 1:HW],
                        allow_slow_non_contiguous=True)
                ctx.dma("sync", xt[:, c * (TT + 1) + 1:(c + 1) * (TT + 1)],
                        xw[c * 128:(c + 1) * 128, 0:TT])
            else:
                ctx.dma("sync", xt[:, c * (TT + 1):(c + 1) * (TT + 1)],
                        xw[c * 128:(c + 1) * 128, t0 - 1: t0 + TT])
        for c in range(NCH):
            src = xt[:, c * (TT + 1) + 1:(c + 1) * (TT + 1)] if c < 4 else \
                  xt[:, c * (TT + 1):(c + 1) * (TT + 1) - 1]
            ctx.op("gpsimd", lambda e, c=c, src=src: e.tensor_copy(xsb[:, c * TT:(c + 1) * TT], src),
                   reads=[src], writes=[xsb[:, c * TT:(c + 1) * TT]])
        qkv_project(lambda c: xsb[:, c * TT:(c + 1) * TT], TT, want_q=True)
        rp = load_ropes(HW + t0, TT)
        _emit_l2norm_rope(ctx, rg, qraw, TT, vecs[:, 68:69], rg.banks[4], rg.banks[5],
                          rg.banks[6], [rp[0]] * NI, [rp[1]] * NI, qcur, "lr", nch=NI)
        _emit_l2norm_rope(ctx, rg, kraw, TT, vecs[:, 69:70], rg.banks[4], rg.banks[5],
                          rg.banks[6], [rp[2]] * NI, [rp[3]] * NI, kcur, "lr", nch=NI)
        mbase = NKC * TT if it == 0 else 0
        _emit_lf_attn_tile(ctx, rg, NI, W, mbase, qcur, kcur, vcur, kprev, vprev,
                           p_b, o_f, recr, Oall, masks, idneg, negb0, onesb)
        for oc in range(NCH):
            bank = rg.banks[oc % 6][:, 0:TT]
            for c in range(NI):
                ctx.mm(bank, wo_sb[:, c * 1024 + oc * 128: c * 1024 + (oc + 1) * 128],
                       Oall[:, c * TT:(c + 1) * TT], c == 0, c == NI - 1)
            ctx.op("scalar", lambda e, bank=bank: e.activation(osum[:, :], bank, AF.Copy),
                   reads=[bank], writes=[osum[:, :]])
            xres = xt[:, oc * (TT + 1) + 1:(oc + 1) * (TT + 1)]
            ctx.op("vector", lambda e, oc=oc, xres=xres: e.tensor_add(xot[:, :], osum[:, :], xres),
                   reads=[osum[:, :], xres], writes=[xot[:, :]])
            ctx.dma("scalar", xo[oc * 128:(oc + 1) * 128, t0:t0 + TT], xot[:, :])
        for c in range(NI):
            ctx.op("gpsimd", lambda e, c=c: e.tensor_copy(
                kprev[:, c * W:(c + 1) * W], kcur[:, c * TT + TT - W:(c + 1) * TT]),
                reads=[kcur[:, c * TT + TT - W:(c + 1) * TT]], writes=[kprev[:, c * W:(c + 1) * W]])
        for i in range(NPV):
            src_v = vcur[:, (TT // 128 - NPV + i) * VG:(TT // 128 - NPV + i + 1) * VG]
            ctx.op("gpsimd", lambda e, i=i, src_v=src_v: e.tensor_copy(
                vprev[:, i * VG:(i + 1) * VG], src_v),
                reads=[src_v], writes=[vprev[:, i * VG:(i + 1) * VG]])
    ctx.barrier()


def _emit_lf_attn_tile(ctx, rg, NI, W, mbase, qcur, kcur, vcur, kprev, vprev,
                       p_b, o_f, recr, Oall, masks, idneg, negb0, onesb):
    ones_row = rg.sb("ones_row", [1, 128])
    INNER = NI * 128
    VG = 2 * NI * 65
    NKC = (W + TT) // 128
    NPK = W // 128

    def epilogue(hc, hp, sig, pvb, sb_bank):
        r0 = 64 * hp
        ctx.op("scalar", lambda e, pvb=pvb, hp=hp: e.activation(
            o_f[0:64, hp * TT:(hp + 1) * TT], pvb[0:64, 0:TT], AF.Copy),
            reads=[pvb[0:64, 0:TT]], writes=[o_f[0:64, hp * TT:(hp + 1) * TT]])
        ctx.op("vector", lambda e, sig=sig, hp=hp: e.reciprocal(
            recr[0:1, hp * TT:(hp + 1) * TT], sig),
            reads=[sig], writes=[recr[0:1, hp * TT:(hp + 1) * TT]])
        rb = rg.banks[4 + 3 * hp][0:64, 0:TT]
        ctx.mm(rb, ones_row[0:1, 0:64], recr[0:1, hp * TT:(hp + 1) * TT], True, True)
        dst = Oall[r0:r0 + 64, hc * TT:(hc + 1) * TT]
        ctx.op("vector", lambda e, rb=rb, dst=dst, hp=hp: e.tensor_mul(
            dst, o_f[0:64, hp * TT:(hp + 1) * TT], rb),
            reads=[o_f[0:64, hp * TT:(hp + 1) * TT], rb], writes=[dst])

    pend = [None]
    for hh in range(2 * NI):
        hc, hp = hh // 2, hh % 2
        r0 = 64 * hp
        sb_bank = rg.banks[4 + 3 * hp][:, 0:TT]
        pvb = rg.banks[5 + hp][0:65, 0:TT]
        sig = rg.banks[5 + hp][64:65, 0:TT]
        qh = qcur[r0:r0 + 64, hc * TT:(hc + 1) * TT]

        g65 = (hc * 2 + hp) * 65

        def vl_of(kc):
            if kc < NPK:
                return vprev[:, kc * VG + g65: kc * VG + g65 + 65]
            return vcur[:, (kc - NPK) * VG + g65: (kc - NPK) * VG + g65 + 65]

        def do_sums2(kc):
            pbs = p_b[:, ((hh % 2) * 2 + kc % 2) * TT:((hh % 2) * 2 + kc % 2 + 1) * TT]
            ctx.mm(pvb, vl_of(kc), pbs, kc == 0, kc == NKC - 1)
        for kc in range(NKC):
            if kc < NPK:
                klhs = kprev[r0:r0 + 64, hc * W + kc * 128: hc * W + (kc + 1) * 128]
            else:
                klhs = kcur[r0:r0 + 64, hc * TT + (kc - NPK) * 128: hc * TT + (kc - NPK + 1) * 128]
            mslice = masks[:, mbase + kc * TT: mbase + (kc + 1) * TT]
            pbuf = p_b[:, ((hh % 2) * 2 + kc % 2) * TT:((hh % 2) * 2 + kc % 2 + 1) * TT]
            ctx.mm(sb_bank, klhs, qh, True, False)
            ctx.mm(sb_bank, idneg[:, :], mslice, False, True)
            ctx.op("scalar", lambda e, sb_bank=sb_bank, pbuf=pbuf: e.activation(
                pbuf, sb_bank, AF.Exp, bias=negb0[:, 0:1]),
                reads=[sb_bank, negb0[:, 0:1]], writes=[pbuf])
            if kc >= 1:
                do_sums2(kc - 1)
        do_sums2(NKC - 1)
        if pend[0] is not None:
            epilogue(*pend[0])
        pend[0] = (hc, hp, sig, pvb, sb_bank)
    epilogue(*pend[0])
    pend[0] = None


def build_l0f_ffn():
    nc = bass.Bass("TRN2", target_bir_lowering=False)
    es = ExitStack()
    ctx = Ctx(nc, es)
    rg = Regs(nc, es)
    io = {}
    for nm, shp, dt_, kind in [("xw1", [D, 1 + SH], F32, "ExternalInput"),
                               ("w1p", [D, 5632], F32, "ExternalInput"),
                               ("w2p", [2816, 1024], F32, "ExternalInput"),
                               ("vecs_in", [128, NV0], F32, "ExternalInput"),
                               ("xo", [D, SH], F32, "ExternalOutput")]:
        io[nm] = nc.dram_tensor(nm, shp, dt_, kind=kind).ap()
    emit_setup(ctx, rg)
    emit_lf_ffn(ctx, rg, io)
    ctx.finalize()
    return nc, es


def emit_lf_ffn(ctx, rg, io):
    NT = SH // TT
    NIC2 = 22
    xw1, w1p, w2p, vecs_d, xo = io["xw1"], io["w1p"], io["w2p"], io["vecs_in"], io["xo"]
    xt2 = rg.sb("ff_xt2", [128, NCH * (TT + 1)])
    xn2 = rg.sb("ff_xn2", [128, NCH * TT], BF16)

    emit_setup(ctx, rg)
    vecs = rg.sb("vecs", [128, NV0])
    ctx.dma("sync", vecs[:, :], vecs_d)
    w1b = rg.sb("ff_w1b", [128, 4 * 8 * 128], BF16)
    w2b = rg.sb("ff_w2b", [128, 2 * 1024], BF16)

    ab = rg.sb("ff_ab", [128, NIC2 * TT], BF16)
    gb = rg.sb("ff_gb", [128, TT], BF16)
    ysum = rg.sb("ff_ysum", [128, TT])
    xot = rg.sb("ff_xot", [128, TT])
    st = make_ln_state(rg, "ffln", TT, [4, 5, 6, 7])
    for it in [NT - 1] + list(range(NT - 1)):
        t0 = it * TT
        xt = (rg.sb("ff_xt", [128, NCH * (TT + 1)]), xt2)[it % 2]
        xn = (rg.sb("ff_xn", [128, NCH * TT], BF16), xn2)[it % 2]
        for c in range(NCH):
            ctx.dma("sync", xt[:, c * (TT + 1):(c + 1) * (TT + 1)],
                    xw1[c * 128:(c + 1) * 128, t0:t0 + TT + 1])
        x_shift = [xt[:, c * (TT + 1) + 1:(c + 1) * (TT + 1)] if c < 4 else
                   xt[:, c * (TT + 1):(c + 1) * (TT + 1) - 1] for c in range(NCH)]
        gamma_cols = [vecs[:, c:c + 1] for c in range(NCH)]
        beta_cols = [vecs[:, 8 + c:8 + c + 1] for c in range(NCH)]
        xn_aps = [xn[:, c * TT:(c + 1) * TT] for c in range(NCH)]
        emit_ln_tile(ctx, rg, st, TT, x_shift, None, xn_aps,
                     gamma_cols=gamma_cols, beta_cols=beta_cols)
        for c2 in range(NIC2):
            sa, sg = (2 * c2) % 4, (2 * c2 + 1) % 4
            wa = w1b[:, sa * 1024:(sa + 1) * 1024]
            wg = w1b[:, sg * 1024:(sg + 1) * 1024]
            ctx.dma("sync", wa.rearrange("p (a c) -> p a c", a=8, c=128),
                    w1p[:, c2 * 128:(c2 + 1) * 128].rearrange("(a b) c -> b a c", a=8, b=128))
            ctx.dma("sync", wg.rearrange("p (a c) -> p a c", a=8, c=128),
                    w1p[:, (NIC2 + c2) * 128:(NIC2 + c2 + 1) * 128].rearrange("(a b) c -> b a c", a=8, b=128))
            bank_a = rg.banks[(2 * c2) % 6][:, 0:TT]
            bank_g = rg.banks[(2 * c2 + 1) % 6][:, 0:TT]
            for c in range(NCH):
                ctx.mm(bank_a, wa[:, c * 128:(c + 1) * 128], xn_aps[c], c == 0, c == NCH - 1)
            for c in range(NCH):
                ctx.mm(bank_g, wg[:, c * 128:(c + 1) * 128], xn_aps[c], c == 0, c == NCH - 1)
            acol = vecs[:, 16 + c2:16 + c2 + 1]
            gcol = vecs[:, 16 + NIC2 + c2:16 + NIC2 + c2 + 1]
            aslice = ab[:, c2 * TT:(c2 + 1) * TT]
            ctx.op("vector", lambda e, bank_a=bank_a, aslice=aslice, acol=acol: e.tensor_scalar_add(
                aslice, bank_a, acol), reads=[bank_a, acol], writes=[aslice])
            ctx.op("scalar", lambda e, bank_g=bank_g, gcol=gcol: e.activation(
                gb[:, :], bank_g, AF.Gelu, bias=gcol), reads=[bank_g, gcol], writes=[gb[:, :]])
            ctx.op("vector", lambda e, aslice=aslice: e.tensor_mul(aslice, aslice, gb[:, :]),
                   reads=[aslice, gb[:, :]], writes=[aslice])
        for c2 in range(NIC2):
            wslot = w2b[:, (c2 % 2) * 1024:(c2 % 2 + 1) * 1024]
            ctx.dma("sync", wslot, w2p[c2 * 128:(c2 + 1) * 128, :])
            for oc in range(NCH):
                ctx.mm(rg.banks[oc][:, 0:TT], wslot[:, oc * 128:(oc + 1) * 128],
                       ab[:, c2 * TT:(c2 + 1) * TT], c2 == 0, c2 == NIC2 - 1)
        for oc in range(NCH):
            bank = rg.banks[oc][:, 0:TT]
            bcol = vecs[:, 60 + oc:60 + oc + 1]
            ctx.op("vector", lambda e, bank=bank, bcol=bcol: e.tensor_scalar_add(
                ysum[:, :], bank, bcol), reads=[bank, bcol], writes=[ysum[:, :]])
            xres = xt[:, oc * (TT + 1) + 1:(oc + 1) * (TT + 1)]
            ctx.op("vector", lambda e, xres=xres: e.tensor_add(xot[:, :], ysum[:, :], xres),
                   reads=[ysum[:, :], xres], writes=[xot[:, :]])
            ctx.dma("scalar", xo[oc * 128:(oc + 1) * 128, t0:t0 + TT], xot[:, :])
    ctx.barrier()


def _l0f_consts(inputs, li):
    vecs = np.zeros((128, NV0), np.float32)
    ln_g = np.asarray(inputs["L0_ffn_ln_g"][li], np.float32)
    ln_b = np.asarray(inputs["L0_ffn_ln_b"][li], np.float32)
    b1 = np.asarray(inputs["L0_ffn_b1"][li], np.float32)
    b2 = np.asarray(inputs["L0_ffn_b2"][li], np.float32)
    qs = np.asarray(inputs["L0_ffn_q_scale"][li], np.float32)
    ks = np.asarray(inputs["L0_ffn_k_scale"][li], np.float32)
    for c in range(8):
        vecs[:, c] = ln_g[c * 128:(c + 1) * 128]
        vecs[:, 8 + c] = ln_b[c * 128:(c + 1) * 128]
        vecs[:, 60 + c] = b2[c * 128:(c + 1) * 128]
    b1p = np.zeros(5632, np.float32)
    b1p[0:2730] = b1[0:2730]
    b1p[2816:2816 + 2730] = b1[2730:5460]
    for c in range(44):
        vecs[:, 16 + c] = b1p[c * 128:(c + 1) * 128]
    vecs[:, 68] = np.tile(qs, 2)
    vecs[:, 69] = np.tile(ks, 2)
    W1 = np.asarray(inputs["L0_ffn_W1"][li], np.float32)
    w1p = np.zeros((D, 5632), np.float32)
    w1p[:, 0:2730] = W1[:, 0:2730]
    w1p[:, 2816:2816 + 2730] = W1[:, 2730:5460]
    W2 = np.asarray(inputs["L0_ffn_W2"][li], np.float32)
    w2p = np.zeros((2816, 1024), np.float32)
    w2p[0:2730, :] = W2
    return vecs, np.ascontiguousarray(w1p), np.ascontiguousarray(w2p)


_HSEL_A = None


def _l0f_static(c):
    ci = core_info(c)
    own0 = ci["h"] * SH
    aq, bq, ak, bk = rope_tables_l0(own0)
    ropes = np.stack([aq, bq, ak, bk])
    hselA = np.zeros((128, 2), np.float32)
    hselA[0:64, 0] = 1.0
    hselA[64:128, 1] = 1.0
    hselB = np.zeros((2, 128), np.float32)
    hselB[0, 0:64] = 1.0
    hselB[1, 64:128] = 1.0
    rotm = rot_matrix(64, 2)
    idneg = (np.eye(128, dtype=np.float32) * -1e5)
    masks = l0f_masks(ci["has_prev"])
    return dict(ropes=ropes, hselA_in=hselA, hselB_in=hselB, rotm_in=rotm,
                idneg_in=idneg, masks_in=masks)


def rope_tables_l0(own0):
    pos = np.arange(-L0F_HALO, SH) + own0
    pos = np.clip(pos, 0, None)
    aq, bq, ak, bk = rope_tables(S, 64, 256, pos, 8.0)
    return [np.ascontiguousarray(np.tile(t, (2, 1))) for t in (aq, bq, ak, bk)]


def rope_tables(n_total, dim_head, window, positions, qk_scale):
    half = np.arange(0, dim_head, 2, dtype=np.float64)
    inv_freq = 1.0 / 10000.0 ** (half / dim_head)
    t = positions.astype(np.float64)
    freqs = t[:, None] * inv_freq[None, :]
    freqs = np.concatenate([freqs, freqs], -1)
    base = (half + 0.4 * dim_head) / (1.4 * dim_head)
    power = (t - n_total // 2) / float(window // 2)
    scale = base[None, :] ** power[:, None]
    scale = np.concatenate([scale, scale], -1)
    cos, sin = np.cos(freqs), np.sin(freqs)
    aq = (cos * scale * qk_scale).T
    bq = (sin * scale * qk_scale).T
    ak = (cos / scale).T
    bk = (sin / scale).T
    return [np.asarray(x, np.float32) for x in (aq, bq, ak, bk)]


def rot_matrix(dim_head, nheads_per_chunk):
    dh = dim_head
    m = np.zeros((dh, dh), np.float32)
    for dcoord in range(dh // 2):
        m[dcoord + dh // 2, dcoord] = -1.0
        m[dcoord, dcoord + dh // 2] = 1.0
    full = np.zeros((128, 128), np.float32)
    for hh2 in range(nheads_per_chunk):
        full[hh2 * dh:(hh2 + 1) * dh, hh2 * dh:(hh2 + 1) * dh] = m
    return full


def l0f_masks(has_prev):
    out = np.zeros((2, 6, 128, TT), np.float32)
    q_rel = np.arange(TT)
    for kc in range(6):
        k_rel = np.arange(128) + kc * 128 - 256
        dmat = q_rel[None, :] - k_rel[:, None]
        invalid = (dmat < 0) | (dmat > 256)
        out[0, kc] = invalid
        inv0 = invalid.copy()
        if not has_prev and kc < 2:
            inv0[:, :] = 1.0
        out[1, kc] = inv0
    return np.ascontiguousarray(out)


def prep_l0f_attn(inputs, c, li):
    st_ = _l0f_static(c)
    vecs, _, _ = _l0f_consts(inputs, li)
    return dict(
        wqkv=np.ascontiguousarray(np.asarray(inputs["L0_ffn_Wqkv"][li], np.float32)),
        wo=np.ascontiguousarray(np.asarray(inputs["L0_ffn_Wo"][li], np.float32)),
        qsrow=np.asarray(inputs["L0_ffn_q_scale"][li], np.float32).reshape(1, 64),
        ksrow=np.asarray(inputs["L0_ffn_k_scale"][li], np.float32).reshape(1, 64),
        vecs_in=vecs, **st_)


def prep_l0f_ffn(inputs, c, li):
    vecs, w1p, w2p = _l0f_consts(inputs, li)
    return dict(w1p=w1p, w2p=w2p, vecs_in=vecs)


def build_combine():
    nc = bass.Bass("TRN2", target_bir_lowering=False)
    es = ExitStack()
    ctx = Ctx(nc, es)
    rg = Regs(nc, es)
    io = {}
    for nm, kind in [("xT", "ExternalInput"), ("sT", "ExternalInput"),
                     ("gT", "ExternalInput"), ("xo", "ExternalOutput")]:
        io[nm] = nc.dram_tensor(nm, [D, SH], F32, kind=kind).ap()
    emit_combine(ctx, rg, io)
    ctx.finalize()
    return nc, es


def emit_combine(ctx, rg, io):
    NT = SH // TT
    xT, sT, gT, xo = io["xT"], io["sT"], io["gT"], io["xo"]
    xb_ = rg.sb("cb_x", [128, TT])
    sb_ = rg.sb("cb_s", [128, TT])
    gb_ = rg.sb("cb_g", [128, TT])
    sg_ = rg.sb("cb_sg", [128, TT])
    ot_ = rg.sb("cb_o", [128, TT])
    for it in range(NT * NCH):
        c, ti = it % NCH, it // NCH
        t0 = ti * TT
        r = slice(c * 128, (c + 1) * 128)
        ctx.dma("sync", xb_[:, :], xT[r, t0:t0 + TT])
        ctx.dma("sync", sb_[:, :], sT[r, t0:t0 + TT])
        ctx.dma("sync", gb_[:, :], gT[r, t0:t0 + TT])
        ctx.op("scalar", lambda e: e.activation(sg_[:, :], gb_[:, :], AF.Silu),
               reads=[gb_[:, :]], writes=[sg_[:, :]])
        ctx.op("vector", lambda e: e.tensor_mul(sg_[:, :], sg_[:, :], sb_[:, :]),
               reads=[sg_[:, :], sb_[:, :]], writes=[sg_[:, :]])
        ctx.op("vector", lambda e: e.tensor_add(ot_[:, :], sg_[:, :], xb_[:, :]),
               reads=[sg_[:, :], xb_[:, :]], writes=[ot_[:, :]])
        ctx.dma("scalar", xo[r, t0:t0 + TT], ot_[:, :])
    ctx.barrier()


def build_ln(ncols):
    nc = bass.Bass("TRN2", target_bir_lowering=False)
    es = ExitStack()
    ctx = Ctx(nc, es)
    rg = Regs(nc, es)
    io = {}
    io["xT"] = nc.dram_tensor("xT", [D, ncols], F32, kind="ExternalInput").ap()
    io["xnf"] = nc.dram_tensor("xnf", [D, ncols], F32, kind="ExternalOutput").ap()
    io["xnb"] = nc.dram_tensor("xnb", [D, ncols], BF16, kind="ExternalOutput").ap()
    emit_setup(ctx, rg)
    emit_ln_pass(ctx, rg, io, ncols)
    ctx.finalize()
    return nc, es


def emit_ln_pass(ctx, rg, io, ncols):
    nt = ncols // TT
    xT, xnf, xnb = io["xT"], io["xnf"], io["xnb"]
    st = make_ln_state(rg, "gln", TT, [0, 1, 2, 3])
    xs = rg.sb("gln_x", [128, NCH * TT])
    of = rg.sb("gln_of", [128, NCH * TT])
    ob = rg.sb("gln_ob", [128, NCH * TT], BF16)
    for it in range(nt):
        t0 = it * TT
        x_aps = []
        for c in range(NCH):
            ap = xs[:, c * TT:(c + 1) * TT]
            ctx.dma("sync", ap, xT[c * 128:(c + 1) * 128, t0:t0 + TT])
            x_aps.append(ap)
        of_aps = [of[:, c * TT:(c + 1) * TT] for c in range(NCH)]
        ob_aps = [ob[:, c * TT:(c + 1) * TT] for c in range(NCH)]
        emit_ln_tile(ctx, rg, st, TT, x_aps, of_aps, ob_aps)
        for c in range(NCH):
            ctx.dma("scalar", xnf[c * 128:(c + 1) * 128, t0:t0 + TT], of_aps[c])
            ctx.dma("gpsimd", xnb[c * 128:(c + 1) * 128, t0:t0 + TT], ob_aps[c])
    ctx.barrier()


NV1 = 84


def build_l1_pre():
    nc = bass.Bass("TRN2", target_bir_lowering=False)
    es = ExitStack()
    ctx = Ctx(nc, es)
    rg = Regs(nc, es)
    NI2 = 16
    io = {}
    for nm, shp, dt_, kind in [("xnw", [D, 1024 + SH], F32, "ExternalInput"),
                               ("w_upT", [D, 4096], F32, "ExternalInput"),
                               ("wqd", [NI2, 128, 128], F32, "ExternalInput"),
                               ("wkd", [NI2, 128, 128], F32, "ExternalInput"),
                               ("wvd", [NI2, 128, 128], F32, "ExternalInput"),
                               ("wigT", [6144, 4], F32, "ExternalInput"),
                               ("wfgT", [6144, 4], F32, "ExternalInput"),
                               ("vecs_in", [128, NV1], F32, "ExternalInput"),
                               ("xc_o", [2048, SH], BF16, "ExternalOutput"),
                               ("z_o", [2048, SH], BF16, "ExternalOutput"),
                               ("qf_o", [2048, SH], BF16, "ExternalOutput"),
                               ("kf_o", [2048, SH], BF16, "ExternalOutput"),
                               ("vtok_o", [SH, 2048], BF16, "ExternalOutput"),
                               ("ktok_o", [SH, 2048], BF16, "ExternalOutput"),
                               ("arow_o", [4, SH], F32, "ExternalOutput"),
                               ("mrow_o", [4, SH], F32, "ExternalOutput"),
                               ("lcrow_o", [4, SH], F32, "ExternalOutput"),
                               ("gscal_o", [4, 8], F32, "ExternalOutput"),
                               ("Mc_o", [2048, 512], F32, "ExternalOutput"),
                               ("ncv_o", [128, 16], F32, "ExternalOutput")]:
        io[nm] = nc.dram_tensor(nm, shp, dt_, kind=kind).ap()
    emit_setup(ctx, rg)
    emit_l1_pre(ctx, rg, io)
    ctx.finalize()
    return nc, es


def emit_l1_pre(ctx, rg, io):
    NT = SH // TT
    NI2 = 16
    xnw, w_upT = io["xnw"], io["w_upT"]
    wqd, wkd, wvd, wigT, wfgT = io["wqd"], io["wkd"], io["wvd"], io["wigT"], io["wfgT"]
    vecs_d = io["vecs_in"]
    xc_o, z_o, qf_o, kf_o = io["xc_o"], io["z_o"], io["qf_o"], io["kf_o"]
    vtok_o, ktok_o = io["vtok_o"], io["ktok_o"]
    arow_o, mrow_o, lcrow_o, gscal_o = io["arow_o"], io["mrow_o"], io["lcrow_o"], io["gscal_o"]
    Mc_o, ncv_o = io["Mc_o"], io["ncv_o"]
    emit_setup(ctx, rg)
    vecs = rg.sb("vecs", [128, NV1])
    ctx.dma("sync", vecs[:, :], vecs_d)
    wup_sb = rg.sb("wup_sb", [128, NCH * 4096], BF16)
    for c in range(NCH):
        ctx.dma("gpsimd", wup_sb[:, c * 4096:(c + 1) * 4096], w_upT[c * 128:(c + 1) * 128, :])
    wq_sb = rg.sb("l1_wq", [128, NI2 * 128], BF16)
    wk_sb = rg.sb("l1_wk", [128, NI2 * 128], BF16)
    wv_sb = rg.sb("l1_wv", [128, NI2 * 128], BF16)
    for c in range(NI2):
        ctx.dma("gpsimd", wq_sb[:, c * 128:(c + 1) * 128], wqd[c, :, :])
        ctx.dma("gpsimd", wk_sb[:, c * 128:(c + 1) * 128], wkd[c, :, :])
        ctx.dma("gpsimd", wv_sb[:, c * 128:(c + 1) * 128], wvd[c, :, :])
    wig_sb = rg.sb("l1_wig", [128, 48 * 4], BF16)
    wfg_sb = rg.sb("l1_wfg", [128, 48 * 4], BF16)
    for c in range(48):
        ctx.dma("gpsimd", wig_sb[:, c * 4:(c + 1) * 4], wigT[c * 128:(c + 1) * 128, :])
        ctx.dma("gpsimd", wfg_sb[:, c * 4:(c + 1) * 4], wfgT[c * 128:(c + 1) * 128, :])
    idf = rg.sb("idf", [128, 128])
    onesq = rg.sb("onesq", [128, 128])
    ctx.op("vector", lambda e: e.memset(onesq[:, :], 1.0), writes=[onesq[:, :]])
    ctx.op("gpsimd", lambda e: e.affine_select(idf[:, :], onesq[:, :], [[-1, 128]],
                                               ALU.is_equal, 0.0, channel_multiplier=1),
           reads=[onesq[:, :]], writes=[idf[:, :]])

    xs = rg.sb("l1_xs", [128, NCH * TT], BF16)
    xh3 = rg.sb("l1_xh3", [128, NCH * 3], BF16)
    xm = rg.sb("l1_xm", [128, NI2 * (TT + 3)], BF16)
    ev1 = rg.sb("l1_ev1", [128, TT])
    ev2 = rg.sb("l1_ev2", [128, TT])
    xcb = rg.sb("l1_xcb", [128, NI2 * TT], BF16)
    qkvb = rg.sb("l1_qkvb", [128, 3 * TT], BF16)
    evb = rg.sb("l1_evb", [128, TT], BF16)
    tokb = rg.sb("l1_tokb", [128, 128], BF16)
    grows = rg.sb("l1_grows", [4, 6 * TT])
    ones4t = rg.sb("l1_ones4t", [4, TT])
    carry = rg.sb("l1_carry", [4, 4])
    gs = rg.sb("l1_gs", [4, 8])
    ones_col = rg.sb("ones_col", [128, 1])
    G = lambda i: grows[0:4, i * TT:(i + 1) * TT]
    ctx.op("vector", lambda e: e.memset(ones4t[0:4, :], 1.0), writes=[ones4t[0:4, :]])
    ctx.op("vector", lambda e: e.memset(carry[0:4, :], 0.0), writes=[carry[0:4, :]])
    ctx.op("vector", lambda e: e.memset(gs[0:4, :], 0.0), writes=[gs[0:4, :]])
    ctx.op("vector", lambda e: e.tensor_scalar_add(carry[0:4, 1:2], carry[0:4, 1:2], -3e38),
           reads=[carry[0:4, 1:2]], writes=[carry[0:4, 1:2]])

    for it in range(NT):
        t0 = it * TT
        for c in range(NCH):
            ctx.dma("gpsimd", xs[:, c * TT:(c + 1) * TT],
                    xnw[c * 128:(c + 1) * 128, 1024 + t0:1024 + t0 + TT])
        if it == 0:
            for c in range(NCH):
                ctx.dma("gpsimd", xh3[:, c * 3:(c + 1) * 3],
                        xnw[c * 128:(c + 1) * 128, 1021:1024])
            for oc in range(NI2):
                bank = rg.banks[0][:, 0:3]
                for c in range(NCH):
                    ctx.mm(bank, wup_sb[:, c * 4096 + oc * 128: c * 4096 + (oc + 1) * 128],
                           xh3[:, c * 3:(c + 1) * 3], c == 0, c == NCH - 1)
                ctx.op("scalar", lambda e, bank=bank, oc=oc: e.activation(
                    xm[:, oc * (TT + 3): oc * (TT + 3) + 3], bank, AF.Copy),
                    reads=[bank], writes=[xm[:, oc * (TT + 3): oc * (TT + 3) + 3]])
        else:
            for oc in range(NI2):
                srcv = xm[:, oc * (TT + 3) + TT: (oc + 1) * (TT + 3)]
                dstv = xm[:, oc * (TT + 3): oc * (TT + 3) + 3]
                ctx.op("gpsimd", lambda e, srcv=srcv, dstv=dstv: e.tensor_copy(dstv, srcv),
                       reads=[srcv], writes=[dstv])
        for oc in range(NI2):
            bank = rg.banks[oc % 2][:, 0:TT]
            for c in range(NCH):
                ctx.mm(bank, wup_sb[:, c * 4096 + oc * 128: c * 4096 + (oc + 1) * 128],
                       xs[:, c * TT:(c + 1) * TT], c == 0, c == NCH - 1)
            dstm = xm[:, oc * (TT + 3) + 3:(oc + 1) * (TT + 3)]
            ctx.op("scalar", lambda e, bank=bank, dstm=dstm: e.activation(dstm, bank, AF.Copy),
                   reads=[bank], writes=[dstm])
        for oc in range(NI2):
            bank = rg.banks[oc % 2][:, 0:TT]
            for c in range(NCH):
                ctx.mm(bank, wup_sb[:, c * 4096 + 2048 + oc * 128: c * 4096 + 2048 + (oc + 1) * 128],
                       xs[:, c * TT:(c + 1) * TT], c == 0, c == NCH - 1)
            ctx.op("scalar", lambda e, bank=bank: e.activation(evb[:, :], bank, AF.Copy),
                   reads=[bank], writes=[evb[:, :]])
            ctx.dma("scalar", z_o[oc * 128:(oc + 1) * 128, t0:t0 + TT], evb[:, :])
        for oc in range(NI2):
            xm_c = xm[:, oc * (TT + 3):(oc + 1) * (TT + 3)]
            ctx.op("vector", lambda e, xm_c=xm_c, oc=oc: e.tensor_scalar_mul(
                ev1[:, :], xm_c[:, 0:TT], vecs[:, oc * 4:oc * 4 + 1]),
                reads=[xm_c[:, 0:TT], vecs[:, oc * 4:oc * 4 + 1]], writes=[ev1[:, :]])
            for j in range(1, 4):
                ctx.op("vector", lambda e, xm_c=xm_c, oc=oc, j=j: e.tensor_scalar_mul(
                    ev2[:, :], xm_c[:, j:j + TT], vecs[:, oc * 4 + j:oc * 4 + j + 1]),
                    reads=[xm_c[:, j:j + TT], vecs[:, oc * 4 + j:oc * 4 + j + 1]], writes=[ev2[:, :]])
                ctx.op("vector", lambda e: e.tensor_add(ev1[:, :], ev1[:, :], ev2[:, :]),
                       reads=[ev1[:, :], ev2[:, :]], writes=[ev1[:, :]])
            ctx.op("vector", lambda e, oc=oc: e.tensor_scalar_add(ev1[:, :], ev1[:, :],
                                                                  vecs[:, 64 + oc:64 + oc + 1]),
                   reads=[ev1[:, :], vecs[:, 64 + oc:64 + oc + 1]], writes=[ev1[:, :]])
            dst = xcb[:, oc * TT:(oc + 1) * TT]
            ctx.op("scalar", lambda e, dst=dst: e.activation(dst, ev1[:, :], AF.Silu),
                   reads=[ev1[:, :]], writes=[dst])
            ctx.dma("scalar", xc_o[oc * 128:(oc + 1) * 128, t0:t0 + TT], dst)
        big = rg.banks[7][0:4, 0:TT]
        bfg = rg.banks[3][0:4, 0:TT]
        for oc in range(NI2):
            bq = rg.banks[4][:, 0:TT]
            bk = rg.banks[5][:, 0:TT]
            bv = rg.banks[6][:, 0:TT]
            xm_body = xm[:, oc * (TT + 3) + 3:(oc + 1) * (TT + 3)]
            ctx.mm(bq, wq_sb[:, oc * 128:(oc + 1) * 128], xcb[:, oc * TT:(oc + 1) * TT], True, True)
            ctx.mm(bk, wk_sb[:, oc * 128:(oc + 1) * 128], xcb[:, oc * TT:(oc + 1) * TT], True, True)
            ctx.mm(bv, wv_sb[:, oc * 128:(oc + 1) * 128], xm_body, True, True)
            for bank, qo in ((bq, 0), (bk, 1), (bv, 2)):
                dstb = qkvb[:, qo * TT:(qo + 1) * TT]
                ctx.op("scalar", lambda e, bank=bank, dstb=dstb: e.activation(dstb, bank, AF.Copy),
                       reads=[bank], writes=[dstb])
            ctx.dma("scalar", qf_o[oc * 128:(oc + 1) * 128, t0:t0 + TT], qkvb[:, 0:TT])
            ctx.dma("scalar", kf_o[oc * 128:(oc + 1) * 128, t0:t0 + TT], qkvb[:, TT:2 * TT])
            for qo, base in ((0, 0), (1, NI2), (2, 2 * NI2)):
                idx = base + oc
                st_ = (oc == 0 and qo == 0)
                sp_ = (oc == NI2 - 1 and qo == 2)
                ctx.mm(big, wig_sb[:, idx * 4:(idx + 1) * 4], qkvb[:, qo * TT:(qo + 1) * TT], st_, sp_)
                ctx.mm(bfg, wfg_sb[:, idx * 4:(idx + 1) * 4], qkvb[:, qo * TT:(qo + 1) * TT], st_, sp_)
            for tc in range(TT // 128):
                bank = rg.banks[(tc + oc) % 2][:, 0:128]
                ctx.mm(bank, xcb[:, oc * TT + tc * 128: oc * TT + (tc + 1) * 128],
                       wk_sb[:, oc * 128:(oc + 1) * 128], True, True)
                ctx.op("scalar", lambda e, bank=bank: e.activation(tokb[:, :], bank, AF.Copy),
                       reads=[bank], writes=[tokb[:, :]])
                ctx.dma("scalar", ktok_o[t0 + tc * 128:t0 + (tc + 1) * 128, oc * 128:(oc + 1) * 128],
                        tokb[:, :])
                bank2 = rg.banks[2][:, 0:128]
                xm_sl = xm[:, oc * (TT + 3) + 3 + tc * 128: oc * (TT + 3) + 3 + (tc + 1) * 128]
                ctx.mm(bank2, xm_sl, wv_sb[:, oc * 128:(oc + 1) * 128], True, True)
                ctx.op("scalar", lambda e, bank2=bank2: e.activation(tokb[:, :], bank2, AF.Copy),
                       reads=[bank2], writes=[tokb[:, :]])
                ctx.dma("scalar", vtok_o[t0 + tc * 128:t0 + (tc + 1) * 128, oc * 128:(oc + 1) * 128],
                        tokb[:, :])
        ctx.op("scalar", lambda e, big=big: e.activation(G(0), big, AF.Copy),
               reads=[big], writes=[G(0)])
        ctx.op("scalar", lambda e, bfg=bfg: e.activation(G(1), bfg, AF.Copy),
               reads=[bfg], writes=[G(1)])
        ctx.op("vector", lambda e: e.tensor_scalar_add(G(0), G(0), vecs[0:4, 80:81]),
               reads=[G(0), vecs[0:4, 80:81]], writes=[G(0)])
        ctx.op("vector", lambda e: e.tensor_scalar_add(G(1), G(1), vecs[0:4, 81:82]),
               reads=[G(1), vecs[0:4, 81:82]], writes=[G(1)])
        ctx.op("scalar", lambda e: e.activation(G(2), G(1), AF.Abs), reads=[G(1)], writes=[G(2)])
        ctx.op("scalar", lambda e: e.activation(G(2), G(2), AF.Exp, scale=-1.0),
               reads=[G(2)], writes=[G(2)])
        ctx.op("scalar", lambda e: e.activation(G(2), G(2), AF.Ln, bias=ones_col[0:4, 0:1]),
               reads=[G(2), ones_col[0:4, 0:1]], writes=[G(2)])
        ctx.op("vector", lambda e: e.tensor_scalar_min(G(1), G(1), 0.0),
               reads=[G(1)], writes=[G(1)])
        ctx.op("vector", lambda e: e.tensor_sub(G(2), G(1), G(2)),
               reads=[G(1), G(2)], writes=[G(2)])
        ctx.op("vector", lambda e: e.tensor_tensor_scan(G(3), ones4t[0:4, :], G(2),
                                                        carry[0:4, 0:1], ALU.mult, ALU.add),
               reads=[ones4t[0:4, :], G(2), carry[0:4, 0:1]], writes=[G(3)])
        ctx.op("vector", lambda e: e.tensor_sub(G(4), G(0), G(3)),
               reads=[G(0), G(3)], writes=[G(4)])
        ctx.op("vector", lambda e: e.tensor_tensor_scan(G(5), ones4t[0:4, :], G(4),
                                                        carry[0:4, 1:2], ALU.mult, ALU.max),
               reads=[ones4t[0:4, :], G(4), carry[0:4, 1:2]], writes=[G(5)])
        ctx.op("gpsimd", lambda e: e.tensor_copy(carry[0:4, 0:1], grows[0:4, 4 * TT - 1:4 * TT]),
               reads=[grows[0:4, 4 * TT - 1:4 * TT]], writes=[carry[0:4, 0:1]])
        ctx.op("gpsimd", lambda e: e.tensor_copy(carry[0:4, 1:2], grows[0:4, 6 * TT - 1:6 * TT]),
               reads=[grows[0:4, 6 * TT - 1:6 * TT]], writes=[carry[0:4, 1:2]])
        ctx.dma("scalar", lcrow_o[0:4, t0:t0 + TT], G(3))
        ctx.dma("scalar", arow_o[0:4, t0:t0 + TT], G(4))
        ctx.dma("scalar", mrow_o[0:4, t0:t0 + TT], G(5))
    ctx.op("gpsimd", lambda e: e.tensor_copy(gs[0:4, 0:1], carry[0:4, 0:1]),
           reads=[carry[0:4, 0:1]], writes=[gs[0:4, 0:1]])
    ctx.op("gpsimd", lambda e: e.tensor_copy(gs[0:4, 1:2], carry[0:4, 1:2]),
           reads=[carry[0:4, 1:2]], writes=[gs[0:4, 1:2]])
    ctx.dma("scalar", gscal_o[0:4, :], gs[0:4, :])
    ctx.op("vector", lambda e: e.tensor_scalar_mul(carry[0:4, 2:3], carry[0:4, 1:2], -1.0),
           reads=[carry[0:4, 1:2]], writes=[carry[0:4, 2:3]])
    idf = rg.sb("idf", [128, 128])
    onesq = rg.sb("onesq", [128, 128])
    ctx.op("vector", lambda e: e.memset(onesq[:, :], 1.0), writes=[onesq[:, :]])
    ctx.op("gpsimd", lambda e: e.affine_select(idf[:, :], onesq[:, :], [[-1, 128]],
                                               ALU.is_equal, 0.0, channel_multiplier=1),
           reads=[onesq[:, :]], writes=[idf[:, :]])
    au = rg.sb("l1_au", [4, 128])
    kt_sb = rg.sb("l1_kt", [128, 512], BF16)
    vt_sb = rg.sb("l1_vt", [128, 512], BF16)
    ku_sb = rg.sb("l1_ku", [128, 512], BF16)
    mc_ev = rg.sb("l1_mcev", [128, 512])
    ncrow = rg.sb("l1_ncrow", [1, 512])
    ncv_sb = rg.sb("l1_ncv", [128, 16])
    ucol1 = rg.sb("l1_ucol1", [128, 4])
    onesb = rg.sb("ones_col_b", [128, 1], BF16)
    for h in range(4):
        for tc in range(SH // 128):
            ctx.dma("sync", kt_sb[:, :], ktok_o[tc * 128:(tc + 1) * 128, h * 512:(h + 1) * 512])
            ctx.dma("sync", vt_sb[:, :], vtok_o[tc * 128:(tc + 1) * 128, h * 512:(h + 1) * 512])
            ctx.dma("sync", au[0:4, 0:128], arow_o[0:4, tc * 128:(tc + 1) * 128])
            ctx.op("scalar", lambda e: e.activation(au[0:4, :], au[0:4, :], AF.Exp,
                                                    bias=carry[0:4, 2:3]),
                   reads=[au[0:4, :], carry[0:4, 2:3]], writes=[au[0:4, :]])
            bankt = rg.banks[5][:, 0:4]
            ctx.op("tensor", lambda e, bankt=bankt: e.transpose(bankt, au[0:4, :], idf[0:4, 0:4]),
                   reads=[au[0:4, :], idf[0:4, 0:4]], writes=[bankt])
            ctx.op("scalar", lambda e, bankt=bankt: e.activation(ucol1[:, :], bankt, AF.Copy),
                   reads=[bankt], writes=[ucol1[:, :]])
            ctx.op("vector", lambda e, h=h: e.tensor_scalar_mul(ku_sb[:, :], kt_sb[:, :],
                                                                ucol1[:, h:h + 1]),
                   reads=[kt_sb[:, :], ucol1[:, h:h + 1]], writes=[ku_sb[:, :]])
            for dc in range(4):
                ctx.mm(rg.banks[dc][:, 0:512], ku_sb[:, dc * 128:(dc + 1) * 128], vt_sb[:, :],
                       tc == 0, tc == SH // 128 - 1)
            ctx.mm(rg.banks[4][0:1, 0:512], onesb[:, :], ku_sb[:, :], tc == 0, tc == SH // 128 - 1)
        for dc in range(4):
            ctx.op("scalar", lambda e, dc=dc: e.activation(mc_ev[:, :], rg.banks[dc][:, 0:512], AF.Copy),
                   reads=[rg.banks[dc][:, 0:512]], writes=[mc_ev[:, :]])
            ctx.dma("scalar", Mc_o[h * 512 + dc * 128: h * 512 + (dc + 1) * 128, :], mc_ev[:, :])
        ctx.op("scalar", lambda e: e.activation(ncrow[0:1, :], rg.banks[4][0:1, 0:512], AF.Copy),
               reads=[rg.banks[4][0:1, 0:512]], writes=[ncrow[0:1, :]])
        for dc in range(4):
            bank = rg.banks[5][:, 0:1]
            ctx.op("tensor", lambda e, dc=dc, bank=bank: e.transpose(
                bank, ncrow[0:1, dc * 128:(dc + 1) * 128], idf[0:1, 0:1]),
                reads=[ncrow[0:1, dc * 128:(dc + 1) * 128], idf[0:1, 0:1]], writes=[bank])
            ctx.op("scalar", lambda e, h=h, dc=dc, bank=bank: e.activation(
                ncv_sb[:, h * 4 + dc:h * 4 + dc + 1], bank, AF.Copy),
                reads=[bank], writes=[ncv_sb[:, h * 4 + dc:h * 4 + dc + 1]])
    ctx.dma("scalar", ncv_o[:, :], ncv_sb[:, :])
    ctx.barrier()


NV2 = 32


def build_l1_mlstm():
    nc = bass.Bass("TRN2", target_bir_lowering=False)
    es = ExitStack()
    ctx = Ctx(nc, es)
    rg = Regs(nc, es)
    io = {}
    for nm, shp, dt_, kind in [("qf_i", [2048, SH], BF16, "ExternalInput"),
                               ("kf_i", [2048, SH], BF16, "ExternalInput"),
                               ("vtok_i", [SH, 2048], BF16, "ExternalInput"),
                               ("xc_i", [2048, SH], BF16, "ExternalInput"),
                               ("z_i", [2048, SH], BF16, "ExternalInput"),
                               ("arow_i", [4, SH], F32, "ExternalInput"),
                               ("mrow_i", [4, SH], F32, "ExternalInput"),
                               ("lcrow_i", [4, SH], F32, "ExternalInput"),
                               ("gscal_i", [4, 8], F32, "ExternalInput"),
                               ("Mc_p", [2048, 512], F32, "ExternalInput"),
                               ("ncv_p", [128, 16], F32, "ExternalInput"),
                               ("pscal", [4, 8], F32, "ExternalInput"),
                               ("w_downT", [2048, 1024], F32, "ExternalInput"),
                               ("vecs_in", [128, NV2], F32, "ExternalInput"),
                               ("sel4_in", [4, 512], F32, "ExternalInput"),
                               ("mdiag_in", [4, 128, TT], F32, "ExternalInput"),
                               ("s1_o", [D, SH], F32, "ExternalOutput")]:
        io[nm] = nc.dram_tensor(nm, shp, dt_, kind=kind).ap()
    emit_setup(ctx, rg)
    emit_l1_mlstm(ctx, rg, io)
    ctx.finalize()
    return nc, es


def emit_l1_mlstm(ctx, rg, io):
    NT = SH // TT
    NI2 = 16
    qf, kf, vtok = io["qf_i"], io["kf_i"], io["vtok_i"]
    xc_i, z_i = io["xc_i"], io["z_i"]
    arow, mrow, lcrow, gscal = io["arow_i"], io["mrow_i"], io["lcrow_i"], io["gscal_i"]
    Mc_p, ncv_p, pscal = io["Mc_p"], io["ncv_p"], io["pscal"]
    w_downT, vecs_d, sel4_d, mdiag_d = io["w_downT"], io["vecs_in"], io["sel4_in"], io["mdiag_in"]
    s1_o = io["s1_o"]
    emit_setup(ctx, rg)
    vecs = rg.sb("vecs", [128, NV2])
    sel4 = rg.sb("sel4", [4, 512])
    mdiag = rg.sb("mdiag", [128, 4 * TT], BF16)
    ctx.dma("sync", vecs[:, :], vecs_d)
    ctx.dma("sync", sel4[0:4, :], sel4_d)
    for i in range(4):
        ctx.dma("gpsimd", mdiag[:, i * TT:(i + 1) * TT], mdiag_d[i, :, :])
    idf = rg.sb("idf", [128, 128])
    onesq = rg.sb("onesq", [128, 128])
    ctx.op("vector", lambda e: e.memset(onesq[:, :], 1.0), writes=[onesq[:, :]])
    ctx.op("gpsimd", lambda e: e.affine_select(idf[:, :], onesq[:, :], [[-1, 128]],
                                               ALU.is_equal, 0.0, channel_multiplier=1),
           reads=[onesq[:, :]], writes=[idf[:, :]])
    wd_sb = rg.sb("wd_sb", [128, NI2 * 1024], BF16)
    for c in range(NI2):
        ctx.dma("gpsimd", wd_sb[:, c * 1024:(c + 1) * 1024], w_downT[c * 128:(c + 1) * 128, :])
    ncb = rg.sb("ncb", [128, 16], BF16)
    ctx.dma("gpsimd", ncb[:, :], ncv_p)

    rows_big = rg.sb("m_rows_big", [68, SH])
    a_sb = rows_big[0:4, :]
    m_sb = rows_big[32:36, :]
    lc_sb = rows_big[64:68, :]
    gsc = rg.sb("m_gsc", [4, 8])
    psc = rg.sb("m_psc", [4, 8])
    ctx.dma("sync", a_sb[0:4, :], arow)
    ctx.dma("sync", m_sb[0:4, :], mrow)
    ctx.dma("sync", lc_sb[0:4, :], lcrow)
    ctx.dma("sync", gsc[0:4, :], gscal)
    ctx.dma("sync", psc[0:4, :], pscal)

    sc2 = rg.sb("m_sc2", [4, 16])
    rows4 = rg.sb("m_rows4", [4, 4 * TT])
    R4 = lambda i: rows4[0:4, i * TT:(i + 1) * TT]
    ucols = rg.sb("m_ucols", [128, (SH // 128) * 4])
    qh_sb = rg.sb("m_qh", [128, 4 * TT], BF16)
    qsp = rg.sb("m_qsp", [128, 4 * TT], BF16)
    khc = rg.sb("m_khc", [128, 2 * 4 * 128], BF16)
    vt_sb = rg.sb("m_vt", [128, 1024], BF16)
    mc_sb = rg.sb("m_mc", [128, 4 * 4 * 512], BF16)
    for h4 in range(4):
        for dc in range(4):
            ctx.dma("gpsimd", mc_sb[:, h4 * 2048 + dc * 512:h4 * 2048 + (dc + 1) * 512],
                    Mc_p[h4 * 512 + dc * 128: h4 * 512 + (dc + 1) * 128, :])
    pb_sb = rg.sb("m_pb", [128, 2 * TT], BF16)
    sob = rg.sb("m_sob", [128, 2])
    hn_f = rg.sb("m_hnf", [128, 4 * TT])
    rowb = rg.sb("m_rowb", [1, 6 * TT])
    hst = rg.sb("m_hst", [128, NI2 * TT], BF16)
    xcz = rg.sb("m_xcz", [128, 2 * NI2 * TT], BF16)
    ev1 = rg.sb("m_ev1", [128, TT])
    ysum = rg.sb("m_ysum", [128, TT])
    ones_col = rg.sb("ones_col", [128, 1])
    ones_row = rg.sb("ones_row", [1, 128])
    onesb = rg.sb("ones_col_b", [128, 1], BF16)
    eps5 = rg.sb("eps5", [1, 1])
    R8 = lambda i: rowb[0:1, i * TT:(i + 1) * TT]

    for qt in range(NT):
        t0 = qt * TT
        qend = t0 + TT
        nkc = qend // 128
        for c in range(NI2):
            ctx.dma("sync", xcz[:, c * TT:(c + 1) * TT], xc_i[c * 128:(c + 1) * 128, t0:t0 + TT])
            ctx.dma("sync", xcz[:, (NI2 + c) * TT:(NI2 + c + 1) * TT],
                    z_i[c * 128:(c + 1) * 128, t0:t0 + TT])
        cqc = sc2[0:4, 0:1]
        ctx.op("vector", lambda e, qend=qend: e.tensor_scalar(
            cqc, m_sb[0:4, qend - 1:qend], psc[0:4, 0:1], psc[0:4, 1:2], ALU.subtract, ALU.max),
            reads=[m_sb[0:4, qend - 1:qend], psc[0:4, 0:2]], writes=[cqc])
        ctx.op("vector", lambda e: e.tensor_scalar_mul(sc2[0:4, 1:2], cqc, -1.0),
               reads=[cqc], writes=[sc2[0:4, 1:2]])
        ctx.op("vector", lambda e: e.tensor_add(sc2[0:4, 4:5], cqc, psc[0:4, 0:1]),
               reads=[cqc, psc[0:4, 0:1]], writes=[sc2[0:4, 4:5]])
        ctx.op("vector", lambda e: e.tensor_scalar_mul(sc2[0:4, 4:5], sc2[0:4, 4:5], -1.0),
               reads=[sc2[0:4, 4:5]], writes=[sc2[0:4, 4:5]])
        for tc in range(nkc):
            ctx.op("scalar", lambda e, tc=tc: e.activation(R4(3)[0:4, 0:128],
                                                           a_sb[0:4, tc * 128:(tc + 1) * 128],
                                                           AF.Exp, bias=sc2[0:4, 4:5]),
                   reads=[a_sb[0:4, tc * 128:(tc + 1) * 128], sc2[0:4, 4:5]],
                   writes=[R4(3)[0:4, 0:128]])
            ctx.op("vector", lambda e: e.tensor_scalar_mul(R4(3)[0:4, 0:128], R4(3)[0:4, 0:128],
                                                           1.0 / math.sqrt(512.0)),
                   reads=[R4(3)[0:4, 0:128]], writes=[R4(3)[0:4, 0:128]])
            bankt = rg.banks[7][:, 0:4]
            ctx.op("tensor", lambda e, bankt=bankt: e.transpose(bankt, R4(3)[0:4, 0:128], idf[0:4, 0:4]),
                   reads=[R4(3)[0:4, 0:128], idf[0:4, 0:4]], writes=[bankt])
            ctx.op("scalar", lambda e, tc=tc, bankt=bankt: e.activation(
                ucols[:, tc * 4:(tc + 1) * 4], bankt, AF.Copy),
                reads=[bankt], writes=[ucols[:, tc * 4:(tc + 1) * 4]])
        ctx.op("scalar", lambda e: e.activation(sc2[0:4, 3:4], cqc, AF.Exp,
                                                scale=-1.0, bias=psc[0:4, 1:2]),
               reads=[cqc, psc[0:4, 1:2]], writes=[sc2[0:4, 3:4]])
        ctx.op("vector", lambda e: e.tensor_scalar_mul(sc2[0:4, 3:4], sc2[0:4, 3:4],
                                                       1.0 / math.sqrt(512.0)),
               reads=[sc2[0:4, 3:4]], writes=[sc2[0:4, 3:4]])
        ctx.op("vector", lambda e, t0=t0: e.tensor_scalar(
            R4(0), m_sb[0:4, t0:t0 + TT], psc[0:4, 0:1], psc[0:4, 1:2], ALU.subtract, ALU.max),
            reads=[m_sb[0:4, t0:t0 + TT], psc[0:4, 0:2]], writes=[R4(0)])
        ctx.op("scalar", lambda e: e.activation(R4(1), R4(0), AF.Exp, bias=sc2[0:4, 1:2]),
               reads=[R4(0), sc2[0:4, 1:2]], writes=[R4(1)])
        ctx.op("vector", lambda e, t0=t0: e.tensor_scalar_add(R4(2), lc_sb[0:4, t0:t0 + TT],
                                                              psc[0:4, 0:1]),
               reads=[lc_sb[0:4, t0:t0 + TT], psc[0:4, 0:1]], writes=[R4(2)])
        ctx.op("scalar", lambda e: e.activation(R4(2), R4(2), AF.Exp, scale=-1.0, bias=sc2[0:4, 1:2]),
               reads=[R4(2), sc2[0:4, 1:2]], writes=[R4(2)])
        for h in range(4):
            for dc in range(4):
                ctx.dma("sync", qh_sb[:, dc * TT:(dc + 1) * TT],
                        qf[h * 512 + dc * 128: h * 512 + (dc + 1) * 128, t0:t0 + TT])
            bsel = rg.banks[7][:, 0:1]
            ctx.mm(bsel, sel4[0:4, h * 128:(h + 1) * 128], sc2[0:4, 3:4], True, True)
            ctx.op("scalar", lambda e, bsel=bsel: e.activation(sob[:, 0:1], bsel, AF.Copy),
                   reads=[bsel], writes=[sob[:, 0:1]])
            for dc in range(4):
                ctx.op("vector", lambda e, dc=dc: e.tensor_scalar_mul(
                    qsp[:, dc * TT:(dc + 1) * TT], qh_sb[:, dc * TT:(dc + 1) * TT], sob[:, 0:1]),
                    reads=[qh_sb[:, dc * TT:(dc + 1) * TT], sob[:, 0:1]],
                    writes=[qsp[:, dc * TT:(dc + 1) * TT]])
            def msums(kc):
                pb = pb_sb[:, (kc % 2) * TT:(kc % 2 + 1) * TT]
                vts = vt_sb[:, (kc % 2) * 512:(kc % 2) * 512 + 512]
                ctx.mm(rg.banks[4][0:1, 0:TT], onesb[:, :], pb, kc == 0, False)
                ctx.dma("sync", vts, vtok[kc * 128:(kc + 1) * 128, h * 512:(h + 1) * 512])
                for dcv in range(4):
                    ctx.mm(rg.banks[dcv][:, 0:TT], vts[:, dcv * 128:(dcv + 1) * 128], pb,
                           kc == 0, False)
            for kc in range(nkc):
                khs = khc[:, (kc % 2) * 512:(kc % 2) * 512 + 512]
                ctx.dma("sync", khs.rearrange("p (a c) -> p a c", a=4, c=128),
                        kf[h * 512:(h + 1) * 512, kc * 128:(kc + 1) * 128].rearrange(
                            "(a b) c -> b a c", a=4, b=128))
                sbank = rg.banks[5 + kc % 2][:, 0:TT]
                for dc in range(4):
                    ctx.mm(sbank, khs[:, dc * 128:(dc + 1) * 128],
                           qh_sb[:, dc * TT:(dc + 1) * TT], dc == 0, dc == 3)
                ucol = ucols[:, kc * 4 + h:kc * 4 + h + 1]
                pb = pb_sb[:, (kc % 2) * TT:(kc % 2 + 1) * TT]
                ctx.op("vector", lambda e, sbank=sbank, pb=pb, ucol=ucol: e.tensor_scalar_mul(
                    pb, sbank, ucol), reads=[sbank, ucol], writes=[pb])
                if kc >= nkc - 4:
                    md = mdiag[:, (kc - (nkc - 4)) * TT:(kc - (nkc - 4) + 1) * TT]
                    ctx.op("vector", lambda e, pb=pb, md=md: e.tensor_mul(pb, pb, md),
                           reads=[pb, md], writes=[pb])
                if kc >= 1:
                    msums(kc - 1)
            msums(nkc - 1)
            for dcv in range(4):
                for dq in range(4):
                    ctx.mm(rg.banks[dcv][:, 0:TT],
                           mc_sb[:, h * 2048 + dq * 512 + dcv * 128: h * 2048 + dq * 512 + (dcv + 1) * 128],
                           qsp[:, dq * TT:(dq + 1) * TT], False, dq == 3)
            for dq in range(4):
                ctx.mm(rg.banks[4][0:1, 0:TT], ncb[:, h * 4 + dq:h * 4 + dq + 1],
                       qsp[:, dq * TT:(dq + 1) * TT], False, dq == 3)
            brow = rg.banks[7][0:1, 0:TT]
            ctx.mm(brow, idf[0:4, h:h + 1], R4(1), True, True)
            ctx.op("scalar", lambda e, brow=brow: e.activation(R8(1), brow, AF.Copy),
                   reads=[brow], writes=[R8(1)])
            ctx.mm(brow, idf[0:4, h:h + 1], R4(2), True, True)
            ctx.op("scalar", lambda e, brow=brow: e.activation(R8(2), brow, AF.Copy),
                   reads=[brow], writes=[R8(2)])
            ctx.op("scalar", lambda e: e.activation(R8(3), rg.banks[4][0:1, 0:TT], AF.Abs),
                   reads=[rg.banks[4][0:1, 0:TT]], writes=[R8(3)])
            ctx.op("vector", lambda e: e.tensor_max(R8(3), R8(3), R8(2)),
                   reads=[R8(3), R8(2)], writes=[R8(3)])
            ctx.op("vector", lambda e: e.scalar_tensor_tensor(R8(3), R8(1), 1e-6, R8(3),
                                                              ALU.mult, ALU.add),
                   reads=[R8(1), R8(3)], writes=[R8(3)])
            ctx.op("vector", lambda e: e.reciprocal(R8(4), R8(3)),
                   reads=[R8(3)], writes=[R8(4)])
            bfac = rg.banks[7][:, 0:TT]
            ctx.mm(bfac, ones_row[0:1, :], R8(4), True, True)
            for dcv in range(4):
                ctx.op("scalar", lambda e, dcv=dcv: e.activation(
                    hn_f[:, dcv * TT:(dcv + 1) * TT], rg.banks[dcv][:, 0:TT], AF.Copy),
                    reads=[rg.banks[dcv][:, 0:TT]], writes=[hn_f[:, dcv * TT:(dcv + 1) * TT]])
                ctx.op("vector", lambda e, dcv=dcv: e.tensor_mul(
                    hn_f[:, dcv * TT:(dcv + 1) * TT], hn_f[:, dcv * TT:(dcv + 1) * TT], bfac),
                    reads=[hn_f[:, dcv * TT:(dcv + 1) * TT], bfac],
                    writes=[hn_f[:, dcv * TT:(dcv + 1) * TT]])
            bsum = rg.banks[4][0:1, 0:TT]
            bsq = rg.banks[5][0:1, 0:TT]
            for dcv in range(4):
                ctx.mm(bsum, ones_col[:, :], hn_f[:, dcv * TT:(dcv + 1) * TT], dcv == 0, dcv == 3)
                ctx.op("scalar", lambda e, dcv=dcv: e.activation(ev1[:, :], hn_f[:, dcv * TT:(dcv + 1) * TT],
                                                                 AF.Square),
                       reads=[hn_f[:, dcv * TT:(dcv + 1) * TT]], writes=[ev1[:, :]])
                ctx.mm(bsq, ones_col[:, :], ev1[:, :], dcv == 0, dcv == 3)
            ctx.op("vector", lambda e: e.tensor_scalar_mul(R8(1), bsum, 1.0 / 512.0),
                   reads=[bsum], writes=[R8(1)])
            ctx.op("vector", lambda e: e.tensor_scalar_mul(R8(2), bsq, 1.0 / 512.0),
                   reads=[bsq], writes=[R8(2)])
            ctx.op("vector", lambda e: e.tensor_mul(R8(3), R8(1), R8(1)),
                   reads=[R8(1)], writes=[R8(3)])
            ctx.op("vector", lambda e: e.tensor_sub(R8(2), R8(2), R8(3)),
                   reads=[R8(2), R8(3)], writes=[R8(2)])
            ctx.op("scalar", lambda e: e.activation(R8(3), R8(2), AF.Sqrt, bias=eps5[0:1, 0:1]),
                   reads=[R8(2), eps5[0:1, 0:1]], writes=[R8(3)])
            ctx.op("vector", lambda e: e.reciprocal(R8(4), R8(3)),
                   reads=[R8(3)], writes=[R8(4)])
            ctx.op("vector", lambda e: e.tensor_mul(R8(5), R8(1), R8(4)),
                   reads=[R8(1), R8(4)], writes=[R8(5)])
            brb = rg.banks[4][:, 0:TT]
            bzb = rg.banks[5][:, 0:TT]
            ctx.mm(brb, ones_row[0:1, :], R8(4), True, True)
            ctx.mm(bzb, ones_row[0:1, :], R8(5), True, True)
            for dcv in range(4):
                hslice = hn_f[:, dcv * TT:(dcv + 1) * TT]
                ctx.op("vector", lambda e, hslice=hslice: e.tensor_mul(hslice, hslice, brb),
                       reads=[hslice, brb], writes=[hslice])
                ctx.op("vector", lambda e, hslice=hslice: e.tensor_sub(hslice, hslice, bzb),
                       reads=[hslice, bzb], writes=[hslice])
                lnw = vecs[:, h * 4 + dcv:h * 4 + dcv + 1]
                ctx.op("vector", lambda e, hslice=hslice, lnw=lnw: e.tensor_scalar_mul(hslice, hslice, lnw),
                       reads=[hslice, lnw], writes=[hslice])
                ic = h * 4 + dcv
                xc_s = xcz[:, ic * TT:(ic + 1) * TT]
                z_s = xcz[:, (NI2 + ic) * TT:(NI2 + ic + 1) * TT]
                skipc = vecs[:, 16 + ic:16 + ic + 1]
                ctx.op("vector", lambda e, hslice=hslice, xc_s=xc_s, skipc=skipc: e.scalar_tensor_tensor(
                    hslice, xc_s, skipc, hslice, ALU.mult, ALU.add),
                    reads=[hslice, xc_s, skipc], writes=[hslice])
                ctx.op("scalar", lambda e, z_s=z_s: e.activation(ev1[:, :], z_s, AF.Silu),
                       reads=[z_s], writes=[ev1[:, :]])
                dsth = hst[:, ic * TT:(ic + 1) * TT]
                ctx.op("vector", lambda e, hslice=hslice, dsth=dsth: e.tensor_mul(dsth, hslice, ev1[:, :]),
                       reads=[hslice, ev1[:, :]], writes=[dsth])
        for ec in range(NCH):
            bank = rg.banks[ec % 2][:, 0:TT]
            for jc in range(NI2):
                ctx.mm(bank, wd_sb[:, jc * 1024 + ec * 128: jc * 1024 + (ec + 1) * 128],
                       hst[:, jc * TT:(jc + 1) * TT], jc == 0, jc == NI2 - 1)
            ctx.op("scalar", lambda e, bank=bank: e.activation(ysum[:, :], bank, AF.Copy),
                   reads=[bank], writes=[ysum[:, :]])
            ctx.dma("scalar", s1_o[ec * 128:(ec + 1) * 128, t0:t0 + TT], ysum[:, :])
    ctx.barrier()


def prep_l1_pre(inputs):
    NI2 = 16
    vecs = np.zeros((128, NV1), np.float32)
    cw = np.asarray(inputs["L1_attn_conv_w"], np.float32)[:, 0, :]
    cb = np.asarray(inputs["L1_attn_conv_b"], np.float32)
    for c in range(NI2):
        for j in range(4):
            vecs[:, c * 4 + j] = cw[c * 128:(c + 1) * 128, j]
        vecs[:, 64 + c] = cb[c * 128:(c + 1) * 128]
    vecs[0:4, 80] = np.asarray(inputs["L1_attn_b_ig"], np.float32)
    vecs[0:4, 81] = np.asarray(inputs["L1_attn_b_fg"], np.float32)

    def blockdiag(w):
        out = np.zeros((NI2, 128, 128), np.float32)
        for c in range(NI2):
            for a2 in range(32):
                blk = w[32 * c + a2]
                out[c, a2 * 4:(a2 + 1) * 4, a2 * 4:(a2 + 1) * 4] = blk.T
        return out
    wqd = blockdiag(np.asarray(inputs["L1_attn_w_q"], np.float32))
    wkd = blockdiag(np.asarray(inputs["L1_attn_w_k"], np.float32))
    wvd = blockdiag(np.asarray(inputs["L1_attn_w_v"], np.float32))
    return dict(
        w_upT=np.ascontiguousarray(np.asarray(inputs["L1_attn_w_up"], np.float32).T),
        wqd=wqd, wkd=wkd, wvd=wvd,
        wigT=np.ascontiguousarray(np.asarray(inputs["L1_attn_w_ig"], np.float32).T),
        wfgT=np.ascontiguousarray(np.asarray(inputs["L1_attn_w_fg"], np.float32).T),
        vecs_in=vecs)


def prep_l1_mlstm(inputs, pre_out, partner_pre_out, has_prev):
    NI2 = 16
    vecs = np.zeros((128, NV2), np.float32)
    lnw = np.asarray(inputs["L1_attn_ln_w"], np.float32)
    skip = np.asarray(inputs["L1_attn_skip"], np.float32)
    for h in range(4):
        for dc in range(4):
            vecs[:, h * 4 + dc] = lnw[h, dc * 128:(dc + 1) * 128]
    for c in range(NI2):
        vecs[:, 16 + c] = skip[c * 128:(c + 1) * 128]
    sel4 = np.zeros((4, 512), np.float32)
    for h in range(4):
        sel4[h, h * 128:(h + 1) * 128] = 1.0
    mdiag = np.zeros((4, 128, TT), np.float32)
    for p in range(4):
        k_rel = np.arange(128) + p * 128
        q_rel = np.arange(TT)
        mdiag[p] = (k_rel[:, None] <= q_rel[None, :]).astype(np.float32)
    if has_prev:
        Mc_p = partner_pre_out["Mc_o"]
        ncv_p = partner_pre_out["ncv_o"]
        pscal = np.zeros((4, 8), np.float32)
        pscal[:, 0] = partner_pre_out["gscal_o"][:, 0]
        pscal[:, 1] = partner_pre_out["gscal_o"][:, 1]
    else:
        Mc_p = np.zeros((2048, 512), np.float32)
        ncv_p = np.zeros((128, 16), np.float32)
        pscal = np.zeros((4, 8), np.float32)
        pscal[:, 1] = -1e30
    return dict(
        qf_i=pre_out["qf_o"], kf_i=pre_out["kf_o"], vtok_i=pre_out["vtok_o"],
        xc_i=pre_out["xc_o"], z_i=pre_out["z_o"],
        arow_i=pre_out["arow_o"], mrow_i=pre_out["mrow_o"], lcrow_i=pre_out["lcrow_o"],
        gscal_i=pre_out["gscal_o"], Mc_p=np.ascontiguousarray(Mc_p),
        ncv_p=np.ascontiguousarray(ncv_p), pscal=pscal,
        w_downT=np.ascontiguousarray(np.asarray(inputs["L1_attn_w_down"], np.float32).T),
        vecs_in=vecs, sel4_in=sel4, mdiag_in=mdiag)


def _l1f_consts(inputs, li):
    vecs = np.zeros((128, NV0), np.float32)
    ln_g = np.asarray(inputs["L1_ffn_ln_g"][li], np.float32)
    ln_b = np.asarray(inputs["L1_ffn_ln_b"][li], np.float32)
    b1 = np.asarray(inputs["L1_ffn_b1"][li], np.float32)
    b2 = np.asarray(inputs["L1_ffn_b2"][li], np.float32)
    qs = np.asarray(inputs["L1_ffn_q_scale"][li], np.float32)
    ks = np.asarray(inputs["L1_ffn_k_scale"][li], np.float32)
    for c in range(8):
        vecs[:, c] = ln_g[c * 128:(c + 1) * 128]
        vecs[:, 8 + c] = ln_b[c * 128:(c + 1) * 128]
        vecs[:, 60 + c] = b2[c * 128:(c + 1) * 128]
    b1p = np.zeros(5632, np.float32)
    b1p[0:2730] = b1[0:2730]
    b1p[2816:2816 + 2730] = b1[2730:5460]
    for c in range(44):
        vecs[:, 16 + c] = b1p[c * 128:(c + 1) * 128]
    vecs[:, 68] = np.tile(qs, 2)
    vecs[:, 69] = np.tile(ks, 2)
    W1 = np.asarray(inputs["L1_ffn_w1"][li], np.float32)
    w1p = np.zeros((D, 5632), np.float32)
    w1p[:, 0:2730] = W1[:, 0:2730]
    w1p[:, 2816:2816 + 2730] = W1[:, 2730:5460]
    W2 = np.asarray(inputs["L1_ffn_w2"][li], np.float32)
    w2p = np.zeros((2816, 1024), np.float32)
    w2p[0:2730, :] = W2
    return vecs, np.ascontiguousarray(w1p), np.ascontiguousarray(w2p)


def l1f_masks(has_prev):
    NKC = 8
    out = np.zeros((2, NKC, 128, TT), np.float32)
    q_rel = np.arange(TT)
    for kc in range(NKC):
        k_rel = np.arange(128) + kc * 128 - 512
        dmat = q_rel[None, :] - k_rel[:, None]
        invalid = (dmat < 0) | (dmat > 512)
        out[0, kc] = invalid
        inv0 = invalid.copy()
        if not has_prev and kc < 4:
            inv0[:, :] = 1.0
        out[1, kc] = inv0
    return np.ascontiguousarray(out)


def rope_tables_l1(own0):
    pos = np.arange(-L0F_HALO, SH) + own0
    pos = np.clip(pos, 0, None)
    aq, bq, ak, bk = rope_tables(S, 64, 512, pos, 8.0)
    return [np.ascontiguousarray(np.tile(t, (2, 1))) for t in (aq, bq, ak, bk)]


def prep_l1f_attn(inputs, c, li):
    ci = core_info(c)
    own0 = ci["h"] * SH
    aq, bq, ak, bk = rope_tables_l1(own0)
    ropes = np.stack([aq, bq, ak, bk])
    hselA = np.zeros((128, 2), np.float32)
    hselA[0:64, 0] = 1.0
    hselA[64:128, 1] = 1.0
    hselB = np.zeros((2, 128), np.float32)
    hselB[0, 0:64] = 1.0
    hselB[1, 64:128] = 1.0
    vecs, _, _ = _l1f_consts(inputs, li)
    return dict(
        wqkv=np.ascontiguousarray(np.asarray(inputs["L1_ffn_wqkv"][li], np.float32)),
        wo=np.ascontiguousarray(np.asarray(inputs["L1_ffn_wo"][li], np.float32)),
        qsrow=np.asarray(inputs["L1_ffn_q_scale"][li], np.float32).reshape(1, 64),
        ksrow=np.asarray(inputs["L1_ffn_k_scale"][li], np.float32).reshape(1, 64),
        vecs_in=vecs, ropes=ropes, hselA_in=hselA, hselB_in=hselB,
        rotm_in=rot_matrix(64, 2), idneg_in=np.eye(128, dtype=np.float32) * -1e5,
        masks_in=l1f_masks(ci["has_prev"]))


def prep_l1f_ffn(inputs, c, li):
    vecs, w1p, w2p = _l1f_consts(inputs, li)
    return dict(w1p=w1p, w2p=w2p, vecs_in=vecs)


_BUILD_CACHE = {}


def _get(name, builder):
    if name not in _BUILD_CACHE:
        _BUILD_CACHE[name] = builder()
    return _BUILD_CACHE[name]


def _run(name, builder, in_maps):
    nc, _ = _get(name, builder)
    res = run_bass_kernel_spmd(nc, in_maps, core_ids=list(range(8)))
    return res.results


def kernel(**inputs):
    cores = list(range(8))
    cis = [core_info(c) for c in cores]
    r1 = _run("b0", build_b0_lnattn, [prep_b0_lnattn(inputs, c) for c in cores])
    xn0b = [r1[c]["xn0b"] for c in cores]
    xn0f = [r1[c]["xn0f"] for c in cores]
    s0 = [r1[c]["s0"] for c in cores]
    xcur = [np.ascontiguousarray(xn0f[c][:, S:]) for c in cores]
    for li in range(6):
        ins = []
        for c in cores:
            hh = cis[c]["h"]
            if hh == 1:
                halo = xcur[c - 1][:, SH - L0F_HALO:]
            else:
                halo = np.zeros((D, L0F_HALO), np.float32)
            xw = np.concatenate([halo, xcur[c]], axis=1)
            m = prep_l0f_attn(inputs, c, li)
            m["xw"] = np.ascontiguousarray(xw)
            ins.append(m)
        ra = _run("l0f_attn", build_l0f_attn, ins)
        xp = [ra[c]["xo"] for c in cores]
        ins = []
        for c in cores:
            hh = cis[c]["h"]
            h1 = xp[c - 1][:, SH - 1:SH] if hh == 1 else np.zeros((D, 1), np.float32)
            xw1 = np.concatenate([h1, xp[c]], axis=1)
            m = prep_l0f_ffn(inputs, c, li)
            m["xw1"] = np.ascontiguousarray(xw1)
            ins.append(m)
        rf = _run("l0f_ffn", build_l0f_ffn, ins)
        xcur = [np.ascontiguousarray(rf[c]["xo"]) for c in cores]
    ins = []
    for c in cores:
        hh = cis[c]["h"]
        x0T_own = np.ascontiguousarray(
            np.asarray(inputs["L0_attn_x"][cis[c]["b"]], np.float32).T[:, hh * SH:(hh + 1) * SH])
        ins.append(dict(xT=x0T_own, sT=np.ascontiguousarray(s0[c]), gT=xcur[c]))
    rc = _run("combine", build_combine, ins)
    x1 = [np.ascontiguousarray(rc[c]["xo"]) for c in cores]
    ins = []
    for c in cores:
        hh = cis[c]["h"]
        halo = x1[c - 1][:, SH - 513:] if hh == 1 else np.zeros((D, 513), np.float32)
        xw = np.concatenate([np.zeros((D, 511), np.float32), halo, x1[c]], axis=1)
        ins.append(dict(xT=np.ascontiguousarray(xw)))
    rl = _run("ln_b1", lambda: build_ln(1024 + SH), ins)
    xn1 = [rl[c]["xnf"] for c in cores]
    ins = []
    for c in cores:
        m = prep_l1_pre(inputs)
        m["xnw"] = np.ascontiguousarray(xn1[c])
        ins.append(m)
    rp = _run("l1_pre", build_l1_pre, ins)
    ins = []
    for c in cores:
        hh = cis[c]["h"]
        partner = rp[c - 1] if hh == 1 else None
        ins.append(prep_l1_mlstm(inputs, rp[c], partner, hh == 1))
    rm = _run("l1_mlstm", build_l1_mlstm, ins)
    s1 = [rm[c]["s1_o"] for c in cores]
    xcur = [np.ascontiguousarray(xn1[c][:, 1024:]) for c in cores]
    for li in range(2):
        ins = []
        for c in cores:
            hh = cis[c]["h"]
            if li == 0:
                xw = np.ascontiguousarray(xn1[c][:, 1024 - L0F_HALO:])
            else:
                halo = xcur[c - 1][:, SH - L0F_HALO:] if hh == 1 else np.zeros((D, L0F_HALO), np.float32)
                xw = np.ascontiguousarray(np.concatenate([halo, xcur[c]], axis=1))
            m = prep_l1f_attn(inputs, c, li)
            m["xw"] = xw
            ins.append(m)
        ra = _run("l1f_attn", build_l1f_attn, ins)
        xp = [ra[c]["xo"] for c in cores]
        ins = []
        for c in cores:
            hh = cis[c]["h"]
            h1 = xp[c - 1][:, SH - 1:SH] if hh == 1 else np.zeros((D, 1), np.float32)
            xw1 = np.concatenate([h1, xp[c]], axis=1)
            m = prep_l1f_ffn(inputs, c, li)
            m["xw1"] = np.ascontiguousarray(xw1)
            ins.append(m)
        rf = _run("l0f_ffn", build_l0f_ffn, ins)
        xcur = [np.ascontiguousarray(rf[c]["xo"]) for c in cores]
    ins = []
    for c in cores:
        ins.append(dict(xT=x1[c], sT=np.ascontiguousarray(s1[c]), gT=xcur[c]))
    rc2 = _run("combine", build_combine, ins)
    out = np.empty((B, S, D), np.float32)
    for c in cores:
        b, hh = cis[c]["b"], cis[c]["h"]
        out[b, hh * SH:(hh + 1) * SH, :] = rc2[c]["xo"].T
    return out


L0H = L0F_HALO
RG_PAIRS = [[0, 1], [2, 3], [4, 5], [6, 7]]
RG_ALL = [list(range(8))]


def _pack_table():
    t = [("pk", (D, 128)), ("pv", (D, 128)), ("pq", (8, D, 128)), ("poT", (8, 128, D)),
         ("rotm", (128, 128)), ("idneg", (128, 128))]
    for li in range(6):
        t += [(f"wqkv0_{li}", (D, 3072)), (f"wo0_{li}", (1024, D)),
              (f"w1p0_{li}", (D, 5632)), (f"w2p0_{li}", (2816, 1024))]
    t += [("w_upT", (D, 4096)), ("wqd", (16, 128, 128)), ("wkd", (16, 128, 128)),
          ("wvd", (16, 128, 128)), ("wigT", (6144, 4)), ("wfgT", (6144, 4)),
          ("w_downT", (2048, 1024))]
    for li in range(2):
        t += [(f"wqkv1_{li}", (D, 1536)), (f"wo1_{li}", (512, D)),
              (f"w1p1_{li}", (D, 5632)), (f"w2p1_{li}", (2816, 1024))]
    offs = {}
    off = 0
    for nm, shp in t:
        offs[nm] = off
        off += int(np.prod(shp))
    pad = (-off) % (8 * 128 * 4096)
    total = off + pad
    return t, offs, total


def _wview(wb_handle, offs, nm, shape):
    dims = []
    strides = []
    st = 1
    for s_ in reversed(shape):
        strides.append(st)
        st *= s_
    strides = list(reversed(strides))
    for s_, stx in zip(shape, strides):
        dims.append([stx, s_])
    return bass.AP(wb_handle, offs[nm], dims)


def build_mega():
    nc = bass.Bass("TRN2", target_bir_lowering=False)
    es = ExitStack()
    ctx = Ctx(nc, es)
    rg = Regs(nc, es)
    ptab, offs, WTOT = _pack_table()
    LS = WTOT // 8

    EI = lambda nm, shp, dt_=F32: nc.dram_tensor(nm, list(shp), dt_, kind="ExternalInput").ap()
    x0T = EI("x0T", [D, S + SH])
    wshard = EI("wshard", [1, LS])
    vecs0 = EI("vecs0", [6, 128, NV0])
    vecs1f = EI("vecs1f", [2, 128, NV0])
    vecsl1 = EI("vecsl1", [128, NV1])
    vecsm = EI("vecsm", [128, NV2])
    qsk0 = EI("qsk0", [6, 2, 64])
    qsk1 = EI("qsk1", [2, 2, 64])
    hselA_d = EI("hselA_d", [128, 2])
    hselB_d = EI("hselB_d", [2, 128])
    sel4_d = EI("sel4_d", [4, 512])
    masks0 = EI("masks0", [2, 6, 128, TT])
    masks1 = EI("masks1", [2, 8, 128, TT])
    mdiag_d = EI("mdiag_d", [4, 128, TT])
    ropes0 = EI("ropes0", [4, 128, L0H + SH])
    ropes1 = EI("ropes1", [4, 128, L0H + SH])
    pcc = EI("pcc", [128, 4])
    out = nc.dram_tensor("out", [D, SH], F32, kind="ExternalOutput").ap()

    IN = lambda nm, shp, dt_=F32: nc.dram_tensor(nm, list(shp), dt_, kind="Internal")
    wsh_bf_t = IN("wsh_bf", [1, LS], BF16)
    wbig_t = IN("wbig", [8 * LS], BF16)
    xn0f_t = IN("xn0f_s", [D, S + SH])
    xn0b_t = IN("xn0b_s", [D, S + SH], BF16)
    s0_t = IN("s0_s", [D, SH])
    xH_t = IN("xH_s", [D, L0H])
    xB_t = IN("xB_s", [D, 1 + SH])
    xC_t = IN("xC_s", [D, SH])
    x1_t = IN("x1_s", [D, SH])
    xn1f_t = IN("xn1f_s", [D, 1024 + SH])
    xn1b_t = IN("xn1b_s", [D, 1024 + SH], BF16)
    xc_t = IN("xc_s", [2048, SH], BF16)
    z_t = IN("z_s", [2048, SH], BF16)
    qf_t = IN("qf_s", [2048, SH], BF16)
    kf_t = IN("kf_s", [2048, SH], BF16)
    vtok_t = IN("vtok_s", [SH, 2048], BF16)
    ktok_t = IN("ktok_s", [SH, 2048], BF16)
    arow_t = IN("arow_s", [4, SH])
    mrow_t = IN("mrow_s", [4, SH])
    lcrow_t = IN("lcrow_s", [4, SH])
    gscal_t = IN("gscal_s", [4, 8])
    Mc_t = IN("Mc_s", [2048, 512])
    ncv_t = IN("ncv_s", [128, 16])
    pscal_t = IN("pscal_s", [4, 8])
    s1_t = IN("s1_s", [D, SH])
    agA_in = IN("agA_in", [D, L0H])
    agA_out = IN("agA_out", [2 * D, L0H])
    agB_in = IN("agB_in", [D, 1])
    agB_out = IN("agB_out", [2 * D, 1])
    agS_mc_out = IN("agS_mc_out", [2 * 2048, 512])
    agS_nc_out = IN("agS_nc_out", [2 * 128, 16])
    agS_gs_out = IN("agS_gs_out", [2 * 4, 8])

    wb = wbig_t
    WV = lambda nm, shp: _wview(wb, offs, nm, shp)

    emit_setup(ctx, rg)
    valid = rg.sb("pc_sb", [128, 4])
    ctx.dma("sync", valid[:, :], pcc)
    zt = rg.sb("zt", [128, 512])
    ctx.op("vector", lambda e: e.memset(zt[:, :], 0.0), writes=[zt[:, :]])
    hb = rg.sb("hg_hb", [128, L0H])
    rg.sb("masks_sb_8", [128, 2 * 6 * TT], BF16)
    rg.sb("rotm", [128, 128], BF16)
    rg.sb("idneg", [128, 128], BF16)

    def halo_mul(dst, srcd, width, tag=""):
        kw = {"allow_slow_non_contiguous": True} if width == 1 else {}
        for c in range(NCH):
            ctx.dma("sync", hb[:, 0:width], srcd[c * 128:(c + 1) * 128, 0:width], **kw)
            ctx.op("vector", lambda e, c=c, width=width: e.tensor_scalar_mul(
                hb[:, 0:width], hb[:, 0:width], valid[:, 0:1]),
                reads=[hb[:, 0:width], valid[:, 0:1]], writes=[hb[:, 0:width]])
            ctx.dma("scalar", dst[c * 128:(c + 1) * 128, 0:width], hb[:, 0:width], **kw)

    rg.push()
    nchunk = LS // (128 * 4096)
    for i in range(nchunk):
        lo = i * 128 * 4096
        ctx.dma("gpsimd", bass.AP(wsh_bf_t, lo, [[4096, 128], [1, 4096]]),
                bass.AP(wshard.tensor, lo, [[4096, 128], [1, 4096]]))
    ctx.collective("AllGather", ALU.bypass, RG_ALL,
                   [wsh_bf_t.ap()], [bass.AP(wbig_t, 0, [[LS, 8], [1, LS]])])
    rg.pop()
    ctx.barrier()

    rg.push()
    io = dict(x0T=x0T, pk=WV("pk", (D, 128)), pv=WV("pv", (D, 128)),
              pq=WV("pq", (8, D, 128)), poT=WV("poT", (8, 128, D)),
              xn0f=xn0f_t.ap(), xn0b=xn0b_t.ap(), s0=s0_t.ap())
    emit_b0_lnattn(ctx, rg, io)
    rg.pop()
    ctx.barrier()

    for li in range(6):
        ctx.barrier()
        if li == 0:
            halo_mul(xH_t.ap(), xn0f_t.ap()[:, SH - L0H:SH], L0H)
            body_ap = xn0f_t.ap()[:, S:S + SH]
        else:
            halo_mul(xH_t.ap(), agA_out.ap()[0:D, :], L0H)
            body_ap = xC_t.ap()
        rg.push()
        io = dict(xw=body_ap, xw_halo=xH_t.ap(),
                  wqkv=WV(f"wqkv0_{li}", (D, 3072)), wo=WV(f"wo0_{li}", (1024, D)),
                  ropes=ropes0, qsrow=qsk0[li, 0:1, :], ksrow=qsk0[li, 1:2, :],
                  vecs_in=vecs0[li], hselA_in=hselA_d, hselB_in=hselB_d,
                  rotm_in=WV("rotm", (128, 128)), idneg_in=WV("idneg", (128, 128)),
                  masks_in=masks0, xo=xB_t.ap()[:, 1:1 + SH], load_consts=(li == 0))
        emit_lf_attn(ctx, rg, io, 8, 256)
        rg.pop()
        ctx.dma("gpsimd", agA_in.ap(), xB_t.ap()[:, 1 + SH - L0H:1 + SH])
        ctx.collective("AllGather", ALU.bypass, RG_PAIRS, [agA_in.ap()], [agA_out.ap()])
        ctx.barrier()
        halo_mul(xB_t.ap()[:, 0:1], agA_out.ap()[0:D, L0H - 1:L0H], 1)
        rg.push()
        io = dict(xw1=xB_t.ap(), w1p=WV(f"w1p0_{li}", (D, 5632)), w2p=WV(f"w2p0_{li}", (2816, 1024)),
                  vecs_in=vecs0[li], xo=xC_t.ap())
        emit_lf_ffn(ctx, rg, io)
        rg.pop()
        if li < 5:
            ctx.dma("gpsimd", agA_in.ap(), xC_t.ap()[:, SH - L0H:SH])
            ctx.collective("AllGather", ALU.bypass, RG_PAIRS, [agA_in.ap()], [agA_out.ap()])
        ctx.barrier()

    rg.push()
    io = dict(xT=x0T[:, S:S + SH], sT=s0_t.ap(), gT=xC_t.ap(), xo=x1_t.ap())
    emit_combine(ctx, rg, io)
    rg.pop()
    ctx.dma("gpsimd", agA_in.ap(), x1_t.ap()[:, SH - L0H:SH])
    ctx.collective("AllGather", ALU.bypass, RG_PAIRS, [agA_in.ap()], [agA_out.ap()])
    ctx.barrier()

    lnin_t = IN("lnin_s", [D, 1024 + SH])
    for c in range(NCH):
        ctx.dma("scalar", lnin_t.ap()[c * 128:(c + 1) * 128, 0:512], zt[:, :])
    halo_mul(lnin_t.ap()[:, 511:1024], agA_out.ap()[0:D, :], L0H, tag="hg3")
    for c in range(NCH):
        ctx.dma("gpsimd", lnin_t.ap()[c * 128:(c + 1) * 128, 1024:1024 + SH],
                x1_t.ap()[c * 128:(c + 1) * 128, :])
    ctx.barrier()
    rg.push()
    io = dict(xT=lnin_t.ap(), xnf=xn1f_t.ap(), xnb=xn1b_t.ap())
    emit_ln_pass(ctx, rg, io, 1024 + SH)
    rg.pop()
    ctx.barrier()

    rg.push()
    io = dict(xnw=xn1f_t.ap(), w_upT=WV("w_upT", (D, 4096)),
              wqd=WV("wqd", (16, 128, 128)), wkd=WV("wkd", (16, 128, 128)),
              wvd=WV("wvd", (16, 128, 128)), wigT=WV("wigT", (6144, 4)),
              wfgT=WV("wfgT", (6144, 4)), vecs_in=vecsl1,
              xc_o=xc_t.ap(), z_o=z_t.ap(), qf_o=qf_t.ap(), kf_o=kf_t.ap(),
              vtok_o=vtok_t.ap(), ktok_o=ktok_t.ap(), arow_o=arow_t.ap(),
              mrow_o=mrow_t.ap(), lcrow_o=lcrow_t.ap(), gscal_o=gscal_t.ap(),
              Mc_o=Mc_t.ap(), ncv_o=ncv_t.ap())
    emit_l1_pre(ctx, rg, io)
    rg.pop()
    ctx.collective("AllGather", ALU.bypass, RG_PAIRS, [Mc_t.ap()], [agS_mc_out.ap()])
    ctx.collective("AllGather", ALU.bypass, RG_PAIRS, [ncv_t.ap()], [agS_nc_out.ap()])
    ctx.collective("AllGather", ALU.bypass, RG_PAIRS, [gscal_t.ap()], [agS_gs_out.ap()])
    pg = rg.sb("pg", [4, 8])
    ctx.dma("sync", pg[0:4, :], agS_gs_out.ap()[0:4, :])
    ctx.op("vector", lambda e: e.tensor_scalar_mul(pg[0:4, 0:2], pg[0:4, 0:2], valid[0:4, 0:1]),
           reads=[pg[0:4, 0:2], valid[0:4, 0:1]], writes=[pg[0:4, 0:2]])
    ctx.op("vector", lambda e: e.tensor_scalar_mul(pg[0:4, 2:3], valid[0:4, 1:2], -1e30),
           reads=[valid[0:4, 1:2]], writes=[pg[0:4, 2:3]])
    ctx.op("vector", lambda e: e.tensor_add(pg[0:4, 1:2], pg[0:4, 1:2], pg[0:4, 2:3]),
           reads=[pg[0:4, 1:3]], writes=[pg[0:4, 1:2]])
    ctx.dma("scalar", pscal_t.ap()[0:4, :], pg[0:4, 0:8])
    ctx.barrier()

    rg.push()
    io = dict(qf_i=qf_t.ap(), kf_i=kf_t.ap(), vtok_i=vtok_t.ap(), xc_i=xc_t.ap(),
              z_i=z_t.ap(), arow_i=arow_t.ap(), mrow_i=mrow_t.ap(), lcrow_i=lcrow_t.ap(),
              gscal_i=gscal_t.ap(), Mc_p=agS_mc_out.ap()[0:2048, :],
              ncv_p=agS_nc_out.ap()[0:128, :], pscal=pscal_t.ap(),
              w_downT=WV("w_downT", (2048, 1024)), vecs_in=vecsm, sel4_in=sel4_d,
              mdiag_in=mdiag_d, s1_o=s1_t.ap())
    emit_l1_mlstm(ctx, rg, io)
    rg.pop()
    ctx.barrier()

    for li in range(2):
        if li == 0:
            xw_ap = xn1f_t.ap()[:, 1024:1024 + SH]
            xwh_ap = xn1f_t.ap()[:, 1024 - L0H:1024]
        else:
            ctx.barrier()
            halo_mul(xH_t.ap(), agA_out.ap()[0:D, :], L0H)
            xw_ap = xC_t.ap()
            xwh_ap = xH_t.ap()
        rg.push()
        io = dict(xw=xw_ap, xw_halo=xwh_ap,
                  wqkv=WV(f"wqkv1_{li}", (D, 1536)), wo=WV(f"wo1_{li}", (512, D)),
                  ropes=ropes1, qsrow=qsk1[li, 0:1, :], ksrow=qsk1[li, 1:2, :],
                  vecs_in=vecs1f[li], hselA_in=hselA_d, hselB_in=hselB_d,
                  rotm_in=WV("rotm", (128, 128)), idneg_in=WV("idneg", (128, 128)),
                  masks_in=masks1, xo=xB_t.ap()[:, 1:1 + SH], load_consts=True)
        emit_lf_attn(ctx, rg, io, 4, 512)
        rg.pop()
        ctx.dma("gpsimd", agA_in.ap(), xB_t.ap()[:, 1 + SH - L0H:1 + SH])
        ctx.collective("AllGather", ALU.bypass, RG_PAIRS, [agA_in.ap()], [agA_out.ap()])
        ctx.barrier()
        halo_mul(xB_t.ap()[:, 0:1], agA_out.ap()[0:D, L0H - 1:L0H], 1)
        rg.push()
        io = dict(xw1=xB_t.ap(), w1p=WV(f"w1p1_{li}", (D, 5632)), w2p=WV(f"w2p1_{li}", (2816, 1024)),
                  vecs_in=vecs1f[li], xo=xC_t.ap())
        emit_lf_ffn(ctx, rg, io)
        rg.pop()
        if li == 0:
            ctx.dma("gpsimd", agA_in.ap(), xC_t.ap()[:, SH - L0H:SH])
            ctx.collective("AllGather", ALU.bypass, RG_PAIRS, [agA_in.ap()], [agA_out.ap()])
        ctx.barrier()

    rg.push()
    io = dict(xT=x1_t.ap(), sT=s1_t.ap(), gT=xC_t.ap(), xo=out)
    emit_combine(ctx, rg, io)
    rg.pop()
    ctx.barrier(cc=True)
    ctx.finalize()
    return nc, es


def build_flat_pack(inputs):
    ptab, offs, WTOT = _pack_table()
    flat = np.zeros(WTOT, np.float32)

    def put(nm, arr):
        a = np.ascontiguousarray(np.asarray(arr, np.float32)).ravel()
        flat[offs[nm]:offs[nm] + a.size] = a
    put("pk", inputs["L0_attn_proj_k"])
    put("pv", inputs["L0_attn_proj_v"])
    put("pq", inputs["L0_attn_proj_q"])
    put("poT", np.asarray(inputs["L0_attn_proj_o"], np.float32).transpose(0, 2, 1))
    put("rotm", rot_matrix(64, 2))
    put("idneg", np.eye(128, dtype=np.float32) * -1e5)
    for li in range(6):
        put(f"wqkv0_{li}", inputs["L0_ffn_Wqkv"][li])
        put(f"wo0_{li}", inputs["L0_ffn_Wo"][li])
        _, w1p, w2p = _l0f_consts(inputs, li)
        put(f"w1p0_{li}", w1p)
        put(f"w2p0_{li}", w2p)
    put("w_upT", np.asarray(inputs["L1_attn_w_up"], np.float32).T)
    l1p = prep_l1_pre(inputs)
    put("wqd", l1p["wqd"])
    put("wkd", l1p["wkd"])
    put("wvd", l1p["wvd"])
    put("wigT", l1p["wigT"])
    put("wfgT", l1p["wfgT"])
    put("w_downT", np.asarray(inputs["L1_attn_w_down"], np.float32).T)
    for li in range(2):
        put(f"wqkv1_{li}", inputs["L1_ffn_wqkv"][li])
        put(f"wo1_{li}", inputs["L1_ffn_wo"][li])
        _, w1p, w2p = _l1f_consts(inputs, li)
        put(f"w1p1_{li}", w1p)
        put(f"w2p1_{li}", w2p)
    return flat


def prep_mega(inputs, c, flat):
    ci = core_info(c)
    b, hh = ci["b"], ci["h"]
    _, _, WTOT = _pack_table()
    LS = WTOT // 8
    shard = np.ascontiguousarray(flat[c * LS:(c + 1) * LS].reshape(1, LS))

    x0 = np.asarray(inputs["L0_attn_x"][b], np.float32)
    x0T = np.ascontiguousarray(x0.T)
    own = x0T[:, hh * SH:(hh + 1) * SH]
    x0T_ext = np.ascontiguousarray(np.concatenate([x0T, own], axis=1))

    vecs0 = np.stack([_l0f_consts(inputs, li)[0] for li in range(6)])
    vecs1f = np.stack([_l1f_consts(inputs, li)[0] for li in range(2)])
    qsk0 = np.stack([np.stack([np.asarray(inputs["L0_ffn_q_scale"][li], np.float32),
                               np.asarray(inputs["L0_ffn_k_scale"][li], np.float32)])
                     for li in range(6)])
    qsk1 = np.stack([np.stack([np.asarray(inputs["L1_ffn_q_scale"][li], np.float32),
                               np.asarray(inputs["L1_ffn_k_scale"][li], np.float32)])
                     for li in range(2)])
    hselA = np.zeros((128, 2), np.float32)
    hselA[0:64, 0] = 1.0
    hselA[64:128, 1] = 1.0
    hselB = np.zeros((2, 128), np.float32)
    hselB[0, 0:64] = 1.0
    hselB[1, 64:128] = 1.0
    sel4 = np.zeros((4, 512), np.float32)
    for h in range(4):
        sel4[h, h * 128:(h + 1) * 128] = 1.0
    mdiag = np.zeros((4, 128, TT), np.float32)
    for p in range(4):
        k_rel = np.arange(128) + p * 128
        q_rel = np.arange(TT)
        mdiag[p] = (k_rel[:, None] <= q_rel[None, :]).astype(np.float32)
    own0 = hh * SH
    r0 = np.stack(rope_tables_l0(own0))
    r1 = np.stack(rope_tables_l1(own0))
    pc = np.zeros((128, 4), np.float32)
    pc[:, 0] = 1.0 if ci["has_prev"] else 0.0
    pc[:, 1] = 0.0 if ci["has_prev"] else 1.0
    return dict(x0T=x0T_ext, wshard=shard, vecs0=vecs0, vecs1f=vecs1f,
                vecsl1=prep_l1_pre(inputs)["vecs_in"], vecsm=prep_l1_mlstm_vecs(inputs),
                qsk0=qsk0, qsk1=qsk1, hselA_d=hselA, hselB_d=hselB, sel4_d=sel4,
                masks0=l0f_masks(ci["has_prev"]), masks1=l1f_masks(ci["has_prev"]),
                mdiag_d=mdiag, ropes0=r0, ropes1=r1, pcc=pc)


def prep_l1_mlstm_vecs(inputs):
    NI2 = 16
    vecs = np.zeros((128, NV2), np.float32)
    lnw = np.asarray(inputs["L1_attn_ln_w"], np.float32)
    skip = np.asarray(inputs["L1_attn_skip"], np.float32)
    for h in range(4):
        for dc in range(4):
            vecs[:, h * 4 + dc] = lnw[h, dc * 128:(dc + 1) * 128]
    for c2 in range(NI2):
        vecs[:, 16 + c2] = skip[c2 * 128:(c2 + 1) * 128]
    return vecs


_MEGA_CACHE = {}


def kernel(**inputs):
    if "nc" not in _MEGA_CACHE:
        _MEGA_CACHE["nc"], _MEGA_CACHE["es"] = build_mega()
    nc = _MEGA_CACHE["nc"]
    flat = build_flat_pack(inputs)
    in_maps = [prep_mega(inputs, c, flat) for c in range(8)]
    res = run_bass_kernel_spmd(nc, in_maps, core_ids=list(range(8)))
    out = np.empty((B, S, D), np.float32)
    for c in range(8):
        ci = core_info(c)
        out[ci["b"], ci["h"] * SH:(ci["h"] + 1) * SH, :] = res.results[c]["out"].T
    return out
